# version 01
# baseline (speedup 1.0000x reference)
import numpy as np
from contextlib import ExitStack
import concourse.bass as bass
import concourse.mybir as mybir
from concourse.bass import ds
from concourse.bass_utils import run_bass_kernel_spmd

F32 = mybir.dt.float32
BF16 = mybir.dt.bfloat16
I32 = mybir.dt.int32
AX = mybir.AxisListType.X
ALU = mybir.AluOpType
AF = mybir.ActivationFunctionType

D = 2048
SEQ = 4096
TPC = 1024
NT = TPC // 128
KC = D // 128
DKV = 512
NKV = DKV // 128
DQ = 1536
NQ = DQ // 128
NSLOT = 4
NTS = SEQ // 512
INTER = 5632
IC = INTER // 128
GRP = 6
ROPE_BASE = 10000.0
GROUPS = [[0, 1, 2, 3], [4, 5, 6, 7]]
NEG = -30000.0


class Plan:
    ENGINES = ("sync", "gpsimd", "scalar", "vector", "tensor")

    def __init__(self):
        self.q = {e: [] for e in self.ENGINES}
        self.cnt = {}

    def val(self, sem):
        return self.cnt.get(sem, 0)

    def bump(self, sem, n):
        self.cnt[sem] = self.val(sem) + n
        return self.cnt[sem]

    def emit(self, eng, f):
        self.q[eng].append(f)

    def wait(self, eng, sem, v=None):
        v = self.val(sem) if v is None else v
        if v <= 0:
            return
        self.emit(eng, lambda e, s=sem, vv=v: e.wait_ge(SEM[s], vv))

    def dma(self, eng, out, in_, sem=None, **kw):
        sem = sem or ("gd" if eng == "gpsimd" else ("ad" if eng == "scalar" else "sd"))
        t = self.bump(sem, 16)
        self.emit(eng, lambda e, o=out, i=in_, s=sem, k=kw: e.dma_start(out=o, in_=i, **k).then_inc(SEM[s], 16))
        return t

    def op(self, eng, sem, build):
        t = self.bump(sem, 1)
        self.emit(eng, lambda e, b=build, s=sem: b(e).then_inc(SEM[s], 1))
        return t

    def barrier(self, engines=None):
        snap = {s: v for s, v in self.cnt.items() if v > 0}
        for eng in (engines or self.ENGINES):
            for s, v in snap.items():
                self.wait(eng, s, v)


SEM = {}


def ln_chain(p, pref, src, bn6, mv, stdc, rstd, nb, eps_c, out, out_ap_fn):
    for j in range(4):
        p.op("vector", "ve", lambda v, j=j: v.bn_stats(bn6[:, 6 * j:6 * j + 6], src[:, 512 * j:512 * (j + 1)]))
    p.wait("vector", "ve")
    p.op("vector", "ve", lambda v: v.bn_aggr(mv[:], bn6[:]))
    p.wait("scalar", "ve")
    sq = p.op("scalar", "ac", lambda s: s.activation(stdc[:], mv[:, 1:2], AF.Sqrt, bias=eps_c[:], scale=1.0))
    p.wait("vector", "ac", sq)
    p.op("vector", "ve", lambda v: v.reciprocal(rstd[:], stdc[:]))
    p.wait("vector", "ve")
    p.op("vector", "ve", lambda v: v.scalar_tensor_tensor(nb[:], mv[:, 0:1], -1.0, rstd[:], ALU.mult, ALU.mult))
    p.wait("scalar", "ve")
    return p.op("scalar", "ac", out_ap_fn)


def build_nc(phases=3, taps=(), serial=False):
    nc = bass.Bass("TRN2", target_bir_lowering=False)
    p = Plan()

    def din(name, shape, dt=F32):
        return nc.dram_tensor(name, shape, dt, kind="ExternalInput").ap()

    x_in = din("x", [TPC, D])
    meta = din("meta", [1, 1], I32)
    wdkv = din("wdkv", [D, DKV])
    bdkv = din("bdkv", [128, NKV])
    wdq = din("wdq", [D, DQ])
    bdq = din("bdq", [128, NQ])
    wkr = din("wkr", [D, 512])
    bkr = din("bkr", [128, NSLOT])
    wuk = din("wuk", [DKV, 512])
    buk = din("buk", [128, NSLOT])
    wuv = din("wuv", [DKV, 512])
    buv = din("buv", [128, NSLOT])
    wuq = din("wuq", [DQ, 512])
    buq = din("buq", [128, NSLOT])
    wo = din("wo", [D, D])
    bo_r = din("bo_r", [1, D])
    gwt = din("gwt", [D, INTER])
    upt = din("upt", [D, INTER])
    dwt = din("dwt", [INTER, D])
    gs = din("gs", [128, IC])
    us = din("us", [128, IC])
    ds_r = din("ds_r", [1, D])
    identc = din("identc", [128, 128])
    rotmc = din("rotmc", [128, 128])
    cos_td = din("cos_t", [128, SEQ])
    sin_td = din("sin_t", [128, SEQ])
    mask_t = din("mask_t", [128, 4 * 512])
    ones_cd = din("ones_c", [128, 1])
    ones_rd = din("ones_r", [1, 128])
    eps_cd = din("eps_cd", [128, 1])

    out = nc.dram_tensor("out", [TPC, D], F32, kind="ExternalOutput").ap()

    shard = nc.dram_tensor("shard", [4096, TPC], BF16, kind="Internal").ap()
    l_all = nc.dram_tensor("l_all", [4 * 4096, TPC], BF16, kind="Internal").ap()
    q_dram = nc.dram_tensor("q_dram", [NSLOT * 256, SEQ], BF16, kind="Internal").ap()
    ctx_shard = nc.dram_tensor("ctx_shard", [512, SEQ], BF16, kind="Internal").ap()
    ctx_all = nc.dram_tensor("ctx_all", [2048, SEQ], BF16, kind="Internal").ap()
    x2_dram = nc.dram_tensor("x2_dram", [TPC, D], F32, kind="Internal").ap()

    tap_aps = {}
    if "shard" in taps:
        tap_aps["t_shard"] = nc.dram_tensor("t_shard", [4 * 4096, TPC], F32, kind="ExternalOutput").ap()
    if "qdram" in taps:
        tap_aps["t_qdram"] = nc.dram_tensor("t_qdram", [NSLOT * 256, SEQ], F32, kind="ExternalOutput").ap()
    if "ctx" in taps:
        tap_aps["t_ctx"] = nc.dram_tensor("t_ctx", [2048, SEQ], F32, kind="ExternalOutput").ap()
    if "x2" in taps:
        tap_aps["t_x2"] = nc.dram_tensor("t_x2", [TPC, D], F32, kind="ExternalOutput").ap()
    if "xn2" in taps:
        tap_aps["t_xn2"] = nc.dram_tensor("t_xn2", [128, KC * TPC], F32, kind="ExternalOutput").ap()
    if "acc" in taps:
        tap_aps["t_acc"] = nc.dram_tensor("t_acc", [128, NT * D], F32, kind="ExternalOutput").ap()

    st = ExitStack()
    sbuf = lambda n, s, d: st.enter_context(nc.sbuf_tensor(n, s, d))

    ident_bf = sbuf("ident_bf", [128, 128], BF16)
    rotm_bf = sbuf("rotm_bf", [128, 128], BF16)
    ones_cb = sbuf("ones_cb", [128, 1], BF16)
    ones_rf = sbuf("ones_rf", [1, 128], F32)
    eps_c = sbuf("eps_c", [128, 1], F32)
    meta_sb = st.enter_context(nc.sbuf_tensor("meta_sb", [1, 1], I32))
    k2p = sbuf("k2p", [1, NSLOT], F32)

    for s in ("sd", "gd", "ad", "cc", "pe", "ve", "ac"):
        SEM[s] = st.enter_context(nc.semaphore(s))

    p.dma("gpsimd", ident_bf[:], identc)
    p.dma("gpsimd", rotm_bf[:], rotmc)
    p.dma("gpsimd", ones_cb[:], ones_cd)
    p.dma("sync", ones_rf[:], ones_rd)
    p.dma("sync", eps_c[:], eps_cd)
    p.dma("sync", meta_sb[:], meta)

    with ExitStack() as st1:
        sb1 = lambda n, s, d: st1.enter_context(nc.sbuf_tensor(n, s, d))
        ps1s = lambda n, s, d: st1.enter_context(nc.psum_tensor(n, s, d))
        xt = [sb1(f"xt{i}", [128, D], F32) for i in range(2)]
        bn6 = sb1("bn6", [128, 24], F32)
        mv = sb1("mv", [128, 2], F32)
        stdc = sb1("stdc", [128, 1], F32)
        rstd = sb1("rstd", [128, 1], F32)
        nb = sb1("nb", [128, 1], F32)
        xn_tm = [sb1(f"xntm{i}", [128, D], BF16) for i in range(2)]
        xn1 = sb1("xn1", [128, KC * TPC], BF16)
        wdkv_bf = sb1("wdkv_bf", [128, KC * DKV], BF16)
        wdq_bf = sb1("wdq_bf", [128, KC * DQ], BF16)
        bdkv_sb = sb1("bdkv_sb", [128, NKV], F32)
        bdq_sb = sb1("bdq_sb", [128, NQ], F32)
        stage = [sb1(f"stg{i}", [128, 512], BF16) for i in range(4)]
        ps_tr = ps1s("ps_tr", [128, 512], BF16)
        ps_p = [ps1s(f"ps_p{i}", [128, 512], F32) for i in range(2)]

        p.dma("gpsimd", wdkv_bf[:].rearrange("p (c n) -> p c n", c=KC), wdkv.rearrange("(c p) n -> p c n", p=128))
        p.dma("gpsimd", wdq_bf[:].rearrange("p (c n) -> p c n", c=KC), wdq.rearrange("(c p) n -> p c n", p=128))
        p.dma("sync", bdkv_sb[:], bdkv)
        p.dma("sync", bdq_sb[:], bdq)

        tr_evacs = 0
        for i in range(NT):
            p.wait("sync", "ve")
            p.wait("sync", "ac")
            x_ld = p.dma("sync", xt[i % 2][:], x_in[128 * i:128 * (i + 1), :])
            p.wait("vector", "sd", x_ld)
            p.wait("scalar", "ve", tr_evacs)
            nrm = ln_chain(p, "l1", xt[i % 2], bn6, mv, stdc, rstd, nb, eps_c, None,
                           lambda s, b=xt[i % 2], o=xn_tm[i % 2]: s.activation(o[:], b[:], AF.Identity, bias=nb[:], scale=rstd[:]))
            p.wait("tensor", "ac", nrm)
            for j4 in range(4):
                p.wait("tensor", "ve", tr_evacs)
                for j in range(4):
                    c = j4 * 4 + j
                    p.op("tensor", "pe", lambda t, c=c, j=j, b=xn_tm[i % 2]: t.transpose(
                        ps_tr[:, 128 * j:128 * (j + 1)], b[:, 128 * c:128 * (c + 1)], ident_bf[:]))
                p.wait("vector", "pe")
                def evac(v, i=i, j4=j4):
                    dv = xn1[:, :].rearrange("p (c n) -> p c n", n=TPC)
                    return v.tensor_copy(dv[:, j4 * 4:(j4 + 1) * 4, 128 * i:128 * (i + 1)],
                                         ps_tr[:].rearrange("p (c n) -> p c n", n=128))
                tr_evacs = p.op("vector", "ve", evac)
            if serial:
                p.barrier()

        p.wait("sync", "ve")
        for c in range(KC):
            p.dma("sync", shard[2048 + 128 * c:2048 + 128 * (c + 1), :], xn1[:, TPC * c:TPC * (c + 1)])

        p.wait("tensor", "gd")
        evac_t = {0: 0, 1: 0}
        for oc in range(KC):
            is_kv = oc < NKV
            for th in range(2):
                par = (2 * oc + th) % 2
                pp = ps_p[par]
                p.wait("tensor", "ac", evac_t[par])
                for k in range(KC):
                    if is_kv:
                        lhs = wdkv_bf[:, DKV * k + 128 * oc: DKV * k + 128 * (oc + 1)]
                    else:
                        lhs = wdq_bf[:, DQ * k + 128 * (oc - NKV): DQ * k + 128 * (oc - NKV + 1)]
                    p.op("tensor", "pe", lambda t, l=lhs, k=k, pp=pp, th=th: t.matmul(
                        pp[:], l, xn1[:, TPC * k + 512 * th: TPC * k + 512 * (th + 1)],
                        start=(k == 0), stop=(k == KC - 1)))
                p.wait("scalar", "pe")
                p.wait("scalar", "sd")
                bcol = bdkv_sb[:, oc:oc + 1] if is_kv else bdq_sb[:, oc - NKV:oc - NKV + 1]
                sg = stage[(2 * oc + th) % 4]
                evac_t[par] = p.op("scalar", "ac", lambda s, pp=pp, b=bcol, sg=sg: s.activation(sg[:], pp[:], AF.Identity, bias=b, scale=1.0))
                p.wait("sync", "ac", evac_t[par])
                row = 128 * oc if is_kv else 512 + 128 * (oc - NKV)
                p.dma("sync", shard[row:row + 128, 512 * th:512 * (th + 1)], sg[:])
            if serial:
                p.barrier()

        p.wait("gpsimd", "sd")
        p.op("gpsimd", "cc", lambda g: g.collective_compute(
            "AllGather", ALU.bypass, replica_groups=GROUPS, ins=[shard.opt()], outs=[l_all.opt()]))

    if "t_shard" in tap_aps:
        p.wait("gpsimd", "cc")
        for r in range(4):
            p.dma("gpsimd", tap_aps["t_shard"][4096 * r:4096 * (r + 1), :], l_all[4096 * r:4096 * (r + 1), :])
    if phases == 1:
        p.barrier()
        finalize(nc, p)
        st.close()
        return nc

    p.barrier()
    with ExitStack() as stS:
        sbS = lambda n, s, d: stS.enter_context(nc.sbuf_tensor(n, s, d))
        kC = [sbS(f"kC{s}", [128, SEQ], BF16) for s in range(NSLOT)]
        kR = [sbS(f"kR{s}", [128, SEQ], BF16) for s in range(NSLOT)]
        vS = [sbS(f"vS{s}", [128, SEQ], BF16) for s in range(NSLOT)]

        with ExitStack() as stB:
            sb2 = lambda n, s, d: stB.enter_context(nc.sbuf_tensor(n, s, d))
            ps2s = lambda n, s, d: stB.enter_context(nc.psum_tensor(n, s, d))
            lat_x = [sb2(f"latx{c}", [128, 512], BF16) for c in range(KC)]
            lat_q = [sb2(f"latq{c}", [128, 512], BF16) for c in range(NQ)]
            lat_kv = [sb2(f"latkv{c}", [128, 512], BF16) for c in range(NKV)]
            cos_s = sb2("cos_s", [128, 512], F32)
            sin_s = sb2("sin_s", [128, 512], F32)
            wkr_bf = sb2("wkr_bf", [128, KC * 512], BF16)
            wuk_bf = sb2("wuk_bf", [128, NKV * 512], BF16)
            wuv_bf = sb2("wuv_bf", [128, NKV * 512], BF16)
            wuq_bf = sb2("wuq_bf", [128, NQ * 512], BF16)
            bkr_sb = sb2("bkr_sb", [128, NSLOT], F32)
            buk_sb = sb2("buk_sb", [128, NSLOT], F32)
            buv_sb = sb2("buv_sb", [128, NSLOT], F32)
            buq_sb = sb2("buq_sb", [128, NSLOT], F32)
            krbf = sb2("krbf", [128, 512], BF16)
            vfm = sb2("vfm", [128, 512], BF16)
            qbf = sb2("qbf", [128, 512], BF16)
            t1 = sb2("t1", [128, 512], F32)
            t2 = sb2("t2", [128, 512], F32)
            ksq = sb2("ksq", [128, 512], BF16)
            k2 = sb2("k2", [1, NSLOT], F32)
            k2t = sb2("k2t", [1, 1], F32)
            ps_a = [ps2s(f"ps_a{s}", [128, 512], F32) for s in range(NSLOT)]
            ps_rot = ps2s("ps_rot", [128, 512], F32)
            ps_vt = ps2s("ps_vt", [128, 512], BF16)
            ps_n = ps2s("ps_n", [1, 512], F32)

            p.dma("gpsimd", wkr_bf[:].rearrange("p (c n) -> p c n", c=KC), wkr.rearrange("(c p) n -> p c n", p=128))
            p.dma("gpsimd", wuk_bf[:].rearrange("p (c n) -> p c n", c=NKV), wuk.rearrange("(c p) n -> p c n", p=128))
            p.dma("gpsimd", wuv_bf[:].rearrange("p (c n) -> p c n", c=NKV), wuv.rearrange("(c p) n -> p c n", p=128))
            p.dma("gpsimd", wuq_bf[:].rearrange("p (c n) -> p c n", c=NQ), wuq.rearrange("(c p) n -> p c n", p=128))
            p.dma("sync", bkr_sb[:], bkr)
            p.dma("sync", buk_sb[:], buk)
            p.dma("sync", buv_sb[:], buv)
            p.dma("sync", buq_sb[:], buq)
            p.wait("sync", "cc")
            p.wait("tensor", "gd")

            for ts in range(NTS):
                r = ts // 2
                cs = 512 * (ts % 2)
                rb = 4096 * r
                p.wait("sync", "pe")
                for c in range(KC):
                    p.dma("sync", lat_x[c][:], l_all[rb + 2048 + 128 * c: rb + 2048 + 128 * (c + 1), cs:cs + 512])
                for c in range(NQ):
                    p.dma("sync", lat_q[c][:], l_all[rb + 512 + 128 * c: rb + 512 + 128 * (c + 1), cs:cs + 512])
                for c in range(NKV):
                    p.dma("sync", lat_kv[c][:], l_all[rb + 128 * c: rb + 128 * (c + 1), cs:cs + 512])
                p.wait("sync", "ve")
                ld = p.dma("sync", cos_s[:], cos_td[:, 512 * ts:512 * (ts + 1)])
                ld = p.dma("sync", sin_s[:], sin_td[:, 512 * ts:512 * (ts + 1)])
                p.wait("tensor", "sd", ld)

                for s in range(NSLOT):
                    for k in range(KC):
                        p.op("tensor", "pe", lambda t, s=s, k=k: t.matmul(
                            ps_a[s][:], wkr_bf[:, 512 * k + 128 * s: 512 * k + 128 * (s + 1)], lat_x[k][:],
                            start=(k == 0), stop=(k == KC - 1)))
                for s in range(NSLOT):
                    p.wait("scalar", "pe")
                    p.wait("scalar", "ve")
                    ev = p.op("scalar", "ac", lambda sc, s=s: sc.activation(krbf[:], ps_a[s][:], AF.Identity, bias=bkr_sb[:, s:s + 1], scale=1.0))
                    p.wait("tensor", "ac", ev)
                    rt = p.op("tensor", "pe", lambda t: t.matmul(ps_rot[:], rotm_bf[:], krbf[:], start=True, stop=True))
                    p.wait("vector", "ac", ev)
                    m1 = p.op("vector", "ve", lambda v: v.tensor_mul(t1[:], krbf[:], cos_s[:]))
                    p.wait("vector", "pe", rt)
                    m2 = p.op("vector", "ve", lambda v: v.tensor_mul(t2[:], ps_rot[:], sin_s[:]))
                    p.wait("vector", "ve", m2)
                    p.op("vector", "ve", lambda v, s=s, ts=ts: v.tensor_add(kR[s][:, 512 * ts:512 * (ts + 1)], t1[:], t2[:]))
                    p.wait("tensor", "ve")

                for s in range(NSLOT):
                    for k in range(NKV):
                        p.op("tensor", "pe", lambda t, s=s, k=k: t.matmul(
                            ps_a[s][:], wuk_bf[:, 512 * k + 128 * s: 512 * k + 128 * (s + 1)], lat_kv[k][:],
                            start=(k == 0), stop=(k == NKV - 1)))
                for s in range(NSLOT):
                    p.wait("scalar", "pe")
                    p.op("scalar", "ac", lambda sc, s=s, ts=ts: sc.activation(
                        kC[s][:, 512 * ts:512 * (ts + 1)], ps_a[s][:], AF.Identity, bias=buk_sb[:, s:s + 1], scale=1.0))
                p.wait("tensor", "ac")

                for s in range(NSLOT):
                    for k in range(NKV):
                        p.op("tensor", "pe", lambda t, s=s, k=k: t.matmul(
                            ps_a[s][:], wuv_bf[:, 512 * k + 128 * s: 512 * k + 128 * (s + 1)], lat_kv[k][:],
                            start=(k == 0), stop=(k == NKV - 1)))
                for s in range(NSLOT):
                    p.wait("scalar", "pe")
                    ev = p.op("scalar", "ac", lambda sc, s=s: sc.activation(vfm[:], ps_a[s][:], AF.Identity, bias=buv_sb[:, s:s + 1], scale=1.0))
                    p.wait("tensor", "ac", ev)
                    p.wait("tensor", "ve")
                    for j in range(4):
                        p.op("tensor", "pe", lambda t, j=j: t.transpose(ps_vt[:, 128 * j:128 * (j + 1)], vfm[:, 128 * j:128 * (j + 1)], ident_bf[:]))
                    p.wait("vector", "pe")
                    p.op("vector", "ve", lambda v, s=s, ts=ts: v.tensor_copy(vS[s][:, 512 * ts:512 * (ts + 1)], ps_vt[:]))
                p.wait("tensor", "ve")

                for s in range(NSLOT):
                    for k in range(NQ):
                        p.op("tensor", "pe", lambda t, s=s, k=k: t.matmul(
                            ps_a[s][:], wuq_bf[:, 512 * k + 128 * s: 512 * k + 128 * (s + 1)], lat_q[k][:],
                            start=(k == 0), stop=(k == NQ - 1)))
                for s in range(NSLOT):
                    p.wait("scalar", "pe")
                    p.wait("scalar", "sd")
                    p.wait("scalar", "ve")
                    ev = p.op("scalar", "ac", lambda sc, s=s: sc.activation(qbf[:], ps_a[s][:], AF.Identity, bias=buq_sb[:, s:s + 1], scale=1.0))
                    p.wait("sync", "ac", ev)
                    p.dma("sync", q_dram[256 * s:256 * s + 128, 512 * ts:512 * (ts + 1)], qbf[:])
                    p.wait("tensor", "ac", ev)
                    rt = p.op("tensor", "pe", lambda t: t.matmul(ps_rot[:], rotm_bf[:], qbf[:], start=True, stop=True))
                    p.wait("vector", "ac", ev)
                    m1 = p.op("vector", "ve", lambda v: v.tensor_mul(t1[:], qbf[:], cos_s[:]))
                    p.wait("vector", "pe", rt)
                    m2 = p.op("vector", "ve", lambda v: v.tensor_mul(t2[:], ps_rot[:], sin_s[:]))
                    p.wait("vector", "ve", m2)
                    ad = p.op("vector", "ve", lambda v: v.tensor_add(t1[:], t1[:], t2[:]))
                    p.wait("vector", "ve", ad)
                    p.wait("vector", "ac")
                    p.wait("vector", "sd")
                    cv = p.op("vector", "ve", lambda v: v.tensor_copy(krbf[:], t1[:]))
                    p.wait("sync", "ve", cv)
                    p.dma("sync", q_dram[256 * s + 128:256 * s + 256, 512 * ts:512 * (ts + 1)], krbf[:])
                    p.wait("tensor", "ve")
                if serial:
                    p.barrier()

            p.op("vector", "ve", lambda v: v.memset(k2[:], 0.0))
            for s in range(NSLOT):
                for sl in range(NTS):
                    p.wait("scalar", "pe")
                    sq1 = p.op("scalar", "ac", lambda sc, s=s, sl=sl: sc.activation(ksq[:], kC[s][:, 512 * sl:512 * (sl + 1)], AF.Square))
                    p.wait("tensor", "ac", sq1)
                    p.wait("tensor", "ve")
                    p.op("tensor", "pe", lambda t: t.matmul(ps_n[:], ones_cb[:], ksq[:], start=True, stop=False))
                    p.wait("scalar", "pe")
                    sq2 = p.op("scalar", "ac", lambda sc, s=s, sl=sl: sc.activation(ksq[:], kR[s][:, 512 * sl:512 * (sl + 1)], AF.Square))
                    p.wait("tensor", "ac", sq2)
                    mm = p.op("tensor", "pe", lambda t: t.matmul(ps_n[:], ones_cb[:], ksq[:], start=False, stop=True))
                    p.wait("vector", "pe", mm)
                    rd = p.op("vector", "ve", lambda v: v.tensor_reduce(k2t[:], ps_n[:], AX, ALU.max))
                    p.wait("vector", "ve", rd)
                    p.op("vector", "ve", lambda v, s=s: v.tensor_tensor(k2[:, s:s + 1], k2[:, s:s + 1], k2t[:], ALU.max))
                    p.wait("vector", "ve")
            p.op("vector", "ve", lambda v: v.tensor_copy(k2p[:], k2[:]))
            p.barrier()

        with ExitStack() as stA2:
            sb3 = lambda n, s, d: stA2.enter_context(nc.sbuf_tensor(n, s, d))
            ps3s = lambda n, s, d: stA2.enter_context(nc.psum_tensor(n, s, d))
            masks = sb3("masks", [128, 2048], F32)
            qc_t = sb3("qc_t", [128, 512], BF16)
            qr_t = sb3("qr_t", [128, 512], BF16)
            prb = [sb3(f"prb{i}", [128, 512], BF16) for i in range(3)]
            ctxs = sb3("ctxs", [128, SEQ], BF16)
            qsq = sb3("qsq", [128, 512], BF16)
            q2 = sb3("q2", [1, 1], F32)
            bstar = sb3("bstar", [1, 1], F32)
            bias_c = sb3("bias_c", [128, 1], F32)
            srow = sb3("srow", [1, 512], F32)
            rrow = sb3("rrow", [1, 512], F32)
            bc_sb = sb3("bc_sb", [128, 512], F32)
            ps_sc = [ps3s(f"ps_sc{i}", [128, 512], F32) for i in range(2)]
            ps_ctx = ps3s("ps_ctx", [128, 512], F32)
            ps_sum = ps3s("ps_sum", [1, 512], F32)
            ps_bc = ps3s("ps_bc", [128, 512], F32)
            ps_b1 = ps3s("ps_b1", [128, 1], F32)
            ps_n2 = ps3s("ps_n2", [1, 512], F32)

            p.dma("sync", masks[:], mask_t)
            exp_cnt = {0: 0, 1: 0}
            cons_cnt = {0: 0, 1: 0, 2: 0}
            for s in range(NSLOT):
                for qc in range(NTS):
                    p.wait("sync", "ac")
                    p.wait("sync", "pe")
                    p.dma("sync", qc_t[:], q_dram[256 * s:256 * s + 128, 512 * qc:512 * (qc + 1)])
                    qld = p.dma("sync", qr_t[:], q_dram[256 * s + 128:256 * s + 256, 512 * qc:512 * (qc + 1)])
                    p.wait("scalar", "sd", qld)
                    p.wait("scalar", "pe")
                    s1 = p.op("scalar", "ac", lambda sc: sc.activation(qsq[:], qc_t[:], AF.Square))
                    p.wait("tensor", "ac", s1)
                    p.wait("tensor", "ve")
                    p.op("tensor", "pe", lambda t: t.matmul(ps_n2[:], ones_cb[:], qsq[:], start=True, stop=False))
                    p.wait("scalar", "pe")
                    s2 = p.op("scalar", "ac", lambda sc: sc.activation(qsq[:], qr_t[:], AF.Square))
                    p.wait("tensor", "ac", s2)
                    nmm = p.op("tensor", "pe", lambda t: t.matmul(ps_n2[:], ones_cb[:], qsq[:], start=False, stop=True))
                    p.wait("vector", "pe", nmm)
                    rd = p.op("vector", "ve", lambda v: v.tensor_reduce(q2[:], ps_n2[:], AX, ALU.max))
                    p.wait("vector", "ve", rd)
                    mu = p.op("vector", "ve", lambda v, s=s: v.tensor_mul(q2[:], q2[:], k2p[:, s:s + 1]))
                    p.wait("scalar", "ve", mu)
                    sq = p.op("scalar", "ac", lambda sc: sc.activation(bstar[:], q2[:], AF.Sqrt))
                    p.wait("tensor", "ac", sq)
                    bmm = p.op("tensor", "pe", lambda t: t.matmul(ps_b1[:], ones_rf[:], bstar[:], start=True, stop=True))
                    p.wait("scalar", "pe", bmm)
                    bc = p.op("scalar", "ac", lambda sc: sc.activation(bias_c[:], ps_b1[:], AF.Identity, scale=-0.0625))
                    p.wait("scalar", "ac", bc)
                    nkt = 4 * (qc + 1)
                    for ik in range(nkt):
                        kb, kt = divmod(ik, 4)
                        par = ik % 2
                        ring = ik % 3
                        p.wait("tensor", "ac", exp_cnt[par])
                        p.op("tensor", "pe", lambda t, s=s, kb=kb, kt=kt, par=par: t.matmul(
                            ps_sc[par][:], kC[s][:, 512 * kb + 128 * kt: 512 * kb + 128 * (kt + 1)], qc_t[:], start=True, stop=False))
                        mm = p.op("tensor", "pe", lambda t, s=s, kb=kb, kt=kt, par=par: t.matmul(
                            ps_sc[par][:], kR[s][:, 512 * kb + 128 * kt: 512 * kb + 128 * (kt + 1)], qr_t[:], start=False, stop=True))
                        gate = ("pe", mm)
                        if kb == qc:
                            p.wait("vector", "pe", mm)
                            mk = p.op("vector", "ve", lambda v, kt=kt, par=par: v.tensor_tensor(
                                ps_sc[par][:], ps_sc[par][:], masks[:, 512 * kt:512 * (kt + 1)], ALU.add))
                            gate = ("ve", mk)
                        p.wait("scalar", gate[0], gate[1])
                        p.wait("scalar", "pe", cons_cnt[ring])
                        ex = p.op("scalar", "ac", lambda sc, par=par, ring=ring: sc.activation(
                            prb[ring][:], ps_sc[par][:], AF.Exp, bias=bias_c[:], scale=0.0625))
                        exp_cnt[par] = ex
                        p.wait("tensor", "ac", ex)
                        p.op("tensor", "pe", lambda t, s=s, kb=kb, kt=kt, ring=ring, first=(ik == 0), last=(ik == nkt - 1): t.matmul(
                            ps_ctx[:], vS[s][:, 512 * kb + 128 * kt: 512 * kb + 128 * (kt + 1)], prb[ring][:], start=first, stop=last))
                        cmm = p.op("tensor", "pe", lambda t, ring=ring, first=(ik == 0), last=(ik == nkt - 1): t.matmul(
                            ps_sum[:], ones_cb[:], prb[ring][:], start=first, stop=last))
                        cons_cnt[ring] = cmm
                    p.wait("vector", "pe")
                    e1 = p.op("vector", "ve", lambda v: v.tensor_copy(srow[:], ps_sum[:]))
                    p.wait("vector", "ve", e1)
                    e2 = p.op("vector", "ve", lambda v: v.reciprocal(rrow[:], srow[:]))
                    p.wait("tensor", "ve", e2)
                    bm = p.op("tensor", "pe", lambda t: t.matmul(ps_bc[:], ones_rf[:], rrow[:], start=True, stop=True))
                    p.wait("vector", "pe", bm)
                    e3 = p.op("vector", "ve", lambda v: v.tensor_copy(bc_sb[:], ps_bc[:]))
                    p.wait("vector", "ve", e3)
                    p.op("vector", "ve", lambda v, qc=qc: v.tensor_mul(ctxs[:, 512 * qc:512 * (qc + 1)], ps_ctx[:], bc_sb[:]))
                    p.wait("tensor", "ve")
                    if serial:
                        p.barrier()
                p.wait("sync", "ve")
                p.dma("sync", ctx_shard[128 * s:128 * (s + 1), :], ctxs[:])
                p.wait("vector", "sd")

            p.wait("gpsimd", "sd")
            p.op("gpsimd", "cc", lambda g: g.collective_compute(
                "AllGather", ALU.bypass, replica_groups=GROUPS, ins=[ctx_shard.opt()], outs=[ctx_all.opt()]))

    if "t_qdram" in tap_aps:
        p.wait("gpsimd", "cc")
        p.dma("gpsimd", tap_aps["t_qdram"][:, :], q_dram)
    if "t_ctx" in tap_aps:
        p.wait("gpsimd", "cc")
        p.dma("gpsimd", tap_aps["t_ctx"][:, :], ctx_all)
    if phases == 2:
        p.barrier()
        finalize(nc, p)
        st.close()
        return nc

    p.barrier()
    with ExitStack() as stA:
        sbA = lambda n, s, d: stA.enter_context(nc.sbuf_tensor(n, s, d))
        bo_row = sbA("bo_row", [1, D], F32)
        ds_row = sbA("ds_row", [1, D], F32)
        ds_bc = sbA("ds_bc", [128, D], F32)
        xn2 = sbA("xn2", [128, KC * TPC], BF16)
        bn6b = sbA("bn6b", [128, 24], F32)
        mvb = sbA("mvb", [128, 2], F32)
        stdb = sbA("stdb", [128, 1], F32)
        rstdb = sbA("rstdb", [128, 1], F32)
        nbb = sbA("nbb", [128, 1], F32)

        p.dma("sync", bo_row[:], bo_r)
        p.dma("sync", ds_row[:], ds_r)

        with ExitStack() as stB3:
            sbB = lambda n, s, d: stB3.enter_context(nc.sbuf_tensor(n, s, d))
            psB = lambda n, s, d: stB3.enter_context(nc.psum_tensor(n, s, d))
            wo_bf = sbB("wo_bf", [128, KC * D], BF16)
            cx = sbB("cx", [128, KC * TPC], BF16)
            bo_bc = sbB("bo_bc", [128, D], F32)
            xt3 = [sbB(f"xt3{i}", [128, D], F32) for i in range(2)]
            y_t = sbB("y_t", [128, D], F32)
            yn_t = sbB("yn_t", [128, D], F32)
            xn2_tm = sbB("xn2_tm", [128, D], BF16)
            ps_o = psB("ps_o", [128, D], F32)
            ps_tr3 = psB("ps_tr3", [128, 512], BF16)
            ps_bb = psB("ps_bb", [128, 512], F32)

            p.dma("gpsimd", wo_bf[:].rearrange("p (c n) -> p c n", c=KC), wo.rearrange("(c p) n -> p c n", p=128))
            p.wait("gpsimd", "sd")
            p.wait("gpsimd", "cc")

            def ctx_reads(g):
                last = None
                with g.register("tok") as tok:
                    g.reg_load(tok, meta_sb[0:1, 0:1])
                    off = g.snap(tok, min_val=0, max_val=3 * TPC)
                    for ch in range(KC):
                        last = g.dma_start(out=cx[:, TPC * ch:TPC * (ch + 1)],
                                           in_=ctx_all[128 * ch:128 * (ch + 1), ds(off, TPC)])
                        last.then_inc(SEM["gd"], 16)
                return last
            p.emit("gpsimd", ctx_reads)
            p.bump("gd", 16 * KC)

            p.wait("tensor", "sd")
            for h in range(4):
                p.wait("tensor", "ve")
                bm = p.op("tensor", "pe", lambda t, h=h: t.matmul(ps_bb[:], ones_rf[:], bo_row[:, 512 * h:512 * (h + 1)], start=True, stop=True))
                p.wait("vector", "pe", bm)
                p.op("vector", "ve", lambda v, h=h: v.tensor_copy(bo_bc[:, 512 * h:512 * (h + 1)], ps_bb[:]))
                p.wait("tensor", "ve")
                bm = p.op("tensor", "pe", lambda t, h=h: t.matmul(ps_bb[:], ones_rf[:], ds_row[:, 512 * h:512 * (h + 1)], start=True, stop=True))
                p.wait("vector", "pe", bm)
                p.op("vector", "ve", lambda v, h=h: v.tensor_copy(ds_bc[:, 512 * h:512 * (h + 1)], ps_bb[:]))

            p.wait("tensor", "gd")
            tr_ev = 0
            for i in range(NT):
                p.wait("sync", "ve")
                xl = p.dma("sync", xt3[i % 2][:], x_in[128 * i:128 * (i + 1), :])
                p.wait("tensor", "ve")
                for c in range(KC):
                    for h in range(4):
                        p.op("tensor", "pe", lambda t, c=c, h=h, i=i: t.matmul(
                            ps_o[:, 512 * h:512 * (h + 1)],
                            cx[:, TPC * c + 128 * i: TPC * c + 128 * (i + 1)],
                            wo_bf[:, D * c + 512 * h: D * c + 512 * (h + 1)],
                            start=(c == 0), stop=(c == KC - 1)))
                p.wait("vector", "pe")
                p.wait("vector", "ac")
                a1 = p.op("vector", "ve", lambda v: v.tensor_add(y_t[:], ps_o[:], bo_bc[:]))
                p.wait("vector", "ve", a1)
                n2 = ln_chain(p, "l2", y_t, bn6b, mvb, stdb, rstdb, nbb, eps_c, None,
                              lambda s: s.activation(yn_t[:], y_t[:], AF.Identity, bias=nbb[:], scale=rstdb[:]))
                p.wait("vector", "ac", n2)
                p.wait("vector", "sd", xl)
                x2a = p.op("vector", "ve", lambda v, b=xt3[i % 2]: v.tensor_add(b[:], b[:], yn_t[:]))
                p.wait("sync", "ve", x2a)
                p.dma("sync", x2_dram[128 * i:128 * (i + 1), :], xt3[i % 2][:])
                p.wait("scalar", "ve", tr_ev)
                n3 = ln_chain(p, "l3", xt3[i % 2], bn6b, mvb, stdb, rstdb, nbb, eps_c, None,
                              lambda s, b=xt3[i % 2]: s.activation(xn2_tm[:], b[:], AF.Identity, bias=nbb[:], scale=rstdb[:]))
                p.wait("tensor", "ac", n3)
                for j4 in range(4):
                    p.wait("tensor", "ve", tr_ev)
                    for j in range(4):
                        c = j4 * 4 + j
                        p.op("tensor", "pe", lambda t, c=c, j=j: t.transpose(
                            ps_tr3[:, 128 * j:128 * (j + 1)], xn2_tm[:, 128 * c:128 * (c + 1)], ident_bf[:]))
                    p.wait("vector", "pe")
                    def evac3(v, i=i, j4=j4):
                        dv = xn2[:, :].rearrange("p (c n) -> p c n", n=TPC)
                        return v.tensor_copy(dv[:, j4 * 4:(j4 + 1) * 4, 128 * i:128 * (i + 1)],
                                             ps_tr3[:].rearrange("p (c n) -> p c n", n=128))
                    tr_ev = p.op("vector", "ve", evac3)
                if serial:
                    p.barrier()

        if "t_x2" in tap_aps:
            p.barrier()
            p.dma("sync", tap_aps["t_x2"][:, :], x2_dram)
        p.barrier()
        with ExitStack() as stC:
            sbC = lambda n, s, d: stC.enter_context(nc.sbuf_tensor(n, s, d))
            psC = lambda n, s, d: stC.enter_context(nc.psum_tensor(n, s, d))
            acc = sbC("acc", [128, NT * D], F32)
            wg_bf = [sbC(f"wg{i}", [128, KC * 128], BF16) for i in range(2)]
            wu_bf = [sbC(f"wu{i}", [128, KC * 128], BF16) for i in range(2)]
            wd_bf = [sbC(f"wd{i}", [128, D], BF16) for i in range(GRP)]
            hdn = [sbC(f"hdn{i}", [128, TPC], BF16) for i in range(GRP)]
            silu_g = sbC("silu_g", [128, TPC], BF16)
            gs_sb = sbC("gs_sb", [128, IC], F32)
            us_sb = sbC("us_sb", [128, IC], F32)
            xt4 = sbC("xt4", [128, D], F32)
            y4 = sbC("y4", [128, D], F32)
            yn4 = sbC("yn4", [128, D], F32)
            ps_g = psC("ps_g", [128, TPC], F32)
            ps_u = psC("ps_u", [128, TPC], F32)
            ps_d = psC("ps_d", [128, D], F32)

            p.dma("sync", gs_sb[:], gs)
            p.dma("sync", us_sb[:], us)
            p.op("vector", "ve", lambda v: v.memset(acc[:], 0.0))
            p.wait("tensor", "ve")
            p.wait("tensor", "sd")

            groups = [list(range(g, min(g + GRP, IC))) for g in range(0, IC, GRP)]
            dn_done = 0
            hv_done = {0: 0, 1: 0}
            for grp in groups:
                for ii, icx in enumerate(grp):
                    p.wait("gpsimd", "ve", hv_done[ii % 2])
                    p.wait("gpsimd", "pe", dn_done)
                    p.dma("gpsimd", wg_bf[ii % 2][:].rearrange("p (c n) -> p c n", c=KC), gwt[:, 128 * icx:128 * (icx + 1)].rearrange("(c p) n -> p c n", p=128))
                    p.dma("gpsimd", wu_bf[ii % 2][:].rearrange("p (c n) -> p c n", c=KC), upt[:, 128 * icx:128 * (icx + 1)].rearrange("(c p) n -> p c n", p=128))
                    wld = p.dma("gpsimd", wd_bf[ii][:], dwt[128 * icx:128 * (icx + 1), :])
                    p.wait("tensor", "gd", wld)
                    p.wait("tensor", "ve", hv_done[ii % 2])
                    for k in range(KC):
                        for th in range(2):
                            p.op("tensor", "pe", lambda t, k=k, th=th, ii=ii: t.matmul(
                                ps_g[:, 512 * th:512 * (th + 1)], wg_bf[ii % 2][:, 128 * k:128 * (k + 1)],
                                xn2[:, TPC * k + 512 * th: TPC * k + 512 * (th + 1)],
                                start=(k == 0), stop=(k == KC - 1)))
                    for k in range(KC):
                        for th in range(2):
                            p.op("tensor", "pe", lambda t, k=k, th=th, ii=ii: t.matmul(
                                ps_u[:, 512 * th:512 * (th + 1)], wu_bf[ii % 2][:, 128 * k:128 * (k + 1)],
                                xn2[:, TPC * k + 512 * th: TPC * k + 512 * (th + 1)],
                                start=(k == 0), stop=(k == KC - 1)))
                    p.wait("scalar", "pe")
                    p.wait("scalar", "ve", hv_done[ii % 2])
                    sl = p.op("scalar", "ac", lambda s, icx=icx: s.activation(silu_g[:], ps_g[:], AF.Silu, scale=gs_sb[:, icx:icx + 1]))
                    p.wait("vector", "ac", sl)
                    p.wait("vector", "pe")
                    hv = p.op("vector", "ve", lambda v, ii=ii, icx=icx: v.scalar_tensor_tensor(
                        hdn[ii][:], ps_u[:], us_sb[:, icx:icx + 1], silu_g[:], ALU.mult, ALU.mult))
                    hv_done[ii % 2] = hv
                    p.wait("tensor", "ve", hv)
                for tt in range(NT):
                    p.wait("tensor", "ve")
                    for ii in range(len(grp)):
                        for th in range(4):
                            p.op("tensor", "pe", lambda t, ii=ii, tt=tt, th=th, first=(ii == 0), last=(ii == len(grp) - 1): t.matmul(
                                ps_d[:, 512 * th:512 * (th + 1)], hdn[ii][:, 128 * tt:128 * (tt + 1)],
                                wd_bf[ii][:, 512 * th:512 * (th + 1)],
                                start=first, stop=last))
                    p.wait("vector", "pe")
                    p.op("vector", "ve", lambda v, tt=tt: v.tensor_add(acc[:, D * tt:D * (tt + 1)], acc[:, D * tt:D * (tt + 1)], ps_d[:]))
                dn_done = p.val("pe")
                if serial:
                    p.barrier()

            if "t_xn2" in tap_aps:
                p.barrier()
                p.dma("gpsimd", tap_aps["t_xn2"][:, :], xn2[:])
            if "t_acc" in tap_aps:
                p.barrier()
                p.dma("sync", tap_aps["t_acc"][:, :], acc[:])
            for i in range(NT):
                p.wait("sync", "ve")
                xl = p.dma("sync", xt4[:], x2_dram[128 * i:128 * (i + 1), :])
                p.wait("vector", "ac")
                ym = p.op("vector", "ve", lambda v, i=i: v.tensor_mul(y4[:], acc[:, D * i:D * (i + 1)], ds_bc[:]))
                p.wait("vector", "ve", ym)
                n4 = ln_chain(p, "l4", y4, bn6b, mvb, stdb, rstdb, nbb, eps_c, None,
                              lambda s: s.activation(yn4[:], y4[:], AF.Identity, bias=nbb[:], scale=rstdb[:]))
                p.wait("vector", "ac", n4)
                p.wait("vector", "sd", xl)
                fo = p.op("vector", "ve", lambda v: v.tensor_add(xt4[:], xt4[:], yn4[:]))
                p.wait("sync", "ve", fo)
                p.dma("sync", out[128 * i:128 * (i + 1), :], xt4[:])
                if serial:
                    p.barrier()

    finalize(nc, p)
    st.close()
    return nc


def finalize(nc, p):
    p.barrier()
    with nc.Block() as block:
        @block.sync
        def _(e):
            for f in p.q["sync"]:
                f(e)

        @block.gpsimd
        def _(e):
            for f in p.q["gpsimd"]:
                f(e)

        @block.scalar
        def _(e):
            for f in p.q["scalar"]:
                f(e)

        @block.vector
        def _(e):
            for f in p.q["vector"]:
                f(e)

        @block.tensor
        def _(e):
            for f in p.q["tensor"]:
                f(e)


def make_consts():
    ident = np.eye(128, dtype=np.float32)
    rotm = np.zeros((128, 128), np.float32)
    for j in range(64):
        rotm[2 * j + 1, 2 * j] = -1.0
        rotm[2 * j, 2 * j + 1] = 1.0
    dd = np.arange(128) // 2
    inv = 1.0 / ROPE_BASE ** (dd.astype(np.float64) * 2.0 / 128.0)
    tpos = np.arange(SEQ, dtype=np.float64)
    ang = inv[:, None] * tpos[None, :]
    cos_t = np.cos(ang).astype(np.float32)
    sin_t = np.sin(ang).astype(np.float32)
    kk = np.arange(128)[:, None]
    qq = np.arange(512)[None, :]
    mask = np.concatenate([np.where(kt * 128 + kk > qq, NEG, 0.0) for kt in range(4)], axis=1).astype(np.float32)
    return dict(identc=ident, rotmc=rotm, cos_t=cos_t, sin_t=sin_t, mask_t=mask,
                ones_c=np.ones((128, 1), np.float32), ones_r=np.ones((1, 128), np.float32),
                eps_cd=np.full((128, 1), 1e-5, np.float32))


def cols(v, n):
    return np.ascontiguousarray(np.asarray(v, np.float32).reshape(n, 128).T)


def make_inmaps(inp):
    x = np.asarray(inp["L0_attn_h"], np.float32)
    consts = make_consts()
    g = lambda k: np.ascontiguousarray(np.asarray(inp[k], np.float32))
    wdkv_f = g("L0_attn_W_DKV"); bdkv_f = cols(inp["L0_attn_b_DKV"], NKV)
    wdq_f = g("L0_attn_W_DQ"); bdq_f = cols(inp["L0_attn_b_DQ"], NQ)
    wkr_f = g("L0_attn_W_KR"); bkr_f = np.asarray(inp["L0_attn_b_KR"], np.float32)
    wuk_f = g("L0_attn_W_UK"); buk_f = np.asarray(inp["L0_attn_b_UK"], np.float32)
    wuv_f = g("L0_attn_W_UV"); buv_f = np.asarray(inp["L0_attn_b_UV"], np.float32)
    wuq_f = g("L0_attn_W_UQ"); buq_f = np.asarray(inp["L0_attn_b_UQ"], np.float32)
    wo_f = g("L0_attn_W_O"); bo_f = np.asarray(inp["L0_attn_b_O"], np.float32).reshape(1, D)
    gwt_f = np.ascontiguousarray(g("L0_ffn_gate_w").T)
    upt_f = np.ascontiguousarray(g("L0_ffn_up_w").T)
    dwt_f = np.ascontiguousarray(g("L0_ffn_down_w").T)
    gs_f = cols(inp["L0_ffn_gate_s"], IC)
    us_f = cols(inp["L0_ffn_up_s"], IC)
    ds_f = np.asarray(inp["L0_ffn_down_s"], np.float32).reshape(1, D)

    maps = []
    for c in range(8):
        b, m = divmod(c, 4)
        sl = slice(512 * m, 512 * (m + 1))
        maps.append(dict(
            x=np.ascontiguousarray(x[b, TPC * m:TPC * (m + 1), :]),
            meta=np.array([[TPC * m]], np.int32),
            wdkv=wdkv_f, bdkv=bdkv_f, wdq=wdq_f, bdq=bdq_f,
            wkr=np.ascontiguousarray(wkr_f[:, sl]), bkr=cols(bkr_f[sl], NSLOT),
            wuk=np.ascontiguousarray(wuk_f[:, sl]), buk=cols(buk_f[sl], NSLOT),
            wuv=np.ascontiguousarray(wuv_f[:, sl]), buv=cols(buv_f[sl], NSLOT),
            wuq=np.ascontiguousarray(wuq_f[:, sl]), buq=cols(buq_f[sl], NSLOT),
            wo=wo_f, bo_r=bo_f, gwt=gwt_f, upt=upt_f, dwt=dwt_f,
            gs=gs_f, us=us_f, ds_r=ds_f,
            **consts,
        ))
    return maps


def kernel(**inp):
    nc = build_nc(phases=3)
    maps = make_inmaps(inp)
    res = run_bass_kernel_spmd(nc, maps, core_ids=list(range(8)))
    outv = np.empty((2, SEQ, D), np.float32)
    for c in range(8):
        b, m = divmod(c, 4)
        outv[b, TPC * m:TPC * (m + 1), :] = res.results[c]["out"]
    return outv


# version 02
# speedup vs baseline: 1.3635x; 1.3635x over previous
import numpy as np
from contextlib import ExitStack
import concourse.bass as bass
import concourse.mybir as mybir
from concourse.bass import ds
from concourse.bass_utils import run_bass_kernel_spmd

F32 = mybir.dt.float32
BF16 = mybir.dt.bfloat16
I32 = mybir.dt.int32
AX = mybir.AxisListType.X
ALU = mybir.AluOpType
AF = mybir.ActivationFunctionType

D = 2048
SEQ = 4096
TPC = 1024
NT = TPC // 128
KC = D // 128
DKV = 512
NKV = DKV // 128
DQ = 1536
NQ = DQ // 128
NSLOT = 4
NTS = SEQ // 512
INTER = 5632
IC = INTER // 128
GRP = 6
ROPE_BASE = 10000.0
GROUPS = [[0, 1, 2, 3], [4, 5, 6, 7]]
NEG = -30000.0


class Plan:
    ENGINES = ("sync", "gpsimd", "scalar", "vector", "tensor")

    def __init__(self):
        self.q = {e: [] for e in self.ENGINES}
        self.cnt = {}

    def val(self, sem):
        return self.cnt.get(sem, 0)

    def bump(self, sem, n):
        self.cnt[sem] = self.val(sem) + n
        return self.cnt[sem]

    def emit(self, eng, f):
        self.q[eng].append(f)

    def wait(self, eng, sem, v=None):
        v = self.val(sem) if v is None else v
        if v <= 0:
            return
        self.emit(eng, lambda e, s=sem, vv=v: e.wait_ge(SEM[s], vv))

    def dma(self, eng, out, in_, sem=None, **kw):
        sem = sem or ("gd" if eng == "gpsimd" else ("ad" if eng == "scalar" else "sd"))
        t = self.bump(sem, 16)
        self.emit(eng, lambda e, o=out, i=in_, s=sem, k=kw: e.dma_start(out=o, in_=i, **k).then_inc(SEM[s], 16))
        return t

    def op(self, eng, sem, build):
        t = self.bump(sem, 1)
        self.emit(eng, lambda e, b=build, s=sem: b(e).then_inc(SEM[s], 1))
        return t

    def barrier(self, engines=None):
        snap = {s: v for s, v in self.cnt.items() if v > 0}
        for eng in (engines or self.ENGINES):
            for s, v in snap.items():
                self.wait(eng, s, v)


SEM = {}


def ln_chain(p, src, bn6, mv, stdc, rstd, nb, eps_c, out_fn):
    for j in range(4):
        p.op("vector", "ve", lambda v, j=j: v.bn_stats(bn6[:, 6 * j:6 * j + 6], src[:, 512 * j:512 * (j + 1)]))
    p.wait("vector", "ve")
    p.op("vector", "ve", lambda v: v.bn_aggr(mv[:], bn6[:]))
    p.wait("scalar", "ve")
    sq = p.op("scalar", "ac", lambda s: s.activation(stdc[:], mv[:, 1:2], AF.Sqrt, bias=eps_c[:], scale=1.0))
    p.wait("vector", "ac", sq)
    p.op("vector", "ve", lambda v: v.reciprocal(rstd[:], stdc[:]))
    p.wait("vector", "ve")
    p.op("vector", "ve", lambda v: v.scalar_tensor_tensor(nb[:], mv[:, 0:1], -1.0, rstd[:], ALU.mult, ALU.mult))
    p.wait("scalar", "ve")
    return p.op("scalar", "ac", out_fn)


def build_nc(phases=3, taps=(), serial=False):
    nc = bass.Bass("TRN2", target_bir_lowering=False)
    p = Plan()

    def din(name, shape, dt=F32):
        return nc.dram_tensor(name, shape, dt, kind="ExternalInput").ap()

    x_in = din("x", [TPC, D])
    meta = din("meta", [1, 1], I32)
    wdkv = din("wdkv", [D, DKV])
    bdkv = din("bdkv", [128, NKV])
    wdq = din("wdq", [D, DQ])
    bdq = din("bdq", [128, NQ])
    wkr = din("wkr", [D, 512])
    bkr = din("bkr", [128, NSLOT])
    wuk = din("wuk", [DKV, 512])
    buk = din("buk", [128, NSLOT])
    wuv = din("wuv", [DKV, 512])
    buv = din("buv", [128, NSLOT])
    wuq = din("wuq", [DQ, 512])
    buq = din("buq", [128, NSLOT])
    wo = din("wo", [D, D])
    bo_r = din("bo_r", [1, D])
    gwt = din("gwt", [D, INTER])
    upt = din("upt", [D, INTER])
    dwt = din("dwt", [INTER, D])
    gs = din("gs", [128, IC])
    us = din("us", [128, IC])
    ds_r = din("ds_r", [1, D])
    identc = din("identc", [128, 128])
    rotmc = din("rotmc", [128, 128])
    cos_td = din("cos_t", [128, SEQ])
    sin_td = din("sin_t", [128, SEQ])
    mask_t = din("mask_t", [128, 4 * 512])
    ones_cd = din("ones_c", [128, 1])
    ones_rd = din("ones_r", [1, 128])
    eps_cd = din("eps_cd", [128, 1])

    out = nc.dram_tensor("out", [TPC, D], F32, kind="ExternalOutput").ap()

    shard = nc.dram_tensor("shard", [4096, TPC], BF16, kind="Internal").ap()
    l_all = nc.dram_tensor("l_all", [4 * 4096, TPC], BF16, kind="Internal").ap()
    q_dram = nc.dram_tensor("q_dram", [NSLOT * 256, SEQ], BF16, kind="Internal").ap()
    ctx_shard = nc.dram_tensor("ctx_shard", [512, SEQ], BF16, kind="Internal").ap()
    ctx_all = nc.dram_tensor("ctx_all", [2048, SEQ], BF16, kind="Internal").ap()
    x2_dram = nc.dram_tensor("x2_dram", [TPC, D], F32, kind="Internal").ap()

    tap_aps = {}
    if "shard" in taps:
        tap_aps["t_shard"] = nc.dram_tensor("t_shard", [4 * 4096, TPC], F32, kind="ExternalOutput").ap()
    if "qdram" in taps:
        tap_aps["t_qdram"] = nc.dram_tensor("t_qdram", [NSLOT * 256, SEQ], F32, kind="ExternalOutput").ap()
    if "ctx" in taps:
        tap_aps["t_ctx"] = nc.dram_tensor("t_ctx", [2048, SEQ], F32, kind="ExternalOutput").ap()
    if "x2" in taps:
        tap_aps["t_x2"] = nc.dram_tensor("t_x2", [TPC, D], F32, kind="ExternalOutput").ap()
    if "xn2" in taps:
        tap_aps["t_xn2"] = nc.dram_tensor("t_xn2", [128, KC * TPC], F32, kind="ExternalOutput").ap()
    if "acc" in taps:
        tap_aps["t_acc"] = nc.dram_tensor("t_acc", [128, NT * D], F32, kind="ExternalOutput").ap()

    st = ExitStack()
    sbuf = lambda n, s, d: st.enter_context(nc.sbuf_tensor(n, s, d))

    ident_bf = sbuf("ident_bf", [128, 128], BF16)
    rotm_bf = sbuf("rotm_bf", [128, 128], BF16)
    ones_cb = sbuf("ones_cb", [128, 1], BF16)
    ones_rf = sbuf("ones_rf", [1, 128], F32)
    eps_c = sbuf("eps_c", [128, 1], F32)
    meta_sb = st.enter_context(nc.sbuf_tensor("meta_sb", [1, 1], I32))
    k2p = sbuf("k2p", [1, NSLOT], F32)
    q2mp = sbuf("q2mp", [1, NSLOT * NTS], F32)

    for s in ("sd", "gd", "ad", "cc", "pe", "ve", "ac"):
        SEM[s] = st.enter_context(nc.semaphore(s))

    p.dma("gpsimd", ident_bf[:], identc)
    p.dma("gpsimd", rotm_bf[:], rotmc)
    p.dma("gpsimd", ones_cb[:], ones_cd)
    p.dma("sync", ones_rf[:], ones_rd)
    p.dma("sync", eps_c[:], eps_cd)
    p.dma("sync", meta_sb[:], meta)

    with ExitStack() as st1:
        sb1 = lambda n, s, d: st1.enter_context(nc.sbuf_tensor(n, s, d))
        ps1s = lambda n, s, d: st1.enter_context(nc.psum_tensor(n, s, d))
        xt = [sb1(f"xt{i}", [128, D], F32) for i in range(2)]
        bn6 = sb1("bn6", [128, 24], F32)
        mv = sb1("mv", [128, 2], F32)
        stdc = sb1("stdc", [128, 1], F32)
        rstd = sb1("rstd", [128, 1], F32)
        nb = sb1("nb", [128, 1], F32)
        xn_tm = [sb1(f"xntm{i}", [128, D], BF16) for i in range(2)]
        xn1 = sb1("xn1", [128, KC * TPC], BF16)
        wdkv_bf = sb1("wdkv_bf", [128, KC * DKV], BF16)
        wdq_bf = sb1("wdq_bf", [128, KC * DQ], BF16)
        bdkv_sb = sb1("bdkv_sb", [128, NKV], F32)
        bdq_sb = sb1("bdq_sb", [128, NQ], F32)
        stage = [sb1(f"stg{i}", [128, 512], BF16) for i in range(4)]
        ps_tr = ps1s("ps_tr", [128, 512], BF16)
        ps_p = [ps1s(f"ps_p{i}", [128, 512], F32) for i in range(2)]

        p.dma("gpsimd", wdkv_bf[:].rearrange("p (c n) -> p c n", c=KC), wdkv.rearrange("(c p) n -> p c n", p=128))
        p.dma("gpsimd", wdq_bf[:].rearrange("p (c n) -> p c n", c=KC), wdq.rearrange("(c p) n -> p c n", p=128))
        p.dma("sync", bdkv_sb[:], bdkv)
        p.dma("sync", bdq_sb[:], bdq)

        tr_evacs = 0
        for i in range(NT):
            p.wait("sync", "ve")
            p.wait("sync", "ac")
            x_ld = p.dma("sync", xt[i % 2][:], x_in[128 * i:128 * (i + 1), :])
            p.wait("vector", "sd", x_ld)
            p.wait("scalar", "ve", tr_evacs)
            nrm = ln_chain(p, xt[i % 2], bn6, mv, stdc, rstd, nb, eps_c,
                           lambda s, b=xt[i % 2], o=xn_tm[i % 2]: s.activation(o[:], b[:], AF.Identity, bias=nb[:], scale=rstd[:]))
            p.wait("tensor", "ac", nrm)
            for j4 in range(4):
                p.wait("tensor", "ve", tr_evacs)
                for j in range(4):
                    c = j4 * 4 + j
                    p.op("tensor", "pe", lambda t, c=c, j=j, b=xn_tm[i % 2]: t.transpose(
                        ps_tr[:, 128 * j:128 * (j + 1)], b[:, 128 * c:128 * (c + 1)], ident_bf[:]))
                p.wait("vector", "pe")
                def evac(v, i=i, j4=j4):
                    dv = xn1[:, :].rearrange("p (c n) -> p c n", n=TPC)
                    return v.tensor_copy(dv[:, j4 * 4:(j4 + 1) * 4, 128 * i:128 * (i + 1)],
                                         ps_tr[:].rearrange("p (c n) -> p c n", n=128))
                tr_evacs = p.op("vector", "ve", evac)
            if serial:
                p.barrier()

        p.wait("sync", "ve")
        for c in range(KC):
            p.dma("sync", shard[2048 + 128 * c:2048 + 128 * (c + 1), :], xn1[:, TPC * c:TPC * (c + 1)])

        p.wait("tensor", "gd")
        evac_t = {0: 0, 1: 0}
        for oc in range(KC):
            is_kv = oc < NKV
            for th in range(2):
                par = (2 * oc + th) % 2
                pp = ps_p[par]
                p.wait("tensor", "ac", evac_t[par])
                for k in range(KC):
                    if is_kv:
                        lhs = wdkv_bf[:, DKV * k + 128 * oc: DKV * k + 128 * (oc + 1)]
                    else:
                        lhs = wdq_bf[:, DQ * k + 128 * (oc - NKV): DQ * k + 128 * (oc - NKV + 1)]
                    p.op("tensor", "pe", lambda t, l=lhs, k=k, pp=pp, th=th: t.matmul(
                        pp[:], l, xn1[:, TPC * k + 512 * th: TPC * k + 512 * (th + 1)],
                        start=(k == 0), stop=(k == KC - 1)))
                p.wait("scalar", "pe")
                p.wait("scalar", "sd")
                bcol = bdkv_sb[:, oc:oc + 1] if is_kv else bdq_sb[:, oc - NKV:oc - NKV + 1]
                sg = stage[(2 * oc + th) % 4]
                evac_t[par] = p.op("scalar", "ac", lambda s, pp=pp, b=bcol, sg=sg: s.activation(sg[:], pp[:], AF.Identity, bias=b, scale=1.0))
                p.wait("sync", "ac", evac_t[par])
                row = 128 * oc if is_kv else 512 + 128 * (oc - NKV)
                p.dma("sync", shard[row:row + 128, 512 * th:512 * (th + 1)], sg[:])
            if serial:
                p.barrier()

        p.wait("gpsimd", "sd")
        p.op("gpsimd", "cc", lambda g: g.collective_compute(
            "AllGather", ALU.bypass, replica_groups=GROUPS, ins=[shard.opt()], outs=[l_all.opt()]))

    if "t_shard" in tap_aps:
        p.wait("gpsimd", "cc")
        for r in range(4):
            p.dma("gpsimd", tap_aps["t_shard"][4096 * r:4096 * (r + 1), :], l_all[4096 * r:4096 * (r + 1), :])
    if phases == 1:
        p.barrier()
        finalize(nc, p)
        st.close()
        return nc

    p.barrier()
    with ExitStack() as stS:
        sbS = lambda n, s, d: stS.enter_context(nc.sbuf_tensor(n, s, d))
        kC = [sbS(f"kC{s}", [128, SEQ], BF16) for s in range(NSLOT)]
        kR = [sbS(f"kR{s}", [128, SEQ], BF16) for s in range(NSLOT)]
        vS = [sbS(f"vS{s}", [128, SEQ], BF16) for s in range(NSLOT)]

        with ExitStack() as stB:
            sb2 = lambda n, s, d: stB.enter_context(nc.sbuf_tensor(n, s, d))
            ps2s = lambda n, s, d: stB.enter_context(nc.psum_tensor(n, s, d))
            lat_x = [sb2(f"latx{c}", [128, 512], BF16) for c in range(KC)]
            lat_q = [sb2(f"latq{c}", [128, 512], BF16) for c in range(NQ)]
            lat_kv = [sb2(f"latkv{c}", [128, 512], BF16) for c in range(NKV)]
            cos_s = sb2("cos_s", [128, 512], F32)
            sin_s = sb2("sin_s", [128, 512], F32)
            wkr_bf = sb2("wkr_bf", [128, KC * 512], BF16)
            wuk_bf = sb2("wuk_bf", [128, NKV * 512], BF16)
            wuv_bf = sb2("wuv_bf", [128, NKV * 512], BF16)
            wuq_bf = sb2("wuq_bf", [128, NQ * 512], BF16)
            bkr_sb = sb2("bkr_sb", [128, NSLOT], F32)
            buk_sb = sb2("buk_sb", [128, NSLOT], F32)
            buv_sb = sb2("buv_sb", [128, NSLOT], F32)
            buq_sb = sb2("buq_sb", [128, NSLOT], F32)
            krbf = sb2("krbf", [128, 512], BF16)
            vfm = sb2("vfm", [128, 512], BF16)
            qbf = sb2("qbf", [128, 512], BF16)
            t1 = sb2("t1", [128, 512], F32)
            t2 = sb2("t2", [128, 512], F32)
            ksq = sb2("ksq", [128, 512], BF16)
            qsq2 = sb2("qsq2", [128, 512], BF16)
            k2 = sb2("k2", [1, NSLOT], F32)
            k2t = sb2("k2t", [1, 1], F32)
            q2m = sb2("q2m", [1, NSLOT * NTS], F32)
            ps_a = [ps2s(f"ps_a{s}", [128, 512], F32) for s in range(NSLOT)]
            ps_rot = ps2s("ps_rot", [128, 512], F32)
            ps_vt = ps2s("ps_vt", [128, 512], BF16)
            ps_n = ps2s("ps_n", [1, 512], F32)

            p.dma("gpsimd", wkr_bf[:].rearrange("p (c n) -> p c n", c=KC), wkr.rearrange("(c p) n -> p c n", p=128))
            p.dma("gpsimd", wuk_bf[:].rearrange("p (c n) -> p c n", c=NKV), wuk.rearrange("(c p) n -> p c n", p=128))
            p.dma("gpsimd", wuv_bf[:].rearrange("p (c n) -> p c n", c=NKV), wuv.rearrange("(c p) n -> p c n", p=128))
            p.dma("gpsimd", wuq_bf[:].rearrange("p (c n) -> p c n", c=NQ), wuq.rearrange("(c p) n -> p c n", p=128))
            p.dma("sync", bkr_sb[:], bkr)
            p.dma("sync", buk_sb[:], buk)
            p.dma("sync", buv_sb[:], buv)
            p.dma("sync", buq_sb[:], buq)
            p.wait("sync", "cc")
            p.wait("tensor", "gd")

            kr_done = q_done = kv_done = 0
            for ts in range(NTS):
                r = ts // 2
                cs = 512 * (ts % 2)
                rb = 4096 * r
                p.wait("sync", "pe", kr_done)
                for c in range(KC):
                    p.dma("sync", lat_x[c][:], l_all[rb + 2048 + 128 * c: rb + 2048 + 128 * (c + 1), cs:cs + 512])
                p.wait("sync", "pe", q_done)
                for c in range(NQ):
                    p.dma("sync", lat_q[c][:], l_all[rb + 512 + 128 * c: rb + 512 + 128 * (c + 1), cs:cs + 512])
                p.wait("sync", "pe", kv_done)
                for c in range(NKV):
                    p.dma("sync", lat_kv[c][:], l_all[rb + 128 * c: rb + 128 * (c + 1), cs:cs + 512])
                p.wait("sync", "ve")
                ld = p.dma("sync", cos_s[:], cos_td[:, 512 * ts:512 * (ts + 1)])
                ld = p.dma("sync", sin_s[:], sin_td[:, 512 * ts:512 * (ts + 1)])
                p.wait("tensor", "sd", ld)

                for s in range(NSLOT):
                    for k in range(KC):
                        p.op("tensor", "pe", lambda t, s=s, k=k: t.matmul(
                            ps_a[s][:], wkr_bf[:, 512 * k + 128 * s: 512 * k + 128 * (s + 1)], lat_x[k][:],
                            start=(k == 0), stop=(k == KC - 1)))
                kr_done = p.val("pe")
                for s in range(NSLOT):
                    p.wait("scalar", "pe")
                    p.wait("scalar", "ve")
                    ev = p.op("scalar", "ac", lambda sc, s=s: sc.activation(krbf[:], ps_a[s][:], AF.Identity, bias=bkr_sb[:, s:s + 1], scale=1.0))
                    p.wait("tensor", "ac", ev)
                    rt = p.op("tensor", "pe", lambda t: t.matmul(ps_rot[:], rotm_bf[:], krbf[:], start=True, stop=True))
                    p.wait("vector", "ac", ev)
                    m1 = p.op("vector", "ve", lambda v: v.tensor_mul(t1[:], krbf[:], cos_s[:]))
                    p.wait("vector", "pe", rt)
                    m2 = p.op("vector", "ve", lambda v: v.tensor_mul(t2[:], ps_rot[:], sin_s[:]))
                    p.wait("vector", "ve", m2)
                    p.op("vector", "ve", lambda v, s=s, ts=ts: v.tensor_add(kR[s][:, 512 * ts:512 * (ts + 1)], t1[:], t2[:]))
                    p.wait("tensor", "ve")

                for s in range(NSLOT):
                    for k in range(NQ):
                        p.op("tensor", "pe", lambda t, s=s, k=k: t.matmul(
                            ps_a[s][:], wuq_bf[:, 512 * k + 128 * s: 512 * k + 128 * (s + 1)], lat_q[k][:],
                            start=(k == 0), stop=(k == NQ - 1)))
                q_done = p.val("pe")
                for s in range(NSLOT):
                    p.wait("scalar", "pe")
                    p.wait("scalar", "sd")
                    p.wait("scalar", "ve")
                    ev = p.op("scalar", "ac", lambda sc, s=s: sc.activation(qbf[:], ps_a[s][:], AF.Identity, bias=buq_sb[:, s:s + 1], scale=1.0))
                    p.wait("sync", "ac", ev)
                    p.dma("sync", q_dram[256 * s:256 * s + 128, 512 * ts:512 * (ts + 1)], qbf[:])
                    p.wait("tensor", "ac", ev)
                    rt = p.op("tensor", "pe", lambda t: t.matmul(ps_rot[:], rotm_bf[:], qbf[:], start=True, stop=True))
                    p.wait("vector", "ac", ev)
                    m1 = p.op("vector", "ve", lambda v: v.tensor_mul(t1[:], qbf[:], cos_s[:]))
                    p.wait("vector", "pe", rt)
                    m2 = p.op("vector", "ve", lambda v: v.tensor_mul(t2[:], ps_rot[:], sin_s[:]))
                    p.wait("vector", "ve", m2)
                    ad = p.op("vector", "ve", lambda v: v.tensor_add(t1[:], t1[:], t2[:]))
                    p.wait("vector", "ve", ad)
                    p.wait("vector", "ac")
                    p.wait("vector", "sd")
                    cv = p.op("vector", "ve", lambda v: v.tensor_copy(krbf[:], t1[:]))
                    p.wait("sync", "ve", cv)
                    p.dma("sync", q_dram[256 * s + 128:256 * s + 256, 512 * ts:512 * (ts + 1)], krbf[:])
                    p.wait("scalar", "pe")
                    n1 = p.op("scalar", "ac", lambda sc: sc.activation(qsq2[:], qbf[:], AF.Square))
                    p.wait("tensor", "ac", n1)
                    p.wait("tensor", "ve")
                    p.op("tensor", "pe", lambda t: t.matmul(ps_n[:], ones_cb[:], qsq2[:], start=True, stop=False))
                    p.wait("scalar", "pe")
                    p.wait("scalar", "ve", cv)
                    n2 = p.op("scalar", "ac", lambda sc: sc.activation(qsq2[:], krbf[:], AF.Square))
                    p.wait("tensor", "ac", n2)
                    nm = p.op("tensor", "pe", lambda t: t.matmul(ps_n[:], ones_cb[:], qsq2[:], start=False, stop=True))
                    p.wait("vector", "pe", nm)
                    p.op("vector", "ve", lambda v, s=s, ts=ts: v.tensor_reduce(q2m[:, 8 * s + ts:8 * s + ts + 1], ps_n[:], AX, ALU.max))
                    p.wait("tensor", "ve")

                for s in range(NSLOT):
                    for k in range(NKV):
                        p.op("tensor", "pe", lambda t, s=s, k=k: t.matmul(
                            ps_a[s][:], wuk_bf[:, 512 * k + 128 * s: 512 * k + 128 * (s + 1)], lat_kv[k][:],
                            start=(k == 0), stop=(k == NKV - 1)))
                for s in range(NSLOT):
                    p.wait("scalar", "pe")
                    p.op("scalar", "ac", lambda sc, s=s, ts=ts: sc.activation(
                        kC[s][:, 512 * ts:512 * (ts + 1)], ps_a[s][:], AF.Identity, bias=buk_sb[:, s:s + 1], scale=1.0))
                p.wait("tensor", "ac")

                for s in range(NSLOT):
                    for k in range(NKV):
                        p.op("tensor", "pe", lambda t, s=s, k=k: t.matmul(
                            ps_a[s][:], wuv_bf[:, 512 * k + 128 * s: 512 * k + 128 * (s + 1)], lat_kv[k][:],
                            start=(k == 0), stop=(k == NKV - 1)))
                kv_done = p.val("pe")
                for s in range(NSLOT):
                    p.wait("scalar", "pe")
                    ev = p.op("scalar", "ac", lambda sc, s=s: sc.activation(vfm[:], ps_a[s][:], AF.Identity, bias=buv_sb[:, s:s + 1], scale=1.0))
                    p.wait("tensor", "ac", ev)
                    p.wait("tensor", "ve")
                    for j in range(4):
                        p.op("tensor", "pe", lambda t, j=j: t.transpose(ps_vt[:, 128 * j:128 * (j + 1)], vfm[:, 128 * j:128 * (j + 1)], ident_bf[:]))
                    p.wait("vector", "pe")
                    p.op("vector", "ve", lambda v, s=s, ts=ts: v.tensor_copy(vS[s][:, 512 * ts:512 * (ts + 1)], ps_vt[:]))
                    p.wait("scalar", "pe")
                p.wait("tensor", "ve")
                if serial:
                    p.barrier()

            p.op("vector", "ve", lambda v: v.memset(k2[:], 0.0))
            for s in range(NSLOT):
                for sl in range(NTS):
                    p.wait("scalar", "pe")
                    sq1 = p.op("scalar", "ac", lambda sc, s=s, sl=sl: sc.activation(ksq[:], kC[s][:, 512 * sl:512 * (sl + 1)], AF.Square))
                    p.wait("tensor", "ac", sq1)
                    p.wait("tensor", "ve")
                    p.op("tensor", "pe", lambda t: t.matmul(ps_n[:], ones_cb[:], ksq[:], start=True, stop=False))
                    p.wait("scalar", "pe")
                    sq2 = p.op("scalar", "ac", lambda sc, s=s, sl=sl: sc.activation(ksq[:], kR[s][:, 512 * sl:512 * (sl + 1)], AF.Square))
                    p.wait("tensor", "ac", sq2)
                    mm = p.op("tensor", "pe", lambda t: t.matmul(ps_n[:], ones_cb[:], ksq[:], start=False, stop=True))
                    p.wait("vector", "pe", mm)
                    rd = p.op("vector", "ve", lambda v: v.tensor_reduce(k2t[:], ps_n[:], AX, ALU.max))
                    p.wait("vector", "ve", rd)
                    p.op("vector", "ve", lambda v, s=s: v.tensor_tensor(k2[:, s:s + 1], k2[:, s:s + 1], k2t[:], ALU.max))
                    p.wait("vector", "ve")
            p.op("vector", "ve", lambda v: v.tensor_copy(k2p[:], k2[:]))
            p.wait("vector", "ve")
            p.op("vector", "ve", lambda v: v.tensor_copy(q2mp[:], q2m[:]))
            p.barrier()

        with ExitStack() as stA2:
            sb3 = lambda n, s, d: stA2.enter_context(nc.sbuf_tensor(n, s, d))
            ps3s = lambda n, s, d: stA2.enter_context(nc.psum_tensor(n, s, d))
            masks = sb3("masks", [128, 2048], F32)
            qc_t = [sb3(f"qc_t{i}", [128, 512], BF16) for i in range(2)]
            qr_t = [sb3(f"qr_t{i}", [128, 512], BF16) for i in range(2)]
            prb = [sb3(f"prb{i}", [128, 512], BF16) for i in range(4)]
            ctxs = sb3("ctxs", [128, SEQ], BF16)
            craw = [sb3(f"craw{i}", [128, 512], F32) for i in range(2)]
            bias_all = sb3("bias_all", [128, NSLOT * NTS], F32)
            b2r = sb3("b2r", [1, NSLOT * NTS], F32)
            srow = [sb3(f"srow{i}", [1, 512], F32) for i in range(2)]
            rrow = sb3("rrow", [1, 512], F32)
            bc_sb = [sb3(f"bc_sb{i}", [128, 512], F32) for i in range(2)]
            ps_sc = [ps3s(f"ps_sc{i}", [128, 512], F32) for i in range(4)]
            ps_ctx = ps3s("ps_ctx", [128, 512], F32)
            ps_sum = ps3s("ps_sum", [1, 512], F32)
            ps_bc = ps3s("ps_bc", [128, 512], F32)

            p.dma("sync", masks[:], mask_t)
            bb = p.op("vector", "ve", lambda v: v.tensor_tensor(
                b2r[:].rearrange("a (s t) -> a s t", s=NSLOT),
                q2mp[:].rearrange("a (s t) -> a s t", s=NSLOT),
                k2p[:].rearrange("a (s t) -> a s t", t=1).broadcast_to((1, NSLOT, NTS)),
                ALU.mult))
            p.wait("scalar", "ve", bb)
            bsq = p.op("scalar", "ac", lambda sc: sc.activation(b2r[:], b2r[:], AF.Sqrt))
            p.wait("tensor", "ac", bsq)
            bmm = p.op("tensor", "pe", lambda t: t.matmul(ps_bc[:, 0:NSLOT * NTS], ones_rf[:], b2r[:], start=True, stop=True))
            p.wait("scalar", "pe", bmm)
            bev = p.op("scalar", "ac", lambda sc: sc.activation(bias_all[:], ps_bc[:, 0:NSLOT * NTS], AF.Identity, scale=-0.0625))
            p.wait("scalar", "ac", bev)
            p.wait("tensor", "ac", bev)

            exp_cnt = {i: 0 for i in range(4)}
            cons_cnt = {i: 0 for i in range(4)}
            LOOK = 2
            for s in range(NSLOT):
                for qc in range(NTS):
                    qp = qc % 2
                    p.wait("sync", "ac")
                    p.wait("sync", "pe")
                    p.dma("sync", qc_t[qp][:], q_dram[256 * s:256 * s + 128, 512 * qc:512 * (qc + 1)])
                    qld = p.dma("sync", qr_t[qp][:], q_dram[256 * s + 128:256 * s + 256, 512 * qc:512 * (qc + 1)])
                    p.wait("tensor", "sd", qld)
                    bcol = bias_all[:, 8 * s + qc:8 * s + qc + 1]
                    nkt = 4 * (qc + 1)
                    pend = []
                    for ik in range(nkt):
                        kb, kt = divmod(ik, 4)
                        par = ik % 4
                        ring = ik % 4
                        p.wait("tensor", "ac", exp_cnt[par])
                        p.op("tensor", "pe", lambda t, s=s, kb=kb, kt=kt, par=par, qp=qp: t.matmul(
                            ps_sc[par][:], kC[s][:, 512 * kb + 128 * kt: 512 * kb + 128 * (kt + 1)], qc_t[qp][:], start=True, stop=False))
                        mm = p.op("tensor", "pe", lambda t, s=s, kb=kb, kt=kt, par=par, qp=qp: t.matmul(
                            ps_sc[par][:], kR[s][:, 512 * kb + 128 * kt: 512 * kb + 128 * (kt + 1)], qr_t[qp][:], start=False, stop=True))
                        gate = ("pe", mm)
                        if kb == qc:
                            p.wait("vector", "pe", mm)
                            mk = p.op("vector", "ve", lambda v, kt=kt, par=par: v.tensor_tensor(
                                ps_sc[par][:], ps_sc[par][:], masks[:, 512 * kt:512 * (kt + 1)], ALU.add))
                            gate = ("ve", mk)
                        p.wait("scalar", gate[0], gate[1])
                        p.wait("scalar", "pe", cons_cnt[ring])
                        ex = p.op("scalar", "ac", lambda sc, par=par, ring=ring, bcol=bcol: sc.activation(
                            prb[ring][:], ps_sc[par][:], AF.Exp, bias=bcol, scale=0.0625))
                        exp_cnt[par] = ex
                        pend.append((ik, ring, ex))
                        if len(pend) > LOOK:
                            jk, jring, jex = pend.pop(0)
                            p.wait("tensor", "ac", jex)
                            p.op("tensor", "pe", lambda t, s=s, jk=jk, jring=jring, first=(jk == 0), last=(jk == nkt - 1): t.matmul(
                                ps_ctx[:], vS[s][:, 128 * jk: 128 * (jk + 1)], prb[jring][:], start=first, stop=last))
                            cmm = p.op("tensor", "pe", lambda t, jring=jring, first=(jk == 0), last=(jk == nkt - 1): t.matmul(
                                ps_sum[:], ones_cb[:], prb[jring][:], start=first, stop=last))
                            cons_cnt[jring] = cmm
                    for jk, jring, jex in pend:
                        p.wait("tensor", "ac", jex)
                        p.op("tensor", "pe", lambda t, s=s, jk=jk, jring=jring, first=(jk == 0), last=(jk == nkt - 1): t.matmul(
                            ps_ctx[:], vS[s][:, 128 * jk: 128 * (jk + 1)], prb[jring][:], start=first, stop=last))
                        cmm = p.op("tensor", "pe", lambda t, jring=jring, first=(jk == 0), last=(jk == nkt - 1): t.matmul(
                            ps_sum[:], ones_cb[:], prb[jring][:], start=first, stop=last))
                        cons_cnt[jring] = cmm
                    p.wait("vector", "pe")
                    e1 = p.op("vector", "ve", lambda v, qp=qp: v.tensor_copy(srow[qp][:], ps_sum[:]))
                    cr = p.op("vector", "ve", lambda v, qp=qp: v.tensor_copy(craw[qp][:], ps_ctx[:]))
                    p.wait("tensor", "ve", cr)
                    p.wait("vector", "ve", e1)
                    e2 = p.op("vector", "ve", lambda v, qp=qp: v.reciprocal(rrow[:], srow[qp][:]))
                    p.wait("tensor", "ve", e2)
                    bm = p.op("tensor", "pe", lambda t: t.matmul(ps_bc[:], ones_rf[:], rrow[:], start=True, stop=True))
                    p.wait("vector", "pe", bm)
                    e3 = p.op("vector", "ve", lambda v, qp=qp: v.tensor_copy(bc_sb[qp][:], ps_bc[:]))
                    p.wait("vector", "ve", e3)
                    p.op("vector", "ve", lambda v, qc=qc, qp=qp: v.tensor_mul(ctxs[:, 512 * qc:512 * (qc + 1)], craw[qp][:], bc_sb[qp][:]))
                    p.wait("tensor", "ve", e3)
                    if serial:
                        p.barrier()
                p.wait("sync", "ve")
                p.dma("sync", ctx_shard[128 * s:128 * (s + 1), :], ctxs[:])
                p.wait("vector", "sd")

            p.wait("gpsimd", "sd")
            p.op("gpsimd", "cc", lambda g: g.collective_compute(
                "AllGather", ALU.bypass, replica_groups=GROUPS, ins=[ctx_shard.opt()], outs=[ctx_all.opt()]))

    if "t_qdram" in tap_aps:
        p.wait("gpsimd", "cc")
        p.dma("gpsimd", tap_aps["t_qdram"][:, :], q_dram)
    if "t_ctx" in tap_aps:
        p.wait("gpsimd", "cc")
        p.dma("gpsimd", tap_aps["t_ctx"][:, :], ctx_all)
    if phases == 2:
        p.barrier()
        finalize(nc, p)
        st.close()
        return nc

    p.barrier()
    with ExitStack() as stA:
        sbA = lambda n, s, d: stA.enter_context(nc.sbuf_tensor(n, s, d))
        bo_row = sbA("bo_row", [1, D], F32)
        ds_row = sbA("ds_row", [1, D], F32)
        ds_bc = sbA("ds_bc", [128, D], F32)
        xn2 = sbA("xn2", [128, KC * TPC], BF16)
        bn6b = [sbA(f"bn6b{i}", [128, 24], F32) for i in range(2)]
        mvb = [sbA(f"mvb{i}", [128, 2], F32) for i in range(2)]
        stdb = [sbA(f"stdb{i}", [128, 1], F32) for i in range(2)]
        rstdb = [sbA(f"rstdb{i}", [128, 1], F32) for i in range(2)]
        nbb = [sbA(f"nbb{i}", [128, 1], F32) for i in range(2)]

        p.dma("sync", bo_row[:], bo_r)
        p.dma("sync", ds_row[:], ds_r)

        with ExitStack() as stB3:
            sbB = lambda n, s, d: stB3.enter_context(nc.sbuf_tensor(n, s, d))
            psB = lambda n, s, d: stB3.enter_context(nc.psum_tensor(n, s, d))
            wo_bf = sbB("wo_bf", [128, KC * D], BF16)
            cx = sbB("cx", [128, KC * TPC], BF16)
            bo_bc = sbB("bo_bc", [128, D], F32)
            xt3 = [sbB(f"xt3{i}", [128, D], F32) for i in range(2)]
            y_tb = [sbB(f"y_tb{i}", [128, D], F32) for i in range(2)]
            yn_tb1 = sbB("yn_tb1", [128, D], F32)
            yn_tb = [yn_tb1, yn_tb1]
            xn2_tm1 = sbB("xn2_tm1", [128, D], BF16)
            xn2_tm = [xn2_tm1, xn2_tm1]
            ps_o = [psB(f"ps_o{i}", [128, TPC], F32) for i in range(2)]
            ps_tr3 = psB("ps_tr3", [128, 512], BF16)
            ps_bb = psB("ps_bb", [128, 512], F32)

            p.dma("gpsimd", wo_bf[:].rearrange("p (c n) -> p c n", c=KC), wo.rearrange("(c p) n -> p c n", p=128))
            p.wait("gpsimd", "sd")
            p.wait("gpsimd", "cc")

            def ctx_reads(g):
                last = None
                with g.register("tok") as tok:
                    g.reg_load(tok, meta_sb[0:1, 0:1])
                    off = g.snap(tok, min_val=0, max_val=3 * TPC)
                    for ch in range(KC):
                        last = g.dma_start(out=cx[:, TPC * ch:TPC * (ch + 1)],
                                           in_=ctx_all[128 * ch:128 * (ch + 1), ds(off, TPC)])
                        last.then_inc(SEM["gd"], 16)
                return last
            p.emit("gpsimd", ctx_reads)
            p.bump("gd", 16 * KC)

            p.wait("tensor", "sd")
            for h in range(4):
                p.wait("tensor", "ve")
                bm = p.op("tensor", "pe", lambda t, h=h: t.matmul(ps_bb[:], ones_rf[:], bo_row[:, 512 * h:512 * (h + 1)], start=True, stop=True))
                p.wait("vector", "pe", bm)
                p.op("vector", "ve", lambda v, h=h: v.tensor_copy(bo_bc[:, 512 * h:512 * (h + 1)], ps_bb[:]))
                p.wait("tensor", "ve")
                bm = p.op("tensor", "pe", lambda t, h=h: t.matmul(ps_bb[:], ones_rf[:], ds_row[:, 512 * h:512 * (h + 1)], start=True, stop=True))
                p.wait("vector", "pe", bm)
                p.op("vector", "ve", lambda v, h=h: v.tensor_copy(ds_bc[:, 512 * h:512 * (h + 1)], ps_bb[:]))

            p.wait("tensor", "gd")
            tr_ev = 0
            oev = {0: 0, 1: 0}
            for i in range(NT):
                p.wait("sync", "ve")
                xl = p.dma("sync", xt3[i % 2][:], x_in[128 * i:128 * (i + 1), :])
                for oh in range(2):
                    p.wait("tensor", "ve", oev[oh])
                    for c in range(KC):
                        for h in range(2):
                            p.op("tensor", "pe", lambda t, c=c, h=h, i=i, oh=oh: t.matmul(
                                ps_o[oh][:, 512 * h:512 * (h + 1)],
                                cx[:, TPC * c + 128 * i: TPC * c + 128 * (i + 1)],
                                wo_bf[:, D * c + TPC * oh + 512 * h: D * c + TPC * oh + 512 * (h + 1)],
                                start=(c == 0), stop=(c == KC - 1)))
                    p.wait("vector", "pe")
                    p.wait("vector", "ac")
                    oev[oh] = p.op("vector", "ve", lambda v, i=i, oh=oh: v.tensor_add(
                        y_tb[i % 2][:, TPC * oh:TPC * (oh + 1)], ps_o[oh][:], bo_bc[:, TPC * oh:TPC * (oh + 1)]))
                p.wait("vector", "ve")
                n2 = ln_chain(p, y_tb[i % 2], bn6b[i % 2], mvb[i % 2], stdb[i % 2], rstdb[i % 2], nbb[i % 2], eps_c,
                              lambda s, i=i: s.activation(yn_tb[i % 2][:], y_tb[i % 2][:], AF.Identity, bias=nbb[i % 2][:], scale=rstdb[i % 2][:]))
                p.wait("vector", "ac", n2)
                p.wait("vector", "sd", xl)
                x2a = p.op("vector", "ve", lambda v, b=xt3[i % 2], i=i: v.tensor_add(b[:], b[:], yn_tb[i % 2][:]))
                p.wait("sync", "ve", x2a)
                p.dma("sync", x2_dram[128 * i:128 * (i + 1), :], xt3[i % 2][:])
                p.wait("scalar", "ve", tr_ev)
                n3 = ln_chain(p, xt3[i % 2], bn6b[i % 2], mvb[i % 2], stdb[i % 2], rstdb[i % 2], nbb[i % 2], eps_c,
                              lambda s, b=xt3[i % 2], i=i: s.activation(xn2_tm[i % 2][:], b[:], AF.Identity, bias=nbb[i % 2][:], scale=rstdb[i % 2][:]))
                p.wait("tensor", "ac", n3)
                for j4 in range(4):
                    p.wait("tensor", "ve", tr_ev)
                    for j in range(4):
                        c = j4 * 4 + j
                        p.op("tensor", "pe", lambda t, c=c, j=j, i=i: t.transpose(
                            ps_tr3[:, 128 * j:128 * (j + 1)], xn2_tm[i % 2][:, 128 * c:128 * (c + 1)], ident_bf[:]))
                    p.wait("vector", "pe")
                    def evac3(v, i=i, j4=j4):
                        dv = xn2[:, :].rearrange("p (c n) -> p c n", n=TPC)
                        return v.tensor_copy(dv[:, j4 * 4:(j4 + 1) * 4, 128 * i:128 * (i + 1)],
                                             ps_tr3[:].rearrange("p (c n) -> p c n", n=128))
                    tr_ev = p.op("vector", "ve", evac3)
                if serial:
                    p.barrier()

        p.barrier()
        with ExitStack() as stC:
            sbC = lambda n, s, d: stC.enter_context(nc.sbuf_tensor(n, s, d))
            psC = lambda n, s, d: stC.enter_context(nc.psum_tensor(n, s, d))
            acc = sbC("acc", [128, NT * D], F32)
            wg_bf = [sbC(f"wg{i}", [128, KC * 128], BF16) for i in range(2)]
            wu_bf = [sbC(f"wu{i}", [128, KC * 128], BF16) for i in range(2)]
            wd_bf = [sbC(f"wd{i}", [128, D], BF16) for i in range(GRP)]
            hdn = [sbC(f"hdn{i}", [128, TPC], BF16) for i in range(GRP)]
            silu_g = sbC("silu_g", [128, TPC], BF16)
            gs_sb = sbC("gs_sb", [128, IC], F32)
            us_sb = sbC("us_sb", [128, IC], F32)
            xt41 = sbC("xt41", [128, D], F32)
            xt4 = [xt41, xt41]
            y41 = sbC("y41", [128, D], F32)
            y4 = [y41, y41]
            yn41 = sbC("yn41", [128, D], F32)
            yn4 = [yn41, yn41]
            ps_g = psC("ps_g", [128, TPC], F32)
            ps_u = psC("ps_u", [128, TPC], F32)
            ps_d = [psC(f"ps_d{i}", [128, TPC], F32) for i in range(2)]

            p.dma("sync", gs_sb[:], gs)
            p.dma("sync", us_sb[:], us)
            p.op("vector", "ve", lambda v: v.memset(acc[:], 0.0))
            p.wait("tensor", "ve")
            p.wait("tensor", "sd")

            groups = [list(range(g, min(g + GRP, IC))) for g in range(0, IC, GRP)]
            dn_done = 0
            hv_done = {0: 0, 1: 0}
            for grp in groups:
                for ii, icx in enumerate(grp):
                    p.wait("gpsimd", "ve", hv_done[ii % 2])
                    p.wait("gpsimd", "pe", dn_done)
                    p.dma("gpsimd", wg_bf[ii % 2][:].rearrange("p (c n) -> p c n", c=KC),
                          gwt[:, 128 * icx:128 * (icx + 1)].rearrange("(c p) n -> p c n", p=128))
                    p.dma("gpsimd", wu_bf[ii % 2][:].rearrange("p (c n) -> p c n", c=KC),
                          upt[:, 128 * icx:128 * (icx + 1)].rearrange("(c p) n -> p c n", p=128))
                    wld = p.dma("gpsimd", wd_bf[ii][:], dwt[128 * icx:128 * (icx + 1), :])
                    p.wait("tensor", "gd", wld)
                    p.wait("tensor", "ve", hv_done[ii % 2])
                    for k in range(KC):
                        for th in range(2):
                            p.op("tensor", "pe", lambda t, k=k, th=th, ii=ii: t.matmul(
                                ps_g[:, 512 * th:512 * (th + 1)], wg_bf[ii % 2][:, 128 * k:128 * (k + 1)],
                                xn2[:, TPC * k + 512 * th: TPC * k + 512 * (th + 1)],
                                start=(k == 0), stop=(k == KC - 1)))
                    for k in range(KC):
                        for th in range(2):
                            p.op("tensor", "pe", lambda t, k=k, th=th, ii=ii: t.matmul(
                                ps_u[:, 512 * th:512 * (th + 1)], wu_bf[ii % 2][:, 128 * k:128 * (k + 1)],
                                xn2[:, TPC * k + 512 * th: TPC * k + 512 * (th + 1)],
                                start=(k == 0), stop=(k == KC - 1)))
                    p.wait("scalar", "pe")
                    p.wait("scalar", "ve", hv_done[ii % 2])
                    sl = p.op("scalar", "ac", lambda s, icx=icx: s.activation(silu_g[:], ps_g[:], AF.Silu, scale=gs_sb[:, icx:icx + 1]))
                    p.wait("vector", "ac", sl)
                    p.wait("vector", "pe")
                    hv = p.op("vector", "ve", lambda v, ii=ii, icx=icx: v.scalar_tensor_tensor(
                        hdn[ii][:], ps_u[:], us_sb[:, icx:icx + 1], silu_g[:], ALU.mult, ALU.mult))
                    hv_done[ii % 2] = hv
                    p.wait("tensor", "ve", hv)
                dev = {0: 0, 1: 0}
                for tt in range(NT):
                    for oh in range(2):
                        dp = oh
                        p.wait("tensor", "ve", dev[dp])
                        for ii in range(len(grp)):
                            for th in range(2):
                                p.op("tensor", "pe", lambda t, ii=ii, tt=tt, th=th, oh=oh, dp=dp, first=(ii == 0), last=(ii == len(grp) - 1): t.matmul(
                                    ps_d[dp][:, 512 * th:512 * (th + 1)], hdn[ii][:, 128 * tt:128 * (tt + 1)],
                                    wd_bf[ii][:, TPC * oh + 512 * th: TPC * oh + 512 * (th + 1)],
                                    start=first, stop=last))
                        p.wait("vector", "pe")
                        dev[dp] = p.op("vector", "ve", lambda v, tt=tt, oh=oh, dp=dp: v.tensor_add(
                            acc[:, D * tt + TPC * oh:D * tt + TPC * (oh + 1)], acc[:, D * tt + TPC * oh:D * tt + TPC * (oh + 1)], ps_d[dp][:]))
                dn_done = p.val("pe")
                if serial:
                    p.barrier()

            if "t_xn2" in tap_aps:
                p.barrier()
                p.dma("gpsimd", tap_aps["t_xn2"][:, :], xn2[:])
            if "t_acc" in tap_aps:
                p.barrier()
                p.dma("sync", tap_aps["t_acc"][:, :], acc[:])
            if "t_x2" in tap_aps:
                p.barrier()
                p.dma("sync", tap_aps["t_x2"][:, :], x2_dram)
            for i in range(NT):
                p.wait("sync", "ve")
                xl = p.dma("sync", xt4[i % 2][:], x2_dram[128 * i:128 * (i + 1), :])
                p.wait("vector", "ac")
                ym = p.op("vector", "ve", lambda v, i=i: v.tensor_mul(y4[i % 2][:], acc[:, D * i:D * (i + 1)], ds_bc[:]))
                p.wait("vector", "ve", ym)
                n4 = ln_chain(p, y4[i % 2], bn6b[i % 2], mvb[i % 2], stdb[i % 2], rstdb[i % 2], nbb[i % 2], eps_c,
                              lambda s, i=i: s.activation(yn4[i % 2][:], y4[i % 2][:], AF.Identity, bias=nbb[i % 2][:], scale=rstdb[i % 2][:]))
                p.wait("vector", "ac", n4)
                p.wait("vector", "sd", xl)
                fo = p.op("vector", "ve", lambda v, i=i: v.tensor_add(xt4[i % 2][:], xt4[i % 2][:], yn4[i % 2][:]))
                p.wait("sync", "ve", fo)
                p.dma("sync", out[128 * i:128 * (i + 1), :], xt4[i % 2][:])
                if serial:
                    p.barrier()

    finalize(nc, p)
    st.close()
    return nc


def finalize(nc, p):
    p.barrier()
    with nc.Block() as block:
        @block.sync
        def _(e):
            for f in p.q["sync"]:
                f(e)

        @block.gpsimd
        def _(e):
            for f in p.q["gpsimd"]:
                f(e)

        @block.scalar
        def _(e):
            for f in p.q["scalar"]:
                f(e)

        @block.vector
        def _(e):
            for f in p.q["vector"]:
                f(e)

        @block.tensor
        def _(e):
            for f in p.q["tensor"]:
                f(e)


def make_consts():
    ident = np.eye(128, dtype=np.float32)
    rotm = np.zeros((128, 128), np.float32)
    for j in range(64):
        rotm[2 * j + 1, 2 * j] = -1.0
        rotm[2 * j, 2 * j + 1] = 1.0
    dd = np.arange(128) // 2
    inv = 1.0 / ROPE_BASE ** (dd.astype(np.float64) * 2.0 / 128.0)
    tpos = np.arange(SEQ, dtype=np.float64)
    ang = inv[:, None] * tpos[None, :]
    cos_t = np.cos(ang).astype(np.float32)
    sin_t = np.sin(ang).astype(np.float32)
    kk = np.arange(128)[:, None]
    qq = np.arange(512)[None, :]
    mask = np.concatenate([np.where(kt * 128 + kk > qq, NEG, 0.0) for kt in range(4)], axis=1).astype(np.float32)
    return dict(identc=ident, rotmc=rotm, cos_t=cos_t, sin_t=sin_t, mask_t=mask,
                ones_c=np.ones((128, 1), np.float32), ones_r=np.ones((1, 128), np.float32),
                eps_cd=np.full((128, 1), 1e-5, np.float32))


def cols(v, n):
    return np.ascontiguousarray(np.asarray(v, np.float32).reshape(n, 128).T)


def make_inmaps(inp):
    x = np.asarray(inp["L0_attn_h"], np.float32)
    consts = make_consts()
    g = lambda k: np.ascontiguousarray(np.asarray(inp[k], np.float32))
    wdkv_f = g("L0_attn_W_DKV"); bdkv_f = cols(inp["L0_attn_b_DKV"], NKV)
    wdq_f = g("L0_attn_W_DQ"); bdq_f = cols(inp["L0_attn_b_DQ"], NQ)
    wkr_f = g("L0_attn_W_KR"); bkr_f = np.asarray(inp["L0_attn_b_KR"], np.float32)
    wuk_f = g("L0_attn_W_UK"); buk_f = np.asarray(inp["L0_attn_b_UK"], np.float32)
    wuv_f = g("L0_attn_W_UV"); buv_f = np.asarray(inp["L0_attn_b_UV"], np.float32)
    wuq_f = g("L0_attn_W_UQ"); buq_f = np.asarray(inp["L0_attn_b_UQ"], np.float32)
    wo_f = g("L0_attn_W_O"); bo_f = np.asarray(inp["L0_attn_b_O"], np.float32).reshape(1, D)
    gwt_f = np.ascontiguousarray(g("L0_ffn_gate_w").T)
    upt_f = np.ascontiguousarray(g("L0_ffn_up_w").T)
    dwt_f = np.ascontiguousarray(g("L0_ffn_down_w").T)
    gs_f = cols(inp["L0_ffn_gate_s"], IC)
    us_f = cols(inp["L0_ffn_up_s"], IC)
    ds_f = np.asarray(inp["L0_ffn_down_s"], np.float32).reshape(1, D)

    maps = []
    for c in range(8):
        b, m = divmod(c, 4)
        sl = slice(512 * m, 512 * (m + 1))
        maps.append(dict(
            x=np.ascontiguousarray(x[b, TPC * m:TPC * (m + 1), :]),
            meta=np.array([[TPC * m]], np.int32),
            wdkv=wdkv_f, bdkv=bdkv_f, wdq=wdq_f, bdq=bdq_f,
            wkr=np.ascontiguousarray(wkr_f[:, sl]), bkr=cols(bkr_f[sl], NSLOT),
            wuk=np.ascontiguousarray(wuk_f[:, sl]), buk=cols(buk_f[sl], NSLOT),
            wuv=np.ascontiguousarray(wuv_f[:, sl]), buv=cols(buv_f[sl], NSLOT),
            wuq=np.ascontiguousarray(wuq_f[:, sl]), buq=cols(buq_f[sl], NSLOT),
            wo=wo_f, bo_r=bo_f, gwt=gwt_f, upt=upt_f, dwt=dwt_f,
            gs=gs_f, us=us_f, ds_r=ds_f,
            **consts,
        ))
    return maps


def kernel(**inp):
    nc = build_nc(phases=3)
    maps = make_inmaps(inp)
    res = run_bass_kernel_spmd(nc, maps, core_ids=list(range(8)))
    outv = np.empty((2, SEQ, D), np.float32)
    for c in range(8):
        b, m = divmod(c, 4)
        outv[b, TPC * m:TPC * (m + 1), :] = res.results[c]["out"]
    return outv


# version 03
# speedup vs baseline: 1.3918x; 1.0208x over previous
import numpy as np
from contextlib import ExitStack
import concourse.bass as bass
import concourse.mybir as mybir
from concourse.bass import ds
from concourse.bass_utils import run_bass_kernel_spmd

F32 = mybir.dt.float32
BF16 = mybir.dt.bfloat16
I32 = mybir.dt.int32
AX = mybir.AxisListType.X
ALU = mybir.AluOpType
AF = mybir.ActivationFunctionType

D = 2048
SEQ = 4096
TPC = 1024
NT = TPC // 128
KC = D // 128
DKV = 512
NKV = DKV // 128
DQ = 1536
NQ = DQ // 128
NSLOT = 4
NTS = SEQ // 512
INTER = 5632
IC = INTER // 128
GRP = 6
ROPE_BASE = 10000.0
GROUPS = [[0, 1, 2, 3], [4, 5, 6, 7]]
NEG = -30000.0


class Plan:
    ENGINES = ("sync", "gpsimd", "scalar", "vector", "tensor")

    def __init__(self):
        self.q = {e: [] for e in self.ENGINES}
        self.cnt = {}

    def val(self, sem):
        return self.cnt.get(sem, 0)

    def bump(self, sem, n):
        self.cnt[sem] = self.val(sem) + n
        return self.cnt[sem]

    def emit(self, eng, f):
        self.q[eng].append(f)

    def wait(self, eng, sem, v=None):
        v = self.val(sem) if v is None else v
        if v <= 0:
            return
        self.emit(eng, lambda e, s=sem, vv=v: e.wait_ge(SEM[s], vv))

    def dma(self, eng, out, in_, sem=None, **kw):
        sem = sem or ("gd" if eng == "gpsimd" else ("ad" if eng == "scalar" else "sd"))
        t = self.bump(sem, 16)
        self.emit(eng, lambda e, o=out, i=in_, s=sem, k=kw: e.dma_start(out=o, in_=i, **k).then_inc(SEM[s], 16))
        return t

    def op(self, eng, sem, build):
        t = self.bump(sem, 1)
        self.emit(eng, lambda e, b=build, s=sem: b(e).then_inc(SEM[s], 1))
        return t

    def barrier(self, engines=None):
        snap = {s: v for s, v in self.cnt.items() if v > 0}
        for eng in (engines or self.ENGINES):
            for s, v in snap.items():
                self.wait(eng, s, v)


SEM = {}


def ln_chain(p, src, bn6, mv, stdc, rstd, nb, eps_c, out_fn):
    for j in range(4):
        p.op("vector", "ve", lambda v, j=j: v.bn_stats(bn6[:, 6 * j:6 * j + 6], src[:, 512 * j:512 * (j + 1)]))
    p.wait("vector", "ve")
    p.op("vector", "ve", lambda v: v.bn_aggr(mv[:], bn6[:]))
    p.wait("scalar", "ve")
    sq = p.op("scalar", "ac", lambda s: s.activation(stdc[:], mv[:, 1:2], AF.Sqrt, bias=eps_c[:], scale=1.0))
    p.wait("vector", "ac", sq)
    p.op("vector", "ve", lambda v: v.reciprocal(rstd[:], stdc[:]))
    p.wait("vector", "ve")
    p.op("vector", "ve", lambda v: v.scalar_tensor_tensor(nb[:], mv[:, 0:1], -1.0, rstd[:], ALU.mult, ALU.mult))
    p.wait("scalar", "ve")
    return p.op("scalar", "ac", out_fn)


def build_nc(phases=3, taps=(), serial=False):
    nc = bass.Bass("TRN2", target_bir_lowering=False)
    p = Plan()

    def din(name, shape, dt=F32):
        return nc.dram_tensor(name, shape, dt, kind="ExternalInput").ap()

    x_in = din("x", [TPC, D])
    meta = din("meta", [1, 1], I32)
    wdkv = din("wdkv", [D, DKV])
    bdkv = din("bdkv", [128, NKV])
    wdq = din("wdq", [D, DQ])
    bdq = din("bdq", [128, NQ])
    wkr = din("wkr", [D, 512])
    bkr = din("bkr", [128, NSLOT])
    wuk = din("wuk", [DKV, 512])
    buk = din("buk", [128, NSLOT])
    wuv = din("wuv", [DKV, 512])
    buv = din("buv", [128, NSLOT])
    wuq = din("wuq", [DQ, 512])
    buq = din("buq", [128, NSLOT])
    wo = din("wo", [D, D])
    bo_r = din("bo_r", [1, D])
    gwt = din("gwt", [D, INTER])
    upt = din("upt", [D, INTER])
    dwt = din("dwt", [INTER, D])
    gs = din("gs", [128, IC])
    us = din("us", [128, IC])
    ds_r = din("ds_r", [1, D])
    identc = din("identc", [128, 128])
    rotmc = din("rotmc", [128, 128])
    cos_td = din("cos_t", [128, SEQ])
    sin_td = din("sin_t", [128, SEQ])
    mask_t = din("mask_t", [128, 4 * 512])
    ones_cd = din("ones_c", [128, 1])
    ones_rd = din("ones_r", [1, 128])
    eps_cd = din("eps_cd", [128, 1])

    out = nc.dram_tensor("out", [TPC, D], F32, kind="ExternalOutput").ap()

    shard = nc.dram_tensor("shard", [4096, TPC], BF16, kind="Internal").ap()
    l_all = nc.dram_tensor("l_all", [4 * 4096, TPC], BF16, kind="Internal").ap()
    q_dram = nc.dram_tensor("q_dram", [NSLOT * 256, SEQ], BF16, kind="Internal").ap()
    ctx_shard = nc.dram_tensor("ctx_shard", [512, SEQ], BF16, kind="Internal").ap()
    ctx_all = nc.dram_tensor("ctx_all", [2048, SEQ], BF16, kind="Internal").ap()
    x2_dram = nc.dram_tensor("x2_dram", [TPC, D], F32, kind="Internal").ap()

    tap_aps = {}
    if "shard" in taps:
        tap_aps["t_shard"] = nc.dram_tensor("t_shard", [4 * 4096, TPC], F32, kind="ExternalOutput").ap()
    if "qdram" in taps:
        tap_aps["t_qdram"] = nc.dram_tensor("t_qdram", [NSLOT * 256, SEQ], F32, kind="ExternalOutput").ap()
    if "ctx" in taps:
        tap_aps["t_ctx"] = nc.dram_tensor("t_ctx", [2048, SEQ], F32, kind="ExternalOutput").ap()
    if "x2" in taps:
        tap_aps["t_x2"] = nc.dram_tensor("t_x2", [TPC, D], F32, kind="ExternalOutput").ap()
    if "xn2" in taps:
        tap_aps["t_xn2"] = nc.dram_tensor("t_xn2", [128, KC * TPC], F32, kind="ExternalOutput").ap()
    if "acc" in taps:
        tap_aps["t_acc"] = nc.dram_tensor("t_acc", [128, NT * D], F32, kind="ExternalOutput").ap()

    st = ExitStack()
    sbuf = lambda n, s, d: st.enter_context(nc.sbuf_tensor(n, s, d))

    ident_bf = sbuf("ident_bf", [128, 128], BF16)
    rotm_bf = sbuf("rotm_bf", [128, 128], BF16)
    ones_cb = sbuf("ones_cb", [128, 1], BF16)
    ones_rf = sbuf("ones_rf", [1, 128], F32)
    eps_c = sbuf("eps_c", [128, 1], F32)
    meta_sb = st.enter_context(nc.sbuf_tensor("meta_sb", [1, 1], I32))
    k2p = sbuf("k2p", [1, NSLOT], F32)
    q2mp = sbuf("q2mp", [1, NSLOT * NTS], F32)

    for s in ("sd", "gd", "ad", "cc", "pe", "ve", "ac"):
        SEM[s] = st.enter_context(nc.semaphore(s))

    p.dma("gpsimd", ident_bf[:], identc)
    p.dma("gpsimd", rotm_bf[:], rotmc)
    p.dma("gpsimd", ones_cb[:], ones_cd)
    p.dma("sync", ones_rf[:], ones_rd)
    p.dma("sync", eps_c[:], eps_cd)
    p.dma("sync", meta_sb[:], meta)

    with ExitStack() as st1:
        sb1 = lambda n, s, d: st1.enter_context(nc.sbuf_tensor(n, s, d))
        ps1s = lambda n, s, d: st1.enter_context(nc.psum_tensor(n, s, d))
        xt = [sb1(f"xt{i}", [128, D], F32) for i in range(2)]
        bn6 = sb1("bn6", [128, 24], F32)
        mv = sb1("mv", [128, 2], F32)
        stdc = sb1("stdc", [128, 1], F32)
        rstd = sb1("rstd", [128, 1], F32)
        nb = sb1("nb", [128, 1], F32)
        xn_tm = [sb1(f"xntm{i}", [128, D], BF16) for i in range(2)]
        xn1 = sb1("xn1", [128, KC * TPC], BF16)
        wdkv_bf = sb1("wdkv_bf", [128, KC * DKV], BF16)
        wdq_bf = sb1("wdq_bf", [128, KC * DQ], BF16)
        bdkv_sb = sb1("bdkv_sb", [128, NKV], F32)
        bdq_sb = sb1("bdq_sb", [128, NQ], F32)
        stage = [sb1(f"stg{i}", [128, 512], BF16) for i in range(4)]
        ps_tr = ps1s("ps_tr", [128, 512], BF16)
        ps_p = [ps1s(f"ps_p{i}", [128, 512], F32) for i in range(2)]

        p.dma("gpsimd", wdkv_bf[:].rearrange("p (c n) -> p c n", c=KC), wdkv.rearrange("(c p) n -> p c n", p=128))
        p.dma("gpsimd", wdq_bf[:].rearrange("p (c n) -> p c n", c=KC), wdq.rearrange("(c p) n -> p c n", p=128))
        p.dma("sync", bdkv_sb[:], bdkv)
        p.dma("sync", bdq_sb[:], bdq)

        tr_evacs = 0
        for i in range(NT):
            p.wait("sync", "ve")
            p.wait("sync", "ac")
            x_ld = p.dma("sync", xt[i % 2][:], x_in[128 * i:128 * (i + 1), :])
            p.wait("vector", "sd", x_ld)
            p.wait("scalar", "ve", tr_evacs)
            nrm = ln_chain(p, xt[i % 2], bn6, mv, stdc, rstd, nb, eps_c,
                           lambda s, b=xt[i % 2], o=xn_tm[i % 2]: s.activation(o[:], b[:], AF.Identity, bias=nb[:], scale=rstd[:]))
            p.wait("tensor", "ac", nrm)
            for j4 in range(4):
                p.wait("tensor", "ve", tr_evacs)
                for j in range(4):
                    c = j4 * 4 + j
                    p.op("tensor", "pe", lambda t, c=c, j=j, b=xn_tm[i % 2]: t.transpose(
                        ps_tr[:, 128 * j:128 * (j + 1)], b[:, 128 * c:128 * (c + 1)], ident_bf[:]))
                p.wait("vector", "pe")
                def evac(v, i=i, j4=j4):
                    dv = xn1[:, :].rearrange("p (c n) -> p c n", n=TPC)
                    return v.tensor_copy(dv[:, j4 * 4:(j4 + 1) * 4, 128 * i:128 * (i + 1)],
                                         ps_tr[:].rearrange("p (c n) -> p c n", n=128))
                tr_evacs = p.op("vector", "ve", evac)
            if serial:
                p.barrier()

        p.wait("sync", "ve")
        for c in range(KC):
            p.dma("sync", shard[2048 + 128 * c:2048 + 128 * (c + 1), :], xn1[:, TPC * c:TPC * (c + 1)])

        p.wait("tensor", "gd")
        evac_t = {0: 0, 1: 0}
        for oc in range(KC):
            is_kv = oc < NKV
            for th in range(2):
                par = (2 * oc + th) % 2
                pp = ps_p[par]
                p.wait("tensor", "ac", evac_t[par])
                for k in range(KC):
                    if is_kv:
                        lhs = wdkv_bf[:, DKV * k + 128 * oc: DKV * k + 128 * (oc + 1)]
                    else:
                        lhs = wdq_bf[:, DQ * k + 128 * (oc - NKV): DQ * k + 128 * (oc - NKV + 1)]
                    p.op("tensor", "pe", lambda t, l=lhs, k=k, pp=pp, th=th: t.matmul(
                        pp[:], l, xn1[:, TPC * k + 512 * th: TPC * k + 512 * (th + 1)],
                        start=(k == 0), stop=(k == KC - 1)))
                p.wait("scalar", "pe")
                p.wait("scalar", "sd")
                bcol = bdkv_sb[:, oc:oc + 1] if is_kv else bdq_sb[:, oc - NKV:oc - NKV + 1]
                sg = stage[(2 * oc + th) % 4]
                evac_t[par] = p.op("scalar", "ac", lambda s, pp=pp, b=bcol, sg=sg: s.activation(sg[:], pp[:], AF.Identity, bias=b, scale=1.0))
                p.wait("sync", "ac", evac_t[par])
                row = 128 * oc if is_kv else 512 + 128 * (oc - NKV)
                p.dma("sync", shard[row:row + 128, 512 * th:512 * (th + 1)], sg[:])
            if serial:
                p.barrier()

        p.wait("gpsimd", "sd")
        p.op("gpsimd", "cc", lambda g: g.collective_compute(
            "AllGather", ALU.bypass, replica_groups=GROUPS, ins=[shard.opt()], outs=[l_all.opt()]))

    if "t_shard" in tap_aps:
        p.wait("gpsimd", "cc")
        for r in range(4):
            p.dma("gpsimd", tap_aps["t_shard"][4096 * r:4096 * (r + 1), :], l_all[4096 * r:4096 * (r + 1), :])
    if phases == 1:
        p.barrier()
        finalize(nc, p)
        st.close()
        return nc

    p.barrier()
    with ExitStack() as stS:
        sbS = lambda n, s, d: stS.enter_context(nc.sbuf_tensor(n, s, d))
        kC = [sbS(f"kC{s}", [128, SEQ], BF16) for s in range(NSLOT)]
        kR = [sbS(f"kR{s}", [128, SEQ], BF16) for s in range(NSLOT)]
        vS = [sbS(f"vS{s}", [128, SEQ], BF16) for s in range(NSLOT)]

        with ExitStack() as stB:
            sb2 = lambda n, s, d: stB.enter_context(nc.sbuf_tensor(n, s, d))
            ps2s = lambda n, s, d: stB.enter_context(nc.psum_tensor(n, s, d))
            lat_x = [sb2(f"latx{c}", [128, 512], BF16) for c in range(KC)]
            lat_q = [sb2(f"latq{c}", [128, 512], BF16) for c in range(NQ)]
            lat_kv = [sb2(f"latkv{c}", [128, 512], BF16) for c in range(NKV)]
            cos_s = sb2("cos_s", [128, 512], F32)
            sin_s = sb2("sin_s", [128, 512], F32)
            wkr_bf = sb2("wkr_bf", [128, KC * 512], BF16)
            wuk_bf = sb2("wuk_bf", [128, NKV * 512], BF16)
            wuv_bf = sb2("wuv_bf", [128, NKV * 512], BF16)
            wuq_bf = sb2("wuq_bf", [128, NQ * 512], BF16)
            bkr_sb = sb2("bkr_sb", [128, NSLOT], F32)
            buk_sb = sb2("buk_sb", [128, NSLOT], F32)
            buv_sb = sb2("buv_sb", [128, NSLOT], F32)
            buq_sb = sb2("buq_sb", [128, NSLOT], F32)
            krbf_b = [sb2(f"krbf{i}", [128, 512], BF16) for i in range(2)]
            vfm = sb2("vfm", [128, 512], BF16)
            qbf_b = [sb2(f"qbf{i}", [128, 512], BF16) for i in range(2)]
            t1_b = [sb2(f"t1_{i}", [128, 512], F32) for i in range(2)]
            t2_b = [sb2(f"t2_{i}", [128, 512], F32) for i in range(2)]
            ksq = sb2("ksq", [128, 512], BF16)
            qsq2 = sb2("qsq2", [128, 512], BF16)
            k2 = sb2("k2", [1, NSLOT], F32)
            k2t = sb2("k2t", [1, 1], F32)
            q2m = sb2("q2m", [1, NSLOT * NTS], F32)
            ps_a = [ps2s(f"ps_a{s}", [128, 512], F32) for s in range(NSLOT)]
            ps_rot_b = [ps2s(f"ps_rot{i}", [128, 512], F32) for i in range(2)]
            ps_vt = ps2s("ps_vt", [128, 512], BF16)
            ps_n = ps2s("ps_n", [1, 512], F32)

            p.dma("gpsimd", wkr_bf[:].rearrange("p (c n) -> p c n", c=KC), wkr.rearrange("(c p) n -> p c n", p=128))
            p.dma("gpsimd", wuk_bf[:].rearrange("p (c n) -> p c n", c=NKV), wuk.rearrange("(c p) n -> p c n", p=128))
            p.dma("gpsimd", wuv_bf[:].rearrange("p (c n) -> p c n", c=NKV), wuv.rearrange("(c p) n -> p c n", p=128))
            p.dma("gpsimd", wuq_bf[:].rearrange("p (c n) -> p c n", c=NQ), wuq.rearrange("(c p) n -> p c n", p=128))
            p.dma("sync", bkr_sb[:], bkr)
            p.dma("sync", buk_sb[:], buk)
            p.dma("sync", buv_sb[:], buv)
            p.dma("sync", buq_sb[:], buq)
            p.wait("sync", "cc")
            p.wait("tensor", "gd")

            kr_done = q_done = kv_done = 0
            for ts in range(NTS):
                ts_ve0 = p.val("ve")
                ts_sd0 = p.val("sd")
                r = ts // 2
                cs = 512 * (ts % 2)
                rb = 4096 * r
                p.wait("sync", "pe", kr_done)
                for c in range(KC):
                    p.dma("sync", lat_x[c][:], l_all[rb + 2048 + 128 * c: rb + 2048 + 128 * (c + 1), cs:cs + 512])
                p.wait("sync", "pe", q_done)
                for c in range(NQ):
                    p.dma("sync", lat_q[c][:], l_all[rb + 512 + 128 * c: rb + 512 + 128 * (c + 1), cs:cs + 512])
                p.wait("sync", "pe", kv_done)
                for c in range(NKV):
                    p.dma("sync", lat_kv[c][:], l_all[rb + 128 * c: rb + 128 * (c + 1), cs:cs + 512])
                p.wait("sync", "ve")
                ld = p.dma("sync", cos_s[:], cos_td[:, 512 * ts:512 * (ts + 1)])
                ld = p.dma("sync", sin_s[:], sin_td[:, 512 * ts:512 * (ts + 1)])
                p.wait("tensor", "sd", ld)

                for s in range(NSLOT):
                    for k in range(KC):
                        p.op("tensor", "pe", lambda t, s=s, k=k: t.matmul(
                            ps_a[s][:], wkr_bf[:, 512 * k + 128 * s: 512 * k + 128 * (s + 1)], lat_x[k][:],
                            start=(k == 0), stop=(k == KC - 1)))
                kr_done = p.val("pe")
                kr_fin = {0: 0, 1: 0}
                for s in range(NSLOT):
                    par = s % 2
                    krbf = krbf_b[par]; t1 = t1_b[par]; t2 = t2_b[par]; ps_rot = ps_rot_b[par]
                    p.wait("scalar", "pe", kr_done - (NSLOT - 1 - s) * KC)
                    p.wait("scalar", "ve", ts_ve0)
                    p.wait("scalar", "sd", ts_sd0)
                    p.wait("scalar", "ve", kr_fin[par])
                    ev = p.op("scalar", "ac", lambda sc, s=s, krbf=krbf: sc.activation(krbf[:], ps_a[s][:], AF.Identity, bias=bkr_sb[:, s:s + 1], scale=1.0))
                    p.wait("tensor", "ac", ev)
                    rt = p.op("tensor", "pe", lambda t, krbf=krbf, ps_rot=ps_rot: t.matmul(ps_rot[:], rotm_bf[:], krbf[:], start=True, stop=True))
                    p.wait("vector", "ac", ev)
                    m1 = p.op("vector", "ve", lambda v, krbf=krbf, t1=t1: v.tensor_mul(t1[:], krbf[:], cos_s[:]))
                    p.wait("vector", "pe", rt)
                    m2 = p.op("vector", "ve", lambda v, t2=t2, ps_rot=ps_rot: v.tensor_mul(t2[:], ps_rot[:], sin_s[:]))
                    p.wait("vector", "ve", m2)
                    kr_fin[par] = p.op("vector", "ve", lambda v, s=s, ts=ts, t1=t1, t2=t2: v.tensor_add(kR[s][:, 512 * ts:512 * (ts + 1)], t1[:], t2[:]))
                p.wait("tensor", "ve", kr_fin[0])
                p.wait("tensor", "ve", kr_fin[1])

                for s in range(NSLOT):
                    for k in range(NQ):
                        p.op("tensor", "pe", lambda t, s=s, k=k: t.matmul(
                            ps_a[s][:], wuq_bf[:, 512 * k + 128 * s: 512 * k + 128 * (s + 1)], lat_q[k][:],
                            start=(k == 0), stop=(k == NQ - 1)))
                q_done = p.val("pe")
                q_cv = {0: 0, 1: 0}
                q_dma = {0: 0, 1: 0}
                nrd_last = 0
                for s in range(NSLOT):
                    par = s % 2
                    qbf = qbf_b[par]; krbf = krbf_b[par]; t1 = t1_b[par]; t2 = t2_b[par]; ps_rot = ps_rot_b[par]
                    p.wait("scalar", "pe", q_done - (NSLOT - 1 - s) * NQ)
                    p.wait("scalar", "ve", ts_ve0)
                    p.wait("scalar", "sd", ts_sd0)
                    p.wait("scalar", "sd", q_dma[par])
                    p.wait("scalar", "ve", q_cv[par])
                    ev = p.op("scalar", "ac", lambda sc, s=s, qbf=qbf: sc.activation(qbf[:], ps_a[s][:], AF.Identity, bias=buq_sb[:, s:s + 1], scale=1.0))
                    p.wait("sync", "ac", ev)
                    p.dma("sync", q_dram[256 * s:256 * s + 128, 512 * ts:512 * (ts + 1)], qbf[:])
                    p.wait("tensor", "ac", ev)
                    rt = p.op("tensor", "pe", lambda t, qbf=qbf, ps_rot=ps_rot: t.matmul(ps_rot[:], rotm_bf[:], qbf[:], start=True, stop=True))
                    p.wait("vector", "ac", ev)
                    m1 = p.op("vector", "ve", lambda v, qbf=qbf, t1=t1: v.tensor_mul(t1[:], qbf[:], cos_s[:]))
                    p.wait("vector", "pe", rt)
                    m2 = p.op("vector", "ve", lambda v, t2=t2, ps_rot=ps_rot: v.tensor_mul(t2[:], ps_rot[:], sin_s[:]))
                    p.wait("vector", "ve", m2)
                    ad = p.op("vector", "ve", lambda v, t1=t1, t2=t2: v.tensor_add(t1[:], t1[:], t2[:]))
                    p.wait("vector", "ve", ad)
                    cv = p.op("vector", "ve", lambda v, krbf=krbf, t1=t1: v.tensor_copy(krbf[:], t1[:]))
                    q_cv[par] = cv
                    p.wait("sync", "ve", cv)
                    q_dma[par] = p.dma("sync", q_dram[256 * s + 128:256 * s + 256, 512 * ts:512 * (ts + 1)], krbf[:])
                    p.wait("scalar", "ve", cv)
                    p.wait("scalar", "pe")
                    n1 = p.op("scalar", "ac", lambda sc, qbf=qbf: sc.activation(qsq2[:], qbf[:], AF.Square))
                    p.wait("tensor", "ac", n1)
                    p.wait("tensor", "ve", nrd_last)
                    p.op("tensor", "pe", lambda t: t.matmul(ps_n[:], ones_cb[:], qsq2[:], start=True, stop=False))
                    p.wait("scalar", "pe")
                    n2 = p.op("scalar", "ac", lambda sc, krbf=krbf: sc.activation(qsq2[:], krbf[:], AF.Square))
                    p.wait("tensor", "ac", n2)
                    nm = p.op("tensor", "pe", lambda t: t.matmul(ps_n[:], ones_cb[:], qsq2[:], start=False, stop=True))
                    p.wait("vector", "pe", nm)
                    nrd_last = p.op("vector", "ve", lambda v, s=s, ts=ts: v.tensor_reduce(q2m[:, 8 * s + ts:8 * s + ts + 1], ps_n[:], AX, ALU.max))
                p.wait("tensor", "ve", nrd_last)

                for s in range(NSLOT):
                    for k in range(NKV):
                        p.op("tensor", "pe", lambda t, s=s, k=k: t.matmul(
                            ps_a[s][:], wuk_bf[:, 512 * k + 128 * s: 512 * k + 128 * (s + 1)], lat_kv[k][:],
                            start=(k == 0), stop=(k == NKV - 1)))
                for s in range(NSLOT):
                    p.wait("scalar", "pe")
                    p.op("scalar", "ac", lambda sc, s=s, ts=ts: sc.activation(
                        kC[s][:, 512 * ts:512 * (ts + 1)], ps_a[s][:], AF.Identity, bias=buk_sb[:, s:s + 1], scale=1.0))
                p.wait("tensor", "ac")

                for s in range(NSLOT):
                    for k in range(NKV):
                        p.op("tensor", "pe", lambda t, s=s, k=k: t.matmul(
                            ps_a[s][:], wuv_bf[:, 512 * k + 128 * s: 512 * k + 128 * (s + 1)], lat_kv[k][:],
                            start=(k == 0), stop=(k == NKV - 1)))
                kv_done = p.val("pe")
                for s in range(NSLOT):
                    p.wait("scalar", "pe")
                    ev = p.op("scalar", "ac", lambda sc, s=s: sc.activation(vfm[:], ps_a[s][:], AF.Identity, bias=buv_sb[:, s:s + 1], scale=1.0))
                    p.wait("tensor", "ac", ev)
                    p.wait("tensor", "ve")
                    for j in range(4):
                        p.op("tensor", "pe", lambda t, j=j: t.transpose(ps_vt[:, 128 * j:128 * (j + 1)], vfm[:, 128 * j:128 * (j + 1)], ident_bf[:]))
                    p.wait("vector", "pe")
                    p.op("vector", "ve", lambda v, s=s, ts=ts: v.tensor_copy(vS[s][:, 512 * ts:512 * (ts + 1)], ps_vt[:]))
                    p.wait("scalar", "pe")
                p.wait("tensor", "ve")
                if serial:
                    p.barrier()

            p.op("vector", "ve", lambda v: v.memset(k2[:], 0.0))
            for s in range(NSLOT):
                for sl in range(NTS):
                    p.wait("scalar", "pe")
                    sq1 = p.op("scalar", "ac", lambda sc, s=s, sl=sl: sc.activation(ksq[:], kC[s][:, 512 * sl:512 * (sl + 1)], AF.Square))
                    p.wait("tensor", "ac", sq1)
                    p.wait("tensor", "ve")
                    p.op("tensor", "pe", lambda t: t.matmul(ps_n[:], ones_cb[:], ksq[:], start=True, stop=False))
                    p.wait("scalar", "pe")
                    sq2 = p.op("scalar", "ac", lambda sc, s=s, sl=sl: sc.activation(ksq[:], kR[s][:, 512 * sl:512 * (sl + 1)], AF.Square))
                    p.wait("tensor", "ac", sq2)
                    mm = p.op("tensor", "pe", lambda t: t.matmul(ps_n[:], ones_cb[:], ksq[:], start=False, stop=True))
                    p.wait("vector", "pe", mm)
                    rd = p.op("vector", "ve", lambda v: v.tensor_reduce(k2t[:], ps_n[:], AX, ALU.max))
                    p.wait("vector", "ve", rd)
                    p.op("vector", "ve", lambda v, s=s: v.tensor_tensor(k2[:, s:s + 1], k2[:, s:s + 1], k2t[:], ALU.max))
                    p.wait("vector", "ve")
            p.op("vector", "ve", lambda v: v.tensor_copy(k2p[:], k2[:]))
            p.wait("vector", "ve")
            p.op("vector", "ve", lambda v: v.tensor_copy(q2mp[:], q2m[:]))
            p.barrier()

        with ExitStack() as stA2:
            sb3 = lambda n, s, d: stA2.enter_context(nc.sbuf_tensor(n, s, d))
            ps3s = lambda n, s, d: stA2.enter_context(nc.psum_tensor(n, s, d))
            masks = sb3("masks", [128, 2048], F32)
            qc_t = [sb3(f"qc_t{i}", [128, 512], BF16) for i in range(2)]
            qr_t = [sb3(f"qr_t{i}", [128, 512], BF16) for i in range(2)]
            prb = [sb3(f"prb{i}", [128, 512], BF16) for i in range(4)]
            ctxs = sb3("ctxs", [128, SEQ], BF16)
            craw = [sb3(f"craw{i}", [128, 512], F32) for i in range(2)]
            bias_all = sb3("bias_all", [128, NSLOT * NTS], F32)
            b2r = sb3("b2r", [1, NSLOT * NTS], F32)
            srow = [sb3(f"srow{i}", [1, 512], F32) for i in range(2)]
            rrow = [sb3(f"rrow{i}", [1, 512], F32) for i in range(2)]
            bc_sb = [sb3(f"bc_sb{i}", [128, 512], F32) for i in range(2)]
            ps_sc = [ps3s(f"ps_sc{i}", [128, 512], F32) for i in range(3)]
            ps_ctx = [ps3s(f"ps_ctx{i}", [128, 512], F32) for i in range(2)]
            ps_sum = [ps3s(f"ps_sum{i}", [1, 512], F32) for i in range(2)]
            ps_bc = ps3s("ps_bc", [128, 512], F32)

            p.dma("sync", masks[:], mask_t)
            bb = p.op("vector", "ve", lambda v: v.tensor_tensor(
                b2r[:].rearrange("a (s t) -> a s t", s=NSLOT),
                q2mp[:].rearrange("a (s t) -> a s t", s=NSLOT),
                k2p[:].rearrange("a (s t) -> a s t", t=1).broadcast_to((1, NSLOT, NTS)),
                ALU.mult))
            p.wait("scalar", "ve", bb)
            bsq = p.op("scalar", "ac", lambda sc: sc.activation(b2r[:], b2r[:], AF.Sqrt))
            p.wait("tensor", "ac", bsq)
            bmm = p.op("tensor", "pe", lambda t: t.matmul(ps_bc[:, 0:NSLOT * NTS], ones_rf[:], b2r[:], start=True, stop=True))
            p.wait("scalar", "pe", bmm)
            bev = p.op("scalar", "ac", lambda sc: sc.activation(bias_all[:], ps_bc[:, 0:NSLOT * NTS], AF.Identity, scale=-0.0625))
            p.wait("scalar", "ac", bev)
            p.wait("tensor", "ac", bev)

            exp_cnt = {i: 0 for i in range(3)}
            cons_cnt = {i: 0 for i in range(4)}
            ctx_free = {0: 0, 1: 0}
            qtile_free = {0: 0, 1: 0}
            bc_free = 0
            LOOK = 2
            for s in range(NSLOT):
                for qc in range(NTS):
                    qp = qc % 2
                    p.wait("sync", "pe", qtile_free[qp])
                    p.dma("sync", qc_t[qp][:], q_dram[256 * s:256 * s + 128, 512 * qc:512 * (qc + 1)])
                    qld = p.dma("sync", qr_t[qp][:], q_dram[256 * s + 128:256 * s + 256, 512 * qc:512 * (qc + 1)])
                    p.wait("tensor", "sd", qld)
                    p.wait("tensor", "ve", ctx_free[qp])
                    bcol = bias_all[:, 8 * s + qc:8 * s + qc + 1]
                    nkt = 4 * (qc + 1)
                    pend = []
                    last_mm = 0
                    for ik in range(nkt):
                        kb, kt = divmod(ik, 4)
                        par = ik % 3
                        ring = ik % 4
                        p.wait("tensor", "ac", exp_cnt[par])
                        p.op("tensor", "pe", lambda t, s=s, kb=kb, kt=kt, par=par, qp=qp: t.matmul(
                            ps_sc[par][:], kC[s][:, 512 * kb + 128 * kt: 512 * kb + 128 * (kt + 1)], qc_t[qp][:], start=True, stop=False))
                        mm = p.op("tensor", "pe", lambda t, s=s, kb=kb, kt=kt, par=par, qp=qp: t.matmul(
                            ps_sc[par][:], kR[s][:, 512 * kb + 128 * kt: 512 * kb + 128 * (kt + 1)], qr_t[qp][:], start=False, stop=True))
                        last_mm = mm
                        gate = ("pe", mm)
                        if kb == qc:
                            p.wait("vector", "pe", mm)
                            mk = p.op("vector", "ve", lambda v, kt=kt, par=par: v.tensor_tensor(
                                ps_sc[par][:], ps_sc[par][:], masks[:, 512 * kt:512 * (kt + 1)], ALU.add))
                            gate = ("ve", mk)
                        p.wait("scalar", gate[0], gate[1])
                        p.wait("scalar", "pe", cons_cnt[ring])
                        ex = p.op("scalar", "ac", lambda sc, par=par, ring=ring, bcol=bcol: sc.activation(
                            prb[ring][:], ps_sc[par][:], AF.Exp, bias=bcol, scale=0.0625))
                        exp_cnt[par] = ex
                        pend.append((ik, ring, ex))
                        if len(pend) > LOOK:
                            jk, jring, jex = pend.pop(0)
                            p.wait("tensor", "ac", jex)
                            p.op("tensor", "pe", lambda t, s=s, jk=jk, jring=jring, qp=qp, first=(jk == 0), last=(jk == nkt - 1): t.matmul(
                                ps_ctx[qp][:], vS[s][:, 128 * jk: 128 * (jk + 1)], prb[jring][:], start=first, stop=last))
                            cmm = p.op("tensor", "pe", lambda t, jring=jring, qp=qp, first=(jk == 0), last=(jk == nkt - 1): t.matmul(
                                ps_sum[qp][:], ones_cb[:], prb[jring][:], start=first, stop=last))
                            cons_cnt[jring] = cmm
                    for jk, jring, jex in pend:
                        p.wait("tensor", "ac", jex)
                        p.op("tensor", "pe", lambda t, s=s, jk=jk, jring=jring, qp=qp, first=(jk == 0), last=(jk == nkt - 1): t.matmul(
                            ps_ctx[qp][:], vS[s][:, 128 * jk: 128 * (jk + 1)], prb[jring][:], start=first, stop=last))
                        cmm = p.op("tensor", "pe", lambda t, jring=jring, qp=qp, first=(jk == 0), last=(jk == nkt - 1): t.matmul(
                            ps_sum[qp][:], ones_cb[:], prb[jring][:], start=first, stop=last))
                        cons_cnt[jring] = cmm
                    qtile_free[qp] = last_mm
                    p.wait("vector", "pe")
                    e1 = p.op("vector", "ve", lambda v, qp=qp: v.tensor_copy(srow[qp][:], ps_sum[qp][:]))
                    cr = p.op("vector", "ve", lambda v, qp=qp: v.tensor_copy(craw[qp][:], ps_ctx[qp][:]))
                    p.wait("vector", "ve", e1)
                    e2 = p.op("vector", "ve", lambda v, qp=qp: v.reciprocal(rrow[qp][:], srow[qp][:]))
                    p.wait("tensor", "ve", bc_free)
                    p.wait("tensor", "ve", e2)
                    bm = p.op("tensor", "pe", lambda t, qp=qp: t.matmul(ps_bc[:], ones_rf[:], rrow[qp][:], start=True, stop=True))
                    p.wait("vector", "pe", bm)
                    e3 = p.op("vector", "ve", lambda v, qp=qp: v.tensor_copy(bc_sb[qp][:], ps_bc[:]))
                    bc_free = e3
                    p.wait("vector", "ve", e3)
                    mf = p.op("vector", "ve", lambda v, qc=qc, qp=qp: v.tensor_mul(ctxs[:, 512 * qc:512 * (qc + 1)], craw[qp][:], bc_sb[qp][:]))
                    ctx_free[qp] = cr
                    if serial:
                        p.barrier()
                p.wait("sync", "ve")
                p.dma("sync", ctx_shard[128 * s:128 * (s + 1), :], ctxs[:])
                p.wait("vector", "sd")

            p.wait("gpsimd", "sd")
            p.op("gpsimd", "cc", lambda g: g.collective_compute(
                "AllGather", ALU.bypass, replica_groups=GROUPS, ins=[ctx_shard.opt()], outs=[ctx_all.opt()]))

    if "t_qdram" in tap_aps:
        p.wait("gpsimd", "cc")
        p.dma("gpsimd", tap_aps["t_qdram"][:, :], q_dram)
    if "t_ctx" in tap_aps:
        p.wait("gpsimd", "cc")
        p.dma("gpsimd", tap_aps["t_ctx"][:, :], ctx_all)
    if phases == 2:
        p.barrier()
        finalize(nc, p)
        st.close()
        return nc

    p.barrier()
    with ExitStack() as stA:
        sbA = lambda n, s, d: stA.enter_context(nc.sbuf_tensor(n, s, d))
        bo_row = sbA("bo_row", [1, D], F32)
        ds_row = sbA("ds_row", [1, D], F32)
        ds_bc = sbA("ds_bc", [128, D], F32)
        xn2 = sbA("xn2", [128, KC * TPC], BF16)
        bn6b = [sbA(f"bn6b{i}", [128, 24], F32) for i in range(2)]
        mvb = [sbA(f"mvb{i}", [128, 2], F32) for i in range(2)]
        stdb = [sbA(f"stdb{i}", [128, 1], F32) for i in range(2)]
        rstdb = [sbA(f"rstdb{i}", [128, 1], F32) for i in range(2)]
        nbb = [sbA(f"nbb{i}", [128, 1], F32) for i in range(2)]

        p.dma("sync", bo_row[:], bo_r)
        p.dma("sync", ds_row[:], ds_r)

        with ExitStack() as stB3:
            sbB = lambda n, s, d: stB3.enter_context(nc.sbuf_tensor(n, s, d))
            psB = lambda n, s, d: stB3.enter_context(nc.psum_tensor(n, s, d))
            wo_bf = sbB("wo_bf", [128, KC * D], BF16)
            cx = sbB("cx", [128, KC * TPC], BF16)
            bo_bc = sbB("bo_bc", [128, D], F32)
            xt3 = [sbB(f"xt3{i}", [128, D], F32) for i in range(2)]
            y_tb = [sbB(f"y_tb{i}", [128, D], F32) for i in range(2)]
            yn_tb1 = sbB("yn_tb1", [128, D], F32)
            yn_tb = [yn_tb1, yn_tb1]
            xn2_tm1 = sbB("xn2_tm1", [128, D], BF16)
            xn2_tm = [xn2_tm1, xn2_tm1]
            ps_o = [psB(f"ps_o{i}", [128, TPC], F32) for i in range(2)]
            ps_tr3 = psB("ps_tr3", [128, 512], BF16)
            ps_bb = psB("ps_bb", [128, 512], F32)

            p.dma("gpsimd", wo_bf[:].rearrange("p (c n) -> p c n", c=KC), wo.rearrange("(c p) n -> p c n", p=128))
            p.wait("gpsimd", "sd")
            p.wait("gpsimd", "cc")

            def ctx_reads(g):
                last = None
                with g.register("tok") as tok:
                    g.reg_load(tok, meta_sb[0:1, 0:1])
                    off = g.snap(tok, min_val=0, max_val=3 * TPC)
                    for ch in range(KC):
                        last = g.dma_start(out=cx[:, TPC * ch:TPC * (ch + 1)],
                                           in_=ctx_all[128 * ch:128 * (ch + 1), ds(off, TPC)])
                        last.then_inc(SEM["gd"], 16)
                return last
            p.emit("gpsimd", ctx_reads)
            p.bump("gd", 16 * KC)

            p.wait("tensor", "sd")
            for h in range(4):
                p.wait("tensor", "ve")
                bm = p.op("tensor", "pe", lambda t, h=h: t.matmul(ps_bb[:], ones_rf[:], bo_row[:, 512 * h:512 * (h + 1)], start=True, stop=True))
                p.wait("vector", "pe", bm)
                p.op("vector", "ve", lambda v, h=h: v.tensor_copy(bo_bc[:, 512 * h:512 * (h + 1)], ps_bb[:]))
                p.wait("tensor", "ve")
                bm = p.op("tensor", "pe", lambda t, h=h: t.matmul(ps_bb[:], ones_rf[:], ds_row[:, 512 * h:512 * (h + 1)], start=True, stop=True))
                p.wait("vector", "pe", bm)
                p.op("vector", "ve", lambda v, h=h: v.tensor_copy(ds_bc[:, 512 * h:512 * (h + 1)], ps_bb[:]))

            p.wait("tensor", "gd")
            tr_ev = 0
            oev = {0: 0, 1: 0}
            for i in range(NT):
                p.wait("sync", "ve")
                xl = p.dma("sync", xt3[i % 2][:], x_in[128 * i:128 * (i + 1), :])
                for oh in range(2):
                    p.wait("tensor", "ve", oev[oh])
                    for c in range(KC):
                        for h in range(2):
                            p.op("tensor", "pe", lambda t, c=c, h=h, i=i, oh=oh: t.matmul(
                                ps_o[oh][:, 512 * h:512 * (h + 1)],
                                cx[:, TPC * c + 128 * i: TPC * c + 128 * (i + 1)],
                                wo_bf[:, D * c + TPC * oh + 512 * h: D * c + TPC * oh + 512 * (h + 1)],
                                start=(c == 0), stop=(c == KC - 1)))
                    p.wait("vector", "pe")
                    p.wait("vector", "ac")
                    oev[oh] = p.op("vector", "ve", lambda v, i=i, oh=oh: v.tensor_add(
                        y_tb[i % 2][:, TPC * oh:TPC * (oh + 1)], ps_o[oh][:], bo_bc[:, TPC * oh:TPC * (oh + 1)]))
                p.wait("vector", "ve")
                n2 = ln_chain(p, y_tb[i % 2], bn6b[i % 2], mvb[i % 2], stdb[i % 2], rstdb[i % 2], nbb[i % 2], eps_c,
                              lambda s, i=i: s.activation(yn_tb[i % 2][:], y_tb[i % 2][:], AF.Identity, bias=nbb[i % 2][:], scale=rstdb[i % 2][:]))
                p.wait("vector", "ac", n2)
                p.wait("vector", "sd", xl)
                x2a = p.op("vector", "ve", lambda v, b=xt3[i % 2], i=i: v.tensor_add(b[:], b[:], yn_tb[i % 2][:]))
                p.wait("sync", "ve", x2a)
                p.dma("sync", x2_dram[128 * i:128 * (i + 1), :], xt3[i % 2][:])
                p.wait("scalar", "ve", tr_ev)
                n3 = ln_chain(p, xt3[i % 2], bn6b[i % 2], mvb[i % 2], stdb[i % 2], rstdb[i % 2], nbb[i % 2], eps_c,
                              lambda s, b=xt3[i % 2], i=i: s.activation(xn2_tm[i % 2][:], b[:], AF.Identity, bias=nbb[i % 2][:], scale=rstdb[i % 2][:]))
                p.wait("tensor", "ac", n3)
                for j4 in range(4):
                    p.wait("tensor", "ve", tr_ev)
                    for j in range(4):
                        c = j4 * 4 + j
                        p.op("tensor", "pe", lambda t, c=c, j=j, i=i: t.transpose(
                            ps_tr3[:, 128 * j:128 * (j + 1)], xn2_tm[i % 2][:, 128 * c:128 * (c + 1)], ident_bf[:]))
                    p.wait("vector", "pe")
                    def evac3(v, i=i, j4=j4):
                        dv = xn2[:, :].rearrange("p (c n) -> p c n", n=TPC)
                        return v.tensor_copy(dv[:, j4 * 4:(j4 + 1) * 4, 128 * i:128 * (i + 1)],
                                             ps_tr3[:].rearrange("p (c n) -> p c n", n=128))
                    tr_ev = p.op("vector", "ve", evac3)
                if serial:
                    p.barrier()

        p.barrier()
        with ExitStack() as stC:
            sbC = lambda n, s, d: stC.enter_context(nc.sbuf_tensor(n, s, d))
            psC = lambda n, s, d: stC.enter_context(nc.psum_tensor(n, s, d))
            acc = sbC("acc", [128, NT * D], F32)
            wg_bf = [sbC(f"wg{i}", [128, KC * 128], BF16) for i in range(2)]
            wu_bf = [sbC(f"wu{i}", [128, KC * 128], BF16) for i in range(2)]
            wd_bf = [sbC(f"wd{i}", [128, D], BF16) for i in range(GRP)]
            hdn = [sbC(f"hdn{i}", [128, TPC], BF16) for i in range(GRP)]
            silu_g = sbC("silu_g", [128, TPC], BF16)
            gs_sb = sbC("gs_sb", [128, IC], F32)
            us_sb = sbC("us_sb", [128, IC], F32)
            xt41 = sbC("xt41", [128, D], F32)
            xt4 = [xt41, xt41]
            y41 = sbC("y41", [128, D], F32)
            y4 = [y41, y41]
            yn41 = sbC("yn41", [128, D], F32)
            yn4 = [yn41, yn41]
            ps_g = psC("ps_g", [128, TPC], F32)
            ps_u = psC("ps_u", [128, TPC], F32)
            ps_d = [psC(f"ps_d{i}", [128, TPC], F32) for i in range(2)]

            p.dma("sync", gs_sb[:], gs)
            p.dma("sync", us_sb[:], us)
            p.op("vector", "ve", lambda v: v.memset(acc[:], 0.0))
            p.wait("tensor", "ve")
            p.wait("tensor", "sd")

            groups = [list(range(g, min(g + GRP, IC))) for g in range(0, IC, GRP)]
            dn_done = 0
            hv_done = {0: 0, 1: 0}
            for grp in groups:
                for ii, icx in enumerate(grp):
                    p.wait("gpsimd", "ve", hv_done[ii % 2])
                    p.wait("gpsimd", "pe", dn_done)
                    p.dma("gpsimd", wg_bf[ii % 2][:].rearrange("p (c n) -> p c n", c=KC),
                          gwt[:, 128 * icx:128 * (icx + 1)].rearrange("(c p) n -> p c n", p=128))
                    p.dma("gpsimd", wu_bf[ii % 2][:].rearrange("p (c n) -> p c n", c=KC),
                          upt[:, 128 * icx:128 * (icx + 1)].rearrange("(c p) n -> p c n", p=128))
                    wld = p.dma("gpsimd", wd_bf[ii][:], dwt[128 * icx:128 * (icx + 1), :])
                    p.wait("tensor", "gd", wld)
                    p.wait("tensor", "ve", hv_done[ii % 2])
                    for k in range(KC):
                        for th in range(2):
                            p.op("tensor", "pe", lambda t, k=k, th=th, ii=ii: t.matmul(
                                ps_g[:, 512 * th:512 * (th + 1)], wg_bf[ii % 2][:, 128 * k:128 * (k + 1)],
                                xn2[:, TPC * k + 512 * th: TPC * k + 512 * (th + 1)],
                                start=(k == 0), stop=(k == KC - 1)))
                    for k in range(KC):
                        for th in range(2):
                            p.op("tensor", "pe", lambda t, k=k, th=th, ii=ii: t.matmul(
                                ps_u[:, 512 * th:512 * (th + 1)], wu_bf[ii % 2][:, 128 * k:128 * (k + 1)],
                                xn2[:, TPC * k + 512 * th: TPC * k + 512 * (th + 1)],
                                start=(k == 0), stop=(k == KC - 1)))
                    p.wait("scalar", "pe")
                    p.wait("scalar", "ve", hv_done[ii % 2])
                    sl = p.op("scalar", "ac", lambda s, icx=icx: s.activation(silu_g[:], ps_g[:], AF.Silu, scale=gs_sb[:, icx:icx + 1]))
                    p.wait("vector", "ac", sl)
                    p.wait("vector", "pe")
                    hv = p.op("vector", "ve", lambda v, ii=ii, icx=icx: v.scalar_tensor_tensor(
                        hdn[ii][:], ps_u[:], us_sb[:, icx:icx + 1], silu_g[:], ALU.mult, ALU.mult))
                    hv_done[ii % 2] = hv
                    p.wait("tensor", "ve", hv)
                dev = {0: 0, 1: 0}
                for tt in range(NT):
                    for oh in range(2):
                        dp = oh
                        p.wait("tensor", "ve", dev[dp])
                        for ii in range(len(grp)):
                            for th in range(2):
                                p.op("tensor", "pe", lambda t, ii=ii, tt=tt, th=th, oh=oh, dp=dp, first=(ii == 0), last=(ii == len(grp) - 1): t.matmul(
                                    ps_d[dp][:, 512 * th:512 * (th + 1)], hdn[ii][:, 128 * tt:128 * (tt + 1)],
                                    wd_bf[ii][:, TPC * oh + 512 * th: TPC * oh + 512 * (th + 1)],
                                    start=first, stop=last))
                        p.wait("vector", "pe")
                        dev[dp] = p.op("vector", "ve", lambda v, tt=tt, oh=oh, dp=dp: v.tensor_add(
                            acc[:, D * tt + TPC * oh:D * tt + TPC * (oh + 1)], acc[:, D * tt + TPC * oh:D * tt + TPC * (oh + 1)], ps_d[dp][:]))
                dn_done = p.val("pe")
                if serial:
                    p.barrier()

            if "t_xn2" in tap_aps:
                p.barrier()
                p.dma("gpsimd", tap_aps["t_xn2"][:, :], xn2[:])
            if "t_acc" in tap_aps:
                p.barrier()
                p.dma("sync", tap_aps["t_acc"][:, :], acc[:])
            if "t_x2" in tap_aps:
                p.barrier()
                p.dma("sync", tap_aps["t_x2"][:, :], x2_dram)
            for i in range(NT):
                p.wait("sync", "ve")
                xl = p.dma("sync", xt4[i % 2][:], x2_dram[128 * i:128 * (i + 1), :])
                p.wait("vector", "ac")
                ym = p.op("vector", "ve", lambda v, i=i: v.tensor_mul(y4[i % 2][:], acc[:, D * i:D * (i + 1)], ds_bc[:]))
                p.wait("vector", "ve", ym)
                n4 = ln_chain(p, y4[i % 2], bn6b[i % 2], mvb[i % 2], stdb[i % 2], rstdb[i % 2], nbb[i % 2], eps_c,
                              lambda s, i=i: s.activation(yn4[i % 2][:], y4[i % 2][:], AF.Identity, bias=nbb[i % 2][:], scale=rstdb[i % 2][:]))
                p.wait("vector", "ac", n4)
                p.wait("vector", "sd", xl)
                fo = p.op("vector", "ve", lambda v, i=i: v.tensor_add(xt4[i % 2][:], xt4[i % 2][:], yn4[i % 2][:]))
                p.wait("sync", "ve", fo)
                p.dma("sync", out[128 * i:128 * (i + 1), :], xt4[i % 2][:])
                if serial:
                    p.barrier()

    finalize(nc, p)
    st.close()
    return nc


def finalize(nc, p):
    p.barrier()
    with nc.Block() as block:
        @block.sync
        def _(e):
            for f in p.q["sync"]:
                f(e)

        @block.gpsimd
        def _(e):
            for f in p.q["gpsimd"]:
                f(e)

        @block.scalar
        def _(e):
            for f in p.q["scalar"]:
                f(e)

        @block.vector
        def _(e):
            for f in p.q["vector"]:
                f(e)

        @block.tensor
        def _(e):
            for f in p.q["tensor"]:
                f(e)


def make_consts():
    ident = np.eye(128, dtype=np.float32)
    rotm = np.zeros((128, 128), np.float32)
    for j in range(64):
        rotm[2 * j + 1, 2 * j] = -1.0
        rotm[2 * j, 2 * j + 1] = 1.0
    dd = np.arange(128) // 2
    inv = 1.0 / ROPE_BASE ** (dd.astype(np.float64) * 2.0 / 128.0)
    tpos = np.arange(SEQ, dtype=np.float64)
    ang = inv[:, None] * tpos[None, :]
    cos_t = np.cos(ang).astype(np.float32)
    sin_t = np.sin(ang).astype(np.float32)
    kk = np.arange(128)[:, None]
    qq = np.arange(512)[None, :]
    mask = np.concatenate([np.where(kt * 128 + kk > qq, NEG, 0.0) for kt in range(4)], axis=1).astype(np.float32)
    return dict(identc=ident, rotmc=rotm, cos_t=cos_t, sin_t=sin_t, mask_t=mask,
                ones_c=np.ones((128, 1), np.float32), ones_r=np.ones((1, 128), np.float32),
                eps_cd=np.full((128, 1), 1e-5, np.float32))


def cols(v, n):
    return np.ascontiguousarray(np.asarray(v, np.float32).reshape(n, 128).T)


def make_inmaps(inp):
    x = np.asarray(inp["L0_attn_h"], np.float32)
    consts = make_consts()
    g = lambda k: np.ascontiguousarray(np.asarray(inp[k], np.float32))
    wdkv_f = g("L0_attn_W_DKV"); bdkv_f = cols(inp["L0_attn_b_DKV"], NKV)
    wdq_f = g("L0_attn_W_DQ"); bdq_f = cols(inp["L0_attn_b_DQ"], NQ)
    wkr_f = g("L0_attn_W_KR"); bkr_f = np.asarray(inp["L0_attn_b_KR"], np.float32)
    wuk_f = g("L0_attn_W_UK"); buk_f = np.asarray(inp["L0_attn_b_UK"], np.float32)
    wuv_f = g("L0_attn_W_UV"); buv_f = np.asarray(inp["L0_attn_b_UV"], np.float32)
    wuq_f = g("L0_attn_W_UQ"); buq_f = np.asarray(inp["L0_attn_b_UQ"], np.float32)
    wo_f = g("L0_attn_W_O"); bo_f = np.asarray(inp["L0_attn_b_O"], np.float32).reshape(1, D)
    gwt_f = np.ascontiguousarray(g("L0_ffn_gate_w").T)
    upt_f = np.ascontiguousarray(g("L0_ffn_up_w").T)
    dwt_f = np.ascontiguousarray(g("L0_ffn_down_w").T)
    gs_f = cols(inp["L0_ffn_gate_s"], IC)
    us_f = cols(inp["L0_ffn_up_s"], IC)
    ds_f = np.asarray(inp["L0_ffn_down_s"], np.float32).reshape(1, D)

    maps = []
    for c in range(8):
        b, m = divmod(c, 4)
        sl = slice(512 * m, 512 * (m + 1))
        maps.append(dict(
            x=np.ascontiguousarray(x[b, TPC * m:TPC * (m + 1), :]),
            meta=np.array([[TPC * m]], np.int32),
            wdkv=wdkv_f, bdkv=bdkv_f, wdq=wdq_f, bdq=bdq_f,
            wkr=np.ascontiguousarray(wkr_f[:, sl]), bkr=cols(bkr_f[sl], NSLOT),
            wuk=np.ascontiguousarray(wuk_f[:, sl]), buk=cols(buk_f[sl], NSLOT),
            wuv=np.ascontiguousarray(wuv_f[:, sl]), buv=cols(buv_f[sl], NSLOT),
            wuq=np.ascontiguousarray(wuq_f[:, sl]), buq=cols(buq_f[sl], NSLOT),
            wo=wo_f, bo_r=bo_f, gwt=gwt_f, upt=upt_f, dwt=dwt_f,
            gs=gs_f, us=us_f, ds_r=ds_f,
            **consts,
        ))
    return maps


def kernel(**inp):
    nc = build_nc(phases=3)
    maps = make_inmaps(inp)
    res = run_bass_kernel_spmd(nc, maps, core_ids=list(range(8)))
    outv = np.empty((2, SEQ, D), np.float32)
    for c in range(8):
        b, m = divmod(c, 4)
        outv[b, TPC * m:TPC * (m + 1), :] = res.results[c]["out"]
    return outv


# version 04
# speedup vs baseline: 1.4346x; 1.0308x over previous
import numpy as np
from contextlib import ExitStack
import concourse.bass as bass
import concourse.mybir as mybir
from concourse.bass import ds
from concourse.bass_utils import run_bass_kernel_spmd

F32 = mybir.dt.float32
BF16 = mybir.dt.bfloat16
I32 = mybir.dt.int32
AX = mybir.AxisListType.X
ALU = mybir.AluOpType
AF = mybir.ActivationFunctionType

D = 2048
SEQ = 4096
TPC = 1024
NT = TPC // 128
KC = D // 128
DKV = 512
NKV = DKV // 128
DQ = 1536
NQ = DQ // 128
NSLOT = 4
NTS = SEQ // 512
INTER = 5632
IC = INTER // 128
GRP = 6
ROPE_BASE = 10000.0
GROUPS = [[0, 1, 2, 3], [4, 5, 6, 7]]
NEG = -30000.0


class Plan:
    ENGINES = ("sync", "gpsimd", "scalar", "vector", "tensor")

    def __init__(self):
        self.q = {e: [] for e in self.ENGINES}
        self.cnt = {}

    def val(self, sem):
        return self.cnt.get(sem, 0)

    def bump(self, sem, n):
        self.cnt[sem] = self.val(sem) + n
        return self.cnt[sem]

    def emit(self, eng, f):
        self.q[eng].append(f)

    def wait(self, eng, sem, v=None):
        v = self.val(sem) if v is None else v
        if v <= 0:
            return
        self.emit(eng, lambda e, s=sem, vv=v: e.wait_ge(SEM[s], vv))

    def dma(self, eng, out, in_, sem=None, **kw):
        sem = sem or ("gd" if eng == "gpsimd" else ("ad" if eng == "scalar" else "sd"))
        t = self.bump(sem, 16)
        self.emit(eng, lambda e, o=out, i=in_, s=sem, k=kw: e.dma_start(out=o, in_=i, **k).then_inc(SEM[s], 16))
        return t

    def op(self, eng, sem, build):
        t = self.bump(sem, 1)
        self.emit(eng, lambda e, b=build, s=sem: b(e).then_inc(SEM[s], 1))
        return t

    def barrier(self, engines=None):
        snap = {s: v for s, v in self.cnt.items() if v > 0}
        for eng in (engines or self.ENGINES):
            for s, v in snap.items():
                self.wait(eng, s, v)


SEM = {}


def ln_chain(p, src, bn6, mv, stdc, rstd, nb, eps_c, out_fn):
    for j in range(4):
        p.op("vector", "ve", lambda v, j=j: v.bn_stats(bn6[:, 6 * j:6 * j + 6], src[:, 512 * j:512 * (j + 1)]))
    p.wait("vector", "ve")
    p.op("vector", "ve", lambda v: v.bn_aggr(mv[:], bn6[:]))
    p.wait("scalar", "ve")
    sq = p.op("scalar", "ac", lambda s: s.activation(stdc[:], mv[:, 1:2], AF.Sqrt, bias=eps_c[:], scale=1.0))
    p.wait("vector", "ac", sq)
    p.op("vector", "ve", lambda v: v.reciprocal(rstd[:], stdc[:]))
    p.wait("vector", "ve")
    p.op("vector", "ve", lambda v: v.scalar_tensor_tensor(nb[:], mv[:, 0:1], -1.0, rstd[:], ALU.mult, ALU.mult))
    p.wait("scalar", "ve")
    return p.op("scalar", "ac", out_fn)


def build_nc(phases=3, taps=(), serial=False):
    nc = bass.Bass("TRN2", target_bir_lowering=False)
    p = Plan()

    def din(name, shape, dt=F32):
        return nc.dram_tensor(name, shape, dt, kind="ExternalInput").ap()

    x_in = din("x", [TPC, D])
    meta = din("meta", [1, 1], I32)
    wdkv = din("wdkv", [D, DKV])
    bdkv = din("bdkv", [128, NKV])
    wdq = din("wdq", [D, DQ])
    bdq = din("bdq", [128, NQ])
    wkr = din("wkr", [D, 512])
    bkr = din("bkr", [128, NSLOT])
    wuk = din("wuk", [DKV, 512])
    buk = din("buk", [128, NSLOT])
    wuv = din("wuv", [DKV, 512])
    buv = din("buv", [128, NSLOT])
    wuq = din("wuq", [DQ, 512])
    buq = din("buq", [128, NSLOT])
    wo = din("wo", [D, D])
    bo_r = din("bo_r", [1, D])
    gwt = din("gwt", [D, INTER])
    upt = din("upt", [D, INTER])
    dwt = din("dwt", [INTER, D])
    gs = din("gs", [128, IC])
    us = din("us", [128, IC])
    ds_r = din("ds_r", [1, D])
    identc = din("identc", [128, 128])
    rotmc = din("rotmc", [128, 128])
    cos_td = din("cos_t", [128, SEQ])
    sin_td = din("sin_t", [128, SEQ])
    mask_t = din("mask_t", [128, 4 * 512])
    ones_cd = din("ones_c", [128, 1])
    ones_rd = din("ones_r", [1, 128])
    eps_cd = din("eps_cd", [128, 1])

    out = nc.dram_tensor("out", [TPC, D], F32, kind="ExternalOutput").ap()

    shard = nc.dram_tensor("shard", [4096, TPC], BF16, kind="Internal").ap()
    l_all = nc.dram_tensor("l_all", [4 * 4096, TPC], BF16, kind="Internal").ap()
    q_dram = nc.dram_tensor("q_dram", [NSLOT * 256, SEQ], BF16, kind="Internal").ap()
    ctx_shard = nc.dram_tensor("ctx_shard", [512, SEQ], BF16, kind="Internal").ap()
    ctx_all = nc.dram_tensor("ctx_all", [2048, SEQ], BF16, kind="Internal").ap()
    x2_dram = nc.dram_tensor("x2_dram", [TPC, D], F32, kind="Internal").ap()

    tap_aps = {}
    if "shard" in taps:
        tap_aps["t_shard"] = nc.dram_tensor("t_shard", [4 * 4096, TPC], F32, kind="ExternalOutput").ap()
    if "qdram" in taps:
        tap_aps["t_qdram"] = nc.dram_tensor("t_qdram", [NSLOT * 256, SEQ], F32, kind="ExternalOutput").ap()
    if "ctx" in taps:
        tap_aps["t_ctx"] = nc.dram_tensor("t_ctx", [2048, SEQ], F32, kind="ExternalOutput").ap()
    if "x2" in taps:
        tap_aps["t_x2"] = nc.dram_tensor("t_x2", [TPC, D], F32, kind="ExternalOutput").ap()
    if "xn2" in taps:
        tap_aps["t_xn2"] = nc.dram_tensor("t_xn2", [128, KC * TPC], F32, kind="ExternalOutput").ap()
    if "acc" in taps:
        tap_aps["t_acc"] = nc.dram_tensor("t_acc", [128, NT * D], F32, kind="ExternalOutput").ap()

    st = ExitStack()
    sbuf = lambda n, s, d: st.enter_context(nc.sbuf_tensor(n, s, d))

    ident_bf = sbuf("ident_bf", [128, 128], BF16)
    rotm_bf = sbuf("rotm_bf", [128, 128], BF16)
    ones_cb = sbuf("ones_cb", [128, 1], BF16)
    ones_rf = sbuf("ones_rf", [1, 128], F32)
    eps_c = sbuf("eps_c", [128, 1], F32)
    meta_sb = st.enter_context(nc.sbuf_tensor("meta_sb", [1, 1], I32))
    k2p = sbuf("k2p", [1, NSLOT], F32)
    q2mp = sbuf("q2mp", [1, NSLOT * NTS], F32)

    for s in ("sd", "gd", "ad", "cc", "pe", "ve", "ac"):
        SEM[s] = st.enter_context(nc.semaphore(s))

    p.dma("gpsimd", ident_bf[:], identc)
    p.dma("gpsimd", rotm_bf[:], rotmc)
    p.dma("gpsimd", ones_cb[:], ones_cd)
    p.dma("sync", ones_rf[:], ones_rd)
    p.dma("sync", eps_c[:], eps_cd)
    p.dma("sync", meta_sb[:], meta)

    with ExitStack() as st1:
        sb1 = lambda n, s, d: st1.enter_context(nc.sbuf_tensor(n, s, d))
        ps1s = lambda n, s, d: st1.enter_context(nc.psum_tensor(n, s, d))
        xt = [sb1(f"xt{i}", [128, D], F32) for i in range(2)]
        bn6 = sb1("bn6", [128, 24], F32)
        mv = sb1("mv", [128, 2], F32)
        stdc = sb1("stdc", [128, 1], F32)
        rstd = sb1("rstd", [128, 1], F32)
        nb = sb1("nb", [128, 1], F32)
        xn_tm = [sb1(f"xntm{i}", [128, D], BF16) for i in range(2)]
        xn1 = sb1("xn1", [128, KC * TPC], BF16)
        wdkv_bf = sb1("wdkv_bf", [128, KC * DKV], BF16)
        wdq_bf = sb1("wdq_bf", [128, KC * DQ], BF16)
        bdkv_sb = sb1("bdkv_sb", [128, NKV], F32)
        bdq_sb = sb1("bdq_sb", [128, NQ], F32)
        stage = [sb1(f"stg{i}", [128, 512], BF16) for i in range(4)]
        ps_tr = ps1s("ps_tr", [128, 512], BF16)
        ps_p = [ps1s(f"ps_p{i}", [128, 512], F32) for i in range(2)]

        p.dma("gpsimd", wdkv_bf[:].rearrange("p (c n) -> p c n", c=KC), wdkv.rearrange("(c p) n -> p c n", p=128))
        p.dma("gpsimd", wdq_bf[:].rearrange("p (c n) -> p c n", c=KC), wdq.rearrange("(c p) n -> p c n", p=128))
        p.dma("sync", bdkv_sb[:], bdkv)
        p.dma("sync", bdq_sb[:], bdq)

        tr_evacs = 0
        for i in range(NT):
            p.wait("sync", "ve")
            p.wait("sync", "ac")
            x_ld = p.dma("sync", xt[i % 2][:], x_in[128 * i:128 * (i + 1), :])
            p.wait("vector", "sd", x_ld)
            p.wait("scalar", "ve", tr_evacs)
            nrm = ln_chain(p, xt[i % 2], bn6, mv, stdc, rstd, nb, eps_c,
                           lambda s, b=xt[i % 2], o=xn_tm[i % 2]: s.activation(o[:], b[:], AF.Identity, bias=nb[:], scale=rstd[:]))
            p.wait("tensor", "ac", nrm)
            for j4 in range(4):
                p.wait("tensor", "ve", tr_evacs)
                for j in range(4):
                    c = j4 * 4 + j
                    p.op("tensor", "pe", lambda t, c=c, j=j, b=xn_tm[i % 2]: t.transpose(
                        ps_tr[:, 128 * j:128 * (j + 1)], b[:, 128 * c:128 * (c + 1)], ident_bf[:]))
                p.wait("vector", "pe")
                def evac(v, i=i, j4=j4):
                    dv = xn1[:, :].rearrange("p (c n) -> p c n", n=TPC)
                    return v.tensor_copy(dv[:, j4 * 4:(j4 + 1) * 4, 128 * i:128 * (i + 1)],
                                         ps_tr[:].rearrange("p (c n) -> p c n", n=128))
                tr_evacs = p.op("vector", "ve", evac)
            if serial:
                p.barrier()

        p.wait("sync", "ve")
        for c in range(KC):
            p.dma("sync", shard[2048 + 128 * c:2048 + 128 * (c + 1), :], xn1[:, TPC * c:TPC * (c + 1)])

        p.wait("tensor", "gd")
        evac_t = {0: 0, 1: 0}
        for oc in range(KC):
            is_kv = oc < NKV
            for th in range(2):
                par = (2 * oc + th) % 2
                pp = ps_p[par]
                p.wait("tensor", "ac", evac_t[par])
                for k in range(KC):
                    if is_kv:
                        lhs = wdkv_bf[:, DKV * k + 128 * oc: DKV * k + 128 * (oc + 1)]
                    else:
                        lhs = wdq_bf[:, DQ * k + 128 * (oc - NKV): DQ * k + 128 * (oc - NKV + 1)]
                    p.op("tensor", "pe", lambda t, l=lhs, k=k, pp=pp, th=th: t.matmul(
                        pp[:], l, xn1[:, TPC * k + 512 * th: TPC * k + 512 * (th + 1)],
                        start=(k == 0), stop=(k == KC - 1)))
                p.wait("scalar", "pe")
                p.wait("scalar", "sd")
                bcol = bdkv_sb[:, oc:oc + 1] if is_kv else bdq_sb[:, oc - NKV:oc - NKV + 1]
                sg = stage[(2 * oc + th) % 4]
                evac_t[par] = p.op("scalar", "ac", lambda s, pp=pp, b=bcol, sg=sg: s.activation(sg[:], pp[:], AF.Identity, bias=b, scale=1.0))
                p.wait("sync", "ac", evac_t[par])
                row = 128 * oc if is_kv else 512 + 128 * (oc - NKV)
                p.dma("sync", shard[row:row + 128, 512 * th:512 * (th + 1)], sg[:])
            if serial:
                p.barrier()

        p.wait("gpsimd", "sd")
        p.op("gpsimd", "cc", lambda g: g.collective_compute(
            "AllGather", ALU.bypass, replica_groups=GROUPS, ins=[shard.opt()], outs=[l_all.opt()]))

    if "t_shard" in tap_aps:
        p.wait("gpsimd", "cc")
        for r in range(4):
            p.dma("gpsimd", tap_aps["t_shard"][4096 * r:4096 * (r + 1), :], l_all[4096 * r:4096 * (r + 1), :])
    if phases == 1:
        p.barrier()
        finalize(nc, p)
        st.close()
        return nc

    p.barrier()
    with ExitStack() as stS:
        sbS = lambda n, s, d: stS.enter_context(nc.sbuf_tensor(n, s, d))
        kC = [sbS(f"kC{s}", [128, SEQ], BF16) for s in range(NSLOT)]
        kR = [sbS(f"kR{s}", [128, SEQ], BF16) for s in range(NSLOT)]
        vS = [sbS(f"vS{s}", [128, SEQ], BF16) for s in range(NSLOT)]

        with ExitStack() as stB:
            sb2 = lambda n, s, d: stB.enter_context(nc.sbuf_tensor(n, s, d))
            ps2s = lambda n, s, d: stB.enter_context(nc.psum_tensor(n, s, d))
            lat_x = [sb2(f"latx{c}", [128, 512], BF16) for c in range(KC)]
            lat_q = [sb2(f"latq{c}", [128, 512], BF16) for c in range(NQ)]
            lat_kv = [sb2(f"latkv{c}", [128, 512], BF16) for c in range(NKV)]
            cos_s = sb2("cos_s", [128, 512], F32)
            sin_s = sb2("sin_s", [128, 512], F32)
            wkr_bf = sb2("wkr_bf", [128, KC * 512], BF16)
            wuk_bf = sb2("wuk_bf", [128, NKV * 512], BF16)
            wuv_bf = sb2("wuv_bf", [128, NKV * 512], BF16)
            wuq_bf = sb2("wuq_bf", [128, NQ * 512], BF16)
            bkr_sb = sb2("bkr_sb", [128, NSLOT], F32)
            buk_sb = sb2("buk_sb", [128, NSLOT], F32)
            buv_sb = sb2("buv_sb", [128, NSLOT], F32)
            buq_sb = sb2("buq_sb", [128, NSLOT], F32)
            krbf_b = [sb2(f"krbf{i}", [128, 512], BF16) for i in range(2)]
            vfm = sb2("vfm", [128, 512], BF16)
            qbf_b = [sb2(f"qbf{i}", [128, 512], BF16) for i in range(2)]
            t1_b = [sb2(f"t1_{i}", [128, 512], F32) for i in range(2)]
            t2_b = [sb2(f"t2_{i}", [128, 512], F32) for i in range(2)]
            ksq = sb2("ksq", [128, 512], BF16)
            qsq2 = sb2("qsq2", [128, 512], BF16)
            k2 = sb2("k2", [1, NSLOT], F32)
            k2t = sb2("k2t", [1, 1], F32)
            q2m = sb2("q2m", [1, NSLOT * NTS], F32)
            ps_a = [ps2s(f"ps_a{s}", [128, 512], F32) for s in range(NSLOT)]
            ps_rot_b = [ps2s(f"ps_rot{i}", [128, 512], F32) for i in range(2)]
            ps_vt = ps2s("ps_vt", [128, 512], BF16)
            ps_n = ps2s("ps_n", [1, 512], F32)

            p.dma("gpsimd", wkr_bf[:].rearrange("p (c n) -> p c n", c=KC), wkr.rearrange("(c p) n -> p c n", p=128))
            p.dma("gpsimd", wuk_bf[:].rearrange("p (c n) -> p c n", c=NKV), wuk.rearrange("(c p) n -> p c n", p=128))
            p.dma("gpsimd", wuv_bf[:].rearrange("p (c n) -> p c n", c=NKV), wuv.rearrange("(c p) n -> p c n", p=128))
            p.dma("gpsimd", wuq_bf[:].rearrange("p (c n) -> p c n", c=NQ), wuq.rearrange("(c p) n -> p c n", p=128))
            p.dma("sync", bkr_sb[:], bkr)
            p.dma("sync", buk_sb[:], buk)
            p.dma("sync", buv_sb[:], buv)
            p.dma("sync", buq_sb[:], buq)
            p.wait("sync", "cc")
            p.wait("tensor", "gd")

            kr_done = q_done = kv_done = 0
            for ts in range(NTS):
                ts_ve0 = p.val("ve")
                ts_sd0 = p.val("sd")
                r = ts // 2
                cs = 512 * (ts % 2)
                rb = 4096 * r
                p.wait("sync", "pe", kr_done)
                for c in range(KC):
                    p.dma("sync", lat_x[c][:], l_all[rb + 2048 + 128 * c: rb + 2048 + 128 * (c + 1), cs:cs + 512])
                p.wait("sync", "pe", q_done)
                for c in range(NQ):
                    p.dma("sync", lat_q[c][:], l_all[rb + 512 + 128 * c: rb + 512 + 128 * (c + 1), cs:cs + 512])
                p.wait("sync", "pe", kv_done)
                for c in range(NKV):
                    p.dma("sync", lat_kv[c][:], l_all[rb + 128 * c: rb + 128 * (c + 1), cs:cs + 512])
                p.wait("sync", "ve")
                ld = p.dma("sync", cos_s[:], cos_td[:, 512 * ts:512 * (ts + 1)])
                ld = p.dma("sync", sin_s[:], sin_td[:, 512 * ts:512 * (ts + 1)])
                p.wait("tensor", "sd", ld)

                for s in range(NSLOT):
                    for k in range(KC):
                        p.op("tensor", "pe", lambda t, s=s, k=k: t.matmul(
                            ps_a[s][:], wkr_bf[:, 512 * k + 128 * s: 512 * k + 128 * (s + 1)], lat_x[k][:],
                            start=(k == 0), stop=(k == KC - 1)))
                kr_done = p.val("pe")
                kr_fin = {0: 0, 1: 0}
                for s in range(NSLOT):
                    par = s % 2
                    krbf = krbf_b[par]; t1 = t1_b[par]; t2 = t2_b[par]; ps_rot = ps_rot_b[par]
                    p.wait("scalar", "pe", kr_done - (NSLOT - 1 - s) * KC)
                    p.wait("scalar", "ve", ts_ve0)
                    p.wait("scalar", "sd", ts_sd0)
                    p.wait("scalar", "ve", kr_fin[par])
                    ev = p.op("scalar", "ac", lambda sc, s=s, krbf=krbf: sc.activation(krbf[:], ps_a[s][:], AF.Identity, bias=bkr_sb[:, s:s + 1], scale=1.0))
                    p.wait("tensor", "ac", ev)
                    rt = p.op("tensor", "pe", lambda t, krbf=krbf, ps_rot=ps_rot: t.matmul(ps_rot[:], rotm_bf[:], krbf[:], start=True, stop=True))
                    p.wait("vector", "ac", ev)
                    m1 = p.op("vector", "ve", lambda v, krbf=krbf, t1=t1: v.tensor_mul(t1[:], krbf[:], cos_s[:]))
                    p.wait("vector", "pe", rt)
                    m2 = p.op("vector", "ve", lambda v, t2=t2, ps_rot=ps_rot: v.tensor_mul(t2[:], ps_rot[:], sin_s[:]))
                    p.wait("vector", "ve", m2)
                    kr_fin[par] = p.op("vector", "ve", lambda v, s=s, ts=ts, t1=t1, t2=t2: v.tensor_add(kR[s][:, 512 * ts:512 * (ts + 1)], t1[:], t2[:]))
                p.wait("tensor", "ve", kr_fin[0])
                p.wait("tensor", "ve", kr_fin[1])

                for s in range(NSLOT):
                    for k in range(NQ):
                        p.op("tensor", "pe", lambda t, s=s, k=k: t.matmul(
                            ps_a[s][:], wuq_bf[:, 512 * k + 128 * s: 512 * k + 128 * (s + 1)], lat_q[k][:],
                            start=(k == 0), stop=(k == NQ - 1)))
                q_done = p.val("pe")
                q_cv = {0: 0, 1: 0}
                q_dma = {0: 0, 1: 0}
                nrd_last = 0
                for s in range(NSLOT):
                    par = s % 2
                    qbf = qbf_b[par]; krbf = krbf_b[par]; t1 = t1_b[par]; t2 = t2_b[par]; ps_rot = ps_rot_b[par]
                    p.wait("scalar", "pe", q_done - (NSLOT - 1 - s) * NQ)
                    p.wait("scalar", "ve", ts_ve0)
                    p.wait("scalar", "sd", ts_sd0)
                    p.wait("scalar", "sd", q_dma[par])
                    p.wait("scalar", "ve", q_cv[par])
                    ev = p.op("scalar", "ac", lambda sc, s=s, qbf=qbf: sc.activation(qbf[:], ps_a[s][:], AF.Identity, bias=buq_sb[:, s:s + 1], scale=1.0))
                    p.wait("sync", "ac", ev)
                    p.dma("sync", q_dram[256 * s:256 * s + 128, 512 * ts:512 * (ts + 1)], qbf[:])
                    p.wait("tensor", "ac", ev)
                    rt = p.op("tensor", "pe", lambda t, qbf=qbf, ps_rot=ps_rot: t.matmul(ps_rot[:], rotm_bf[:], qbf[:], start=True, stop=True))
                    p.wait("vector", "ac", ev)
                    m1 = p.op("vector", "ve", lambda v, qbf=qbf, t1=t1: v.tensor_mul(t1[:], qbf[:], cos_s[:]))
                    p.wait("vector", "pe", rt)
                    m2 = p.op("vector", "ve", lambda v, t2=t2, ps_rot=ps_rot: v.tensor_mul(t2[:], ps_rot[:], sin_s[:]))
                    p.wait("vector", "ve", m2)
                    ad = p.op("vector", "ve", lambda v, t1=t1, t2=t2: v.tensor_add(t1[:], t1[:], t2[:]))
                    p.wait("vector", "ve", ad)
                    cv = p.op("vector", "ve", lambda v, krbf=krbf, t1=t1: v.tensor_copy(krbf[:], t1[:]))
                    q_cv[par] = cv
                    p.wait("sync", "ve", cv)
                    q_dma[par] = p.dma("sync", q_dram[256 * s + 128:256 * s + 256, 512 * ts:512 * (ts + 1)], krbf[:])
                    p.wait("scalar", "ve", cv)
                    p.wait("scalar", "pe")
                    n1 = p.op("scalar", "ac", lambda sc, qbf=qbf: sc.activation(qsq2[:], qbf[:], AF.Square))
                    p.wait("tensor", "ac", n1)
                    p.wait("tensor", "ve", nrd_last)
                    p.op("tensor", "pe", lambda t: t.matmul(ps_n[:], ones_cb[:], qsq2[:], start=True, stop=False))
                    p.wait("scalar", "pe")
                    n2 = p.op("scalar", "ac", lambda sc, krbf=krbf: sc.activation(qsq2[:], krbf[:], AF.Square))
                    p.wait("tensor", "ac", n2)
                    nm = p.op("tensor", "pe", lambda t: t.matmul(ps_n[:], ones_cb[:], qsq2[:], start=False, stop=True))
                    p.wait("vector", "pe", nm)
                    nrd_last = p.op("vector", "ve", lambda v, s=s, ts=ts: v.tensor_reduce(q2m[:, 8 * s + ts:8 * s + ts + 1], ps_n[:], AX, ALU.max))
                p.wait("tensor", "ve", nrd_last)

                for s in range(NSLOT):
                    for k in range(NKV):
                        p.op("tensor", "pe", lambda t, s=s, k=k: t.matmul(
                            ps_a[s][:], wuk_bf[:, 512 * k + 128 * s: 512 * k + 128 * (s + 1)], lat_kv[k][:],
                            start=(k == 0), stop=(k == NKV - 1)))
                for s in range(NSLOT):
                    p.wait("scalar", "pe")
                    p.op("scalar", "ac", lambda sc, s=s, ts=ts: sc.activation(
                        kC[s][:, 512 * ts:512 * (ts + 1)], ps_a[s][:], AF.Identity, bias=buk_sb[:, s:s + 1], scale=1.0))
                p.wait("tensor", "ac")

                for s in range(NSLOT):
                    for k in range(NKV):
                        p.op("tensor", "pe", lambda t, s=s, k=k: t.matmul(
                            ps_a[s][:], wuv_bf[:, 512 * k + 128 * s: 512 * k + 128 * (s + 1)], lat_kv[k][:],
                            start=(k == 0), stop=(k == NKV - 1)))
                kv_done = p.val("pe")
                for s in range(NSLOT):
                    p.wait("scalar", "pe")
                    ev = p.op("scalar", "ac", lambda sc, s=s: sc.activation(vfm[:], ps_a[s][:], AF.Identity, bias=buv_sb[:, s:s + 1], scale=1.0))
                    p.wait("tensor", "ac", ev)
                    p.wait("tensor", "ve")
                    for j in range(4):
                        p.op("tensor", "pe", lambda t, j=j: t.transpose(ps_vt[:, 128 * j:128 * (j + 1)], vfm[:, 128 * j:128 * (j + 1)], ident_bf[:]))
                    p.wait("vector", "pe")
                    p.op("vector", "ve", lambda v, s=s, ts=ts: v.tensor_copy(vS[s][:, 512 * ts:512 * (ts + 1)], ps_vt[:]))
                    p.wait("scalar", "pe")
                p.wait("tensor", "ve")
                if serial:
                    p.barrier()

            p.op("vector", "ve", lambda v: v.memset(k2[:], 0.0))
            for s in range(NSLOT):
                for sl in range(NTS):
                    p.wait("scalar", "pe")
                    sq1 = p.op("scalar", "ac", lambda sc, s=s, sl=sl: sc.activation(ksq[:], kC[s][:, 512 * sl:512 * (sl + 1)], AF.Square))
                    p.wait("tensor", "ac", sq1)
                    p.wait("tensor", "ve")
                    p.op("tensor", "pe", lambda t: t.matmul(ps_n[:], ones_cb[:], ksq[:], start=True, stop=False))
                    p.wait("scalar", "pe")
                    sq2 = p.op("scalar", "ac", lambda sc, s=s, sl=sl: sc.activation(ksq[:], kR[s][:, 512 * sl:512 * (sl + 1)], AF.Square))
                    p.wait("tensor", "ac", sq2)
                    mm = p.op("tensor", "pe", lambda t: t.matmul(ps_n[:], ones_cb[:], ksq[:], start=False, stop=True))
                    p.wait("vector", "pe", mm)
                    rd = p.op("vector", "ve", lambda v: v.tensor_reduce(k2t[:], ps_n[:], AX, ALU.max))
                    p.wait("vector", "ve", rd)
                    p.op("vector", "ve", lambda v, s=s: v.tensor_tensor(k2[:, s:s + 1], k2[:, s:s + 1], k2t[:], ALU.max))
                    p.wait("vector", "ve")
            p.op("vector", "ve", lambda v: v.tensor_copy(k2p[:], k2[:]))
            p.wait("vector", "ve")
            p.op("vector", "ve", lambda v: v.tensor_copy(q2mp[:], q2m[:]))
            p.barrier()

        with ExitStack() as stA2:
            sb3 = lambda n, s, d: stA2.enter_context(nc.sbuf_tensor(n, s, d))
            ps3s = lambda n, s, d: stA2.enter_context(nc.psum_tensor(n, s, d))
            masks = sb3("masks", [128, 2048], F32)
            qc_t = [sb3(f"qc_t{i}", [128, 512], BF16) for i in range(2)]
            qr_t = [sb3(f"qr_t{i}", [128, 512], BF16) for i in range(2)]
            prb = [sb3(f"prb{i}", [128, 512], BF16) for i in range(4)]
            ctxs = sb3("ctxs", [128, SEQ], BF16)
            craw = [sb3(f"craw{i}", [128, 512], F32) for i in range(2)]
            bias_all = sb3("bias_all", [128, NSLOT * NTS], F32)
            b2r = sb3("b2r", [1, NSLOT * NTS], F32)
            srow = [sb3(f"srow{i}", [1, 512], F32) for i in range(2)]
            rrow = [sb3(f"rrow{i}", [1, 512], F32) for i in range(2)]
            bc_sb = [sb3(f"bc_sb{i}", [128, 512], F32) for i in range(2)]
            ps_sc = [ps3s(f"ps_sc{i}", [128, 512], F32) for i in range(3)]
            ps_ctx = [ps3s(f"ps_ctx{i}", [128, 512], F32) for i in range(2)]
            ps_sum = [ps3s(f"ps_sum{i}", [1, 512], F32) for i in range(2)]
            ps_bc = ps3s("ps_bc", [128, 512], F32)

            p.dma("sync", masks[:], mask_t)
            bb = p.op("vector", "ve", lambda v: v.tensor_tensor(
                b2r[:].rearrange("a (s t) -> a s t", s=NSLOT),
                q2mp[:].rearrange("a (s t) -> a s t", s=NSLOT),
                k2p[:].rearrange("a (s t) -> a s t", t=1).broadcast_to((1, NSLOT, NTS)),
                ALU.mult))
            p.wait("scalar", "ve", bb)
            bsq = p.op("scalar", "ac", lambda sc: sc.activation(b2r[:], b2r[:], AF.Sqrt))
            p.wait("tensor", "ac", bsq)
            bmm = p.op("tensor", "pe", lambda t: t.matmul(ps_bc[:, 0:NSLOT * NTS], ones_rf[:], b2r[:], start=True, stop=True))
            p.wait("scalar", "pe", bmm)
            bev = p.op("scalar", "ac", lambda sc: sc.activation(bias_all[:], ps_bc[:, 0:NSLOT * NTS], AF.Identity, scale=-0.0625))
            p.wait("scalar", "ac", bev)
            p.wait("tensor", "ac", bev)

            exp_cnt = {i: 0 for i in range(3)}
            cons_cnt = {i: 0 for i in range(4)}
            ctx_free = {0: 0, 1: 0}
            qtile_free = {0: 0, 1: 0}
            bc_free = 0
            LOOK = 2

            def load_q(s, qc):
                qp = qc % 2
                p.wait("sync", "pe", qtile_free[qp])
                p.dma("sync", qc_t[qp][:], q_dram[256 * s:256 * s + 128, 512 * qc:512 * (qc + 1)])
                return p.dma("sync", qr_t[qp][:], q_dram[256 * s + 128:256 * s + 256, 512 * qc:512 * (qc + 1)])

            qld = {(0, 0): load_q(0, 0)}
            for s in range(NSLOT):
                for qc in range(NTS):
                    qp = qc % 2
                    p.wait("tensor", "sd", qld[(s, qc)])
                    p.wait("tensor", "ve", ctx_free[qp])
                    bcol = bias_all[:, 8 * s + qc:8 * s + qc + 1]
                    nkt = 4 * (qc + 1)
                    pend = []
                    last_mm = 0

                    for ik in range(nkt):
                        kb, kt = divmod(ik, 4)
                        par = ik % 3
                        ring = ik % 4
                        diag = kb == qc
                        q0 = 128 * kt if diag else 0
                        if pend and len(pend) > LOOK - 1:
                            p.wait("tensor", "ac", pend[0][2])
                        elif exp_cnt[par]:
                            p.wait("tensor", "ac", exp_cnt[par])
                        if len(pend) > LOOK:
                            jk, jring, jex, jq0 = pend.pop(0)
                            p.op("tensor", "pe", lambda t, s=s, jk=jk, jring=jring, qp=qp, jq0=jq0, first=(jk == 0), last=(jk == nkt - 1): t.matmul(
                                ps_ctx[qp][:, jq0:512], vS[s][:, 128 * jk: 128 * (jk + 1)], prb[jring][:, jq0:512], start=first, stop=last))
                            cmm = p.op("tensor", "pe", lambda t, jring=jring, qp=qp, jq0=jq0, first=(jk == 0), last=(jk == nkt - 1): t.matmul(
                                ps_sum[qp][:, jq0:512], ones_cb[:], prb[jring][:, jq0:512], start=first, stop=last))
                            cons_cnt[jring] = cmm
                        p.op("tensor", "pe", lambda t, s=s, kb=kb, kt=kt, par=par, qp=qp, q0=q0: t.matmul(
                            ps_sc[par][:, q0:512], kC[s][:, 512 * kb + 128 * kt: 512 * kb + 128 * (kt + 1)], qc_t[qp][:, q0:512], start=True, stop=False))
                        mm = p.op("tensor", "pe", lambda t, s=s, kb=kb, kt=kt, par=par, qp=qp, q0=q0: t.matmul(
                            ps_sc[par][:, q0:512], kR[s][:, 512 * kb + 128 * kt: 512 * kb + 128 * (kt + 1)], qr_t[qp][:, q0:512], start=False, stop=True))
                        last_mm = mm
                        gate = ("pe", mm)
                        if diag:
                            p.wait("vector", "pe", mm)
                            mk = p.op("vector", "ve", lambda v, kt=kt, par=par, q0=q0: v.tensor_tensor(
                                ps_sc[par][:, q0:512], ps_sc[par][:, q0:512], masks[:, 512 * kt + q0:512 * (kt + 1)], ALU.add))
                            gate = ("ve", mk)
                        p.wait("scalar", gate[0], gate[1])
                        p.wait("scalar", "pe", cons_cnt[ring])
                        ex = p.op("scalar", "ac", lambda sc, par=par, ring=ring, bcol=bcol, q0=q0: sc.activation(
                            prb[ring][:, q0:512], ps_sc[par][:, q0:512], AF.Exp, bias=bcol, scale=0.0625))
                        exp_cnt[par] = ex
                        pend.append((ik, ring, ex, q0))
                    nqc = qc + 1
                    ns = s
                    if nqc == NTS:
                        nqc = 0
                        ns = s + 1
                    if ns < NSLOT:
                        qld[(ns, nqc)] = load_q(ns, nqc)
                    for jk, jring, jex, jq0 in pend:
                        p.wait("tensor", "ac", jex)
                        p.op("tensor", "pe", lambda t, s=s, jk=jk, jring=jring, qp=qp, jq0=jq0, first=(jk == 0), last=(jk == nkt - 1): t.matmul(
                            ps_ctx[qp][:, jq0:512], vS[s][:, 128 * jk: 128 * (jk + 1)], prb[jring][:, jq0:512], start=first, stop=last))
                        cmm = p.op("tensor", "pe", lambda t, jring=jring, qp=qp, jq0=jq0, first=(jk == 0), last=(jk == nkt - 1): t.matmul(
                            ps_sum[qp][:, jq0:512], ones_cb[:], prb[jring][:, jq0:512], start=first, stop=last))
                        cons_cnt[jring] = cmm
                    qtile_free[qp] = last_mm
                    p.wait("vector", "pe")
                    e1 = p.op("vector", "ve", lambda v, qp=qp: v.tensor_copy(srow[qp][:], ps_sum[qp][:]))
                    cr = p.op("vector", "ve", lambda v, qp=qp: v.tensor_copy(craw[qp][:], ps_ctx[qp][:]))
                    p.wait("vector", "ve", e1)
                    e2 = p.op("vector", "ve", lambda v, qp=qp: v.reciprocal(rrow[qp][:], srow[qp][:]))
                    p.wait("tensor", "ve", bc_free)
                    p.wait("tensor", "ve", e2)
                    bm = p.op("tensor", "pe", lambda t, qp=qp: t.matmul(ps_bc[:], ones_rf[:], rrow[qp][:], start=True, stop=True))
                    p.wait("vector", "pe", bm)
                    e3 = p.op("vector", "ve", lambda v, qp=qp: v.tensor_copy(bc_sb[qp][:], ps_bc[:]))
                    bc_free = e3
                    p.wait("vector", "ve", e3)
                    mf = p.op("vector", "ve", lambda v, qc=qc, qp=qp: v.tensor_mul(ctxs[:, 512 * qc:512 * (qc + 1)], craw[qp][:], bc_sb[qp][:]))
                    ctx_free[qp] = cr
                    if serial:
                        p.barrier()
                p.wait("sync", "ve")
                p.dma("sync", ctx_shard[128 * s:128 * (s + 1), :], ctxs[:])
                p.wait("vector", "sd")

            p.wait("gpsimd", "sd")
            p.op("gpsimd", "cc", lambda g: g.collective_compute(
                "AllGather", ALU.bypass, replica_groups=GROUPS, ins=[ctx_shard.opt()], outs=[ctx_all.opt()]))

    if "t_qdram" in tap_aps:
        p.wait("gpsimd", "cc")
        p.dma("gpsimd", tap_aps["t_qdram"][:, :], q_dram)
    if "t_ctx" in tap_aps:
        p.wait("gpsimd", "cc")
        p.dma("gpsimd", tap_aps["t_ctx"][:, :], ctx_all)
    if phases == 2:
        p.barrier()
        finalize(nc, p)
        st.close()
        return nc

    p.barrier()
    with ExitStack() as stA:
        sbA = lambda n, s, d: stA.enter_context(nc.sbuf_tensor(n, s, d))
        bo_row = sbA("bo_row", [1, D], F32)
        ds_row = sbA("ds_row", [1, D], F32)
        ds_bc = sbA("ds_bc", [128, D], F32)
        xn2 = sbA("xn2", [128, KC * TPC], BF16)
        bn6b = [sbA(f"bn6b{i}", [128, 24], F32) for i in range(2)]
        mvb = [sbA(f"mvb{i}", [128, 2], F32) for i in range(2)]
        stdb = [sbA(f"stdb{i}", [128, 1], F32) for i in range(2)]
        rstdb = [sbA(f"rstdb{i}", [128, 1], F32) for i in range(2)]
        nbb = [sbA(f"nbb{i}", [128, 1], F32) for i in range(2)]

        p.dma("sync", bo_row[:], bo_r)
        p.dma("sync", ds_row[:], ds_r)

        with ExitStack() as stB3:
            sbB = lambda n, s, d: stB3.enter_context(nc.sbuf_tensor(n, s, d))
            psB = lambda n, s, d: stB3.enter_context(nc.psum_tensor(n, s, d))
            wo_bf = sbB("wo_bf", [128, KC * D], BF16)
            cx = sbB("cx", [128, KC * TPC], BF16)
            bo_bc = sbB("bo_bc", [128, D], F32)
            xt3 = [sbB(f"xt3{i}", [128, D], F32) for i in range(2)]
            y_tb = [sbB(f"y_tb{i}", [128, D], F32) for i in range(2)]
            yn_tb1 = sbB("yn_tb1", [128, D], F32)
            yn_tb = [yn_tb1, yn_tb1]
            xn2_tm1 = sbB("xn2_tm1", [128, D], BF16)
            xn2_tm = [xn2_tm1, xn2_tm1]
            ps_o = [psB(f"ps_o{i}", [128, TPC], F32) for i in range(2)]
            ps_tr3 = psB("ps_tr3", [128, 512], BF16)
            ps_bb = psB("ps_bb", [128, 512], F32)

            p.dma("gpsimd", wo_bf[:].rearrange("p (c n) -> p c n", c=KC), wo.rearrange("(c p) n -> p c n", p=128))
            p.wait("gpsimd", "sd")
            p.wait("gpsimd", "cc")

            def ctx_reads(g):
                last = None
                with g.register("tok") as tok:
                    g.reg_load(tok, meta_sb[0:1, 0:1])
                    off = g.snap(tok, min_val=0, max_val=3 * TPC)
                    for ch in range(KC):
                        last = g.dma_start(out=cx[:, TPC * ch:TPC * (ch + 1)],
                                           in_=ctx_all[128 * ch:128 * (ch + 1), ds(off, TPC)])
                        last.then_inc(SEM["gd"], 16)
                return last
            p.emit("gpsimd", ctx_reads)
            p.bump("gd", 16 * KC)

            p.wait("tensor", "sd")
            for h in range(4):
                p.wait("tensor", "ve")
                bm = p.op("tensor", "pe", lambda t, h=h: t.matmul(ps_bb[:], ones_rf[:], bo_row[:, 512 * h:512 * (h + 1)], start=True, stop=True))
                p.wait("vector", "pe", bm)
                p.op("vector", "ve", lambda v, h=h: v.tensor_copy(bo_bc[:, 512 * h:512 * (h + 1)], ps_bb[:]))
                p.wait("tensor", "ve")
                bm = p.op("tensor", "pe", lambda t, h=h: t.matmul(ps_bb[:], ones_rf[:], ds_row[:, 512 * h:512 * (h + 1)], start=True, stop=True))
                p.wait("vector", "pe", bm)
                p.op("vector", "ve", lambda v, h=h: v.tensor_copy(ds_bc[:, 512 * h:512 * (h + 1)], ps_bb[:]))

            p.wait("tensor", "gd")
            tr_ev = 0
            oev = {0: 0, 1: 0}
            for i in range(NT):
                p.wait("sync", "ve")
                xl = p.dma("sync", xt3[i % 2][:], x_in[128 * i:128 * (i + 1), :])
                for oh in range(2):
                    p.wait("tensor", "ve", oev[oh])
                    for c in range(KC):
                        for h in range(2):
                            p.op("tensor", "pe", lambda t, c=c, h=h, i=i, oh=oh: t.matmul(
                                ps_o[oh][:, 512 * h:512 * (h + 1)],
                                cx[:, TPC * c + 128 * i: TPC * c + 128 * (i + 1)],
                                wo_bf[:, D * c + TPC * oh + 512 * h: D * c + TPC * oh + 512 * (h + 1)],
                                start=(c == 0), stop=(c == KC - 1)))
                    p.wait("vector", "pe")
                    p.wait("vector", "ac")
                    oev[oh] = p.op("vector", "ve", lambda v, i=i, oh=oh: v.tensor_add(
                        y_tb[i % 2][:, TPC * oh:TPC * (oh + 1)], ps_o[oh][:], bo_bc[:, TPC * oh:TPC * (oh + 1)]))
                p.wait("vector", "ve")
                n2 = ln_chain(p, y_tb[i % 2], bn6b[i % 2], mvb[i % 2], stdb[i % 2], rstdb[i % 2], nbb[i % 2], eps_c,
                              lambda s, i=i: s.activation(yn_tb[i % 2][:], y_tb[i % 2][:], AF.Identity, bias=nbb[i % 2][:], scale=rstdb[i % 2][:]))
                p.wait("vector", "ac", n2)
                p.wait("vector", "sd", xl)
                x2a = p.op("vector", "ve", lambda v, b=xt3[i % 2], i=i: v.tensor_add(b[:], b[:], yn_tb[i % 2][:]))
                p.wait("sync", "ve", x2a)
                p.dma("sync", x2_dram[128 * i:128 * (i + 1), :], xt3[i % 2][:])
                p.wait("scalar", "ve", tr_ev)
                n3 = ln_chain(p, xt3[i % 2], bn6b[i % 2], mvb[i % 2], stdb[i % 2], rstdb[i % 2], nbb[i % 2], eps_c,
                              lambda s, b=xt3[i % 2], i=i: s.activation(xn2_tm[i % 2][:], b[:], AF.Identity, bias=nbb[i % 2][:], scale=rstdb[i % 2][:]))
                p.wait("tensor", "ac", n3)
                for j4 in range(4):
                    p.wait("tensor", "ve", tr_ev)
                    for j in range(4):
                        c = j4 * 4 + j
                        p.op("tensor", "pe", lambda t, c=c, j=j, i=i: t.transpose(
                            ps_tr3[:, 128 * j:128 * (j + 1)], xn2_tm[i % 2][:, 128 * c:128 * (c + 1)], ident_bf[:]))
                    p.wait("vector", "pe")
                    def evac3(v, i=i, j4=j4):
                        dv = xn2[:, :].rearrange("p (c n) -> p c n", n=TPC)
                        return v.tensor_copy(dv[:, j4 * 4:(j4 + 1) * 4, 128 * i:128 * (i + 1)],
                                             ps_tr3[:].rearrange("p (c n) -> p c n", n=128))
                    tr_ev = p.op("vector", "ve", evac3)
                if serial:
                    p.barrier()

        p.barrier()
        with ExitStack() as stC:
            sbC = lambda n, s, d: stC.enter_context(nc.sbuf_tensor(n, s, d))
            psC = lambda n, s, d: stC.enter_context(nc.psum_tensor(n, s, d))
            acc = sbC("acc", [128, NT * D], F32)
            wg_bf = [sbC(f"wg{i}", [128, KC * 128], BF16) for i in range(2)]
            wu_bf = [sbC(f"wu{i}", [128, KC * 128], BF16) for i in range(2)]
            wd_bf = [sbC(f"wd{i}", [128, D], BF16) for i in range(GRP)]
            hdn = [sbC(f"hdn{i}", [128, TPC], BF16) for i in range(GRP)]
            silu_g = sbC("silu_g", [128, TPC], BF16)
            gs_sb = sbC("gs_sb", [128, IC], F32)
            us_sb = sbC("us_sb", [128, IC], F32)
            xt41 = sbC("xt41", [128, D], F32)
            xt4 = [xt41, xt41]
            y41 = sbC("y41", [128, D], F32)
            y4 = [y41, y41]
            yn41 = sbC("yn41", [128, D], F32)
            yn4 = [yn41, yn41]
            ps_g = psC("ps_g", [128, TPC], F32)
            ps_u = psC("ps_u", [128, TPC], F32)
            ps_d = [psC(f"ps_d{i}", [128, TPC], F32) for i in range(2)]

            p.dma("sync", gs_sb[:], gs)
            p.dma("sync", us_sb[:], us)
            p.op("vector", "ve", lambda v: v.memset(acc[:], 0.0))
            p.wait("tensor", "ve")
            p.wait("tensor", "sd")

            groups = [list(range(g, min(g + GRP, IC))) for g in range(0, IC, GRP)]
            dn_done = 0
            hv_done = {0: 0, 1: 0}
            for grp in groups:
                for ii, icx in enumerate(grp):
                    p.wait("gpsimd", "ve", hv_done[ii % 2])
                    p.wait("gpsimd", "pe", dn_done)
                    p.dma("gpsimd", wg_bf[ii % 2][:].rearrange("p (c n) -> p c n", c=KC),
                          gwt[:, 128 * icx:128 * (icx + 1)].rearrange("(c p) n -> p c n", p=128))
                    p.dma("gpsimd", wu_bf[ii % 2][:].rearrange("p (c n) -> p c n", c=KC),
                          upt[:, 128 * icx:128 * (icx + 1)].rearrange("(c p) n -> p c n", p=128))
                    wld = p.dma("gpsimd", wd_bf[ii][:], dwt[128 * icx:128 * (icx + 1), :])
                    p.wait("tensor", "gd", wld)
                    p.wait("tensor", "ve", hv_done[ii % 2])
                    for k in range(KC):
                        for th in range(2):
                            p.op("tensor", "pe", lambda t, k=k, th=th, ii=ii: t.matmul(
                                ps_g[:, 512 * th:512 * (th + 1)], wg_bf[ii % 2][:, 128 * k:128 * (k + 1)],
                                xn2[:, TPC * k + 512 * th: TPC * k + 512 * (th + 1)],
                                start=(k == 0), stop=(k == KC - 1)))
                    for k in range(KC):
                        for th in range(2):
                            p.op("tensor", "pe", lambda t, k=k, th=th, ii=ii: t.matmul(
                                ps_u[:, 512 * th:512 * (th + 1)], wu_bf[ii % 2][:, 128 * k:128 * (k + 1)],
                                xn2[:, TPC * k + 512 * th: TPC * k + 512 * (th + 1)],
                                start=(k == 0), stop=(k == KC - 1)))
                    p.wait("scalar", "pe")
                    p.wait("scalar", "ve", hv_done[ii % 2])
                    sl = p.op("scalar", "ac", lambda s, icx=icx: s.activation(silu_g[:], ps_g[:], AF.Silu, scale=gs_sb[:, icx:icx + 1]))
                    p.wait("vector", "ac", sl)
                    p.wait("vector", "pe")
                    hv = p.op("vector", "ve", lambda v, ii=ii, icx=icx: v.scalar_tensor_tensor(
                        hdn[ii][:], ps_u[:], us_sb[:, icx:icx + 1], silu_g[:], ALU.mult, ALU.mult))
                    hv_done[ii % 2] = hv
                    p.wait("tensor", "ve", hv)
                dev = {0: 0, 1: 0}
                for tt in range(NT):
                    for oh in range(2):
                        dp = oh
                        p.wait("tensor", "ve", dev[dp])
                        for ii in range(len(grp)):
                            for th in range(2):
                                p.op("tensor", "pe", lambda t, ii=ii, tt=tt, th=th, oh=oh, dp=dp, first=(ii == 0), last=(ii == len(grp) - 1): t.matmul(
                                    ps_d[dp][:, 512 * th:512 * (th + 1)], hdn[ii][:, 128 * tt:128 * (tt + 1)],
                                    wd_bf[ii][:, TPC * oh + 512 * th: TPC * oh + 512 * (th + 1)],
                                    start=first, stop=last))
                        p.wait("vector", "pe")
                        dev[dp] = p.op("vector", "ve", lambda v, tt=tt, oh=oh, dp=dp: v.tensor_add(
                            acc[:, D * tt + TPC * oh:D * tt + TPC * (oh + 1)], acc[:, D * tt + TPC * oh:D * tt + TPC * (oh + 1)], ps_d[dp][:]))
                dn_done = p.val("pe")
                if serial:
                    p.barrier()

            if "t_xn2" in tap_aps:
                p.barrier()
                p.dma("gpsimd", tap_aps["t_xn2"][:, :], xn2[:])
            if "t_acc" in tap_aps:
                p.barrier()
                p.dma("sync", tap_aps["t_acc"][:, :], acc[:])
            if "t_x2" in tap_aps:
                p.barrier()
                p.dma("sync", tap_aps["t_x2"][:, :], x2_dram)
            for i in range(NT):
                p.wait("sync", "ve")
                xl = p.dma("sync", xt4[i % 2][:], x2_dram[128 * i:128 * (i + 1), :])
                p.wait("vector", "ac")
                ym = p.op("vector", "ve", lambda v, i=i: v.tensor_mul(y4[i % 2][:], acc[:, D * i:D * (i + 1)], ds_bc[:]))
                p.wait("vector", "ve", ym)
                n4 = ln_chain(p, y4[i % 2], bn6b[i % 2], mvb[i % 2], stdb[i % 2], rstdb[i % 2], nbb[i % 2], eps_c,
                              lambda s, i=i: s.activation(yn4[i % 2][:], y4[i % 2][:], AF.Identity, bias=nbb[i % 2][:], scale=rstdb[i % 2][:]))
                p.wait("vector", "ac", n4)
                p.wait("vector", "sd", xl)
                fo = p.op("vector", "ve", lambda v, i=i: v.tensor_add(xt4[i % 2][:], xt4[i % 2][:], yn4[i % 2][:]))
                p.wait("sync", "ve", fo)
                p.dma("sync", out[128 * i:128 * (i + 1), :], xt4[i % 2][:])
                if serial:
                    p.barrier()

    finalize(nc, p)
    st.close()
    return nc


def finalize(nc, p):
    p.barrier()
    with nc.Block() as block:
        @block.sync
        def _(e):
            for f in p.q["sync"]:
                f(e)

        @block.gpsimd
        def _(e):
            for f in p.q["gpsimd"]:
                f(e)

        @block.scalar
        def _(e):
            for f in p.q["scalar"]:
                f(e)

        @block.vector
        def _(e):
            for f in p.q["vector"]:
                f(e)

        @block.tensor
        def _(e):
            for f in p.q["tensor"]:
                f(e)


def make_consts():
    ident = np.eye(128, dtype=np.float32)
    rotm = np.zeros((128, 128), np.float32)
    for j in range(64):
        rotm[2 * j + 1, 2 * j] = -1.0
        rotm[2 * j, 2 * j + 1] = 1.0
    dd = np.arange(128) // 2
    inv = 1.0 / ROPE_BASE ** (dd.astype(np.float64) * 2.0 / 128.0)
    tpos = np.arange(SEQ, dtype=np.float64)
    ang = inv[:, None] * tpos[None, :]
    cos_t = np.cos(ang).astype(np.float32)
    sin_t = np.sin(ang).astype(np.float32)
    kk = np.arange(128)[:, None]
    qq = np.arange(512)[None, :]
    mask = np.concatenate([np.where(kt * 128 + kk > qq, NEG, 0.0) for kt in range(4)], axis=1).astype(np.float32)
    return dict(identc=ident, rotmc=rotm, cos_t=cos_t, sin_t=sin_t, mask_t=mask,
                ones_c=np.ones((128, 1), np.float32), ones_r=np.ones((1, 128), np.float32),
                eps_cd=np.full((128, 1), 1e-5, np.float32))


def cols(v, n):
    return np.ascontiguousarray(np.asarray(v, np.float32).reshape(n, 128).T)


def make_inmaps(inp):
    x = np.asarray(inp["L0_attn_h"], np.float32)
    consts = make_consts()
    g = lambda k: np.ascontiguousarray(np.asarray(inp[k], np.float32))
    wdkv_f = g("L0_attn_W_DKV"); bdkv_f = cols(inp["L0_attn_b_DKV"], NKV)
    wdq_f = g("L0_attn_W_DQ"); bdq_f = cols(inp["L0_attn_b_DQ"], NQ)
    wkr_f = g("L0_attn_W_KR"); bkr_f = np.asarray(inp["L0_attn_b_KR"], np.float32)
    wuk_f = g("L0_attn_W_UK"); buk_f = np.asarray(inp["L0_attn_b_UK"], np.float32)
    wuv_f = g("L0_attn_W_UV"); buv_f = np.asarray(inp["L0_attn_b_UV"], np.float32)
    wuq_f = g("L0_attn_W_UQ"); buq_f = np.asarray(inp["L0_attn_b_UQ"], np.float32)
    wo_f = g("L0_attn_W_O"); bo_f = np.asarray(inp["L0_attn_b_O"], np.float32).reshape(1, D)
    gwt_f = np.ascontiguousarray(g("L0_ffn_gate_w").T)
    upt_f = np.ascontiguousarray(g("L0_ffn_up_w").T)
    dwt_f = np.ascontiguousarray(g("L0_ffn_down_w").T)
    gs_f = cols(inp["L0_ffn_gate_s"], IC)
    us_f = cols(inp["L0_ffn_up_s"], IC)
    ds_f = np.asarray(inp["L0_ffn_down_s"], np.float32).reshape(1, D)

    maps = []
    for c in range(8):
        b, m = divmod(c, 4)
        sl = slice(512 * m, 512 * (m + 1))
        maps.append(dict(
            x=np.ascontiguousarray(x[b, TPC * m:TPC * (m + 1), :]),
            meta=np.array([[TPC * m]], np.int32),
            wdkv=wdkv_f, bdkv=bdkv_f, wdq=wdq_f, bdq=bdq_f,
            wkr=np.ascontiguousarray(wkr_f[:, sl]), bkr=cols(bkr_f[sl], NSLOT),
            wuk=np.ascontiguousarray(wuk_f[:, sl]), buk=cols(buk_f[sl], NSLOT),
            wuv=np.ascontiguousarray(wuv_f[:, sl]), buv=cols(buv_f[sl], NSLOT),
            wuq=np.ascontiguousarray(wuq_f[:, sl]), buq=cols(buq_f[sl], NSLOT),
            wo=wo_f, bo_r=bo_f, gwt=gwt_f, upt=upt_f, dwt=dwt_f,
            gs=gs_f, us=us_f, ds_r=ds_f,
            **consts,
        ))
    return maps


def kernel(**inp):
    nc = build_nc(phases=3)
    maps = make_inmaps(inp)
    res = run_bass_kernel_spmd(nc, maps, core_ids=list(range(8)))
    outv = np.empty((2, SEQ, D), np.float32)
    for c in range(8):
        b, m = divmod(c, 4)
        outv[b, TPC * m:TPC * (m + 1), :] = res.results[c]["out"]
    return outv


# version 05
# speedup vs baseline: 1.4426x; 1.0056x over previous
import numpy as np
from contextlib import ExitStack
import concourse.bass as bass
import concourse.mybir as mybir
from concourse.bass import ds
from concourse.bass_utils import run_bass_kernel_spmd

F32 = mybir.dt.float32
BF16 = mybir.dt.bfloat16
I32 = mybir.dt.int32
AX = mybir.AxisListType.X
ALU = mybir.AluOpType
AF = mybir.ActivationFunctionType

D = 2048
SEQ = 4096
TPC = 1024
NT = TPC // 128
KC = D // 128
DKV = 512
NKV = DKV // 128
DQ = 1536
NQ = DQ // 128
NSLOT = 4
NTS = SEQ // 512
INTER = 5632
IC = INTER // 128
GRP = 6
ROPE_BASE = 10000.0
GROUPS = [[0, 1, 2, 3], [4, 5, 6, 7]]
NEG = -30000.0


class Plan:
    ENGINES = ("sync", "gpsimd", "scalar", "vector", "tensor")

    def __init__(self):
        self.q = {e: [] for e in self.ENGINES}
        self.cnt = {}

    def val(self, sem):
        return self.cnt.get(sem, 0)

    def bump(self, sem, n):
        self.cnt[sem] = self.val(sem) + n
        return self.cnt[sem]

    def emit(self, eng, f):
        self.q[eng].append(f)

    def wait(self, eng, sem, v=None):
        v = self.val(sem) if v is None else v
        if v <= 0:
            return
        self.emit(eng, lambda e, s=sem, vv=v: e.wait_ge(SEM[s], vv))

    def dma(self, eng, out, in_, sem=None, **kw):
        sem = sem or ("gd" if eng == "gpsimd" else ("ad" if eng == "scalar" else "sd"))
        t = self.bump(sem, 16)
        self.emit(eng, lambda e, o=out, i=in_, s=sem, k=kw: e.dma_start(out=o, in_=i, **k).then_inc(SEM[s], 16))
        return t

    def op(self, eng, sem, build):
        t = self.bump(sem, 1)
        self.emit(eng, lambda e, b=build, s=sem: b(e).then_inc(SEM[s], 1))
        return t

    def barrier(self, engines=None):
        snap = {s: v for s, v in self.cnt.items() if v > 0}
        for eng in (engines or self.ENGINES):
            for s, v in snap.items():
                self.wait(eng, s, v)


SEM = {}


def ln_chain(p, src, bn6, mv, stdc, rstd, nb, eps_c, out_fn):
    for j in range(4):
        p.op("vector", "ve", lambda v, j=j: v.bn_stats(bn6[:, 6 * j:6 * j + 6], src[:, 512 * j:512 * (j + 1)]))
    p.wait("vector", "ve")
    p.op("vector", "ve", lambda v: v.bn_aggr(mv[:], bn6[:]))
    p.wait("scalar", "ve")
    sq = p.op("scalar", "ac", lambda s: s.activation(stdc[:], mv[:, 1:2], AF.Sqrt, bias=eps_c[:], scale=1.0))
    p.wait("vector", "ac", sq)
    p.op("vector", "ve", lambda v: v.reciprocal(rstd[:], stdc[:]))
    p.wait("vector", "ve")
    p.op("vector", "ve", lambda v: v.scalar_tensor_tensor(nb[:], mv[:, 0:1], -1.0, rstd[:], ALU.mult, ALU.mult))
    p.wait("scalar", "ve")
    return p.op("scalar", "ac", out_fn)


def build_nc(phases=3, taps=(), serial=False):
    nc = bass.Bass("TRN2", target_bir_lowering=False)
    p = Plan()

    def din(name, shape, dt=F32):
        return nc.dram_tensor(name, shape, dt, kind="ExternalInput").ap()

    x_in = din("x", [TPC, D])
    meta = din("meta", [1, 1], I32)
    wdkv = din("wdkv", [D, DKV])
    bdkv = din("bdkv", [128, NKV])
    wdq = din("wdq", [D, DQ])
    bdq = din("bdq", [128, NQ])
    wkr = din("wkr", [D, 512])
    bkr = din("bkr", [128, NSLOT])
    wuk = din("wuk", [DKV, 512])
    buk = din("buk", [128, NSLOT])
    wuv = din("wuv", [DKV, 512])
    buv = din("buv", [128, NSLOT])
    wuq = din("wuq", [DQ, 512])
    buq = din("buq", [128, NSLOT])
    wo = din("wo", [D, D])
    bo_r = din("bo_r", [1, D])
    gwt = din("gwt", [D, INTER])
    upt = din("upt", [D, INTER])
    dwt = din("dwt", [INTER, D])
    gs = din("gs", [128, IC])
    us = din("us", [128, IC])
    ds_r = din("ds_r", [1, D])
    identc = din("identc", [128, 128])
    rotmc = din("rotmc", [128, 128])
    cos_td = din("cos_t", [128, SEQ])
    sin_td = din("sin_t", [128, SEQ])
    mask_t = din("mask_t", [128, 4 * 512])
    ones_cd = din("ones_c", [128, 1])
    ones_rd = din("ones_r", [1, 128])
    eps_cd = din("eps_cd", [128, 1])

    out = nc.dram_tensor("out", [TPC, D], F32, kind="ExternalOutput").ap()

    shard_x = nc.dram_tensor("shard_x", [2048, TPC], BF16, kind="Internal").ap()
    shard_l = nc.dram_tensor("shard_l", [2048, TPC], BF16, kind="Internal").ap()
    l_all_x = nc.dram_tensor("l_all_x", [4 * 2048, TPC], BF16, kind="Internal").ap()
    l_all_l = nc.dram_tensor("l_all_l", [4 * 2048, TPC], BF16, kind="Internal").ap()
    q_dram = nc.dram_tensor("q_dram", [NSLOT * 256, SEQ], BF16, kind="Internal").ap()
    ctx_shard = nc.dram_tensor("ctx_shard", [512, SEQ], BF16, kind="Internal").ap()
    ctx_all = nc.dram_tensor("ctx_all", [2048, SEQ], BF16, kind="Internal").ap()
    x2_dram = nc.dram_tensor("x2_dram", [TPC, D], F32, kind="Internal").ap()

    tap_aps = {}
    if "shard" in taps:
        tap_aps["t_shard"] = nc.dram_tensor("t_shard", [4 * 4096, TPC], F32, kind="ExternalOutput").ap()
    if "qdram" in taps:
        tap_aps["t_qdram"] = nc.dram_tensor("t_qdram", [NSLOT * 256, SEQ], F32, kind="ExternalOutput").ap()
    if "ctx" in taps:
        tap_aps["t_ctx"] = nc.dram_tensor("t_ctx", [2048, SEQ], F32, kind="ExternalOutput").ap()
    if "x2" in taps:
        tap_aps["t_x2"] = nc.dram_tensor("t_x2", [TPC, D], F32, kind="ExternalOutput").ap()
    if "xn2" in taps:
        tap_aps["t_xn2"] = nc.dram_tensor("t_xn2", [128, KC * TPC], F32, kind="ExternalOutput").ap()
    if "acc" in taps:
        tap_aps["t_acc"] = nc.dram_tensor("t_acc", [128, NT * D], F32, kind="ExternalOutput").ap()

    st = ExitStack()
    sbuf = lambda n, s, d: st.enter_context(nc.sbuf_tensor(n, s, d))

    ident_bf = sbuf("ident_bf", [128, 128], BF16)
    rotm_bf = sbuf("rotm_bf", [128, 128], BF16)
    ones_cb = sbuf("ones_cb", [128, 1], BF16)
    ones_rf = sbuf("ones_rf", [1, 128], F32)
    eps_c = sbuf("eps_c", [128, 1], F32)
    meta_sb = st.enter_context(nc.sbuf_tensor("meta_sb", [1, 1], I32))
    k2p = sbuf("k2p", [1, NSLOT], F32)
    q2mp = sbuf("q2mp", [1, NSLOT * NTS], F32)

    for s in ("sd", "gd", "ad", "cc", "pe", "ve", "ac"):
        SEM[s] = st.enter_context(nc.semaphore(s))

    p.dma("gpsimd", ident_bf[:], identc)
    p.dma("gpsimd", rotm_bf[:], rotmc)
    p.dma("gpsimd", ones_cb[:], ones_cd)
    p.dma("sync", ones_rf[:], ones_rd)
    p.dma("sync", eps_c[:], eps_cd)
    p.dma("sync", meta_sb[:], meta)

    with ExitStack() as st1:
        sb1 = lambda n, s, d: st1.enter_context(nc.sbuf_tensor(n, s, d))
        ps1s = lambda n, s, d: st1.enter_context(nc.psum_tensor(n, s, d))
        xt = [sb1(f"xt{i}", [128, D], F32) for i in range(2)]
        bn6 = sb1("bn6", [128, 24], F32)
        mv = sb1("mv", [128, 2], F32)
        stdc = sb1("stdc", [128, 1], F32)
        rstd = sb1("rstd", [128, 1], F32)
        nb = sb1("nb", [128, 1], F32)
        xn_tm = [sb1(f"xntm{i}", [128, D], BF16) for i in range(2)]
        xn1 = sb1("xn1", [128, KC * TPC], BF16)
        wdkv_bf = sb1("wdkv_bf", [128, KC * DKV], BF16)
        wdq_bf = sb1("wdq_bf", [128, KC * DQ], BF16)
        bdkv_sb = sb1("bdkv_sb", [128, NKV], F32)
        bdq_sb = sb1("bdq_sb", [128, NQ], F32)
        stage = [sb1(f"stg{i}", [128, 512], BF16) for i in range(4)]
        ps_tr = ps1s("ps_tr", [128, 512], BF16)
        ps_p = [ps1s(f"ps_p{i}", [128, 512], F32) for i in range(2)]

        p.dma("gpsimd", wdkv_bf[:].rearrange("p (c n) -> p c n", c=KC), wdkv.rearrange("(c p) n -> p c n", p=128))
        p.dma("gpsimd", wdq_bf[:].rearrange("p (c n) -> p c n", c=KC), wdq.rearrange("(c p) n -> p c n", p=128))
        p.dma("sync", bdkv_sb[:], bdkv)
        p.dma("sync", bdq_sb[:], bdq)

        tr_evacs = 0
        for i in range(NT):
            p.wait("sync", "ve")
            p.wait("sync", "ac")
            x_ld = p.dma("sync", xt[i % 2][:], x_in[128 * i:128 * (i + 1), :])
            p.wait("vector", "sd", x_ld)
            p.wait("scalar", "ve", tr_evacs)
            nrm = ln_chain(p, xt[i % 2], bn6, mv, stdc, rstd, nb, eps_c,
                           lambda s, b=xt[i % 2], o=xn_tm[i % 2]: s.activation(o[:], b[:], AF.Identity, bias=nb[:], scale=rstd[:]))
            p.wait("tensor", "ac", nrm)
            for j4 in range(4):
                p.wait("tensor", "ve", tr_evacs)
                for j in range(4):
                    c = j4 * 4 + j
                    p.op("tensor", "pe", lambda t, c=c, j=j, b=xn_tm[i % 2]: t.transpose(
                        ps_tr[:, 128 * j:128 * (j + 1)], b[:, 128 * c:128 * (c + 1)], ident_bf[:]))
                p.wait("vector", "pe")
                def evac(v, i=i, j4=j4):
                    dv = xn1[:, :].rearrange("p (c n) -> p c n", n=TPC)
                    return v.tensor_copy(dv[:, j4 * 4:(j4 + 1) * 4, 128 * i:128 * (i + 1)],
                                         ps_tr[:].rearrange("p (c n) -> p c n", n=128))
                tr_evacs = p.op("vector", "ve", evac)
            if serial:
                p.barrier()

        p.wait("sync", "ve")
        for c in range(KC):
            p.dma("sync", shard_x[128 * c:128 * (c + 1), :], xn1[:, TPC * c:TPC * (c + 1)])
        p.wait("gpsimd", "sd")
        p.op("gpsimd", "cc", lambda g: g.collective_compute(
            "AllGather", ALU.bypass, replica_groups=GROUPS, ins=[shard_x.opt()], outs=[l_all_x.opt()]))

        p.wait("tensor", "gd")
        evac_t = {0: 0, 1: 0}
        for oc in range(KC):
            is_kv = oc < NKV
            for th in range(2):
                par = (2 * oc + th) % 2
                pp = ps_p[par]
                p.wait("tensor", "ac", evac_t[par])
                for k in range(KC):
                    if is_kv:
                        lhs = wdkv_bf[:, DKV * k + 128 * oc: DKV * k + 128 * (oc + 1)]
                    else:
                        lhs = wdq_bf[:, DQ * k + 128 * (oc - NKV): DQ * k + 128 * (oc - NKV + 1)]
                    p.op("tensor", "pe", lambda t, l=lhs, k=k, pp=pp, th=th: t.matmul(
                        pp[:], l, xn1[:, TPC * k + 512 * th: TPC * k + 512 * (th + 1)],
                        start=(k == 0), stop=(k == KC - 1)))
                p.wait("scalar", "pe")
                p.wait("scalar", "sd")
                bcol = bdkv_sb[:, oc:oc + 1] if is_kv else bdq_sb[:, oc - NKV:oc - NKV + 1]
                sg = stage[(2 * oc + th) % 4]
                evac_t[par] = p.op("scalar", "ac", lambda s, pp=pp, b=bcol, sg=sg: s.activation(sg[:], pp[:], AF.Identity, bias=b, scale=1.0))
                p.wait("sync", "ac", evac_t[par])
                row = 128 * oc if is_kv else 512 + 128 * (oc - NKV)
                p.dma("sync", shard_l[row:row + 128, 512 * th:512 * (th + 1)], sg[:])
            if serial:
                p.barrier()

        p.wait("gpsimd", "sd")
        p.op("gpsimd", "cc", lambda g: g.collective_compute(
            "AllGather", ALU.bypass, replica_groups=GROUPS, ins=[shard_l.opt()], outs=[l_all_l.opt()]))

    if "t_shard" in tap_aps:
        p.wait("gpsimd", "cc")
        for r in range(4):
            p.dma("gpsimd", tap_aps["t_shard"][4096 * r:4096 * r + 512, :], l_all_l[2048 * r:2048 * r + 512, :])
            p.dma("gpsimd", tap_aps["t_shard"][4096 * r + 512:4096 * r + 2048, :], l_all_l[2048 * r + 512:2048 * (r + 1), :])
            p.dma("gpsimd", tap_aps["t_shard"][4096 * r + 2048:4096 * (r + 1), :], l_all_x[2048 * r:2048 * (r + 1), :])
    if phases == 1:
        p.barrier()
        finalize(nc, p)
        st.close()
        return nc

    p.barrier()
    with ExitStack() as stS:
        sbS = lambda n, s, d: stS.enter_context(nc.sbuf_tensor(n, s, d))
        kC = [sbS(f"kC{s}", [128, SEQ], BF16) for s in range(NSLOT)]
        kR = [sbS(f"kR{s}", [128, SEQ], BF16) for s in range(NSLOT)]
        vS = [sbS(f"vS{s}", [128, SEQ], BF16) for s in range(NSLOT)]

        with ExitStack() as stB:
            sb2 = lambda n, s, d: stB.enter_context(nc.sbuf_tensor(n, s, d))
            ps2s = lambda n, s, d: stB.enter_context(nc.psum_tensor(n, s, d))
            lat_x = [sb2(f"latx{c}", [128, 512], BF16) for c in range(KC)]
            lat_q = [sb2(f"latq{c}", [128, 512], BF16) for c in range(NQ)]
            lat_kv = [sb2(f"latkv{c}", [128, 512], BF16) for c in range(NKV)]
            cos_s = sb2("cos_s", [128, 512], F32)
            sin_s = sb2("sin_s", [128, 512], F32)
            wkr_bf = sb2("wkr_bf", [128, KC * 512], BF16)
            wuk_bf = sb2("wuk_bf", [128, NKV * 512], BF16)
            wuv_bf = sb2("wuv_bf", [128, NKV * 512], BF16)
            wuq_bf = sb2("wuq_bf", [128, NQ * 512], BF16)
            bkr_sb = sb2("bkr_sb", [128, NSLOT], F32)
            buk_sb = sb2("buk_sb", [128, NSLOT], F32)
            buv_sb = sb2("buv_sb", [128, NSLOT], F32)
            buq_sb = sb2("buq_sb", [128, NSLOT], F32)
            krbf_b = [sb2(f"krbf{i}", [128, 512], BF16) for i in range(2)]
            vfm = sb2("vfm", [128, 512], BF16)
            qbf_b = [sb2(f"qbf{i}", [128, 512], BF16) for i in range(2)]
            t1_b = [sb2(f"t1_{i}", [128, 512], F32) for i in range(2)]
            t2_b = [sb2(f"t2_{i}", [128, 512], F32) for i in range(2)]
            ksq = sb2("ksq", [128, 512], BF16)
            qsq2 = sb2("qsq2", [128, 512], BF16)
            k2 = sb2("k2", [1, NSLOT], F32)
            k2t = sb2("k2t", [1, 1], F32)
            q2m = sb2("q2m", [1, NSLOT * NTS], F32)
            ps_a = [ps2s(f"ps_a{s}", [128, 512], F32) for s in range(NSLOT)]
            ps_rot_b = [ps2s(f"ps_rot{i}", [128, 512], F32) for i in range(2)]
            ps_vt = ps2s("ps_vt", [128, 512], BF16)
            ps_n = ps2s("ps_n", [1, 512], F32)

            p.dma("gpsimd", wkr_bf[:].rearrange("p (c n) -> p c n", c=KC), wkr.rearrange("(c p) n -> p c n", p=128))
            p.dma("gpsimd", wuk_bf[:].rearrange("p (c n) -> p c n", c=NKV), wuk.rearrange("(c p) n -> p c n", p=128))
            p.dma("gpsimd", wuv_bf[:].rearrange("p (c n) -> p c n", c=NKV), wuv.rearrange("(c p) n -> p c n", p=128))
            p.dma("gpsimd", wuq_bf[:].rearrange("p (c n) -> p c n", c=NQ), wuq.rearrange("(c p) n -> p c n", p=128))
            p.dma("sync", bkr_sb[:], bkr)
            p.dma("sync", buk_sb[:], buk)
            p.dma("sync", buv_sb[:], buv)
            p.dma("sync", buq_sb[:], buq)
            p.wait("tensor", "gd")

            kr_done = q_done = kv_done = 0
            for ts in range(NTS):
                ts_ve0 = p.val("ve")
                ts_sd0 = p.val("sd")
                r = ts // 2
                cs = 512 * (ts % 2)
                rb = 4096 * r
                rbx = 2048 * r
                if ts == 0:
                    p.wait("sync", "cc", 1)
                p.wait("sync", "pe", kr_done)
                for c in range(KC):
                    p.dma("sync", lat_x[c][:], l_all_x[rbx + 128 * c: rbx + 128 * (c + 1), cs:cs + 512])
                p.wait("sync", "ve")
                ld = p.dma("sync", cos_s[:], cos_td[:, 512 * ts:512 * (ts + 1)])
                ldx = p.dma("sync", sin_s[:], sin_td[:, 512 * ts:512 * (ts + 1)])
                if ts == 0:
                    p.wait("sync", "cc", 2)
                p.wait("sync", "pe", q_done)
                for c in range(NQ):
                    ldq = p.dma("sync", lat_q[c][:], l_all_l[rbx + 512 + 128 * c: rbx + 512 + 128 * (c + 1), cs:cs + 512])
                p.wait("sync", "pe", kv_done)
                for c in range(NKV):
                    ldkv = p.dma("sync", lat_kv[c][:], l_all_l[rbx + 128 * c: rbx + 128 * (c + 1), cs:cs + 512])
                p.wait("tensor", "sd", ldx)

                for s in range(NSLOT):
                    for k in range(KC):
                        p.op("tensor", "pe", lambda t, s=s, k=k: t.matmul(
                            ps_a[s][:], wkr_bf[:, 512 * k + 128 * s: 512 * k + 128 * (s + 1)], lat_x[k][:],
                            start=(k == 0), stop=(k == KC - 1)))
                kr_done = p.val("pe")
                kr_fin = {0: 0, 1: 0}
                kr_ev = 0
                for s in range(NSLOT):
                    par = s % 2
                    krbf = krbf_b[par]; t1 = t1_b[par]; t2 = t2_b[par]; ps_rot = ps_rot_b[par]
                    p.wait("scalar", "pe", kr_done - (NSLOT - 1 - s) * KC)
                    p.wait("scalar", "ve", ts_ve0)
                    p.wait("scalar", "sd", ts_sd0)
                    p.wait("scalar", "ve", kr_fin[par])
                    ev = p.op("scalar", "ac", lambda sc, s=s, krbf=krbf: sc.activation(krbf[:], ps_a[s][:], AF.Identity, bias=bkr_sb[:, s:s + 1], scale=1.0))
                    kr_ev = ev
                    p.wait("tensor", "ac", ev)
                    p.wait("tensor", "ve", kr_fin[par])
                    rt = p.op("tensor", "pe", lambda t, krbf=krbf, ps_rot=ps_rot: t.matmul(ps_rot[:], rotm_bf[:], krbf[:], start=True, stop=True))
                    p.wait("vector", "ac", ev)
                    m1 = p.op("vector", "ve", lambda v, krbf=krbf, t1=t1: v.tensor_mul(t1[:], krbf[:], cos_s[:]))
                    p.wait("vector", "pe", rt)
                    m2 = p.op("vector", "ve", lambda v, t2=t2, ps_rot=ps_rot: v.tensor_mul(t2[:], ps_rot[:], sin_s[:]))
                    p.wait("vector", "ve", m2)
                    kr_fin[par] = p.op("vector", "ve", lambda v, s=s, ts=ts, t1=t1, t2=t2: v.tensor_add(kR[s][:, 512 * ts:512 * (ts + 1)], t1[:], t2[:]))
                p.wait("tensor", "ac", kr_ev)
                krfin_final = dict(kr_fin)

                p.wait("tensor", "sd", ldq)
                for s in range(NSLOT):
                    for k in range(NQ):
                        p.op("tensor", "pe", lambda t, s=s, k=k: t.matmul(
                            ps_a[s][:], wuq_bf[:, 512 * k + 128 * s: 512 * k + 128 * (s + 1)], lat_q[k][:],
                            start=(k == 0), stop=(k == NQ - 1)))
                q_done = p.val("pe")
                q_cv = {0: 0, 1: 0}
                q_dma = {0: 0, 1: 0}
                nrd_last = 0
                for s in range(NSLOT):
                    par = s % 2
                    qbf = qbf_b[par]; krbf = krbf_b[par]; t1 = t1_b[par]; t2 = t2_b[par]; ps_rot = ps_rot_b[par]
                    p.wait("scalar", "pe", q_done - (NSLOT - 1 - s) * NQ)
                    p.wait("scalar", "ve", ts_ve0)
                    p.wait("scalar", "sd", ts_sd0)
                    p.wait("scalar", "sd", q_dma[par])
                    p.wait("scalar", "ve", q_cv[par])
                    ev = p.op("scalar", "ac", lambda sc, s=s, qbf=qbf: sc.activation(qbf[:], ps_a[s][:], AF.Identity, bias=buq_sb[:, s:s + 1], scale=1.0))
                    q_last_ev = ev
                    p.wait("sync", "ac", ev)
                    p.dma("sync", q_dram[256 * s:256 * s + 128, 512 * ts:512 * (ts + 1)], qbf[:])
                    p.wait("tensor", "ac", ev)
                    p.wait("tensor", "ve", q_cv[par])
                    p.wait("tensor", "ve", krfin_final[par])
                    rt = p.op("tensor", "pe", lambda t, qbf=qbf, ps_rot=ps_rot: t.matmul(ps_rot[:], rotm_bf[:], qbf[:], start=True, stop=True))
                    p.wait("vector", "ac", ev)
                    m1 = p.op("vector", "ve", lambda v, qbf=qbf, t1=t1: v.tensor_mul(t1[:], qbf[:], cos_s[:]))
                    p.wait("vector", "pe", rt)
                    m2 = p.op("vector", "ve", lambda v, t2=t2, ps_rot=ps_rot: v.tensor_mul(t2[:], ps_rot[:], sin_s[:]))
                    p.wait("vector", "ve", m2)
                    ad = p.op("vector", "ve", lambda v, t1=t1, t2=t2: v.tensor_add(t1[:], t1[:], t2[:]))
                    p.wait("vector", "ve", ad)
                    cv = p.op("vector", "ve", lambda v, krbf=krbf, t1=t1: v.tensor_copy(krbf[:], t1[:]))
                    q_cv[par] = cv
                    p.wait("sync", "ve", cv)
                    q_dma[par] = p.dma("sync", q_dram[256 * s + 128:256 * s + 256, 512 * ts:512 * (ts + 1)], krbf[:])
                    p.wait("scalar", "ve", cv)
                    p.wait("scalar", "pe")
                    n1 = p.op("scalar", "ac", lambda sc, qbf=qbf: sc.activation(qsq2[:], qbf[:], AF.Square))
                    p.wait("tensor", "ac", n1)
                    p.wait("tensor", "ve", nrd_last)
                    p.op("tensor", "pe", lambda t: t.matmul(ps_n[:], ones_cb[:], qsq2[:], start=True, stop=False))
                    p.wait("scalar", "pe")
                    n2 = p.op("scalar", "ac", lambda sc, krbf=krbf: sc.activation(qsq2[:], krbf[:], AF.Square))
                    p.wait("tensor", "ac", n2)
                    nm = p.op("tensor", "pe", lambda t: t.matmul(ps_n[:], ones_cb[:], qsq2[:], start=False, stop=True))
                    p.wait("vector", "pe", nm)
                    nrd_last = p.op("vector", "ve", lambda v, s=s, ts=ts: v.tensor_reduce(q2m[:, 8 * s + ts:8 * s + ts + 1], ps_n[:], AX, ALU.max))
                p.wait("tensor", "ac", q_last_ev)

                p.wait("tensor", "sd", ldkv)
                for s in range(NSLOT):
                    for k in range(NKV):
                        p.op("tensor", "pe", lambda t, s=s, k=k: t.matmul(
                            ps_a[s][:], wuk_bf[:, 512 * k + 128 * s: 512 * k + 128 * (s + 1)], lat_kv[k][:],
                            start=(k == 0), stop=(k == NKV - 1)))
                for s in range(NSLOT):
                    p.wait("scalar", "pe")
                    p.op("scalar", "ac", lambda sc, s=s, ts=ts: sc.activation(
                        kC[s][:, 512 * ts:512 * (ts + 1)], ps_a[s][:], AF.Identity, bias=buk_sb[:, s:s + 1], scale=1.0))
                p.wait("tensor", "ac")

                for s in range(NSLOT):
                    for k in range(NKV):
                        p.op("tensor", "pe", lambda t, s=s, k=k: t.matmul(
                            ps_a[s][:], wuv_bf[:, 512 * k + 128 * s: 512 * k + 128 * (s + 1)], lat_kv[k][:],
                            start=(k == 0), stop=(k == NKV - 1)))
                kv_done = p.val("pe")
                for s in range(NSLOT):
                    p.wait("scalar", "pe")
                    ev = p.op("scalar", "ac", lambda sc, s=s: sc.activation(vfm[:], ps_a[s][:], AF.Identity, bias=buv_sb[:, s:s + 1], scale=1.0))
                    p.wait("tensor", "ac", ev)
                    p.wait("tensor", "ve")
                    for j in range(4):
                        p.op("tensor", "pe", lambda t, j=j: t.transpose(ps_vt[:, 128 * j:128 * (j + 1)], vfm[:, 128 * j:128 * (j + 1)], ident_bf[:]))
                    p.wait("vector", "pe")
                    p.op("vector", "ve", lambda v, s=s, ts=ts: v.tensor_copy(vS[s][:, 512 * ts:512 * (ts + 1)], ps_vt[:]))
                    p.wait("scalar", "pe")
                p.wait("tensor", "ac")
                if serial:
                    p.barrier()

            p.op("vector", "ve", lambda v: v.memset(k2[:], 0.0))
            for s in range(NSLOT):
                for sl in range(NTS):
                    p.wait("scalar", "pe")
                    sq1 = p.op("scalar", "ac", lambda sc, s=s, sl=sl: sc.activation(ksq[:], kC[s][:, 512 * sl:512 * (sl + 1)], AF.Square))
                    p.wait("tensor", "ac", sq1)
                    p.wait("tensor", "ve")
                    p.op("tensor", "pe", lambda t: t.matmul(ps_n[:], ones_cb[:], ksq[:], start=True, stop=False))
                    p.wait("scalar", "pe")
                    sq2 = p.op("scalar", "ac", lambda sc, s=s, sl=sl: sc.activation(ksq[:], kR[s][:, 512 * sl:512 * (sl + 1)], AF.Square))
                    p.wait("tensor", "ac", sq2)
                    mm = p.op("tensor", "pe", lambda t: t.matmul(ps_n[:], ones_cb[:], ksq[:], start=False, stop=True))
                    p.wait("vector", "pe", mm)
                    rd = p.op("vector", "ve", lambda v: v.tensor_reduce(k2t[:], ps_n[:], AX, ALU.max))
                    p.wait("vector", "ve", rd)
                    p.op("vector", "ve", lambda v, s=s: v.tensor_tensor(k2[:, s:s + 1], k2[:, s:s + 1], k2t[:], ALU.max))
                    p.wait("vector", "ve")
            p.op("vector", "ve", lambda v: v.tensor_copy(k2p[:], k2[:]))
            p.wait("vector", "ve")
            p.op("vector", "ve", lambda v: v.tensor_copy(q2mp[:], q2m[:]))
            p.barrier()

        wo_st = ExitStack()
        wo_bf = wo_st.enter_context(nc.sbuf_tensor("wo_bf", [128, KC * D], BF16, side="right"))
        p.dma("gpsimd", wo_bf[:].rearrange("p (c n) -> p c n", c=KC), wo.rearrange("(c p) n -> p c n", p=128))
        with ExitStack() as stA2:
            sb3 = lambda n, s, d: stA2.enter_context(nc.sbuf_tensor(n, s, d))
            ps3s = lambda n, s, d: stA2.enter_context(nc.psum_tensor(n, s, d))
            masks = sb3("masks", [128, 2048], F32)
            qc_t = [sb3(f"qc_t{i}", [128, 512], BF16) for i in range(2)]
            qr_t = [sb3(f"qr_t{i}", [128, 512], BF16) for i in range(2)]
            prb = [sb3(f"prb{i}", [128, 512], BF16) for i in range(4)]
            ctxs = sb3("ctxs", [128, SEQ], BF16)
            craw = [sb3(f"craw{i}", [128, 512], F32) for i in range(2)]
            bias_all = sb3("bias_all", [128, NSLOT * NTS], F32)
            b2r = sb3("b2r", [1, NSLOT * NTS], F32)
            srow = [sb3(f"srow{i}", [1, 512], F32) for i in range(2)]
            rrow = [sb3(f"rrow{i}", [1, 512], F32) for i in range(2)]
            bc_sb = [sb3(f"bc_sb{i}", [128, 512], F32) for i in range(2)]
            ps_sc = [ps3s(f"ps_sc{i}", [128, 512], F32) for i in range(3)]
            ps_ctx = [ps3s(f"ps_ctx{i}", [128, 512], F32) for i in range(2)]
            ps_sum = [ps3s(f"ps_sum{i}", [1, 512], F32) for i in range(2)]
            ps_bc = ps3s("ps_bc", [128, 512], F32)

            p.dma("sync", masks[:], mask_t)
            bb = p.op("vector", "ve", lambda v: v.tensor_tensor(
                b2r[:].rearrange("a (s t) -> a s t", s=NSLOT),
                q2mp[:].rearrange("a (s t) -> a s t", s=NSLOT),
                k2p[:].rearrange("a (s t) -> a s t", t=1).broadcast_to((1, NSLOT, NTS)),
                ALU.mult))
            p.wait("scalar", "ve", bb)
            bsq = p.op("scalar", "ac", lambda sc: sc.activation(b2r[:], b2r[:], AF.Sqrt))
            p.wait("tensor", "ac", bsq)
            bmm = p.op("tensor", "pe", lambda t: t.matmul(ps_bc[:, 0:NSLOT * NTS], ones_rf[:], b2r[:], start=True, stop=True))
            p.wait("scalar", "pe", bmm)
            bev = p.op("scalar", "ac", lambda sc: sc.activation(bias_all[:], ps_bc[:, 0:NSLOT * NTS], AF.Identity, scale=-0.0625))
            p.wait("scalar", "ac", bev)
            p.wait("tensor", "ac", bev)

            exp_cnt = {i: 0 for i in range(3)}
            cons_cnt = {i: 0 for i in range(4)}
            ctx_free = {0: 0, 1: 0}
            qtile_free = {0: 0, 1: 0}
            bc_free = 0
            LOOK = 2

            def load_q(s, qc):
                qp = qc % 2
                p.wait("sync", "pe", qtile_free[qp])
                p.dma("sync", qc_t[qp][:], q_dram[256 * s:256 * s + 128, 512 * qc:512 * (qc + 1)])
                return p.dma("sync", qr_t[qp][:], q_dram[256 * s + 128:256 * s + 256, 512 * qc:512 * (qc + 1)])

            qld = {(0, 0): load_q(0, 0)}
            for s in range(NSLOT):
                for qc in range(NTS):
                    qp = qc % 2
                    p.wait("tensor", "sd", qld[(s, qc)])
                    p.wait("tensor", "ve", ctx_free[qp])
                    bcol = bias_all[:, 8 * s + qc:8 * s + qc + 1]
                    nkt = 4 * (qc + 1)
                    pend = []
                    last_mm = 0

                    for ik in range(nkt):
                        kb, kt = divmod(ik, 4)
                        par = ik % 3
                        ring = ik % 4
                        diag = kb == qc
                        q0 = 128 * kt if diag else 0
                        if pend and len(pend) > LOOK - 1:
                            p.wait("tensor", "ac", pend[0][2])
                        elif exp_cnt[par]:
                            p.wait("tensor", "ac", exp_cnt[par])
                        if len(pend) > LOOK:
                            jk, jring, jex, jq0 = pend.pop(0)
                            p.op("tensor", "pe", lambda t, s=s, jk=jk, jring=jring, qp=qp, jq0=jq0, first=(jk == 0), last=(jk == nkt - 1): t.matmul(
                                ps_ctx[qp][:, jq0:512], vS[s][:, 128 * jk: 128 * (jk + 1)], prb[jring][:, jq0:512], start=first, stop=last))
                            cmm = p.op("tensor", "pe", lambda t, jring=jring, qp=qp, jq0=jq0, first=(jk == 0), last=(jk == nkt - 1): t.matmul(
                                ps_sum[qp][:, jq0:512], ones_cb[:], prb[jring][:, jq0:512], start=first, stop=last))
                            cons_cnt[jring] = cmm
                        p.op("tensor", "pe", lambda t, s=s, kb=kb, kt=kt, par=par, qp=qp, q0=q0: t.matmul(
                            ps_sc[par][:, q0:512], kC[s][:, 512 * kb + 128 * kt: 512 * kb + 128 * (kt + 1)], qc_t[qp][:, q0:512], start=True, stop=False))
                        mm = p.op("tensor", "pe", lambda t, s=s, kb=kb, kt=kt, par=par, qp=qp, q0=q0: t.matmul(
                            ps_sc[par][:, q0:512], kR[s][:, 512 * kb + 128 * kt: 512 * kb + 128 * (kt + 1)], qr_t[qp][:, q0:512], start=False, stop=True))
                        last_mm = mm
                        gate = ("pe", mm)
                        if diag:
                            p.wait("vector", "pe", mm)
                            mk = p.op("vector", "ve", lambda v, kt=kt, par=par, q0=q0: v.tensor_tensor(
                                ps_sc[par][:, q0:512], ps_sc[par][:, q0:512], masks[:, 512 * kt + q0:512 * (kt + 1)], ALU.add))
                            gate = ("ve", mk)
                        p.wait("scalar", gate[0], gate[1])
                        p.wait("scalar", "pe", cons_cnt[ring])
                        ex = p.op("scalar", "ac", lambda sc, par=par, ring=ring, bcol=bcol, q0=q0: sc.activation(
                            prb[ring][:, q0:512], ps_sc[par][:, q0:512], AF.Exp, bias=bcol, scale=0.0625))
                        exp_cnt[par] = ex
                        pend.append((ik, ring, ex, q0))
                    nqc = qc + 1
                    ns = s
                    if nqc == NTS:
                        nqc = 0
                        ns = s + 1
                    if ns < NSLOT:
                        qld[(ns, nqc)] = load_q(ns, nqc)
                    for jk, jring, jex, jq0 in pend:
                        p.wait("tensor", "ac", jex)
                        p.op("tensor", "pe", lambda t, s=s, jk=jk, jring=jring, qp=qp, jq0=jq0, first=(jk == 0), last=(jk == nkt - 1): t.matmul(
                            ps_ctx[qp][:, jq0:512], vS[s][:, 128 * jk: 128 * (jk + 1)], prb[jring][:, jq0:512], start=first, stop=last))
                        cmm = p.op("tensor", "pe", lambda t, jring=jring, qp=qp, jq0=jq0, first=(jk == 0), last=(jk == nkt - 1): t.matmul(
                            ps_sum[qp][:, jq0:512], ones_cb[:], prb[jring][:, jq0:512], start=first, stop=last))
                        cons_cnt[jring] = cmm
                    qtile_free[qp] = last_mm
                    p.wait("vector", "pe")
                    e1 = p.op("vector", "ve", lambda v, qp=qp: v.tensor_copy(srow[qp][:], ps_sum[qp][:]))
                    cr = p.op("vector", "ve", lambda v, qp=qp: v.tensor_copy(craw[qp][:], ps_ctx[qp][:]))
                    p.wait("vector", "ve", e1)
                    e2 = p.op("vector", "ve", lambda v, qp=qp: v.reciprocal(rrow[qp][:], srow[qp][:]))
                    p.wait("tensor", "ve", bc_free)
                    p.wait("tensor", "ve", e2)
                    bm = p.op("tensor", "pe", lambda t, qp=qp: t.matmul(ps_bc[:], ones_rf[:], rrow[qp][:], start=True, stop=True))
                    p.wait("vector", "pe", bm)
                    e3 = p.op("vector", "ve", lambda v, qp=qp: v.tensor_copy(bc_sb[qp][:], ps_bc[:]))
                    bc_free = e3
                    p.wait("vector", "ve", e3)
                    mf = p.op("vector", "ve", lambda v, qc=qc, qp=qp: v.tensor_mul(ctxs[:, 512 * qc:512 * (qc + 1)], craw[qp][:], bc_sb[qp][:]))
                    ctx_free[qp] = cr
                    if serial:
                        p.barrier()
                p.wait("sync", "ve")
                p.dma("sync", ctx_shard[128 * s:128 * (s + 1), :], ctxs[:])
                p.wait("vector", "sd")

            p.wait("gpsimd", "sd")
            p.op("gpsimd", "cc", lambda g: g.collective_compute(
                "AllGather", ALU.bypass, replica_groups=GROUPS, ins=[ctx_shard.opt()], outs=[ctx_all.opt()]))

    if "t_qdram" in tap_aps:
        p.wait("gpsimd", "cc")
        p.dma("gpsimd", tap_aps["t_qdram"][:, :], q_dram)
    if "t_ctx" in tap_aps:
        p.wait("gpsimd", "cc")
        p.dma("gpsimd", tap_aps["t_ctx"][:, :], ctx_all)
    if phases == 2:
        p.barrier()
        finalize(nc, p)
        st.close()
        return nc

    p.wait("sync", "pe")
    p.wait("sync", "ve")
    p.wait("sync", "ac")
    p.wait("sync", "gd")
    with ExitStack() as stA:
        sbA = lambda n, s, d: stA.enter_context(nc.sbuf_tensor(n, s, d))
        bo_row = sbA("bo_row", [1, D], F32)
        ds_row = sbA("ds_row", [1, D], F32)
        ds_bc = sbA("ds_bc", [128, D], F32)
        xn2 = sbA("xn2", [128, KC * TPC], BF16)
        bn6b = [sbA(f"bn6b{i}", [128, 24], F32) for i in range(2)]
        mvb = [sbA(f"mvb{i}", [128, 2], F32) for i in range(2)]
        stdb = [sbA(f"stdb{i}", [128, 1], F32) for i in range(2)]
        rstdb = [sbA(f"rstdb{i}", [128, 1], F32) for i in range(2)]
        nbb = [sbA(f"nbb{i}", [128, 1], F32) for i in range(2)]

        p.dma("sync", bo_row[:], bo_r)
        p.dma("sync", ds_row[:], ds_r)

        with ExitStack() as stB3:
            sbB = lambda n, s, d: stB3.enter_context(nc.sbuf_tensor(n, s, d))
            psB = lambda n, s, d: stB3.enter_context(nc.psum_tensor(n, s, d))
            cx = sbB("cx", [128, KC * TPC], BF16)
            bo_bc = sbB("bo_bc", [128, D], F32)
            xt3 = [sbB(f"xt3{i}", [128, D], F32) for i in range(2)]
            y_tb = [sbB(f"y_tb{i}", [128, D], F32) for i in range(2)]
            yn_tb1 = sbB("yn_tb1", [128, D], F32)
            yn_tb = [yn_tb1, yn_tb1]
            xn2_tm1 = sbB("xn2_tm1", [128, D], BF16)
            xn2_tm = [xn2_tm1, xn2_tm1]
            ps_o = [psB(f"ps_o{i}", [128, TPC], F32) for i in range(2)]
            ps_tr3 = psB("ps_tr3", [128, 512], BF16)
            ps_bb = psB("ps_bb", [128, 512], F32)

            p.wait("gpsimd", "sd")
            p.wait("gpsimd", "cc")

            def ctx_reads(g):
                last = None
                with g.register("tok") as tok:
                    g.reg_load(tok, meta_sb[0:1, 0:1])
                    off = g.snap(tok, min_val=0, max_val=3 * TPC)
                    for ch in range(KC):
                        last = g.dma_start(out=cx[:, TPC * ch:TPC * (ch + 1)],
                                           in_=ctx_all[128 * ch:128 * (ch + 1), ds(off, TPC)])
                        last.then_inc(SEM["gd"], 16)
                return last
            p.emit("gpsimd", ctx_reads)
            p.bump("gd", 16 * KC)

            p.wait("tensor", "sd")
            for h in range(4):
                p.wait("tensor", "ve")
                bm = p.op("tensor", "pe", lambda t, h=h: t.matmul(ps_bb[:], ones_rf[:], bo_row[:, 512 * h:512 * (h + 1)], start=True, stop=True))
                p.wait("vector", "pe", bm)
                p.op("vector", "ve", lambda v, h=h: v.tensor_copy(bo_bc[:, 512 * h:512 * (h + 1)], ps_bb[:]))
                p.wait("tensor", "ve")
                bm = p.op("tensor", "pe", lambda t, h=h: t.matmul(ps_bb[:], ones_rf[:], ds_row[:, 512 * h:512 * (h + 1)], start=True, stop=True))
                p.wait("vector", "pe", bm)
                p.op("vector", "ve", lambda v, h=h: v.tensor_copy(ds_bc[:, 512 * h:512 * (h + 1)], ps_bb[:]))

            p.wait("tensor", "gd")
            tr_ev = 0
            oev = {0: 0, 1: 0}
            for i in range(NT):
                p.wait("sync", "ve")
                xl = p.dma("sync", xt3[i % 2][:], x_in[128 * i:128 * (i + 1), :])
                for oh in range(2):
                    p.wait("tensor", "ve", oev[oh])
                    for c in range(KC):
                        for h in range(2):
                            p.op("tensor", "pe", lambda t, c=c, h=h, i=i, oh=oh: t.matmul(
                                ps_o[oh][:, 512 * h:512 * (h + 1)],
                                cx[:, TPC * c + 128 * i: TPC * c + 128 * (i + 1)],
                                wo_bf[:, D * c + TPC * oh + 512 * h: D * c + TPC * oh + 512 * (h + 1)],
                                start=(c == 0), stop=(c == KC - 1)))
                    p.wait("vector", "pe")
                    p.wait("vector", "ac")
                    oev[oh] = p.op("vector", "ve", lambda v, i=i, oh=oh: v.tensor_add(
                        y_tb[i % 2][:, TPC * oh:TPC * (oh + 1)], ps_o[oh][:], bo_bc[:, TPC * oh:TPC * (oh + 1)]))
                p.wait("vector", "ve")
                n2 = ln_chain(p, y_tb[i % 2], bn6b[i % 2], mvb[i % 2], stdb[i % 2], rstdb[i % 2], nbb[i % 2], eps_c,
                              lambda s, i=i: s.activation(yn_tb[i % 2][:], y_tb[i % 2][:], AF.Identity, bias=nbb[i % 2][:], scale=rstdb[i % 2][:]))
                p.wait("vector", "ac", n2)
                p.wait("vector", "sd", xl)
                x2a = p.op("vector", "ve", lambda v, b=xt3[i % 2], i=i: v.tensor_add(b[:], b[:], yn_tb[i % 2][:]))
                p.wait("sync", "ve", x2a)
                p.dma("sync", x2_dram[128 * i:128 * (i + 1), :], xt3[i % 2][:])
                p.wait("scalar", "ve", tr_ev)
                n3 = ln_chain(p, xt3[i % 2], bn6b[i % 2], mvb[i % 2], stdb[i % 2], rstdb[i % 2], nbb[i % 2], eps_c,
                              lambda s, b=xt3[i % 2], i=i: s.activation(xn2_tm[i % 2][:], b[:], AF.Identity, bias=nbb[i % 2][:], scale=rstdb[i % 2][:]))
                p.wait("tensor", "ac", n3)
                for j4 in range(4):
                    p.wait("tensor", "ve", tr_ev)
                    for j in range(4):
                        c = j4 * 4 + j
                        p.op("tensor", "pe", lambda t, c=c, j=j, i=i: t.transpose(
                            ps_tr3[:, 128 * j:128 * (j + 1)], xn2_tm[i % 2][:, 128 * c:128 * (c + 1)], ident_bf[:]))
                    p.wait("vector", "pe")
                    def evac3(v, i=i, j4=j4):
                        dv = xn2[:, :].rearrange("p (c n) -> p c n", n=TPC)
                        return v.tensor_copy(dv[:, j4 * 4:(j4 + 1) * 4, 128 * i:128 * (i + 1)],
                                             ps_tr3[:].rearrange("p (c n) -> p c n", n=128))
                    tr_ev = p.op("vector", "ve", evac3)
                if serial:
                    p.barrier()

        wo_st.close()
        p.barrier()
        with ExitStack() as stC:
            sbC = lambda n, s, d: stC.enter_context(nc.sbuf_tensor(n, s, d))
            psC = lambda n, s, d: stC.enter_context(nc.psum_tensor(n, s, d))
            acc = sbC("acc", [128, NT * D], F32)
            wg_bf = [sbC(f"wg{i}", [128, KC * 128], BF16) for i in range(2)]
            wu_bf = [sbC(f"wu{i}", [128, KC * 128], BF16) for i in range(2)]
            wd_bf = [sbC(f"wd{i}", [128, D], BF16) for i in range(GRP)]
            hdn = [sbC(f"hdn{i}", [128, TPC], BF16) for i in range(GRP)]
            silu_g = sbC("silu_g", [128, TPC], BF16)
            gs_sb = sbC("gs_sb", [128, IC], F32)
            us_sb = sbC("us_sb", [128, IC], F32)
            xt41 = sbC("xt41", [128, D], F32)
            xt4 = [xt41, xt41]
            y41 = sbC("y41", [128, D], F32)
            y4 = [y41, y41]
            yn41 = sbC("yn41", [128, D], F32)
            yn4 = [yn41, yn41]
            ps_g = psC("ps_g", [128, TPC], F32)
            ps_u = psC("ps_u", [128, TPC], F32)
            ps_d = [psC(f"ps_d{i}", [128, TPC], F32) for i in range(2)]

            p.dma("sync", gs_sb[:], gs)
            p.dma("sync", us_sb[:], us)
            p.op("vector", "ve", lambda v: v.memset(acc[:], 0.0))
            p.wait("tensor", "ve")
            p.wait("tensor", "sd")

            groups = [list(range(g, min(g + GRP, IC))) for g in range(0, IC, GRP)]
            dn_done = 0
            hv_done = {0: 0, 1: 0}
            for grp in groups:
                for ii, icx in enumerate(grp):
                    p.wait("gpsimd", "ve", hv_done[ii % 2])
                    p.wait("gpsimd", "pe", dn_done)
                    p.dma("gpsimd", wg_bf[ii % 2][:].rearrange("p (c n) -> p c n", c=KC),
                          gwt[:, 128 * icx:128 * (icx + 1)].rearrange("(c p) n -> p c n", p=128))
                    p.dma("gpsimd", wu_bf[ii % 2][:].rearrange("p (c n) -> p c n", c=KC),
                          upt[:, 128 * icx:128 * (icx + 1)].rearrange("(c p) n -> p c n", p=128))
                    wld = p.dma("gpsimd", wd_bf[ii][:], dwt[128 * icx:128 * (icx + 1), :])
                    p.wait("tensor", "gd", wld)
                    p.wait("tensor", "ve", hv_done[ii % 2])
                    for k in range(KC):
                        for th in range(2):
                            p.op("tensor", "pe", lambda t, k=k, th=th, ii=ii: t.matmul(
                                ps_g[:, 512 * th:512 * (th + 1)], wg_bf[ii % 2][:, 128 * k:128 * (k + 1)],
                                xn2[:, TPC * k + 512 * th: TPC * k + 512 * (th + 1)],
                                start=(k == 0), stop=(k == KC - 1)))
                    for k in range(KC):
                        for th in range(2):
                            p.op("tensor", "pe", lambda t, k=k, th=th, ii=ii: t.matmul(
                                ps_u[:, 512 * th:512 * (th + 1)], wu_bf[ii % 2][:, 128 * k:128 * (k + 1)],
                                xn2[:, TPC * k + 512 * th: TPC * k + 512 * (th + 1)],
                                start=(k == 0), stop=(k == KC - 1)))
                    p.wait("scalar", "pe")
                    p.wait("scalar", "ve", hv_done[ii % 2])
                    sl = p.op("scalar", "ac", lambda s, icx=icx: s.activation(silu_g[:], ps_g[:], AF.Silu, scale=gs_sb[:, icx:icx + 1]))
                    p.wait("vector", "ac", sl)
                    p.wait("vector", "pe")
                    hv = p.op("vector", "ve", lambda v, ii=ii, icx=icx: v.scalar_tensor_tensor(
                        hdn[ii][:], ps_u[:], us_sb[:, icx:icx + 1], silu_g[:], ALU.mult, ALU.mult))
                    hv_done[ii % 2] = hv
                    p.wait("tensor", "ve", hv)
                dev = {0: 0, 1: 0}
                for tt in range(NT):
                    for oh in range(2):
                        dp = oh
                        p.wait("tensor", "ve", dev[dp])
                        for ii in range(len(grp)):
                            for th in range(2):
                                p.op("tensor", "pe", lambda t, ii=ii, tt=tt, th=th, oh=oh, dp=dp, first=(ii == 0), last=(ii == len(grp) - 1): t.matmul(
                                    ps_d[dp][:, 512 * th:512 * (th + 1)], hdn[ii][:, 128 * tt:128 * (tt + 1)],
                                    wd_bf[ii][:, TPC * oh + 512 * th: TPC * oh + 512 * (th + 1)],
                                    start=first, stop=last))
                        p.wait("vector", "pe")
                        dev[dp] = p.op("vector", "ve", lambda v, tt=tt, oh=oh, dp=dp: v.tensor_add(
                            acc[:, D * tt + TPC * oh:D * tt + TPC * (oh + 1)], acc[:, D * tt + TPC * oh:D * tt + TPC * (oh + 1)], ps_d[dp][:]))
                dn_done = p.val("pe")
                if serial:
                    p.barrier()

            if "t_xn2" in tap_aps:
                p.barrier()
                p.dma("gpsimd", tap_aps["t_xn2"][:, :], xn2[:])
            if "t_acc" in tap_aps:
                p.barrier()
                p.dma("sync", tap_aps["t_acc"][:, :], acc[:])
            if "t_x2" in tap_aps:
                p.barrier()
                p.dma("sync", tap_aps["t_x2"][:, :], x2_dram)
            for i in range(NT):
                p.wait("sync", "ve")
                xl = p.dma("sync", xt4[i % 2][:], x2_dram[128 * i:128 * (i + 1), :])
                p.wait("vector", "ac")
                ym = p.op("vector", "ve", lambda v, i=i: v.tensor_mul(y4[i % 2][:], acc[:, D * i:D * (i + 1)], ds_bc[:]))
                p.wait("vector", "ve", ym)
                n4 = ln_chain(p, y4[i % 2], bn6b[i % 2], mvb[i % 2], stdb[i % 2], rstdb[i % 2], nbb[i % 2], eps_c,
                              lambda s, i=i: s.activation(yn4[i % 2][:], y4[i % 2][:], AF.Identity, bias=nbb[i % 2][:], scale=rstdb[i % 2][:]))
                p.wait("vector", "ac", n4)
                p.wait("vector", "sd", xl)
                fo = p.op("vector", "ve", lambda v, i=i: v.tensor_add(xt4[i % 2][:], xt4[i % 2][:], yn4[i % 2][:]))
                p.wait("sync", "ve", fo)
                p.dma("sync", out[128 * i:128 * (i + 1), :], xt4[i % 2][:])
                if serial:
                    p.barrier()

    finalize(nc, p)
    st.close()
    return nc


def finalize(nc, p):
    p.barrier()
    with nc.Block() as block:
        @block.sync
        def _(e):
            for f in p.q["sync"]:
                f(e)

        @block.gpsimd
        def _(e):
            for f in p.q["gpsimd"]:
                f(e)

        @block.scalar
        def _(e):
            for f in p.q["scalar"]:
                f(e)

        @block.vector
        def _(e):
            for f in p.q["vector"]:
                f(e)

        @block.tensor
        def _(e):
            for f in p.q["tensor"]:
                f(e)


def make_consts():
    ident = np.eye(128, dtype=np.float32)
    rotm = np.zeros((128, 128), np.float32)
    for j in range(64):
        rotm[2 * j + 1, 2 * j] = -1.0
        rotm[2 * j, 2 * j + 1] = 1.0
    dd = np.arange(128) // 2
    inv = 1.0 / ROPE_BASE ** (dd.astype(np.float64) * 2.0 / 128.0)
    tpos = np.arange(SEQ, dtype=np.float64)
    ang = inv[:, None] * tpos[None, :]
    cos_t = np.cos(ang).astype(np.float32)
    sin_t = np.sin(ang).astype(np.float32)
    kk = np.arange(128)[:, None]
    qq = np.arange(512)[None, :]
    mask = np.concatenate([np.where(kt * 128 + kk > qq, NEG, 0.0) for kt in range(4)], axis=1).astype(np.float32)
    return dict(identc=ident, rotmc=rotm, cos_t=cos_t, sin_t=sin_t, mask_t=mask,
                ones_c=np.ones((128, 1), np.float32), ones_r=np.ones((1, 128), np.float32),
                eps_cd=np.full((128, 1), 1e-5, np.float32))


def cols(v, n):
    return np.ascontiguousarray(np.asarray(v, np.float32).reshape(n, 128).T)


def make_inmaps(inp):
    x = np.asarray(inp["L0_attn_h"], np.float32)
    consts = make_consts()
    g = lambda k: np.ascontiguousarray(np.asarray(inp[k], np.float32))
    wdkv_f = g("L0_attn_W_DKV"); bdkv_f = cols(inp["L0_attn_b_DKV"], NKV)
    wdq_f = g("L0_attn_W_DQ"); bdq_f = cols(inp["L0_attn_b_DQ"], NQ)
    wkr_f = g("L0_attn_W_KR"); bkr_f = np.asarray(inp["L0_attn_b_KR"], np.float32)
    wuk_f = g("L0_attn_W_UK"); buk_f = np.asarray(inp["L0_attn_b_UK"], np.float32)
    wuv_f = g("L0_attn_W_UV"); buv_f = np.asarray(inp["L0_attn_b_UV"], np.float32)
    wuq_f = g("L0_attn_W_UQ"); buq_f = np.asarray(inp["L0_attn_b_UQ"], np.float32)
    wo_f = g("L0_attn_W_O"); bo_f = np.asarray(inp["L0_attn_b_O"], np.float32).reshape(1, D)
    gwt_f = np.ascontiguousarray(g("L0_ffn_gate_w").T)
    upt_f = np.ascontiguousarray(g("L0_ffn_up_w").T)
    dwt_f = np.ascontiguousarray(g("L0_ffn_down_w").T)
    gs_f = cols(inp["L0_ffn_gate_s"], IC)
    us_f = cols(inp["L0_ffn_up_s"], IC)
    ds_f = np.asarray(inp["L0_ffn_down_s"], np.float32).reshape(1, D)

    maps = []
    for c in range(8):
        b, m = divmod(c, 4)
        sl = slice(512 * m, 512 * (m + 1))
        maps.append(dict(
            x=np.ascontiguousarray(x[b, TPC * m:TPC * (m + 1), :]),
            meta=np.array([[TPC * m]], np.int32),
            wdkv=wdkv_f, bdkv=bdkv_f, wdq=wdq_f, bdq=bdq_f,
            wkr=np.ascontiguousarray(wkr_f[:, sl]), bkr=cols(bkr_f[sl], NSLOT),
            wuk=np.ascontiguousarray(wuk_f[:, sl]), buk=cols(buk_f[sl], NSLOT),
            wuv=np.ascontiguousarray(wuv_f[:, sl]), buv=cols(buv_f[sl], NSLOT),
            wuq=np.ascontiguousarray(wuq_f[:, sl]), buq=cols(buq_f[sl], NSLOT),
            wo=wo_f, bo_r=bo_f, gwt=gwt_f, upt=upt_f, dwt=dwt_f,
            gs=gs_f, us=us_f, ds_r=ds_f,
            **consts,
        ))
    return maps


def kernel(**inp):
    nc = build_nc(phases=3)
    maps = make_inmaps(inp)
    res = run_bass_kernel_spmd(nc, maps, core_ids=list(range(8)))
    outv = np.empty((2, SEQ, D), np.float32)
    for c in range(8):
        b, m = divmod(c, 4)
        outv[b, TPC * m:TPC * (m + 1), :] = res.results[c]["out"]
    return outv


# version 06
# speedup vs baseline: 1.4469x; 1.0030x over previous
import numpy as np
from contextlib import ExitStack
import concourse.bass as bass
import concourse.mybir as mybir
from concourse.bass import ds
from concourse.bass_utils import run_bass_kernel_spmd

F32 = mybir.dt.float32
BF16 = mybir.dt.bfloat16
I32 = mybir.dt.int32
AX = mybir.AxisListType.X
ALU = mybir.AluOpType
AF = mybir.ActivationFunctionType

D = 2048
SEQ = 4096
TPC = 1024
NT = TPC // 128
KC = D // 128
DKV = 512
NKV = DKV // 128
DQ = 1536
NQ = DQ // 128
NSLOT = 4
NTS = SEQ // 512
INTER = 5632
IC = INTER // 128
GRP = 6
ROPE_BASE = 10000.0
GROUPS = [[0, 1, 2, 3], [4, 5, 6, 7]]
NEG = -30000.0


class Plan:
    ENGINES = ("sync", "gpsimd", "scalar", "vector", "tensor")

    def __init__(self):
        self.q = {e: [] for e in self.ENGINES}
        self.cnt = {}

    def val(self, sem):
        return self.cnt.get(sem, 0)

    def bump(self, sem, n):
        self.cnt[sem] = self.val(sem) + n
        return self.cnt[sem]

    def emit(self, eng, f):
        self.q[eng].append(f)

    def wait(self, eng, sem, v=None):
        v = self.val(sem) if v is None else v
        if v <= 0:
            return
        self.emit(eng, lambda e, s=sem, vv=v: e.wait_ge(SEM[s], vv))

    def dma(self, eng, out, in_, sem=None, **kw):
        sem = sem or ("gd" if eng == "gpsimd" else ("ad" if eng == "scalar" else "sd"))
        t = self.bump(sem, 16)
        self.emit(eng, lambda e, o=out, i=in_, s=sem, k=kw: e.dma_start(out=o, in_=i, **k).then_inc(SEM[s], 16))
        return t

    def op(self, eng, sem, build):
        t = self.bump(sem, 1)
        self.emit(eng, lambda e, b=build, s=sem: b(e).then_inc(SEM[s], 1))
        return t

    def barrier(self, engines=None):
        snap = {s: v for s, v in self.cnt.items() if v > 0}
        for eng in (engines or self.ENGINES):
            for s, v in snap.items():
                self.wait(eng, s, v)


SEM = {}


def ln_chain(p, src, bn6, mv, stdc, rstd, nb, eps_c, out_fn):
    for j in range(4):
        p.op("vector", "ve", lambda v, j=j: v.bn_stats(bn6[:, 6 * j:6 * j + 6], src[:, 512 * j:512 * (j + 1)]))
    p.wait("vector", "ve")
    p.wait("vector", "ac")
    p.op("vector", "ve", lambda v: v.bn_aggr(mv[:], bn6[:]))
    p.wait("scalar", "ve")
    sq = p.op("scalar", "ac", lambda s: s.activation(stdc[:], mv[:, 1:2], AF.Sqrt, bias=eps_c[:], scale=1.0))
    p.wait("vector", "ac", sq)
    p.op("vector", "ve", lambda v: v.reciprocal(rstd[:], stdc[:]))
    p.wait("vector", "ve")
    p.op("vector", "ve", lambda v: v.scalar_tensor_tensor(nb[:], mv[:, 0:1], -1.0, rstd[:], ALU.mult, ALU.mult))
    p.wait("scalar", "ve")
    return p.op("scalar", "ac", out_fn)


def build_nc(phases=3, taps=(), serial=False):
    nc = bass.Bass("TRN2", target_bir_lowering=False)
    p = Plan()

    def din(name, shape, dt=F32):
        return nc.dram_tensor(name, shape, dt, kind="ExternalInput").ap()

    x_in = din("x", [TPC, D])
    meta = din("meta", [1, 1], I32)
    wdkv = din("wdkv", [D, DKV])
    bdkv = din("bdkv", [128, NKV])
    wdq = din("wdq", [D, DQ])
    bdq = din("bdq", [128, NQ])
    wkr = din("wkr", [D, 512])
    bkr = din("bkr", [128, NSLOT])
    wuk = din("wuk", [DKV, 512])
    buk = din("buk", [128, NSLOT])
    wuv = din("wuv", [DKV, 512])
    buv = din("buv", [128, NSLOT])
    wuq = din("wuq", [DQ, 512])
    buq = din("buq", [128, NSLOT])
    wo = din("wo", [D, D])
    bo_r = din("bo_r", [1, D])
    gwt = din("gwt", [D, INTER])
    upt = din("upt", [D, INTER])
    dwt = din("dwt", [INTER, D])
    gs = din("gs", [128, IC])
    us = din("us", [128, IC])
    ds_r = din("ds_r", [1, D])
    identc = din("identc", [128, 128])
    rotmc = din("rotmc", [128, 128])
    cos_td = din("cos_t", [128, SEQ])
    sin_td = din("sin_t", [128, SEQ])
    mask_t = din("mask_t", [128, 4 * 512])
    ones_cd = din("ones_c", [128, 1])
    ones_rd = din("ones_r", [1, 128])
    eps_cd = din("eps_cd", [128, 1])

    out = nc.dram_tensor("out", [TPC, D], F32, kind="ExternalOutput").ap()

    shard_x = nc.dram_tensor("shard_x", [2048, TPC], BF16, kind="Internal").ap()
    shard_l = nc.dram_tensor("shard_l", [2048, TPC], BF16, kind="Internal").ap()
    l_all_x = nc.dram_tensor("l_all_x", [4 * 2048, TPC], BF16, kind="Internal").ap()
    l_all_l = nc.dram_tensor("l_all_l", [4 * 2048, TPC], BF16, kind="Internal").ap()
    q_dram = nc.dram_tensor("q_dram", [NSLOT * 256, SEQ], BF16, kind="Internal").ap()
    ctx_shard = nc.dram_tensor("ctx_shard", [512, SEQ], BF16, kind="Internal").ap()
    ctx_all = nc.dram_tensor("ctx_all", [2048, SEQ], BF16, kind="Internal").ap()
    x2_dram = nc.dram_tensor("x2_dram", [TPC, D], F32, kind="Internal").ap()

    tap_aps = {}
    if "shard" in taps:
        tap_aps["t_shard"] = nc.dram_tensor("t_shard", [4 * 4096, TPC], F32, kind="ExternalOutput").ap()
    if "qdram" in taps:
        tap_aps["t_qdram"] = nc.dram_tensor("t_qdram", [NSLOT * 256, SEQ], F32, kind="ExternalOutput").ap()
    if "ctx" in taps:
        tap_aps["t_ctx"] = nc.dram_tensor("t_ctx", [2048, SEQ], F32, kind="ExternalOutput").ap()
    if "x2" in taps:
        tap_aps["t_x2"] = nc.dram_tensor("t_x2", [TPC, D], F32, kind="ExternalOutput").ap()
    if "xn2" in taps:
        tap_aps["t_xn2"] = nc.dram_tensor("t_xn2", [128, KC * TPC], F32, kind="ExternalOutput").ap()
    if "acc" in taps:
        tap_aps["t_acc"] = nc.dram_tensor("t_acc", [128, NT * D], F32, kind="ExternalOutput").ap()

    st = ExitStack()
    sbuf = lambda n, s, d: st.enter_context(nc.sbuf_tensor(n, s, d))

    ident_bf = sbuf("ident_bf", [128, 128], BF16)
    rotm_bf = sbuf("rotm_bf", [128, 128], BF16)
    ones_cb = sbuf("ones_cb", [128, 1], BF16)
    ones_rf = sbuf("ones_rf", [1, 128], F32)
    eps_c = sbuf("eps_c", [128, 1], F32)
    meta_sb = st.enter_context(nc.sbuf_tensor("meta_sb", [1, 1], I32))
    k2p = sbuf("k2p", [1, NSLOT], F32)
    q2mp = sbuf("q2mp", [1, NSLOT * NTS], F32)

    for s in ("sd", "gd", "ad", "cc", "pe", "ve", "ac",
              "xd0", "xd1", "st0", "st1", "st2", "st3",
              "lx0", "lx1", "lq0", "lq1", "lk0", "lk1",
              "qt0", "qt1", "qs0", "qs1", "w2d", "wod",
              "wf0", "wf1", "shx", "cxs", "x2d"):
        SEM[s] = st.enter_context(nc.semaphore(s))

    p.dma("gpsimd", ident_bf[:], identc)
    p.dma("gpsimd", rotm_bf[:], rotmc)
    p.dma("gpsimd", ones_cb[:], ones_cd)
    p.dma("sync", ones_rf[:], ones_rd)
    p.dma("sync", eps_c[:], eps_cd)
    p.dma("sync", meta_sb[:], meta)

    with ExitStack() as st1:
        sb1 = lambda n, s, d: st1.enter_context(nc.sbuf_tensor(n, s, d))
        ps1s = lambda n, s, d: st1.enter_context(nc.psum_tensor(n, s, d))
        xt = [sb1(f"xt{i}", [128, D], F32) for i in range(2)]
        bn6 = [sb1(f"bn6_{i}", [128, 24], F32) for i in range(2)]
        mv = [sb1(f"mv_{i}", [128, 2], F32) for i in range(2)]
        stdc = [sb1(f"stdc_{i}", [128, 1], F32) for i in range(2)]
        rstd = [sb1(f"rstd_{i}", [128, 1], F32) for i in range(2)]
        nb = [sb1(f"nb_{i}", [128, 1], F32) for i in range(2)]
        xn_tm = [sb1(f"xntm{i}", [128, D], BF16) for i in range(2)]
        xn1 = sb1("xn1", [128, KC * TPC], BF16)
        wdkv_bf = sb1("wdkv_bf", [128, KC * DKV], BF16)
        wdq_bf = sb1("wdq_bf", [128, KC * DQ], BF16)
        bdkv_sb = sb1("bdkv_sb", [128, NKV], F32)
        bdq_sb = sb1("bdq_sb", [128, NQ], F32)
        stage = [sb1(f"stg{i}", [128, 512], BF16) for i in range(4)]
        ps_tr = ps1s("ps_tr", [128, 512], BF16)
        ps_p = [ps1s(f"ps_p{i}", [128, 512], F32) for i in range(2)]

        p.dma("gpsimd", wdkv_bf[:].rearrange("p (c n) -> p c n", c=KC), wdkv.rearrange("(c p) n -> p c n", p=128))
        p.dma("gpsimd", wdq_bf[:].rearrange("p (c n) -> p c n", c=KC), wdq.rearrange("(c p) n -> p c n", p=128))
        p.dma("sync", bdkv_sb[:], bdkv)
        p.dma("sync", bdq_sb[:], bdq)

        tr_evacs = 0
        xt_nrm = {0: 0, 1: 0}
        tr_done = {0: 0, 1: 0}
        for i in range(NT):
            par = i % 2
            p.wait("sync", "ac", xt_nrm[par])
            x_ld = p.dma("sync", xt[par][:], x_in[128 * i:128 * (i + 1), :], sem="xd%d" % par)
            p.wait("vector", "xd%d" % par, x_ld)
            p.wait("scalar", "ve", tr_done[par])
            nrm = ln_chain(p, xt[par], bn6[par], mv[par], stdc[par], rstd[par], nb[par], eps_c,
                           lambda s, b=xt[par], o=xn_tm[par], par=par: s.activation(o[:], b[:], AF.Identity, bias=nb[par][:], scale=rstd[par][:]))
            xt_nrm[par] = nrm
            p.wait("tensor", "ac", nrm)
            for j4 in range(4):
                p.wait("tensor", "ve", tr_evacs)
                for j in range(4):
                    c = j4 * 4 + j
                    p.op("tensor", "pe", lambda t, c=c, j=j, b=xn_tm[par]: t.transpose(
                        ps_tr[:, 128 * j:128 * (j + 1)], b[:, 128 * c:128 * (c + 1)], ident_bf[:]))
                p.wait("vector", "pe")
                def evac(v, i=i, j4=j4):
                    dv = xn1[:, :].rearrange("p (c n) -> p c n", n=TPC)
                    return v.tensor_copy(dv[:, j4 * 4:(j4 + 1) * 4, 128 * i:128 * (i + 1)],
                                         ps_tr[:].rearrange("p (c n) -> p c n", n=128))
                tr_evacs = p.op("vector", "ve", evac)
            tr_done[par] = tr_evacs
            if serial:
                p.barrier()

        p.wait("sync", "ve")
        for c in range(KC):
            p.dma("sync", shard_x[128 * c:128 * (c + 1), :], xn1[:, TPC * c:TPC * (c + 1)], sem="shx")
        p.wait("gpsimd", "shx")
        p.op("gpsimd", "cc", lambda g: g.collective_compute(
            "AllGather", ALU.bypass, replica_groups=GROUPS, ins=[shard_x.opt()], outs=[l_all_x.opt()]))

        p.wait("tensor", "gd")
        p.wait("tensor", "ve", tr_evacs)
        evac_t = {0: 0, 1: 0}
        stage_dma = {0: 0, 1: 0, 2: 0, 3: 0}
        for oc in range(KC):
            is_kv = oc < NKV
            for th in range(2):
                par = (2 * oc + th) % 2
                sp4 = (2 * oc + th) % 4
                pp = ps_p[par]
                p.wait("tensor", "ac", evac_t[par])
                for k in range(KC):
                    if is_kv:
                        lhs = wdkv_bf[:, DKV * k + 128 * oc: DKV * k + 128 * (oc + 1)]
                    else:
                        lhs = wdq_bf[:, DQ * k + 128 * (oc - NKV): DQ * k + 128 * (oc - NKV + 1)]
                    p.op("tensor", "pe", lambda t, l=lhs, k=k, pp=pp, th=th: t.matmul(
                        pp[:], l, xn1[:, TPC * k + 512 * th: TPC * k + 512 * (th + 1)],
                        start=(k == 0), stop=(k == KC - 1)))
                p.wait("scalar", "pe")
                p.wait("scalar", "st%d" % sp4, stage_dma[sp4])
                bcol = bdkv_sb[:, oc:oc + 1] if is_kv else bdq_sb[:, oc - NKV:oc - NKV + 1]
                sg = stage[sp4]
                evac_t[par] = p.op("scalar", "ac", lambda s, pp=pp, b=bcol, sg=sg: s.activation(sg[:], pp[:], AF.Identity, bias=b, scale=1.0))
                p.wait("sync", "ac", evac_t[par])
                row = 128 * oc if is_kv else 512 + 128 * (oc - NKV)
                stage_dma[sp4] = p.dma("sync", shard_l[row:row + 128, 512 * th:512 * (th + 1)], sg[:], sem="st%d" % sp4)
            if serial:
                p.barrier()

        for j in range(4):
            p.wait("gpsimd", "st%d" % j)
        p.op("gpsimd", "cc", lambda g: g.collective_compute(
            "AllGather", ALU.bypass, replica_groups=GROUPS, ins=[shard_l.opt()], outs=[l_all_l.opt()]))

    if "t_shard" in tap_aps:
        p.wait("gpsimd", "cc")
        for r in range(4):
            p.dma("gpsimd", tap_aps["t_shard"][4096 * r:4096 * r + 512, :], l_all_l[2048 * r:2048 * r + 512, :])
            p.dma("gpsimd", tap_aps["t_shard"][4096 * r + 512:4096 * r + 2048, :], l_all_l[2048 * r + 512:2048 * (r + 1), :])
            p.dma("gpsimd", tap_aps["t_shard"][4096 * r + 2048:4096 * (r + 1), :], l_all_x[2048 * r:2048 * (r + 1), :])
    if phases == 1:
        p.barrier()
        finalize(nc, p)
        st.close()
        return nc

    p.barrier()
    with ExitStack() as stS:
        sbS = lambda n, s, d: stS.enter_context(nc.sbuf_tensor(n, s, d))
        kC = [sbS(f"kC{s}", [128, SEQ], BF16) for s in range(NSLOT)]
        kR = [sbS(f"kR{s}", [128, SEQ], BF16) for s in range(NSLOT)]
        vS = [sbS(f"vS{s}", [128, SEQ], BF16) for s in range(NSLOT)]

        with ExitStack() as stB:
            sb2 = lambda n, s, d: stB.enter_context(nc.sbuf_tensor(n, s, d))
            ps2s = lambda n, s, d: stB.enter_context(nc.psum_tensor(n, s, d))
            lat_x = [sb2(f"latx{c}", [128, 512], BF16) for c in range(KC)]
            lat_q = [sb2(f"latq{c}", [128, 512], BF16) for c in range(NQ)]
            lat_kv = [sb2(f"latkv{c}", [128, 512], BF16) for c in range(NKV)]
            cos_s = sb2("cos_s", [128, 512], F32)
            sin_s = sb2("sin_s", [128, 512], F32)
            wkr_bf = sb2("wkr_bf", [128, KC * 512], BF16)
            wuk_bf = sb2("wuk_bf", [128, NKV * 512], BF16)
            wuv_bf = sb2("wuv_bf", [128, NKV * 512], BF16)
            wuq_bf = sb2("wuq_bf", [128, NQ * 512], BF16)
            bkr_sb = sb2("bkr_sb", [128, NSLOT], F32)
            buk_sb = sb2("buk_sb", [128, NSLOT], F32)
            buv_sb = sb2("buv_sb", [128, NSLOT], F32)
            buq_sb = sb2("buq_sb", [128, NSLOT], F32)
            krbf_b = [sb2(f"krbf{i}", [128, 512], BF16) for i in range(2)]
            vfm = sb2("vfm", [128, 512], BF16)
            qbf_b = [sb2(f"qbf{i}", [128, 512], BF16) for i in range(2)]
            t1_b = [sb2(f"t1_{i}", [128, 512], F32) for i in range(2)]
            t2_b = [sb2(f"t2_{i}", [128, 512], F32) for i in range(2)]
            ksq = sb2("ksq", [128, 512], BF16)
            qsq2 = sb2("qsq2", [128, 512], BF16)
            k2 = sb2("k2", [1, NSLOT], F32)
            k2t = sb2("k2t", [1, 1], F32)
            q2m = sb2("q2m", [1, NSLOT * NTS], F32)
            ps_a = [ps2s(f"ps_a{s}", [128, 512], F32) for s in range(NSLOT)]
            ps_rot_b = [ps2s(f"ps_rot{i}", [128, 512], F32) for i in range(2)]
            ps_vt = ps2s("ps_vt", [128, 512], BF16)
            ps_n = ps2s("ps_n", [1, 512], F32)

            p.dma("gpsimd", wkr_bf[:].rearrange("p (c n) -> p c n", c=KC), wkr.rearrange("(c p) n -> p c n", p=128), sem="w2d")
            p.dma("gpsimd", wuk_bf[:].rearrange("p (c n) -> p c n", c=NKV), wuk.rearrange("(c p) n -> p c n", p=128), sem="w2d")
            p.dma("gpsimd", wuv_bf[:].rearrange("p (c n) -> p c n", c=NKV), wuv.rearrange("(c p) n -> p c n", p=128), sem="w2d")
            p.dma("gpsimd", wuq_bf[:].rearrange("p (c n) -> p c n", c=NQ), wuq.rearrange("(c p) n -> p c n", p=128), sem="w2d")
            p.dma("sync", bkr_sb[:], bkr)
            p.dma("sync", buk_sb[:], buk)
            p.dma("sync", buv_sb[:], buv)
            p.dma("sync", buq_sb[:], buq)
            p.wait("tensor", "w2d")

            kr_done = q_done = kv_done = 0
            q_cv = {0: 0, 1: 0}
            q_dma = {0: 0, 1: 0}
            for ts in range(NTS):
                ts_ve0 = p.val("ve")
                ts_sd0 = p.val("sd")
                r = ts // 2
                cs = 512 * (ts % 2)
                rb = 4096 * r
                rbx = 2048 * r
                if ts == 0:
                    p.wait("sync", "cc", 1)
                lxs = "lx%d" % (ts % 2)
                lqs = "lq%d" % (ts % 2)
                lks = "lk%d" % (ts % 2)
                p.wait("sync", "pe", kr_done)
                for c in range(KC):
                    p.dma("sync", lat_x[c][:], l_all_x[rbx + 128 * c: rbx + 128 * (c + 1), cs:cs + 512], sem=lxs)
                p.wait("sync", "ve")
                ld = p.dma("sync", cos_s[:], cos_td[:, 512 * ts:512 * (ts + 1)], sem=lxs)
                ldx = p.dma("sync", sin_s[:], sin_td[:, 512 * ts:512 * (ts + 1)], sem=lxs)
                if ts == 0:
                    p.wait("sync", "cc", 2)
                p.wait("sync", "pe", q_done)
                for c in range(NQ):
                    ldq = p.dma("sync", lat_q[c][:], l_all_l[rbx + 512 + 128 * c: rbx + 512 + 128 * (c + 1), cs:cs + 512], sem=lqs)
                p.wait("sync", "pe", kv_done)
                for c in range(NKV):
                    ldkv = p.dma("sync", lat_kv[c][:], l_all_l[rbx + 128 * c: rbx + 128 * (c + 1), cs:cs + 512], sem=lks)
                p.wait("tensor", lxs, ldx)

                for s in range(NSLOT):
                    for k in range(KC):
                        p.op("tensor", "pe", lambda t, s=s, k=k: t.matmul(
                            ps_a[s][:], wkr_bf[:, 512 * k + 128 * s: 512 * k + 128 * (s + 1)], lat_x[k][:],
                            start=(k == 0), stop=(k == KC - 1)))
                kr_done = p.val("pe")
                p.wait("vector", lxs, ldx)
                kr_fin = {0: 0, 1: 0}
                kr_ev = 0
                for s in range(NSLOT):
                    par = s % 2
                    krbf = krbf_b[par]; t1 = t1_b[par]; t2 = t2_b[par]; ps_rot = ps_rot_b[par]
                    p.wait("scalar", "pe", kr_done - (NSLOT - 1 - s) * KC)
                    p.wait("scalar", "ve", ts_ve0)
                    p.wait("scalar", "qs%d" % par, q_dma[par])
                    p.wait("scalar", "ve", kr_fin[par])
                    ev = p.op("scalar", "ac", lambda sc, s=s, krbf=krbf: sc.activation(krbf[:], ps_a[s][:], AF.Identity, bias=bkr_sb[:, s:s + 1], scale=1.0))
                    kr_ev = ev
                    p.wait("tensor", "ac", ev)
                    p.wait("tensor", "ve", kr_fin[par])
                    rt = p.op("tensor", "pe", lambda t, krbf=krbf, ps_rot=ps_rot: t.matmul(ps_rot[:], rotm_bf[:], krbf[:], start=True, stop=True))
                    p.wait("vector", "ac", ev)
                    m1 = p.op("vector", "ve", lambda v, krbf=krbf, t1=t1: v.tensor_mul(t1[:], krbf[:], cos_s[:]))
                    p.wait("vector", "pe", rt)
                    m2 = p.op("vector", "ve", lambda v, t2=t2, ps_rot=ps_rot: v.tensor_mul(t2[:], ps_rot[:], sin_s[:]))
                    p.wait("vector", "ve", m2)
                    kr_fin[par] = p.op("vector", "ve", lambda v, s=s, ts=ts, t1=t1, t2=t2: v.tensor_add(kR[s][:, 512 * ts:512 * (ts + 1)], t1[:], t2[:]))
                p.wait("tensor", "ac", kr_ev)
                krfin_final = dict(kr_fin)

                p.wait("tensor", lqs, ldq)
                for s in range(NSLOT):
                    for k in range(NQ):
                        p.op("tensor", "pe", lambda t, s=s, k=k: t.matmul(
                            ps_a[s][:], wuq_bf[:, 512 * k + 128 * s: 512 * k + 128 * (s + 1)], lat_q[k][:],
                            start=(k == 0), stop=(k == NQ - 1)))
                q_done = p.val("pe")
                nrd_last = 0
                for s in range(NSLOT):
                    par = s % 2
                    qbf = qbf_b[par]; krbf = krbf_b[par]; t1 = t1_b[par]; t2 = t2_b[par]; ps_rot = ps_rot_b[par]
                    p.wait("scalar", "pe", q_done - (NSLOT - 1 - s) * NQ)
                    p.wait("scalar", "ve", ts_ve0)
                    p.wait("scalar", "qs%d" % par, q_dma[par])
                    p.wait("scalar", "ve", q_cv[par])
                    ev = p.op("scalar", "ac", lambda sc, s=s, qbf=qbf: sc.activation(qbf[:], ps_a[s][:], AF.Identity, bias=buq_sb[:, s:s + 1], scale=1.0))
                    q_last_ev = ev
                    p.wait("sync", "ac", ev)
                    p.dma("sync", q_dram[256 * s:256 * s + 128, 512 * ts:512 * (ts + 1)], qbf[:], sem="qs%d" % par)
                    p.wait("tensor", "ac", ev)
                    p.wait("tensor", "ve", q_cv[par])
                    p.wait("tensor", "ve", krfin_final[par])
                    rt = p.op("tensor", "pe", lambda t, qbf=qbf, ps_rot=ps_rot: t.matmul(ps_rot[:], rotm_bf[:], qbf[:], start=True, stop=True))
                    p.wait("vector", "ac", ev)
                    m1 = p.op("vector", "ve", lambda v, qbf=qbf, t1=t1: v.tensor_mul(t1[:], qbf[:], cos_s[:]))
                    p.wait("vector", "pe", rt)
                    m2 = p.op("vector", "ve", lambda v, t2=t2, ps_rot=ps_rot: v.tensor_mul(t2[:], ps_rot[:], sin_s[:]))
                    p.wait("vector", "ve", m2)
                    ad = p.op("vector", "ve", lambda v, t1=t1, t2=t2: v.tensor_add(t1[:], t1[:], t2[:]))
                    p.wait("vector", "ve", ad)
                    cv = p.op("vector", "ve", lambda v, krbf=krbf, t1=t1: v.tensor_copy(krbf[:], t1[:]))
                    q_cv[par] = cv
                    p.wait("sync", "ve", cv)
                    q_dma[par] = p.dma("sync", q_dram[256 * s + 128:256 * s + 256, 512 * ts:512 * (ts + 1)], krbf[:], sem="qs%d" % par)
                    p.wait("scalar", "ve", cv)
                    p.wait("scalar", "pe")
                    n1 = p.op("scalar", "ac", lambda sc, qbf=qbf: sc.activation(qsq2[:], qbf[:], AF.Square))
                    p.wait("tensor", "ac", n1)
                    p.wait("tensor", "ve", nrd_last)
                    p.op("tensor", "pe", lambda t: t.matmul(ps_n[:], ones_cb[:], qsq2[:], start=True, stop=False))
                    p.wait("scalar", "pe")
                    n2 = p.op("scalar", "ac", lambda sc, krbf=krbf: sc.activation(qsq2[:], krbf[:], AF.Square))
                    p.wait("tensor", "ac", n2)
                    nm = p.op("tensor", "pe", lambda t: t.matmul(ps_n[:], ones_cb[:], qsq2[:], start=False, stop=True))
                    p.wait("vector", "pe", nm)
                    nrd_last = p.op("vector", "ve", lambda v, s=s, ts=ts: v.tensor_reduce(q2m[:, 8 * s + ts:8 * s + ts + 1], ps_n[:], AX, ALU.max))
                p.wait("tensor", "ac", q_last_ev)

                p.wait("tensor", lks, ldkv)
                for s in range(NSLOT):
                    for k in range(NKV):
                        p.op("tensor", "pe", lambda t, s=s, k=k: t.matmul(
                            ps_a[s][:], wuk_bf[:, 512 * k + 128 * s: 512 * k + 128 * (s + 1)], lat_kv[k][:],
                            start=(k == 0), stop=(k == NKV - 1)))
                for s in range(NSLOT):
                    p.wait("scalar", "pe")
                    p.op("scalar", "ac", lambda sc, s=s, ts=ts: sc.activation(
                        kC[s][:, 512 * ts:512 * (ts + 1)], ps_a[s][:], AF.Identity, bias=buk_sb[:, s:s + 1], scale=1.0))
                p.wait("tensor", "ac")

                for s in range(NSLOT):
                    for k in range(NKV):
                        p.op("tensor", "pe", lambda t, s=s, k=k: t.matmul(
                            ps_a[s][:], wuv_bf[:, 512 * k + 128 * s: 512 * k + 128 * (s + 1)], lat_kv[k][:],
                            start=(k == 0), stop=(k == NKV - 1)))
                kv_done = p.val("pe")
                for s in range(NSLOT):
                    p.wait("scalar", "pe")
                    ev = p.op("scalar", "ac", lambda sc, s=s: sc.activation(vfm[:], ps_a[s][:], AF.Identity, bias=buv_sb[:, s:s + 1], scale=1.0))
                    p.wait("tensor", "ac", ev)
                    p.wait("tensor", "ve")
                    for j in range(4):
                        p.op("tensor", "pe", lambda t, j=j: t.transpose(ps_vt[:, 128 * j:128 * (j + 1)], vfm[:, 128 * j:128 * (j + 1)], ident_bf[:]))
                    p.wait("vector", "pe")
                    p.op("vector", "ve", lambda v, s=s, ts=ts: v.tensor_copy(vS[s][:, 512 * ts:512 * (ts + 1)], ps_vt[:]))
                    p.wait("scalar", "pe")
                p.wait("tensor", "ac")
                if serial:
                    p.barrier()

            p.op("vector", "ve", lambda v: v.memset(k2[:], 0.0))
            for s in range(NSLOT):
                for sl in range(NTS):
                    p.wait("scalar", "pe")
                    sq1 = p.op("scalar", "ac", lambda sc, s=s, sl=sl: sc.activation(ksq[:], kC[s][:, 512 * sl:512 * (sl + 1)], AF.Square))
                    p.wait("tensor", "ac", sq1)
                    p.wait("tensor", "ve")
                    p.op("tensor", "pe", lambda t: t.matmul(ps_n[:], ones_cb[:], ksq[:], start=True, stop=False))
                    p.wait("scalar", "pe")
                    sq2 = p.op("scalar", "ac", lambda sc, s=s, sl=sl: sc.activation(ksq[:], kR[s][:, 512 * sl:512 * (sl + 1)], AF.Square))
                    p.wait("tensor", "ac", sq2)
                    mm = p.op("tensor", "pe", lambda t: t.matmul(ps_n[:], ones_cb[:], ksq[:], start=False, stop=True))
                    p.wait("vector", "pe", mm)
                    rd = p.op("vector", "ve", lambda v: v.tensor_reduce(k2t[:], ps_n[:], AX, ALU.max))
                    p.wait("vector", "ve", rd)
                    p.op("vector", "ve", lambda v, s=s: v.tensor_tensor(k2[:, s:s + 1], k2[:, s:s + 1], k2t[:], ALU.max))
                    p.wait("vector", "ve")
            p.op("vector", "ve", lambda v: v.tensor_copy(k2p[:], k2[:]))
            p.wait("vector", "ve")
            p.op("vector", "ve", lambda v: v.tensor_copy(q2mp[:], q2m[:]))
            p.barrier()

        wo_st = ExitStack()
        wo_bf = wo_st.enter_context(nc.sbuf_tensor("wo_bf", [128, KC * D], BF16, side="right"))
        p.dma("gpsimd", wo_bf[:].rearrange("p (c n) -> p c n", c=KC), wo.rearrange("(c p) n -> p c n", p=128), sem="wod")
        with ExitStack() as stA2:
            sb3 = lambda n, s, d: stA2.enter_context(nc.sbuf_tensor(n, s, d))
            ps3s = lambda n, s, d: stA2.enter_context(nc.psum_tensor(n, s, d))
            masks = sb3("masks", [128, 2048], F32)
            qc_t = [sb3(f"qc_t{i}", [128, 512], BF16) for i in range(2)]
            qr_t = [sb3(f"qr_t{i}", [128, 512], BF16) for i in range(2)]
            prb = [sb3(f"prb{i}", [128, 512], BF16) for i in range(4)]
            ctxs = sb3("ctxs", [128, SEQ], BF16)
            craw = [sb3(f"craw{i}", [128, 512], F32) for i in range(2)]
            bias_all = sb3("bias_all", [128, NSLOT * NTS], F32)
            b2r = sb3("b2r", [1, NSLOT * NTS], F32)
            srow = [sb3(f"srow{i}", [1, 512], F32) for i in range(2)]
            rrow = [sb3(f"rrow{i}", [1, 512], F32) for i in range(2)]
            bc_sb = [sb3(f"bc_sb{i}", [128, 512], F32) for i in range(2)]
            ps_sc = [ps3s(f"ps_sc{i}", [128, 512], F32) for i in range(3)]
            ps_ctx = [ps3s(f"ps_ctx{i}", [128, 512], F32) for i in range(2)]
            ps_sum = [ps3s(f"ps_sum{i}", [1, 512], F32) for i in range(2)]
            ps_bc = ps3s("ps_bc", [128, 512], F32)

            p.dma("gpsimd", masks[:], mask_t, sem="w2d")
            p.wait("vector", "w2d")
            bb = p.op("vector", "ve", lambda v: v.tensor_tensor(
                b2r[:].rearrange("a (s t) -> a s t", s=NSLOT),
                q2mp[:].rearrange("a (s t) -> a s t", s=NSLOT),
                k2p[:].rearrange("a (s t) -> a s t", t=1).broadcast_to((1, NSLOT, NTS)),
                ALU.mult))
            p.wait("scalar", "ve", bb)
            bsq = p.op("scalar", "ac", lambda sc: sc.activation(b2r[:], b2r[:], AF.Sqrt))
            p.wait("tensor", "ac", bsq)
            bmm = p.op("tensor", "pe", lambda t: t.matmul(ps_bc[:, 0:NSLOT * NTS], ones_rf[:], b2r[:], start=True, stop=True))
            p.wait("scalar", "pe", bmm)
            bev = p.op("scalar", "ac", lambda sc: sc.activation(bias_all[:], ps_bc[:, 0:NSLOT * NTS], AF.Identity, scale=-0.0625))
            p.wait("scalar", "ac", bev)
            p.wait("tensor", "ac", bev)

            exp_cnt = {i: 0 for i in range(3)}
            cons_cnt = {i: 0 for i in range(4)}
            ctx_free = {0: 0, 1: 0}
            qtile_free = {0: 0, 1: 0}
            bc_free = 0
            LOOK = 2

            def load_q(s, qc):
                qp = qc % 2
                p.wait("sync", "pe", qtile_free[qp])
                p.dma("sync", qc_t[qp][:], q_dram[256 * s:256 * s + 128, 512 * qc:512 * (qc + 1)], sem="qt%d" % qp)
                return p.dma("sync", qr_t[qp][:], q_dram[256 * s + 128:256 * s + 256, 512 * qc:512 * (qc + 1)], sem="qt%d" % qp)

            qld = {(0, 0): load_q(0, 0)}
            for s in range(NSLOT):
                for qc in range(NTS):
                    qp = qc % 2
                    p.wait("tensor", "qt%d" % qp, qld[(s, qc)])
                    p.wait("tensor", "ve", ctx_free[qp])
                    bcol = bias_all[:, 8 * s + qc:8 * s + qc + 1]
                    nkt = 4 * (qc + 1)
                    pend = []
                    last_mm = 0

                    for ik in range(nkt):
                        kb, kt = divmod(ik, 4)
                        par = ik % 3
                        ring = ik % 4
                        diag = kb == qc
                        q0 = 128 * kt if diag else 0
                        if pend and len(pend) > LOOK - 1:
                            p.wait("tensor", "ac", pend[0][2])
                        elif exp_cnt[par]:
                            p.wait("tensor", "ac", exp_cnt[par])
                        if len(pend) > LOOK:
                            jk, jring, jex, jq0 = pend.pop(0)
                            p.op("tensor", "pe", lambda t, s=s, jk=jk, jring=jring, qp=qp, jq0=jq0, first=(jk == 0), last=(jk == nkt - 1): t.matmul(
                                ps_ctx[qp][:, jq0:512], vS[s][:, 128 * jk: 128 * (jk + 1)], prb[jring][:, jq0:512], start=first, stop=last))
                            cmm = p.op("tensor", "pe", lambda t, jring=jring, qp=qp, jq0=jq0, first=(jk == 0), last=(jk == nkt - 1): t.matmul(
                                ps_sum[qp][:, jq0:512], ones_cb[:], prb[jring][:, jq0:512], start=first, stop=last))
                            cons_cnt[jring] = cmm
                        p.op("tensor", "pe", lambda t, s=s, kb=kb, kt=kt, par=par, qp=qp, q0=q0: t.matmul(
                            ps_sc[par][:, q0:512], kC[s][:, 512 * kb + 128 * kt: 512 * kb + 128 * (kt + 1)], qc_t[qp][:, q0:512], start=True, stop=False))
                        mm = p.op("tensor", "pe", lambda t, s=s, kb=kb, kt=kt, par=par, qp=qp, q0=q0: t.matmul(
                            ps_sc[par][:, q0:512], kR[s][:, 512 * kb + 128 * kt: 512 * kb + 128 * (kt + 1)], qr_t[qp][:, q0:512], start=False, stop=True))
                        last_mm = mm
                        gate = ("pe", mm)
                        if diag:
                            p.wait("vector", "pe", mm)
                            mk = p.op("vector", "ve", lambda v, kt=kt, par=par, q0=q0: v.tensor_tensor(
                                ps_sc[par][:, q0:512], ps_sc[par][:, q0:512], masks[:, 512 * kt + q0:512 * (kt + 1)], ALU.add))
                            gate = ("ve", mk)
                        p.wait("scalar", gate[0], gate[1])
                        p.wait("scalar", "pe", cons_cnt[ring])
                        ex = p.op("scalar", "ac", lambda sc, par=par, ring=ring, bcol=bcol, q0=q0: sc.activation(
                            prb[ring][:, q0:512], ps_sc[par][:, q0:512], AF.Exp, bias=bcol, scale=0.0625))
                        exp_cnt[par] = ex
                        pend.append((ik, ring, ex, q0))
                    nqc = qc + 1
                    ns = s
                    if nqc == NTS:
                        nqc = 0
                        ns = s + 1
                    if ns < NSLOT:
                        qld[(ns, nqc)] = load_q(ns, nqc)
                    for jk, jring, jex, jq0 in pend:
                        p.wait("tensor", "ac", jex)
                        p.op("tensor", "pe", lambda t, s=s, jk=jk, jring=jring, qp=qp, jq0=jq0, first=(jk == 0), last=(jk == nkt - 1): t.matmul(
                            ps_ctx[qp][:, jq0:512], vS[s][:, 128 * jk: 128 * (jk + 1)], prb[jring][:, jq0:512], start=first, stop=last))
                        cmm = p.op("tensor", "pe", lambda t, jring=jring, qp=qp, jq0=jq0, first=(jk == 0), last=(jk == nkt - 1): t.matmul(
                            ps_sum[qp][:, jq0:512], ones_cb[:], prb[jring][:, jq0:512], start=first, stop=last))
                        cons_cnt[jring] = cmm
                    qtile_free[qp] = last_mm
                    p.wait("vector", "pe")
                    e1 = p.op("vector", "ve", lambda v, qp=qp: v.tensor_copy(srow[qp][:], ps_sum[qp][:]))
                    cr = p.op("vector", "ve", lambda v, qp=qp: v.tensor_copy(craw[qp][:], ps_ctx[qp][:]))
                    p.wait("vector", "ve", e1)
                    e2 = p.op("vector", "ve", lambda v, qp=qp: v.reciprocal(rrow[qp][:], srow[qp][:]))
                    p.wait("tensor", "ve", bc_free)
                    p.wait("tensor", "ve", e2)
                    bm = p.op("tensor", "pe", lambda t, qp=qp: t.matmul(ps_bc[:], ones_rf[:], rrow[qp][:], start=True, stop=True))
                    p.wait("vector", "pe", bm)
                    e3 = p.op("vector", "ve", lambda v, qp=qp: v.tensor_copy(bc_sb[qp][:], ps_bc[:]))
                    bc_free = e3
                    p.wait("vector", "ve", e3)
                    mf = p.op("vector", "ve", lambda v, qc=qc, qp=qp: v.tensor_mul(ctxs[:, 512 * qc:512 * (qc + 1)], craw[qp][:], bc_sb[qp][:]))
                    ctx_free[qp] = cr
                    if serial:
                        p.barrier()
                p.wait("sync", "ve")
                p.dma("sync", ctx_shard[128 * s:128 * (s + 1), :], ctxs[:], sem="cxs")
                p.wait("vector", "cxs")

            p.wait("gpsimd", "cxs")
            p.op("gpsimd", "cc", lambda g: g.collective_compute(
                "AllGather", ALU.bypass, replica_groups=GROUPS, ins=[ctx_shard.opt()], outs=[ctx_all.opt()]))

    if "t_qdram" in tap_aps:
        p.wait("gpsimd", "cc")
        p.dma("gpsimd", tap_aps["t_qdram"][:, :], q_dram)
    if "t_ctx" in tap_aps:
        p.wait("gpsimd", "cc")
        p.dma("gpsimd", tap_aps["t_ctx"][:, :], ctx_all)
    if phases == 2:
        p.barrier()
        finalize(nc, p)
        st.close()
        return nc

    p.wait("sync", "pe")
    p.wait("sync", "ve")
    p.wait("sync", "ac")
    p.wait("sync", "gd")
    with ExitStack() as stA:
        sbA = lambda n, s, d: stA.enter_context(nc.sbuf_tensor(n, s, d))
        bo_row = sbA("bo_row", [1, D], F32)
        ds_row = sbA("ds_row", [1, D], F32)
        ds_bc = sbA("ds_bc", [128, D], F32)
        xn2 = sbA("xn2", [128, KC * TPC], BF16)
        bn6b = [sbA(f"bn6b{i}", [128, 24], F32) for i in range(2)]
        mvb = [sbA(f"mvb{i}", [128, 2], F32) for i in range(2)]
        stdb = [sbA(f"stdb{i}", [128, 1], F32) for i in range(2)]
        rstdb = [sbA(f"rstdb{i}", [128, 1], F32) for i in range(2)]
        nbb = [sbA(f"nbb{i}", [128, 1], F32) for i in range(2)]

        p.dma("sync", bo_row[:], bo_r)
        p.dma("sync", ds_row[:], ds_r)

        with ExitStack() as stB3:
            sbB = lambda n, s, d: stB3.enter_context(nc.sbuf_tensor(n, s, d))
            psB = lambda n, s, d: stB3.enter_context(nc.psum_tensor(n, s, d))
            cx = sbB("cx", [128, KC * TPC], BF16)
            bo_bc = sbB("bo_bc", [128, D], F32)
            xt3 = [sbB(f"xt3{i}", [128, D], F32) for i in range(2)]
            y_tb = [sbB(f"y_tb{i}", [128, D], F32) for i in range(2)]
            yn_tb1 = sbB("yn_tb1", [128, D], F32)
            yn_tb = [yn_tb1, yn_tb1]
            xn2_tm1 = sbB("xn2_tm1", [128, D], BF16)
            xn2_tm = [xn2_tm1, xn2_tm1]
            ps_o = [psB(f"ps_o{i}", [128, TPC], F32) for i in range(2)]
            ps_tr3 = psB("ps_tr3", [128, 512], BF16)
            ps_bb = psB("ps_bb", [128, 512], F32)

            p.wait("gpsimd", "sd")
            p.wait("gpsimd", "cc")

            def ctx_reads(g):
                last = None
                with g.register("tok") as tok:
                    g.reg_load(tok, meta_sb[0:1, 0:1])
                    off = g.snap(tok, min_val=0, max_val=3 * TPC)
                    for ch in range(KC):
                        last = g.dma_start(out=cx[:, TPC * ch:TPC * (ch + 1)],
                                           in_=ctx_all[128 * ch:128 * (ch + 1), ds(off, TPC)])
                        last.then_inc(SEM["wod"], 16)
                return last
            p.emit("gpsimd", ctx_reads)
            p.bump("wod", 16 * KC)

            p.wait("tensor", "sd")
            for h in range(4):
                p.wait("tensor", "ve")
                bm = p.op("tensor", "pe", lambda t, h=h: t.matmul(ps_bb[:], ones_rf[:], bo_row[:, 512 * h:512 * (h + 1)], start=True, stop=True))
                p.wait("vector", "pe", bm)
                p.op("vector", "ve", lambda v, h=h: v.tensor_copy(bo_bc[:, 512 * h:512 * (h + 1)], ps_bb[:]))
                p.wait("tensor", "ve")
                bm = p.op("tensor", "pe", lambda t, h=h: t.matmul(ps_bb[:], ones_rf[:], ds_row[:, 512 * h:512 * (h + 1)], start=True, stop=True))
                p.wait("vector", "pe", bm)
                p.op("vector", "ve", lambda v, h=h: v.tensor_copy(ds_bc[:, 512 * h:512 * (h + 1)], ps_bb[:]))

            p.wait("tensor", "wod")
            tr_ev = 0
            oev = {0: 0, 1: 0}
            for i in range(NT):
                p.wait("sync", "ve")
                xl = p.dma("sync", xt3[i % 2][:], x_in[128 * i:128 * (i + 1), :], sem="xd%d" % (i % 2))
                for oh in range(2):
                    p.wait("tensor", "ve", oev[oh])
                    for c in range(KC):
                        for h in range(2):
                            p.op("tensor", "pe", lambda t, c=c, h=h, i=i, oh=oh: t.matmul(
                                ps_o[oh][:, 512 * h:512 * (h + 1)],
                                cx[:, TPC * c + 128 * i: TPC * c + 128 * (i + 1)],
                                wo_bf[:, D * c + TPC * oh + 512 * h: D * c + TPC * oh + 512 * (h + 1)],
                                start=(c == 0), stop=(c == KC - 1)))
                    p.wait("vector", "pe")
                    p.wait("vector", "ac")
                    oev[oh] = p.op("vector", "ve", lambda v, i=i, oh=oh: v.tensor_add(
                        y_tb[i % 2][:, TPC * oh:TPC * (oh + 1)], ps_o[oh][:], bo_bc[:, TPC * oh:TPC * (oh + 1)]))
                p.wait("vector", "ve")
                n2 = ln_chain(p, y_tb[i % 2], bn6b[i % 2], mvb[i % 2], stdb[i % 2], rstdb[i % 2], nbb[i % 2], eps_c,
                              lambda s, i=i: s.activation(yn_tb[i % 2][:], y_tb[i % 2][:], AF.Identity, bias=nbb[i % 2][:], scale=rstdb[i % 2][:]))
                p.wait("vector", "ac", n2)
                p.wait("vector", "xd%d" % (i % 2), xl)
                x2a = p.op("vector", "ve", lambda v, b=xt3[i % 2], i=i: v.tensor_add(b[:], b[:], yn_tb[i % 2][:]))
                p.wait("sync", "ve", x2a)
                p.dma("sync", x2_dram[128 * i:128 * (i + 1), :], xt3[i % 2][:], sem="x2d")
                p.wait("scalar", "ve", tr_ev)
                n3 = ln_chain(p, xt3[i % 2], bn6b[i % 2], mvb[i % 2], stdb[i % 2], rstdb[i % 2], nbb[i % 2], eps_c,
                              lambda s, b=xt3[i % 2], i=i: s.activation(xn2_tm[i % 2][:], b[:], AF.Identity, bias=nbb[i % 2][:], scale=rstdb[i % 2][:]))
                p.wait("tensor", "ac", n3)
                for j4 in range(4):
                    p.wait("tensor", "ve", tr_ev)
                    for j in range(4):
                        c = j4 * 4 + j
                        p.op("tensor", "pe", lambda t, c=c, j=j, i=i: t.transpose(
                            ps_tr3[:, 128 * j:128 * (j + 1)], xn2_tm[i % 2][:, 128 * c:128 * (c + 1)], ident_bf[:]))
                    p.wait("vector", "pe")
                    def evac3(v, i=i, j4=j4):
                        dv = xn2[:, :].rearrange("p (c n) -> p c n", n=TPC)
                        return v.tensor_copy(dv[:, j4 * 4:(j4 + 1) * 4, 128 * i:128 * (i + 1)],
                                             ps_tr3[:].rearrange("p (c n) -> p c n", n=128))
                    tr_ev = p.op("vector", "ve", evac3)
                if serial:
                    p.barrier()

        wo_st.close()
        p.barrier()
        with ExitStack() as stC:
            sbC = lambda n, s, d: stC.enter_context(nc.sbuf_tensor(n, s, d))
            psC = lambda n, s, d: stC.enter_context(nc.psum_tensor(n, s, d))
            acc = sbC("acc", [128, NT * D], F32)
            wg_bf = [sbC(f"wg{i}", [128, KC * 128], BF16) for i in range(2)]
            wu_bf = [sbC(f"wu{i}", [128, KC * 128], BF16) for i in range(2)]
            wd_bf = [sbC(f"wd{i}", [128, D], BF16) for i in range(GRP)]
            hdn = [sbC(f"hdn{i}", [128, TPC], BF16) for i in range(GRP)]
            silu_g = sbC("silu_g", [128, TPC], BF16)
            gs_sb = sbC("gs_sb", [128, IC], F32)
            us_sb = sbC("us_sb", [128, IC], F32)
            xt41 = sbC("xt41", [128, D], F32)
            xt4 = [xt41, xt41]
            y41 = sbC("y41", [128, D], F32)
            y4 = [y41, y41]
            yn41 = sbC("yn41", [128, D], F32)
            yn4 = [yn41, yn41]
            ps_g = psC("ps_g", [128, TPC], F32)
            ps_u = psC("ps_u", [128, TPC], F32)
            ps_d = [psC(f"ps_d{i}", [128, TPC], F32) for i in range(2)]

            p.dma("sync", gs_sb[:], gs)
            p.dma("sync", us_sb[:], us)
            p.op("vector", "ve", lambda v: v.memset(acc[:], 0.0))
            p.wait("tensor", "ve")
            p.wait("tensor", "sd")

            groups = [list(range(g, min(g + GRP, IC))) for g in range(0, IC, GRP)]
            dn_done = 0
            hv_done = {0: 0, 1: 0}
            for grp in groups:
                for ii, icx in enumerate(grp):
                    p.wait("gpsimd", "ve", hv_done[ii % 2])
                    p.wait("gpsimd", "pe", dn_done)
                    wfs = "wf%d" % (ii % 2)
                    p.dma("gpsimd", wg_bf[ii % 2][:].rearrange("p (c n) -> p c n", c=KC),
                          gwt[:, 128 * icx:128 * (icx + 1)].rearrange("(c p) n -> p c n", p=128), sem=wfs)
                    p.dma("gpsimd", wu_bf[ii % 2][:].rearrange("p (c n) -> p c n", c=KC),
                          upt[:, 128 * icx:128 * (icx + 1)].rearrange("(c p) n -> p c n", p=128), sem=wfs)
                    wld = p.dma("gpsimd", wd_bf[ii][:], dwt[128 * icx:128 * (icx + 1), :], sem=wfs)
                    p.wait("tensor", wfs, wld)
                    p.wait("tensor", "ve", hv_done[ii % 2])
                    for k in range(KC):
                        for th in range(2):
                            p.op("tensor", "pe", lambda t, k=k, th=th, ii=ii: t.matmul(
                                ps_g[:, 512 * th:512 * (th + 1)], wg_bf[ii % 2][:, 128 * k:128 * (k + 1)],
                                xn2[:, TPC * k + 512 * th: TPC * k + 512 * (th + 1)],
                                start=(k == 0), stop=(k == KC - 1)))
                    for k in range(KC):
                        for th in range(2):
                            p.op("tensor", "pe", lambda t, k=k, th=th, ii=ii: t.matmul(
                                ps_u[:, 512 * th:512 * (th + 1)], wu_bf[ii % 2][:, 128 * k:128 * (k + 1)],
                                xn2[:, TPC * k + 512 * th: TPC * k + 512 * (th + 1)],
                                start=(k == 0), stop=(k == KC - 1)))
                    p.wait("scalar", "pe")
                    p.wait("scalar", "ve", hv_done[ii % 2])
                    sl = p.op("scalar", "ac", lambda s, icx=icx: s.activation(silu_g[:], ps_g[:], AF.Silu, scale=gs_sb[:, icx:icx + 1]))
                    p.wait("vector", "ac", sl)
                    p.wait("vector", "pe")
                    hv = p.op("vector", "ve", lambda v, ii=ii, icx=icx: v.scalar_tensor_tensor(
                        hdn[ii][:], ps_u[:], us_sb[:, icx:icx + 1], silu_g[:], ALU.mult, ALU.mult))
                    hv_done[ii % 2] = hv
                    p.wait("tensor", "ve", hv)
                dev = {0: 0, 1: 0}
                for tt in range(NT):
                    for oh in range(2):
                        dp = oh
                        p.wait("tensor", "ve", dev[dp])
                        for ii in range(len(grp)):
                            for th in range(2):
                                p.op("tensor", "pe", lambda t, ii=ii, tt=tt, th=th, oh=oh, dp=dp, first=(ii == 0), last=(ii == len(grp) - 1): t.matmul(
                                    ps_d[dp][:, 512 * th:512 * (th + 1)], hdn[ii][:, 128 * tt:128 * (tt + 1)],
                                    wd_bf[ii][:, TPC * oh + 512 * th: TPC * oh + 512 * (th + 1)],
                                    start=first, stop=last))
                        p.wait("vector", "pe")
                        dev[dp] = p.op("vector", "ve", lambda v, tt=tt, oh=oh, dp=dp: v.tensor_add(
                            acc[:, D * tt + TPC * oh:D * tt + TPC * (oh + 1)], acc[:, D * tt + TPC * oh:D * tt + TPC * (oh + 1)], ps_d[dp][:]))
                dn_done = p.val("pe")
                if serial:
                    p.barrier()

            if "t_xn2" in tap_aps:
                p.barrier()
                p.dma("gpsimd", tap_aps["t_xn2"][:, :], xn2[:])
            if "t_acc" in tap_aps:
                p.barrier()
                p.dma("sync", tap_aps["t_acc"][:, :], acc[:])
            if "t_x2" in tap_aps:
                p.barrier()
                p.dma("sync", tap_aps["t_x2"][:, :], x2_dram)
            p.wait("sync", "x2d")
            for i in range(NT):
                p.wait("sync", "ve")
                xl = p.dma("sync", xt4[i % 2][:], x2_dram[128 * i:128 * (i + 1), :], sem="xd%d" % (i % 2))
                p.wait("vector", "ac")
                ym = p.op("vector", "ve", lambda v, i=i: v.tensor_mul(y4[i % 2][:], acc[:, D * i:D * (i + 1)], ds_bc[:]))
                p.wait("vector", "ve", ym)
                n4 = ln_chain(p, y4[i % 2], bn6b[i % 2], mvb[i % 2], stdb[i % 2], rstdb[i % 2], nbb[i % 2], eps_c,
                              lambda s, i=i: s.activation(yn4[i % 2][:], y4[i % 2][:], AF.Identity, bias=nbb[i % 2][:], scale=rstdb[i % 2][:]))
                p.wait("vector", "ac", n4)
                p.wait("vector", "xd%d" % (i % 2), xl)
                fo = p.op("vector", "ve", lambda v, i=i: v.tensor_add(xt4[i % 2][:], xt4[i % 2][:], yn4[i % 2][:]))
                p.wait("sync", "ve", fo)
                p.dma("sync", out[128 * i:128 * (i + 1), :], xt4[i % 2][:])
                if serial:
                    p.barrier()

    finalize(nc, p)
    st.close()
    return nc


def finalize(nc, p):
    p.barrier()
    with nc.Block() as block:
        @block.sync
        def _(e):
            for f in p.q["sync"]:
                f(e)

        @block.gpsimd
        def _(e):
            for f in p.q["gpsimd"]:
                f(e)

        @block.scalar
        def _(e):
            for f in p.q["scalar"]:
                f(e)

        @block.vector
        def _(e):
            for f in p.q["vector"]:
                f(e)

        @block.tensor
        def _(e):
            for f in p.q["tensor"]:
                f(e)


def make_consts():
    ident = np.eye(128, dtype=np.float32)
    rotm = np.zeros((128, 128), np.float32)
    for j in range(64):
        rotm[2 * j + 1, 2 * j] = -1.0
        rotm[2 * j, 2 * j + 1] = 1.0
    dd = np.arange(128) // 2
    inv = 1.0 / ROPE_BASE ** (dd.astype(np.float64) * 2.0 / 128.0)
    tpos = np.arange(SEQ, dtype=np.float64)
    ang = inv[:, None] * tpos[None, :]
    cos_t = np.cos(ang).astype(np.float32)
    sin_t = np.sin(ang).astype(np.float32)
    kk = np.arange(128)[:, None]
    qq = np.arange(512)[None, :]
    mask = np.concatenate([np.where(kt * 128 + kk > qq, NEG, 0.0) for kt in range(4)], axis=1).astype(np.float32)
    return dict(identc=ident, rotmc=rotm, cos_t=cos_t, sin_t=sin_t, mask_t=mask,
                ones_c=np.ones((128, 1), np.float32), ones_r=np.ones((1, 128), np.float32),
                eps_cd=np.full((128, 1), 1e-5, np.float32))


def cols(v, n):
    return np.ascontiguousarray(np.asarray(v, np.float32).reshape(n, 128).T)


def make_inmaps(inp):
    x = np.asarray(inp["L0_attn_h"], np.float32)
    consts = make_consts()
    g = lambda k: np.ascontiguousarray(np.asarray(inp[k], np.float32))
    wdkv_f = g("L0_attn_W_DKV"); bdkv_f = cols(inp["L0_attn_b_DKV"], NKV)
    wdq_f = g("L0_attn_W_DQ"); bdq_f = cols(inp["L0_attn_b_DQ"], NQ)
    wkr_f = g("L0_attn_W_KR"); bkr_f = np.asarray(inp["L0_attn_b_KR"], np.float32)
    wuk_f = g("L0_attn_W_UK"); buk_f = np.asarray(inp["L0_attn_b_UK"], np.float32)
    wuv_f = g("L0_attn_W_UV"); buv_f = np.asarray(inp["L0_attn_b_UV"], np.float32)
    wuq_f = g("L0_attn_W_UQ"); buq_f = np.asarray(inp["L0_attn_b_UQ"], np.float32)
    wo_f = g("L0_attn_W_O"); bo_f = np.asarray(inp["L0_attn_b_O"], np.float32).reshape(1, D)
    gwt_f = np.ascontiguousarray(g("L0_ffn_gate_w").T)
    upt_f = np.ascontiguousarray(g("L0_ffn_up_w").T)
    dwt_f = np.ascontiguousarray(g("L0_ffn_down_w").T)
    gs_f = cols(inp["L0_ffn_gate_s"], IC)
    us_f = cols(inp["L0_ffn_up_s"], IC)
    ds_f = np.asarray(inp["L0_ffn_down_s"], np.float32).reshape(1, D)

    maps = []
    for c in range(8):
        b, m = divmod(c, 4)
        sl = slice(512 * m, 512 * (m + 1))
        maps.append(dict(
            x=np.ascontiguousarray(x[b, TPC * m:TPC * (m + 1), :]),
            meta=np.array([[TPC * m]], np.int32),
            wdkv=wdkv_f, bdkv=bdkv_f, wdq=wdq_f, bdq=bdq_f,
            wkr=np.ascontiguousarray(wkr_f[:, sl]), bkr=cols(bkr_f[sl], NSLOT),
            wuk=np.ascontiguousarray(wuk_f[:, sl]), buk=cols(buk_f[sl], NSLOT),
            wuv=np.ascontiguousarray(wuv_f[:, sl]), buv=cols(buv_f[sl], NSLOT),
            wuq=np.ascontiguousarray(wuq_f[:, sl]), buq=cols(buq_f[sl], NSLOT),
            wo=wo_f, bo_r=bo_f, gwt=gwt_f, upt=upt_f, dwt=dwt_f,
            gs=gs_f, us=us_f, ds_r=ds_f,
            **consts,
        ))
    return maps


def kernel(**inp):
    nc = build_nc(phases=3)
    maps = make_inmaps(inp)
    res = run_bass_kernel_spmd(nc, maps, core_ids=list(range(8)))
    outv = np.empty((2, SEQ, D), np.float32)
    for c in range(8):
        b, m = divmod(c, 4)
        outv[b, TPC * m:TPC * (m + 1), :] = res.results[c]["out"]
    return outv


# version 07
# speedup vs baseline: 1.5271x; 1.0554x over previous
import numpy as np
from contextlib import ExitStack
import concourse.bass as bass
import concourse.mybir as mybir
from concourse.bass import ds
from concourse.bass_utils import run_bass_kernel_spmd

F32 = mybir.dt.float32
BF16 = mybir.dt.bfloat16
I32 = mybir.dt.int32
AX = mybir.AxisListType.X
ALU = mybir.AluOpType
AF = mybir.ActivationFunctionType

D = 2048
SEQ = 4096
TPC = 1024
NT = TPC // 128
KC = D // 128
DKV = 512
NKV = DKV // 128
DQ = 1536
NQ = DQ // 128
NSLOT = 4
NTS = SEQ // 512
INTER = 5632
IC = INTER // 128
GRP = 4
ROPE_BASE = 10000.0
GROUPS = [[0, 1, 2, 3], [4, 5, 6, 7]]
NEG = -30000.0


class Plan:
    ENGINES = ("sync", "gpsimd", "scalar", "vector", "tensor")

    def __init__(self):
        self.q = {e: [] for e in self.ENGINES}
        self.cnt = {}

    def val(self, sem):
        return self.cnt.get(sem, 0)

    def bump(self, sem, n):
        self.cnt[sem] = self.val(sem) + n
        return self.cnt[sem]

    def emit(self, eng, f):
        self.q[eng].append(f)

    def wait(self, eng, sem, v=None):
        v = self.val(sem) if v is None else v
        if v <= 0:
            return
        self.emit(eng, lambda e, s=sem, vv=v: e.wait_ge(SEM[s], vv))

    def dma(self, eng, out, in_, sem=None, **kw):
        sem = sem or ("gd" if eng == "gpsimd" else ("ad" if eng == "scalar" else "sd"))
        t = self.bump(sem, 16)
        self.emit(eng, lambda e, o=out, i=in_, s=sem, k=kw: e.dma_start(out=o, in_=i, **k).then_inc(SEM[s], 16))
        return t

    def op(self, eng, sem, build):
        t = self.bump(sem, 1)
        self.emit(eng, lambda e, b=build, s=sem: b(e).then_inc(SEM[s], 1))
        return t

    def barrier(self, engines=None):
        snap = {s: v for s, v in self.cnt.items() if v > 0}
        for eng in (engines or self.ENGINES):
            for s, v in snap.items():
                self.wait(eng, s, v)


SEM = {}


def ln_chain(p, src, bn6, mv, stdc, rstd, nb, eps_c, out_fn):
    for j in range(4):
        p.op("vector", "ve", lambda v, j=j: v.bn_stats(bn6[:, 6 * j:6 * j + 6], src[:, 512 * j:512 * (j + 1)]))
    p.wait("vector", "ve")
    p.wait("vector", "ac")
    p.op("vector", "ve", lambda v: v.bn_aggr(mv[:], bn6[:]))
    p.wait("scalar", "ve")
    sq = p.op("scalar", "ac", lambda s: s.activation(stdc[:], mv[:, 1:2], AF.Sqrt, bias=eps_c[:], scale=1.0))
    p.wait("vector", "ac", sq)
    p.op("vector", "ve", lambda v: v.reciprocal(rstd[:], stdc[:]))
    p.wait("vector", "ve")
    p.op("vector", "ve", lambda v: v.scalar_tensor_tensor(nb[:], mv[:, 0:1], -1.0, rstd[:], ALU.mult, ALU.mult))
    p.wait("scalar", "ve")
    return p.op("scalar", "ac", out_fn)


def build_nc(phases=3, taps=(), serial=False):
    nc = bass.Bass("TRN2", target_bir_lowering=False)
    p = Plan()

    def din(name, shape, dt=F32):
        return nc.dram_tensor(name, shape, dt, kind="ExternalInput").ap()

    x_in = din("x", [TPC, D])
    meta = din("meta", [1, 1], I32)
    wdkv = din("wdkv", [D, DKV])
    bdkv = din("bdkv", [128, NKV])
    wdq = din("wdq", [D, DQ])
    bdq = din("bdq", [128, NQ])
    wkr = din("wkr", [D, 512])
    bkr = din("bkr", [128, NSLOT])
    wuk = din("wuk", [DKV, 512])
    buk = din("buk", [128, NSLOT])
    wuv = din("wuv", [DKV, 512])
    buv = din("buv", [128, NSLOT])
    wuq = din("wuq", [DQ, 512])
    buq = din("buq", [128, NSLOT])
    wo = din("wo", [D, D])
    bo_r = din("bo_r", [1, D])
    gwt = din("gwt", [D, INTER])
    upt = din("upt", [D, INTER])
    dwt = din("dwt", [INTER, D])
    gs = din("gs", [128, IC])
    us = din("us", [128, IC])
    ds_r = din("ds_r", [1, D])
    identc = din("identc", [128, 128])
    rotmc = din("rotmc", [128, 128])
    cos_td = din("cos_t", [128, SEQ])
    sin_td = din("sin_t", [128, SEQ])
    mask_t = din("mask_t", [128, 4 * 512])
    ones_cd = din("ones_c", [128, 1])
    ones_rd = din("ones_r", [1, 128])
    eps_cd = din("eps_cd", [128, 1])

    out = nc.dram_tensor("out", [TPC, D], F32, kind="ExternalOutput").ap()

    shard_x = nc.dram_tensor("shard_x", [2048, TPC], BF16, kind="Internal").ap()
    shard_l = nc.dram_tensor("shard_l", [2048, TPC], BF16, kind="Internal").ap()
    l_all_x = nc.dram_tensor("l_all_x", [4 * 2048, TPC], BF16, kind="Internal").ap()
    l_all_l = nc.dram_tensor("l_all_l", [4 * 2048, TPC], BF16, kind="Internal").ap()
    q_dram = nc.dram_tensor("q_dram", [NSLOT * 256, SEQ], BF16, kind="Internal").ap()
    ctx_shardA = nc.dram_tensor("ctx_shardA", [256, SEQ], BF16, kind="Internal").ap()
    ctx_shardB = nc.dram_tensor("ctx_shardB", [256, SEQ], BF16, kind="Internal").ap()
    ctx_allA = nc.dram_tensor("ctx_allA", [1024, SEQ], BF16, kind="Internal").ap()
    ctx_allB = nc.dram_tensor("ctx_allB", [1024, SEQ], BF16, kind="Internal").ap()
    x2_dram = nc.dram_tensor("x2_dram", [TPC, D], F32, kind="Internal").ap()

    tap_aps = {}
    if "shard" in taps:
        tap_aps["t_shard"] = nc.dram_tensor("t_shard", [4 * 4096, TPC], F32, kind="ExternalOutput").ap()
    if "qdram" in taps:
        tap_aps["t_qdram"] = nc.dram_tensor("t_qdram", [NSLOT * 256, SEQ], F32, kind="ExternalOutput").ap()
    if "ctx" in taps:
        tap_aps["t_ctx"] = nc.dram_tensor("t_ctx", [2048, SEQ], F32, kind="ExternalOutput").ap()
    if "x2" in taps:
        tap_aps["t_x2"] = nc.dram_tensor("t_x2", [TPC, D], F32, kind="ExternalOutput").ap()
    if "xn2" in taps:
        tap_aps["t_xn2"] = nc.dram_tensor("t_xn2", [128, KC * TPC], F32, kind="ExternalOutput").ap()
    if "acc" in taps:
        tap_aps["t_acc"] = nc.dram_tensor("t_acc", [128, NT * D], F32, kind="ExternalOutput").ap()
    if "cxv" in taps:
        tap_aps["t_cxv"] = nc.dram_tensor("t_cxv", [128, KC * TPC], F32, kind="ExternalOutput").ap()

    st = ExitStack()
    sbuf = lambda n, s, d: st.enter_context(nc.sbuf_tensor(n, s, d))

    ident_bf = sbuf("ident_bf", [128, 128], BF16)
    rotm_bf = sbuf("rotm_bf", [128, 128], BF16)
    ones_cb = sbuf("ones_cb", [128, 1], BF16)
    ones_rf = sbuf("ones_rf", [1, 128], F32)
    eps_c = sbuf("eps_c", [128, 1], F32)
    meta_sb = st.enter_context(nc.sbuf_tensor("meta_sb", [1, 1], I32))
    k2p = sbuf("k2p", [1, NSLOT], F32)
    q2mp = sbuf("q2mp", [1, NSLOT * NTS], F32)

    for s in ("sd", "gd", "ad", "cc", "pe", "ve", "ac",
              "xd0", "xd1", "st0", "st1", "st2", "st3",
              "lx0", "lx1", "lq0", "lq1", "lk0", "lk1",
              "qt0", "qt1", "qs0", "qs1", "w2d", "wod",
              "wf0", "wf1", "shx", "cxs", "x2d", "wodb"):
        SEM[s] = st.enter_context(nc.semaphore(s))

    p.dma("gpsimd", ident_bf[:], identc)
    p.dma("gpsimd", rotm_bf[:], rotmc)
    p.dma("gpsimd", ones_cb[:], ones_cd)
    p.dma("sync", ones_rf[:], ones_rd)
    p.dma("sync", eps_c[:], eps_cd)
    p.dma("sync", meta_sb[:], meta)

    w2_st = ExitStack()
    sbr = lambda n, s, d: w2_st.enter_context(nc.sbuf_tensor(n, s, d, side="right"))
    wkr_bf = sbr("wkr_bf", [128, KC * 512], BF16)
    wuk_bf = sbr("wuk_bf", [128, NKV * 512], BF16)
    wuv_bf = sbr("wuv_bf", [128, NKV * 512], BF16)
    wuq_bf = sbr("wuq_bf", [128, NQ * 512], BF16)
    bkr_sb = sbr("bkr_sb", [128, NSLOT], F32)
    buk_sb = sbr("buk_sb", [128, NSLOT], F32)
    buv_sb = sbr("buv_sb", [128, NSLOT], F32)
    buq_sb = sbr("buq_sb", [128, NSLOT], F32)
    masks = sbr("masks", [128, 2048], F32)
    p.dma("gpsimd", wkr_bf[:].rearrange("p (c n) -> p c n", c=KC), wkr.rearrange("(c p) n -> p c n", p=128), sem="w2d")
    p.dma("gpsimd", wuk_bf[:].rearrange("p (c n) -> p c n", c=NKV), wuk.rearrange("(c p) n -> p c n", p=128), sem="w2d")
    p.dma("gpsimd", wuv_bf[:].rearrange("p (c n) -> p c n", c=NKV), wuv.rearrange("(c p) n -> p c n", p=128), sem="w2d")
    p.dma("gpsimd", wuq_bf[:].rearrange("p (c n) -> p c n", c=NQ), wuq.rearrange("(c p) n -> p c n", p=128), sem="w2d")
    p.dma("gpsimd", masks[:], mask_t, sem="w2d")
    p.dma("sync", bkr_sb[:], bkr)
    p.dma("sync", buk_sb[:], buk)
    p.dma("sync", buv_sb[:], buv)
    p.dma("sync", buq_sb[:], buq)

    with ExitStack() as st1:
        sb1 = lambda n, s, d: st1.enter_context(nc.sbuf_tensor(n, s, d))
        ps1s = lambda n, s, d: st1.enter_context(nc.psum_tensor(n, s, d))
        xt = [sb1(f"xt{i}", [128, D], F32) for i in range(2)]
        bn6 = [sb1(f"bn6_{i}", [128, 24], F32) for i in range(2)]
        mv = [sb1(f"mv_{i}", [128, 2], F32) for i in range(2)]
        stdc = [sb1(f"stdc_{i}", [128, 1], F32) for i in range(2)]
        rstd = [sb1(f"rstd_{i}", [128, 1], F32) for i in range(2)]
        nb = [sb1(f"nb_{i}", [128, 1], F32) for i in range(2)]
        xn_tm = [sb1(f"xntm{i}", [128, D], BF16) for i in range(2)]
        xn1 = sb1("xn1", [128, KC * TPC], BF16)
        wdkv_bf = sb1("wdkv_bf", [128, KC * DKV], BF16)
        wdq_bf = sb1("wdq_bf", [128, KC * DQ], BF16)
        bdkv_sb = sb1("bdkv_sb", [128, NKV], F32)
        bdq_sb = sb1("bdq_sb", [128, NQ], F32)
        stage = [sb1(f"stg{i}", [128, 512], BF16) for i in range(4)]
        ps_tr = ps1s("ps_tr", [128, 512], BF16)
        ps_p = [ps1s(f"ps_p{i}", [128, 512], F32) for i in range(2)]

        p.dma("gpsimd", wdkv_bf[:].rearrange("p (c n) -> p c n", c=KC), wdkv.rearrange("(c p) n -> p c n", p=128))
        p.dma("gpsimd", wdq_bf[:].rearrange("p (c n) -> p c n", c=KC), wdq.rearrange("(c p) n -> p c n", p=128))
        p.dma("sync", bdkv_sb[:], bdkv)
        p.dma("sync", bdq_sb[:], bdq)

        tr_evacs = 0
        xt_nrm = {0: 0, 1: 0}
        tr_done = {0: 0, 1: 0}
        for i in range(NT):
            par = i % 2
            p.wait("sync", "ac", xt_nrm[par])
            x_ld = p.dma("sync", xt[par][:], x_in[128 * i:128 * (i + 1), :], sem="xd%d" % par)
            p.wait("vector", "xd%d" % par, x_ld)
            p.wait("scalar", "ve", tr_done[par])
            nrm = ln_chain(p, xt[par], bn6[par], mv[par], stdc[par], rstd[par], nb[par], eps_c,
                           lambda s, b=xt[par], o=xn_tm[par], par=par: s.activation(o[:], b[:], AF.Identity, bias=nb[par][:], scale=rstd[par][:]))
            xt_nrm[par] = nrm
            p.wait("tensor", "ac", nrm)
            for j4 in range(4):
                p.wait("tensor", "ve", tr_evacs)
                for j in range(4):
                    c = j4 * 4 + j
                    p.op("tensor", "pe", lambda t, c=c, j=j, b=xn_tm[par]: t.transpose(
                        ps_tr[:, 128 * j:128 * (j + 1)], b[:, 128 * c:128 * (c + 1)], ident_bf[:]))
                p.wait("vector", "pe")
                def evac(v, i=i, j4=j4):
                    dv = xn1[:, :].rearrange("p (c n) -> p c n", n=TPC)
                    return v.tensor_copy(dv[:, j4 * 4:(j4 + 1) * 4, 128 * i:128 * (i + 1)],
                                         ps_tr[:].rearrange("p (c n) -> p c n", n=128))
                tr_evacs = p.op("vector", "ve", evac)
            tr_done[par] = tr_evacs
            if serial:
                p.barrier()

        p.wait("sync", "ve")
        for c in range(KC):
            p.dma("sync", shard_x[128 * c:128 * (c + 1), :], xn1[:, TPC * c:TPC * (c + 1)], sem="shx")
        p.wait("gpsimd", "shx")
        p.op("gpsimd", "cc", lambda g: g.collective_compute(
            "AllGather", ALU.bypass, replica_groups=GROUPS, ins=[shard_x.opt()], outs=[l_all_x.opt()]))

        p.wait("tensor", "gd")
        p.wait("tensor", "ve", tr_evacs)
        evac_t = {0: 0, 1: 0}
        stage_dma = {0: 0, 1: 0, 2: 0, 3: 0}
        for oc in range(KC):
            is_kv = oc < NKV
            for th in range(2):
                par = (2 * oc + th) % 2
                sp4 = (2 * oc + th) % 4
                pp = ps_p[par]
                p.wait("tensor", "ac", evac_t[par])
                for k in range(KC):
                    if is_kv:
                        lhs = wdkv_bf[:, DKV * k + 128 * oc: DKV * k + 128 * (oc + 1)]
                    else:
                        lhs = wdq_bf[:, DQ * k + 128 * (oc - NKV): DQ * k + 128 * (oc - NKV + 1)]
                    p.op("tensor", "pe", lambda t, l=lhs, k=k, pp=pp, th=th: t.matmul(
                        pp[:], l, xn1[:, TPC * k + 512 * th: TPC * k + 512 * (th + 1)],
                        start=(k == 0), stop=(k == KC - 1)))
                p.wait("scalar", "pe")
                p.wait("scalar", "st%d" % sp4, stage_dma[sp4])
                bcol = bdkv_sb[:, oc:oc + 1] if is_kv else bdq_sb[:, oc - NKV:oc - NKV + 1]
                sg = stage[sp4]
                evac_t[par] = p.op("scalar", "ac", lambda s, pp=pp, b=bcol, sg=sg: s.activation(sg[:], pp[:], AF.Identity, bias=b, scale=1.0))
                p.wait("sync", "ac", evac_t[par])
                row = 128 * oc if is_kv else 512 + 128 * (oc - NKV)
                stage_dma[sp4] = p.dma("sync", shard_l[row:row + 128, 512 * th:512 * (th + 1)], sg[:], sem="st%d" % sp4)
            if serial:
                p.barrier()

        for j in range(4):
            p.wait("gpsimd", "st%d" % j)
        p.op("gpsimd", "cc", lambda g: g.collective_compute(
            "AllGather", ALU.bypass, replica_groups=GROUPS, ins=[shard_l.opt()], outs=[l_all_l.opt()]))

    if "t_shard" in tap_aps:
        p.wait("gpsimd", "cc")
        for r in range(4):
            p.dma("gpsimd", tap_aps["t_shard"][4096 * r:4096 * r + 512, :], l_all_l[2048 * r:2048 * r + 512, :])
            p.dma("gpsimd", tap_aps["t_shard"][4096 * r + 512:4096 * r + 2048, :], l_all_l[2048 * r + 512:2048 * (r + 1), :])
            p.dma("gpsimd", tap_aps["t_shard"][4096 * r + 2048:4096 * (r + 1), :], l_all_x[2048 * r:2048 * (r + 1), :])
    if phases == 1:
        p.barrier()
        finalize(nc, p)
        st.close()
        return nc

    p.barrier()
    with ExitStack() as stS:
        sbS = lambda n, s, d: stS.enter_context(nc.sbuf_tensor(n, s, d))
        kC = [sbS(f"kC{s}", [128, SEQ], BF16) for s in range(NSLOT)]
        kR = [sbS(f"kR{s}", [128, SEQ], BF16) for s in range(NSLOT)]
        vS = [sbS(f"vS{s}", [128, SEQ], BF16) for s in range(NSLOT)]

        with ExitStack() as stB:
            sb2 = lambda n, s, d: stB.enter_context(nc.sbuf_tensor(n, s, d))
            ps2s = lambda n, s, d: stB.enter_context(nc.psum_tensor(n, s, d))
            lat_x = [sb2(f"latx{c}", [128, 512], BF16) for c in range(KC)]
            lat_q = [sb2(f"latq{c}", [128, 512], BF16) for c in range(NQ)]
            lat_kv = [sb2(f"latkv{c}", [128, 512], BF16) for c in range(NKV)]
            cos_s = sb2("cos_s", [128, 512], F32)
            sin_s = sb2("sin_s", [128, 512], F32)
            krbf_b = [sb2(f"krbf{i}", [128, 512], BF16) for i in range(2)]
            vfm = sb2("vfm", [128, 512], BF16)
            qbf_b = [sb2(f"qbf{i}", [128, 512], BF16) for i in range(2)]
            t1_b = [sb2(f"t1_{i}", [128, 512], F32) for i in range(2)]
            t2_b = [sb2(f"t2_{i}", [128, 512], F32) for i in range(2)]
            ksq = sb2("ksq", [128, 512], BF16)
            qsq2 = sb2("qsq2", [128, 512], BF16)
            k2 = sb2("k2", [1, NSLOT], F32)
            k2t = sb2("k2t", [1, 1], F32)
            q2m = sb2("q2m", [1, NSLOT * NTS], F32)
            ps_a = [ps2s(f"ps_a{s}", [128, 512], F32) for s in range(NSLOT)]
            ps_rot_b = [ps2s(f"ps_rot{i}", [128, 512], F32) for i in range(2)]
            ps_vt = ps2s("ps_vt", [128, 512], BF16)
            ps_n = ps2s("ps_n", [1, 512], F32)

            p.wait("tensor", "w2d")

            p.op("vector", "ve", lambda v: v.memset(k2[:], 0.0))
            kr_done = q_done = kv_done = 0
            q_cv = {0: 0, 1: 0}
            q_dma = {0: 0, 1: 0}
            for ts in range(NTS):
                ts_ve0 = p.val("ve")
                ts_sd0 = p.val("sd")
                r = ts // 2
                cs = 512 * (ts % 2)
                rb = 4096 * r
                rbx = 2048 * r
                if ts == 0:
                    p.wait("sync", "cc", 1)
                lxs = "lx%d" % (ts % 2)
                lqs = "lq%d" % (ts % 2)
                lks = "lk%d" % (ts % 2)
                p.wait("sync", "pe", kr_done)
                for c in range(KC):
                    p.dma("sync", lat_x[c][:], l_all_x[rbx + 128 * c: rbx + 128 * (c + 1), cs:cs + 512], sem=lxs)
                p.wait("sync", "ve")
                ld = p.dma("sync", cos_s[:], cos_td[:, 512 * ts:512 * (ts + 1)], sem=lxs)
                ldx = p.dma("sync", sin_s[:], sin_td[:, 512 * ts:512 * (ts + 1)], sem=lxs)
                if ts == 0:
                    p.wait("sync", "cc", 2)
                p.wait("sync", "pe", q_done)
                for c in range(NQ):
                    ldq = p.dma("sync", lat_q[c][:], l_all_l[rbx + 512 + 128 * c: rbx + 512 + 128 * (c + 1), cs:cs + 512], sem=lqs)
                p.wait("sync", "pe", kv_done)
                for c in range(NKV):
                    ldkv = p.dma("sync", lat_kv[c][:], l_all_l[rbx + 128 * c: rbx + 128 * (c + 1), cs:cs + 512], sem=lks)
                p.wait("tensor", lxs, ldx)

                for s in range(NSLOT):
                    for k in range(KC):
                        p.op("tensor", "pe", lambda t, s=s, k=k: t.matmul(
                            ps_a[s][:], wkr_bf[:, 512 * k + 128 * s: 512 * k + 128 * (s + 1)], lat_x[k][:],
                            start=(k == 0), stop=(k == KC - 1)))
                kr_done = p.val("pe")
                p.wait("vector", lxs, ldx)
                kr_fin = {0: 0, 1: 0}
                kr_ev = 0
                for s in range(NSLOT):
                    par = s % 2
                    krbf = krbf_b[par]; t1 = t1_b[par]; t2 = t2_b[par]; ps_rot = ps_rot_b[par]
                    p.wait("scalar", "pe", kr_done - (NSLOT - 1 - s) * KC)
                    p.wait("scalar", "ve", ts_ve0)
                    p.wait("scalar", "qs%d" % par, q_dma[par])
                    p.wait("scalar", "ve", kr_fin[par])
                    ev = p.op("scalar", "ac", lambda sc, s=s, krbf=krbf: sc.activation(krbf[:], ps_a[s][:], AF.Identity, bias=bkr_sb[:, s:s + 1], scale=1.0))
                    kr_ev = ev
                    p.wait("tensor", "ac", ev)
                    p.wait("tensor", "ve", kr_fin[par])
                    rt = p.op("tensor", "pe", lambda t, krbf=krbf, ps_rot=ps_rot: t.matmul(ps_rot[:], rotm_bf[:], krbf[:], start=True, stop=True))
                    p.wait("vector", "ac", ev)
                    m1 = p.op("vector", "ve", lambda v, krbf=krbf, t1=t1: v.tensor_mul(t1[:], krbf[:], cos_s[:]))
                    p.wait("vector", "pe", rt)
                    m2 = p.op("vector", "ve", lambda v, t2=t2, ps_rot=ps_rot: v.tensor_mul(t2[:], ps_rot[:], sin_s[:]))
                    p.wait("vector", "ve", m2)
                    kr_fin[par] = p.op("vector", "ve", lambda v, s=s, ts=ts, t1=t1, t2=t2: v.tensor_add(kR[s][:, 512 * ts:512 * (ts + 1)], t1[:], t2[:]))
                p.wait("tensor", "ac", kr_ev)
                krfin_final = dict(kr_fin)

                p.wait("tensor", lqs, ldq)
                for s in range(NSLOT):
                    for k in range(NQ):
                        p.op("tensor", "pe", lambda t, s=s, k=k: t.matmul(
                            ps_a[s][:], wuq_bf[:, 512 * k + 128 * s: 512 * k + 128 * (s + 1)], lat_q[k][:],
                            start=(k == 0), stop=(k == NQ - 1)))
                q_done = p.val("pe")
                nrd_last = 0
                for s in range(NSLOT):
                    par = s % 2
                    qbf = qbf_b[par]; krbf = krbf_b[par]; t1 = t1_b[par]; t2 = t2_b[par]; ps_rot = ps_rot_b[par]
                    p.wait("scalar", "pe", q_done - (NSLOT - 1 - s) * NQ)
                    p.wait("scalar", "ve", ts_ve0)
                    p.wait("scalar", "qs%d" % par, q_dma[par])
                    p.wait("scalar", "ve", q_cv[par])
                    ev = p.op("scalar", "ac", lambda sc, s=s, qbf=qbf: sc.activation(qbf[:], ps_a[s][:], AF.Identity, bias=buq_sb[:, s:s + 1], scale=1.0))
                    q_last_ev = ev
                    p.wait("sync", "ac", ev)
                    p.dma("sync", q_dram[256 * s:256 * s + 128, 512 * ts:512 * (ts + 1)], qbf[:], sem="qs%d" % par)
                    p.wait("tensor", "ac", ev)
                    p.wait("tensor", "ve", q_cv[par])
                    p.wait("tensor", "ve", krfin_final[par])
                    rt = p.op("tensor", "pe", lambda t, qbf=qbf, ps_rot=ps_rot: t.matmul(ps_rot[:], rotm_bf[:], qbf[:], start=True, stop=True))
                    p.wait("vector", "ac", ev)
                    m1 = p.op("vector", "ve", lambda v, qbf=qbf, t1=t1: v.tensor_mul(t1[:], qbf[:], cos_s[:]))
                    p.wait("vector", "pe", rt)
                    m2 = p.op("vector", "ve", lambda v, t2=t2, ps_rot=ps_rot: v.tensor_mul(t2[:], ps_rot[:], sin_s[:]))
                    p.wait("vector", "ve", m2)
                    ad = p.op("vector", "ve", lambda v, t1=t1, t2=t2: v.tensor_add(t1[:], t1[:], t2[:]))
                    p.wait("vector", "ve", ad)
                    cv = p.op("vector", "ve", lambda v, krbf=krbf, t1=t1: v.tensor_copy(krbf[:], t1[:]))
                    q_cv[par] = cv
                    p.wait("sync", "ve", cv)
                    q_dma[par] = p.dma("sync", q_dram[256 * s + 128:256 * s + 256, 512 * ts:512 * (ts + 1)], krbf[:], sem="qs%d" % par)
                    p.wait("scalar", "ve", cv)
                    p.wait("scalar", "pe")
                    n1 = p.op("scalar", "ac", lambda sc, qbf=qbf: sc.activation(qsq2[:], qbf[:], AF.Square))
                    p.wait("tensor", "ac", n1)
                    p.wait("tensor", "ve", nrd_last)
                    p.op("tensor", "pe", lambda t: t.matmul(ps_n[:], ones_cb[:], qsq2[:], start=True, stop=False))
                    p.wait("scalar", "pe")
                    n2 = p.op("scalar", "ac", lambda sc, krbf=krbf: sc.activation(qsq2[:], krbf[:], AF.Square))
                    p.wait("tensor", "ac", n2)
                    nm = p.op("tensor", "pe", lambda t: t.matmul(ps_n[:], ones_cb[:], qsq2[:], start=False, stop=True))
                    p.wait("vector", "pe", nm)
                    nrd_last = p.op("vector", "ve", lambda v, s=s, ts=ts: v.tensor_reduce(q2m[:, 8 * s + ts:8 * s + ts + 1], ps_n[:], AX, ALU.max))
                p.wait("tensor", "ac", q_last_ev)

                p.wait("tensor", lks, ldkv)
                for s in range(NSLOT):
                    for k in range(NKV):
                        p.op("tensor", "pe", lambda t, s=s, k=k: t.matmul(
                            ps_a[s][:], wuk_bf[:, 512 * k + 128 * s: 512 * k + 128 * (s + 1)], lat_kv[k][:],
                            start=(k == 0), stop=(k == NKV - 1)))
                for s in range(NSLOT):
                    p.wait("scalar", "pe")
                    p.op("scalar", "ac", lambda sc, s=s, ts=ts: sc.activation(
                        kC[s][:, 512 * ts:512 * (ts + 1)], ps_a[s][:], AF.Identity, bias=buk_sb[:, s:s + 1], scale=1.0))
                p.wait("tensor", "ac")

                for s in range(NSLOT):
                    for k in range(NKV):
                        p.op("tensor", "pe", lambda t, s=s, k=k: t.matmul(
                            ps_a[s][:], wuv_bf[:, 512 * k + 128 * s: 512 * k + 128 * (s + 1)], lat_kv[k][:],
                            start=(k == 0), stop=(k == NKV - 1)))
                kv_done = p.val("pe")
                for s in range(NSLOT):
                    p.wait("scalar", "pe")
                    ev = p.op("scalar", "ac", lambda sc, s=s: sc.activation(vfm[:], ps_a[s][:], AF.Identity, bias=buv_sb[:, s:s + 1], scale=1.0))
                    p.wait("tensor", "ac", ev)
                    p.wait("tensor", "ve")
                    for j in range(4):
                        p.op("tensor", "pe", lambda t, j=j: t.transpose(ps_vt[:, 128 * j:128 * (j + 1)], vfm[:, 128 * j:128 * (j + 1)], ident_bf[:]))
                    p.wait("vector", "pe")
                    p.op("vector", "ve", lambda v, s=s, ts=ts: v.tensor_copy(vS[s][:, 512 * ts:512 * (ts + 1)], ps_vt[:]))
                    p.wait("scalar", "pe")
                p.wait("tensor", "ac")
                p.wait("scalar", "ve")
                for s in range(NSLOT):
                    p.wait("scalar", "pe")
                    sq1 = p.op("scalar", "ac", lambda sc, s=s, ts=ts: sc.activation(ksq[:], kC[s][:, 512 * ts:512 * (ts + 1)], AF.Square))
                    p.wait("tensor", "ac", sq1)
                    p.wait("tensor", "ve")
                    p.op("tensor", "pe", lambda t: t.matmul(ps_n[:], ones_cb[:], ksq[:], start=True, stop=False))
                    p.wait("scalar", "pe")
                    sq2 = p.op("scalar", "ac", lambda sc, s=s, ts=ts: sc.activation(ksq[:], kR[s][:, 512 * ts:512 * (ts + 1)], AF.Square))
                    p.wait("tensor", "ac", sq2)
                    mm = p.op("tensor", "pe", lambda t: t.matmul(ps_n[:], ones_cb[:], ksq[:], start=False, stop=True))
                    p.wait("vector", "pe", mm)
                    rd = p.op("vector", "ve", lambda v: v.tensor_reduce(k2t[:], ps_n[:], AX, ALU.max))
                    p.wait("vector", "ve", rd)
                    p.op("vector", "ve", lambda v, s=s: v.tensor_tensor(k2[:, s:s + 1], k2[:, s:s + 1], k2t[:], ALU.max))
                    p.wait("vector", "ve")
                p.wait("tensor", "ve")
                if serial:
                    p.barrier()

            p.op("vector", "ve", lambda v: v.tensor_copy(k2p[:], k2[:]))
            p.wait("vector", "ve")
            p.op("vector", "ve", lambda v: v.tensor_copy(q2mp[:], q2m[:]))
            p.barrier()

        with ExitStack() as stA2:
            sb3 = lambda n, s, d: stA2.enter_context(nc.sbuf_tensor(n, s, d))
            ps3s = lambda n, s, d: stA2.enter_context(nc.psum_tensor(n, s, d))
            qc_t = [sb3(f"qc_t{i}", [128, 512], BF16) for i in range(2)]
            qr_t = [sb3(f"qr_t{i}", [128, 512], BF16) for i in range(2)]
            prb = [sb3(f"prb{i}", [128, 512], BF16) for i in range(4)]
            ctxs = sb3("ctxs", [128, SEQ], BF16)
            craw = [sb3(f"craw{i}", [128, 512], F32) for i in range(2)]
            bias_all = sb3("bias_all", [128, NSLOT * NTS], F32)
            b2r = sb3("b2r", [1, NSLOT * NTS], F32)
            srow = [sb3(f"srow{i}", [1, 512], F32) for i in range(2)]
            rrow = [sb3(f"rrow{i}", [1, 512], F32) for i in range(2)]
            bc_sb = [sb3(f"bc_sb{i}", [128, 512], F32) for i in range(2)]
            ps_sc = [ps3s(f"ps_sc{i}", [128, 512], F32) for i in range(3)]
            ps_ctx = [ps3s(f"ps_ctx{i}", [128, 512], F32) for i in range(2)]
            ps_sum = [ps3s(f"ps_sum{i}", [1, 512], F32) for i in range(2)]
            ps_bc = ps3s("ps_bc", [128, 512], F32)

            p.wait("vector", "w2d")
            bb = p.op("vector", "ve", lambda v: v.tensor_tensor(
                b2r[:].rearrange("a (s t) -> a s t", s=NSLOT),
                q2mp[:].rearrange("a (s t) -> a s t", s=NSLOT),
                k2p[:].rearrange("a (s t) -> a s t", t=1).broadcast_to((1, NSLOT, NTS)),
                ALU.mult))
            p.wait("scalar", "ve", bb)
            bsq = p.op("scalar", "ac", lambda sc: sc.activation(b2r[:], b2r[:], AF.Sqrt))
            p.wait("tensor", "ac", bsq)
            bmm = p.op("tensor", "pe", lambda t: t.matmul(ps_bc[:, 0:NSLOT * NTS], ones_rf[:], b2r[:], start=True, stop=True))
            p.wait("scalar", "pe", bmm)
            bev = p.op("scalar", "ac", lambda sc: sc.activation(bias_all[:], ps_bc[:, 0:NSLOT * NTS], AF.Identity, scale=-0.0625))
            p.wait("scalar", "ac", bev)
            p.wait("tensor", "ac", bev)

            exp_cnt = {i: 0 for i in range(3)}
            cons_cnt = {i: 0 for i in range(4)}
            ctx_free = {0: 0, 1: 0}
            qtile_free = {0: 0, 1: 0}
            bc_free = 0
            LOOK = 2

            def load_q(s, qc):
                qp = qc % 2
                p.wait("sync", "pe", qtile_free[qp])
                p.dma("sync", qc_t[qp][:], q_dram[256 * s:256 * s + 128, 512 * qc:512 * (qc + 1)], sem="qt%d" % qp)
                return p.dma("sync", qr_t[qp][:], q_dram[256 * s + 128:256 * s + 256, 512 * qc:512 * (qc + 1)], sem="qt%d" % qp)

            qld = {(0, 0): load_q(0, 0)}
            for s in range(NSLOT):
                for qc in range(NTS):
                    qp = qc % 2
                    p.wait("tensor", "qt%d" % qp, qld[(s, qc)])
                    p.wait("tensor", "ve", ctx_free[qp])
                    bcol = bias_all[:, 8 * s + qc:8 * s + qc + 1]
                    nkt = 4 * (qc + 1)
                    pend = []
                    last_mm = 0

                    for ik in range(nkt):
                        kb, kt = divmod(ik, 4)
                        par = ik % 3
                        ring = ik % 4
                        diag = kb == qc
                        q0 = 128 * kt if diag else 0
                        if pend and len(pend) > LOOK - 1:
                            p.wait("tensor", "ac", pend[0][2])
                        elif exp_cnt[par]:
                            p.wait("tensor", "ac", exp_cnt[par])
                        if len(pend) > LOOK:
                            jk, jring, jex, jq0 = pend.pop(0)
                            p.op("tensor", "pe", lambda t, s=s, jk=jk, jring=jring, qp=qp, jq0=jq0, first=(jk == 0), last=(jk == nkt - 1): t.matmul(
                                ps_ctx[qp][:, jq0:512], vS[s][:, 128 * jk: 128 * (jk + 1)], prb[jring][:, jq0:512], start=first, stop=last))
                            cmm = p.op("tensor", "pe", lambda t, jring=jring, qp=qp, jq0=jq0, first=(jk == 0), last=(jk == nkt - 1): t.matmul(
                                ps_sum[qp][:, jq0:512], ones_cb[:], prb[jring][:, jq0:512], start=first, stop=last))
                            cons_cnt[jring] = cmm
                        p.op("tensor", "pe", lambda t, s=s, kb=kb, kt=kt, par=par, qp=qp, q0=q0: t.matmul(
                            ps_sc[par][:, q0:512], kC[s][:, 512 * kb + 128 * kt: 512 * kb + 128 * (kt + 1)], qc_t[qp][:, q0:512], start=True, stop=False))
                        mm = p.op("tensor", "pe", lambda t, s=s, kb=kb, kt=kt, par=par, qp=qp, q0=q0: t.matmul(
                            ps_sc[par][:, q0:512], kR[s][:, 512 * kb + 128 * kt: 512 * kb + 128 * (kt + 1)], qr_t[qp][:, q0:512], start=False, stop=True))
                        last_mm = mm
                        gate = ("pe", mm)
                        if diag:
                            p.wait("vector", "pe", mm)
                            mk = p.op("vector", "ve", lambda v, kt=kt, par=par, q0=q0: v.tensor_tensor(
                                ps_sc[par][:, q0:512], ps_sc[par][:, q0:512], masks[:, 512 * kt + q0:512 * (kt + 1)], ALU.add))
                            gate = ("ve", mk)
                        p.wait("scalar", gate[0], gate[1])
                        p.wait("scalar", "pe", cons_cnt[ring])
                        ex = p.op("scalar", "ac", lambda sc, par=par, ring=ring, bcol=bcol, q0=q0: sc.activation(
                            prb[ring][:, q0:512], ps_sc[par][:, q0:512], AF.Exp, bias=bcol, scale=0.0625))
                        exp_cnt[par] = ex
                        pend.append((ik, ring, ex, q0))
                    nqc = qc + 1
                    ns = s
                    if nqc == NTS:
                        nqc = 0
                        ns = s + 1
                    if ns < NSLOT:
                        qld[(ns, nqc)] = load_q(ns, nqc)
                    for jk, jring, jex, jq0 in pend:
                        p.wait("tensor", "ac", jex)
                        p.op("tensor", "pe", lambda t, s=s, jk=jk, jring=jring, qp=qp, jq0=jq0, first=(jk == 0), last=(jk == nkt - 1): t.matmul(
                            ps_ctx[qp][:, jq0:512], vS[s][:, 128 * jk: 128 * (jk + 1)], prb[jring][:, jq0:512], start=first, stop=last))
                        cmm = p.op("tensor", "pe", lambda t, jring=jring, qp=qp, jq0=jq0, first=(jk == 0), last=(jk == nkt - 1): t.matmul(
                            ps_sum[qp][:, jq0:512], ones_cb[:], prb[jring][:, jq0:512], start=first, stop=last))
                        cons_cnt[jring] = cmm
                    qtile_free[qp] = last_mm
                    p.wait("vector", "pe")
                    e1 = p.op("vector", "ve", lambda v, qp=qp: v.tensor_copy(srow[qp][:], ps_sum[qp][:]))
                    cr = p.op("vector", "ve", lambda v, qp=qp: v.tensor_copy(craw[qp][:], ps_ctx[qp][:]))
                    p.wait("vector", "ve", e1)
                    e2 = p.op("vector", "ve", lambda v, qp=qp: v.reciprocal(rrow[qp][:], srow[qp][:]))
                    p.wait("tensor", "ve", bc_free)
                    p.wait("tensor", "ve", e2)
                    bm = p.op("tensor", "pe", lambda t, qp=qp: t.matmul(ps_bc[:], ones_rf[:], rrow[qp][:], start=True, stop=True))
                    p.wait("vector", "pe", bm)
                    e3 = p.op("vector", "ve", lambda v, qp=qp: v.tensor_copy(bc_sb[qp][:], ps_bc[:]))
                    bc_free = e3
                    p.wait("vector", "ve", e3)
                    mf = p.op("vector", "ve", lambda v, qc=qc, qp=qp: v.tensor_mul(ctxs[:, 512 * qc:512 * (qc + 1)], craw[qp][:], bc_sb[qp][:]))
                    ctx_free[qp] = cr
                    if serial:
                        p.barrier()
                p.wait("sync", "ve")
                dst = ctx_shardA[128 * s:128 * (s + 1), :] if s < 2 else ctx_shardB[128 * (s - 2):128 * (s - 1), :]
                p.dma("sync", dst, ctxs[:], sem="cxs")
                p.wait("vector", "cxs")
                if s == 1:
                    p.wait("gpsimd", "cxs")
                    p.op("gpsimd", "cc", lambda g: g.collective_compute(
                        "AllGather", ALU.bypass, replica_groups=GROUPS, ins=[ctx_shardA.opt()], outs=[ctx_allA.opt()]))

            p.wait("gpsimd", "cxs")
            p.op("gpsimd", "cc", lambda g: g.collective_compute(
                "AllGather", ALU.bypass, replica_groups=GROUPS, ins=[ctx_shardB.opt()], outs=[ctx_allB.opt()]))

    if "t_qdram" in tap_aps:
        p.wait("gpsimd", "cc")
        p.dma("gpsimd", tap_aps["t_qdram"][:, :], q_dram)
    if "t_ctx" in tap_aps:
        p.wait("gpsimd", "cc")
        for r in range(4):
            for s2 in range(2):
                p.dma("gpsimd", tap_aps["t_ctx"][512 * r + 128 * s2:512 * r + 128 * (s2 + 1), :], ctx_allA[256 * r + 128 * s2:256 * r + 128 * (s2 + 1), :])
                p.dma("gpsimd", tap_aps["t_ctx"][512 * r + 128 * (s2 + 2):512 * r + 128 * (s2 + 3), :], ctx_allB[256 * r + 128 * s2:256 * r + 128 * (s2 + 1), :])
    if phases == 2:
        p.barrier()
        finalize(nc, p)
        st.close()
        return nc

    w2_st.close()
    p.wait("sync", "pe")
    p.wait("sync", "ve")
    p.wait("sync", "ac")
    p.wait("sync", "gd")
    with ExitStack() as stA:
        sbA = lambda n, s, d: stA.enter_context(nc.sbuf_tensor(n, s, d))
        bo_row = sbA("bo_row", [1, D], F32)
        ds_row = sbA("ds_row", [1, D], F32)
        ds_bc = sbA("ds_bc", [128, D], F32)
        xn2 = sbA("xn2", [128, KC * TPC], BF16)
        bn6b = [sbA(f"bn6b{i}", [128, 24], F32) for i in range(2)]
        mvb = [sbA(f"mvb{i}", [128, 2], F32) for i in range(2)]
        stdb = [sbA(f"stdb{i}", [128, 1], F32) for i in range(2)]
        rstdb = [sbA(f"rstdb{i}", [128, 1], F32) for i in range(2)]
        nbb = [sbA(f"nbb{i}", [128, 1], F32) for i in range(2)]

        p.dma("sync", bo_row[:], bo_r)
        p.dma("sync", ds_row[:], ds_r)

        with ExitStack() as stB3:
            sbB = lambda n, s, d: stB3.enter_context(nc.sbuf_tensor(n, s, d))
            psB = lambda n, s, d: stB3.enter_context(nc.psum_tensor(n, s, d))
            wo_bf = sbB("wo_bf", [128, KC * D], BF16)
            cx = sbB("cx", [128, KC * TPC], BF16)
            bo_bc = sbB("bo_bc", [128, D], F32)
            xt3 = [sbB(f"xt3{i}", [128, D], F32) for i in range(2)]
            y_tb = [sbB(f"y_tb{i}", [128, D], F32) for i in range(2)]
            yn_tb1 = sbB("yn_tb1", [128, D], F32)
            yn_tb = [yn_tb1, yn_tb1]
            xn2_tm1 = sbB("xn2_tm1", [128, D], BF16)
            xn2_tm = [xn2_tm1, xn2_tm1]
            ps_o = [psB(f"ps_o{i}", [128, TPC], F32) for i in range(2)]
            ps_tr3 = psB("ps_tr3", [128, 512], BF16)
            ps_bb = psB("ps_bb", [128, 512], F32)

            p.dma("gpsimd", wo_bf[:].rearrange("p (c n) -> p c n", c=KC), wo.rearrange("(c p) n -> p c n", p=128), sem="wod")
            p.wait("gpsimd", "sd")
            p.wait("gpsimd", "cc", 3)

            def ctx_reads_a(g):
                last = None
                with g.register("tok") as tok:
                    g.reg_load(tok, meta_sb[0:1, 0:1])
                    off = g.snap(tok, min_val=0, max_val=3 * TPC)
                    for r in range(4):
                        for s2 in range(2):
                            ch = 4 * r + s2
                            last = g.dma_start(out=cx[:, TPC * ch:TPC * (ch + 1)],
                                               in_=ctx_allA[256 * r + 128 * s2:256 * r + 128 * (s2 + 1), ds(off, TPC)])
                            last.then_inc(SEM["wod"], 16)
                return last
            p.emit("gpsimd", ctx_reads_a)
            p.bump("wod", 16 * 8)

            def ctx_reads_b(g):
                last = None
                with g.register("tok2") as tok:
                    g.reg_load(tok, meta_sb[0:1, 0:1])
                    off = g.snap(tok, min_val=0, max_val=3 * TPC)
                    for r in range(4):
                        for s2 in range(2):
                            ch = 4 * r + s2 + 2
                            last = g.dma_start(out=cx[:, TPC * ch:TPC * (ch + 1)],
                                               in_=ctx_allB[256 * r + 128 * s2:256 * r + 128 * (s2 + 1), ds(off, TPC)])
                            last.then_inc(SEM["wodb"], 16)
                return last
            p.wait("gpsimd", "cc")
            p.emit("gpsimd", ctx_reads_b)
            p.bump("wodb", 16 * 8)

            p.wait("tensor", "sd")
            for h in range(4):
                p.wait("tensor", "ve")
                bm = p.op("tensor", "pe", lambda t, h=h: t.matmul(ps_bb[:], ones_rf[:], bo_row[:, 512 * h:512 * (h + 1)], start=True, stop=True))
                p.wait("vector", "pe", bm)
                p.op("vector", "ve", lambda v, h=h: v.tensor_copy(bo_bc[:, 512 * h:512 * (h + 1)], ps_bb[:]))
                p.wait("tensor", "ve")
                bm = p.op("tensor", "pe", lambda t, h=h: t.matmul(ps_bb[:], ones_rf[:], ds_row[:, 512 * h:512 * (h + 1)], start=True, stop=True))
                p.wait("vector", "pe", bm)
                p.op("vector", "ve", lambda v, h=h: v.tensor_copy(ds_bc[:, 512 * h:512 * (h + 1)], ps_bb[:]))

            if "t_cxv" in tap_aps:
                p.wait("gpsimd", "wod")
                p.wait("gpsimd", "wodb")
                p.dma("gpsimd", tap_aps["t_cxv"][:, :], cx[:])
            p.wait("tensor", "wod")
            chunk_order = [4 * r + s2 for s2 in range(2) for r in range(4)] + [4 * r + s2 + 2 for s2 in range(2) for r in range(4)]
            tr_ev = 0
            oev = {0: 0, 1: 0}
            for i in range(NT):
                p.wait("sync", "ve")
                xl = p.dma("sync", xt3[i % 2][:], x_in[128 * i:128 * (i + 1), :], sem="xd%d" % (i % 2))
                for oh in range(2):
                    p.wait("tensor", "ve", oev[oh])
                    for ci, c in enumerate(chunk_order):
                        if i == 0 and ci == 8:
                            p.wait("tensor", "wodb")
                        for h in range(2):
                            p.op("tensor", "pe", lambda t, c=c, h=h, i=i, oh=oh, ci=ci: t.matmul(
                                ps_o[oh][:, 512 * h:512 * (h + 1)],
                                cx[:, TPC * c + 128 * i: TPC * c + 128 * (i + 1)],
                                wo_bf[:, D * c + TPC * oh + 512 * h: D * c + TPC * oh + 512 * (h + 1)],
                                start=(ci == 0), stop=(ci == KC - 1)))
                    p.wait("vector", "pe")
                    p.wait("vector", "ac")
                    oev[oh] = p.op("vector", "ve", lambda v, i=i, oh=oh: v.tensor_add(
                        y_tb[i % 2][:, TPC * oh:TPC * (oh + 1)], ps_o[oh][:], bo_bc[:, TPC * oh:TPC * (oh + 1)]))
                p.wait("vector", "ve")
                n2 = ln_chain(p, y_tb[i % 2], bn6b[i % 2], mvb[i % 2], stdb[i % 2], rstdb[i % 2], nbb[i % 2], eps_c,
                              lambda s, i=i: s.activation(yn_tb[i % 2][:], y_tb[i % 2][:], AF.Identity, bias=nbb[i % 2][:], scale=rstdb[i % 2][:]))
                p.wait("vector", "ac", n2)
                p.wait("vector", "xd%d" % (i % 2), xl)
                x2a = p.op("vector", "ve", lambda v, b=xt3[i % 2], i=i: v.tensor_add(b[:], b[:], yn_tb[i % 2][:]))
                p.wait("sync", "ve", x2a)
                p.dma("sync", x2_dram[128 * i:128 * (i + 1), :], xt3[i % 2][:], sem="x2d")
                p.wait("scalar", "ve", tr_ev)
                n3 = ln_chain(p, xt3[i % 2], bn6b[i % 2], mvb[i % 2], stdb[i % 2], rstdb[i % 2], nbb[i % 2], eps_c,
                              lambda s, b=xt3[i % 2], i=i: s.activation(xn2_tm[i % 2][:], b[:], AF.Identity, bias=nbb[i % 2][:], scale=rstdb[i % 2][:]))
                p.wait("tensor", "ac", n3)
                for j4 in range(4):
                    p.wait("tensor", "ve", tr_ev)
                    for j in range(4):
                        c = j4 * 4 + j
                        p.op("tensor", "pe", lambda t, c=c, j=j, i=i: t.transpose(
                            ps_tr3[:, 128 * j:128 * (j + 1)], xn2_tm[i % 2][:, 128 * c:128 * (c + 1)], ident_bf[:]))
                    p.wait("vector", "pe")
                    def evac3(v, i=i, j4=j4):
                        dv = xn2[:, :].rearrange("p (c n) -> p c n", n=TPC)
                        return v.tensor_copy(dv[:, j4 * 4:(j4 + 1) * 4, 128 * i:128 * (i + 1)],
                                             ps_tr3[:].rearrange("p (c n) -> p c n", n=128))
                    tr_ev = p.op("vector", "ve", evac3)
                if serial:
                    p.barrier()

        p.barrier()
        with ExitStack() as stC:
            sbC = lambda n, s, d: stC.enter_context(nc.sbuf_tensor(n, s, d))
            psC = lambda n, s, d: stC.enter_context(nc.psum_tensor(n, s, d))
            acc = sbC("acc", [128, NT * D], F32)
            wg_bf = [sbC(f"wg{i}", [128, KC * 128], BF16) for i in range(2)]
            wu_bf = [sbC(f"wu{i}", [128, KC * 128], BF16) for i in range(2)]
            wd_bf = [[sbC(f"wd{g}_{i}", [128, D], BF16) for i in range(GRP)] for g in range(2)]
            hdn = [sbC(f"hdn{i}", [128, TPC], BF16) for i in range(GRP)]
            silu_g = sbC("silu_g", [128, TPC], BF16)
            gs_sb = sbC("gs_sb", [128, IC], F32)
            us_sb = sbC("us_sb", [128, IC], F32)
            xt41 = sbC("xt41", [128, D], F32)
            xt4 = [xt41, xt41]
            y41 = sbC("y41", [128, D], F32)
            y4 = [y41, y41]
            yn41 = sbC("yn41", [128, D], F32)
            yn4 = [yn41, yn41]
            ps_g = psC("ps_g", [128, TPC], F32)
            ps_u = psC("ps_u", [128, TPC], F32)
            ps_d = [psC(f"ps_d{i}", [128, TPC], F32) for i in range(2)]

            p.dma("sync", gs_sb[:], gs)
            p.dma("sync", us_sb[:], us)
            p.op("vector", "ve", lambda v: v.memset(acc[:], 0.0))
            p.wait("tensor", "ve")
            p.wait("tensor", "sd")

            groups = [list(range(g, min(g + GRP, IC))) for g in range(0, IC, GRP)]
            dn_done = {0: 0, 1: 0}
            hv_done = {0: 0, 1: 0}
            for gi, grp in enumerate(groups):
                gp = gi % 2
                for ii, icx in enumerate(grp):
                    p.wait("gpsimd", "ve", hv_done[ii % 2])
                    p.wait("gpsimd", "pe", dn_done[gp])
                    wfs = "wf%d" % (ii % 2)
                    p.dma("gpsimd", wg_bf[ii % 2][:].rearrange("p (c n) -> p c n", c=KC),
                          gwt[:, 128 * icx:128 * (icx + 1)].rearrange("(c p) n -> p c n", p=128), sem=wfs)
                    p.dma("gpsimd", wu_bf[ii % 2][:].rearrange("p (c n) -> p c n", c=KC),
                          upt[:, 128 * icx:128 * (icx + 1)].rearrange("(c p) n -> p c n", p=128), sem=wfs)
                    wld = p.dma("gpsimd", wd_bf[gp][ii][:], dwt[128 * icx:128 * (icx + 1), :], sem=wfs)
                    p.wait("tensor", wfs, wld)
                    p.wait("tensor", "ve", hv_done[ii % 2])
                    for k in range(KC):
                        for th in range(2):
                            p.op("tensor", "pe", lambda t, k=k, th=th, ii=ii: t.matmul(
                                ps_g[:, 512 * th:512 * (th + 1)], wg_bf[ii % 2][:, 128 * k:128 * (k + 1)],
                                xn2[:, TPC * k + 512 * th: TPC * k + 512 * (th + 1)],
                                start=(k == 0), stop=(k == KC - 1)))
                    for k in range(KC):
                        for th in range(2):
                            p.op("tensor", "pe", lambda t, k=k, th=th, ii=ii: t.matmul(
                                ps_u[:, 512 * th:512 * (th + 1)], wu_bf[ii % 2][:, 128 * k:128 * (k + 1)],
                                xn2[:, TPC * k + 512 * th: TPC * k + 512 * (th + 1)],
                                start=(k == 0), stop=(k == KC - 1)))
                    p.wait("scalar", "pe")
                    p.wait("scalar", "ve", hv_done[ii % 2])
                    sl = p.op("scalar", "ac", lambda s, icx=icx: s.activation(silu_g[:], ps_g[:], AF.Silu, scale=gs_sb[:, icx:icx + 1]))
                    p.wait("vector", "ac", sl)
                    p.wait("vector", "pe")
                    hv = p.op("vector", "ve", lambda v, ii=ii, icx=icx: v.scalar_tensor_tensor(
                        hdn[ii][:], ps_u[:], us_sb[:, icx:icx + 1], silu_g[:], ALU.mult, ALU.mult))
                    hv_done[ii % 2] = hv
                    p.wait("tensor", "ve", hv)
                dev = {0: 0, 1: 0}
                for tt in range(NT):
                    for oh in range(2):
                        dp = oh
                        p.wait("tensor", "ve", dev[dp])
                        for ii in range(len(grp)):
                            for th in range(2):
                                p.op("tensor", "pe", lambda t, ii=ii, tt=tt, th=th, oh=oh, dp=dp, gp=gp, first=(ii == 0), last=(ii == len(grp) - 1): t.matmul(
                                    ps_d[dp][:, 512 * th:512 * (th + 1)], hdn[ii][:, 128 * tt:128 * (tt + 1)],
                                    wd_bf[gp][ii][:, TPC * oh + 512 * th: TPC * oh + 512 * (th + 1)],
                                    start=first, stop=last))
                        p.wait("vector", "pe")
                        dev[dp] = p.op("vector", "ve", lambda v, tt=tt, oh=oh, dp=dp: v.tensor_add(
                            acc[:, D * tt + TPC * oh:D * tt + TPC * (oh + 1)], acc[:, D * tt + TPC * oh:D * tt + TPC * (oh + 1)], ps_d[dp][:]))
                dn_done[gp] = p.val("pe")
                if serial:
                    p.barrier()

            if "t_xn2" in tap_aps:
                p.barrier()
                p.dma("gpsimd", tap_aps["t_xn2"][:, :], xn2[:])
            if "t_acc" in tap_aps:
                p.barrier()
                p.dma("sync", tap_aps["t_acc"][:, :], acc[:])
            if "t_x2" in tap_aps:
                p.barrier()
                p.dma("sync", tap_aps["t_x2"][:, :], x2_dram)
            p.wait("sync", "x2d")
            for i in range(NT):
                p.wait("sync", "ve")
                xl = p.dma("sync", xt4[i % 2][:], x2_dram[128 * i:128 * (i + 1), :], sem="xd%d" % (i % 2))
                p.wait("vector", "ac")
                ym = p.op("vector", "ve", lambda v, i=i: v.tensor_mul(y4[i % 2][:], acc[:, D * i:D * (i + 1)], ds_bc[:]))
                p.wait("vector", "ve", ym)
                n4 = ln_chain(p, y4[i % 2], bn6b[i % 2], mvb[i % 2], stdb[i % 2], rstdb[i % 2], nbb[i % 2], eps_c,
                              lambda s, i=i: s.activation(yn4[i % 2][:], y4[i % 2][:], AF.Identity, bias=nbb[i % 2][:], scale=rstdb[i % 2][:]))
                p.wait("vector", "ac", n4)
                p.wait("vector", "xd%d" % (i % 2), xl)
                fo = p.op("vector", "ve", lambda v, i=i: v.tensor_add(xt4[i % 2][:], xt4[i % 2][:], yn4[i % 2][:]))
                p.wait("sync", "ve", fo)
                p.dma("sync", out[128 * i:128 * (i + 1), :], xt4[i % 2][:])
                if serial:
                    p.barrier()

    finalize(nc, p)
    st.close()
    return nc


def finalize(nc, p):
    p.barrier()
    with nc.Block() as block:
        @block.sync
        def _(e):
            for f in p.q["sync"]:
                f(e)

        @block.gpsimd
        def _(e):
            for f in p.q["gpsimd"]:
                f(e)

        @block.scalar
        def _(e):
            for f in p.q["scalar"]:
                f(e)

        @block.vector
        def _(e):
            for f in p.q["vector"]:
                f(e)

        @block.tensor
        def _(e):
            for f in p.q["tensor"]:
                f(e)


def make_consts():
    ident = np.eye(128, dtype=np.float32)
    rotm = np.zeros((128, 128), np.float32)
    for j in range(64):
        rotm[2 * j + 1, 2 * j] = -1.0
        rotm[2 * j, 2 * j + 1] = 1.0
    dd = np.arange(128) // 2
    inv = 1.0 / ROPE_BASE ** (dd.astype(np.float64) * 2.0 / 128.0)
    tpos = np.arange(SEQ, dtype=np.float64)
    ang = inv[:, None] * tpos[None, :]
    cos_t = np.cos(ang).astype(np.float32)
    sin_t = np.sin(ang).astype(np.float32)
    kk = np.arange(128)[:, None]
    qq = np.arange(512)[None, :]
    mask = np.concatenate([np.where(kt * 128 + kk > qq, NEG, 0.0) for kt in range(4)], axis=1).astype(np.float32)
    return dict(identc=ident, rotmc=rotm, cos_t=cos_t, sin_t=sin_t, mask_t=mask,
                ones_c=np.ones((128, 1), np.float32), ones_r=np.ones((1, 128), np.float32),
                eps_cd=np.full((128, 1), 1e-5, np.float32))


def cols(v, n):
    return np.ascontiguousarray(np.asarray(v, np.float32).reshape(n, 128).T)


def make_inmaps(inp):
    x = np.asarray(inp["L0_attn_h"], np.float32)
    consts = make_consts()
    g = lambda k: np.ascontiguousarray(np.asarray(inp[k], np.float32))
    wdkv_f = g("L0_attn_W_DKV"); bdkv_f = cols(inp["L0_attn_b_DKV"], NKV)
    wdq_f = g("L0_attn_W_DQ"); bdq_f = cols(inp["L0_attn_b_DQ"], NQ)
    wkr_f = g("L0_attn_W_KR"); bkr_f = np.asarray(inp["L0_attn_b_KR"], np.float32)
    wuk_f = g("L0_attn_W_UK"); buk_f = np.asarray(inp["L0_attn_b_UK"], np.float32)
    wuv_f = g("L0_attn_W_UV"); buv_f = np.asarray(inp["L0_attn_b_UV"], np.float32)
    wuq_f = g("L0_attn_W_UQ"); buq_f = np.asarray(inp["L0_attn_b_UQ"], np.float32)
    wo_f = g("L0_attn_W_O"); bo_f = np.asarray(inp["L0_attn_b_O"], np.float32).reshape(1, D)
    gwt_f = np.ascontiguousarray(g("L0_ffn_gate_w").T)
    upt_f = np.ascontiguousarray(g("L0_ffn_up_w").T)
    dwt_f = np.ascontiguousarray(g("L0_ffn_down_w").T)
    gs_f = cols(inp["L0_ffn_gate_s"], IC)
    us_f = cols(inp["L0_ffn_up_s"], IC)
    ds_f = np.asarray(inp["L0_ffn_down_s"], np.float32).reshape(1, D)

    maps = []
    for c in range(8):
        b, m = divmod(c, 4)
        sl = slice(512 * m, 512 * (m + 1))
        maps.append(dict(
            x=np.ascontiguousarray(x[b, TPC * m:TPC * (m + 1), :]),
            meta=np.array([[TPC * m]], np.int32),
            wdkv=wdkv_f, bdkv=bdkv_f, wdq=wdq_f, bdq=bdq_f,
            wkr=np.ascontiguousarray(wkr_f[:, sl]), bkr=cols(bkr_f[sl], NSLOT),
            wuk=np.ascontiguousarray(wuk_f[:, sl]), buk=cols(buk_f[sl], NSLOT),
            wuv=np.ascontiguousarray(wuv_f[:, sl]), buv=cols(buv_f[sl], NSLOT),
            wuq=np.ascontiguousarray(wuq_f[:, sl]), buq=cols(buq_f[sl], NSLOT),
            wo=wo_f, bo_r=bo_f, gwt=gwt_f, upt=upt_f, dwt=dwt_f,
            gs=gs_f, us=us_f, ds_r=ds_f,
            **consts,
        ))
    return maps


def kernel(**inp):
    nc = build_nc(phases=3)
    maps = make_inmaps(inp)
    res = run_bass_kernel_spmd(nc, maps, core_ids=list(range(8)))
    outv = np.empty((2, SEQ, D), np.float32)
    for c in range(8):
        b, m = divmod(c, 4)
        outv[b, TPC * m:TPC * (m + 1), :] = res.results[c]["out"]
    return outv


# version 08
# speedup vs baseline: 1.5337x; 1.0043x over previous
import numpy as np
from contextlib import ExitStack
import concourse.bass as bass
import concourse.mybir as mybir
from concourse.bass import ds
from concourse.bass_utils import run_bass_kernel_spmd

F32 = mybir.dt.float32
BF16 = mybir.dt.bfloat16
I32 = mybir.dt.int32
AX = mybir.AxisListType.X
ALU = mybir.AluOpType
AF = mybir.ActivationFunctionType

D = 2048
SEQ = 4096
TPC = 1024
NT = TPC // 128
KC = D // 128
DKV = 512
NKV = DKV // 128
DQ = 1536
NQ = DQ // 128
NSLOT = 4
NTS = SEQ // 512
INTER = 5632
IC = INTER // 128
GRP = 4
ROPE_BASE = 10000.0
GROUPS = [[0, 1, 2, 3], [4, 5, 6, 7]]
NEG = -30000.0


class Plan:
    ENGINES = ("sync", "gpsimd", "scalar", "vector", "tensor")

    def __init__(self):
        self.q = {e: [] for e in self.ENGINES}
        self.cnt = {}

    def val(self, sem):
        return self.cnt.get(sem, 0)

    def bump(self, sem, n):
        self.cnt[sem] = self.val(sem) + n
        return self.cnt[sem]

    def emit(self, eng, f):
        self.q[eng].append(f)

    def wait(self, eng, sem, v=None):
        v = self.val(sem) if v is None else v
        if v <= 0:
            return
        self.emit(eng, lambda e, s=sem, vv=v: e.wait_ge(SEM[s], vv))

    def dma(self, eng, out, in_, sem=None, **kw):
        sem = sem or ("gd" if eng == "gpsimd" else ("ad" if eng == "scalar" else "sd"))
        t = self.bump(sem, 16)
        self.emit(eng, lambda e, o=out, i=in_, s=sem, k=kw: e.dma_start(out=o, in_=i, **k).then_inc(SEM[s], 16))
        return t

    def op(self, eng, sem, build):
        t = self.bump(sem, 1)
        self.emit(eng, lambda e, b=build, s=sem: b(e).then_inc(SEM[s], 1))
        return t

    def barrier(self, engines=None):
        snap = {s: v for s, v in self.cnt.items() if v > 0}
        for eng in (engines or self.ENGINES):
            for s, v in snap.items():
                self.wait(eng, s, v)


SEM = {}


def ln_chain(p, src, bn6, mv, stdc, rstd, nb, eps_c, out_fn):
    for j in range(4):
        p.op("vector", "ve", lambda v, j=j: v.bn_stats(bn6[:, 6 * j:6 * j + 6], src[:, 512 * j:512 * (j + 1)]))
    p.wait("vector", "ve")
    p.wait("vector", "ac")
    p.op("vector", "ve", lambda v: v.bn_aggr(mv[:], bn6[:]))
    p.wait("scalar", "ve")
    sq = p.op("scalar", "ac", lambda s: s.activation(stdc[:], mv[:, 1:2], AF.Sqrt, bias=eps_c[:], scale=1.0))
    p.wait("vector", "ac", sq)
    p.op("vector", "ve", lambda v: v.reciprocal(rstd[:], stdc[:]))
    p.wait("vector", "ve")
    p.op("vector", "ve", lambda v: v.scalar_tensor_tensor(nb[:], mv[:, 0:1], -1.0, rstd[:], ALU.mult, ALU.mult))
    p.wait("scalar", "ve")
    return p.op("scalar", "ac", out_fn)


def build_nc(phases=3, taps=(), serial=False):
    nc = bass.Bass("TRN2", target_bir_lowering=False)
    p = Plan()

    def din(name, shape, dt=F32):
        return nc.dram_tensor(name, shape, dt, kind="ExternalInput").ap()

    x_in = din("x", [TPC, D])
    meta = din("meta", [1, 1], I32)
    wdkv = din("wdkv", [D, DKV])
    bdkv = din("bdkv", [128, NKV])
    wdq = din("wdq", [D, DQ])
    bdq = din("bdq", [128, NQ])
    wkr = din("wkr", [D, 512])
    bkr = din("bkr", [128, NSLOT])
    wuk = din("wuk", [DKV, 512])
    buk = din("buk", [128, NSLOT])
    wuv = din("wuv", [DKV, 512])
    buv = din("buv", [128, NSLOT])
    wuq = din("wuq", [DQ, 512])
    buq = din("buq", [128, NSLOT])
    wo = din("wo", [D, D])
    bo_r = din("bo_r", [1, D])
    gwt = din("gwt", [D, INTER])
    upt = din("upt", [D, INTER])
    dwt = din("dwt", [INTER, D])
    gs = din("gs", [128, IC])
    us = din("us", [128, IC])
    ds_r = din("ds_r", [1, D])
    identc = din("identc", [128, 128])
    rotmc = din("rotmc", [128, 128])
    cos_td = din("cos_t", [128, SEQ])
    sin_td = din("sin_t", [128, SEQ])
    mask_t = din("mask_t", [128, 4 * 512])
    ones_cd = din("ones_c", [128, 1])
    ones_rd = din("ones_r", [1, 128])
    eps_cd = din("eps_cd", [128, 1])

    out = nc.dram_tensor("out", [TPC, D], F32, kind="ExternalOutput").ap()

    shard_x = nc.dram_tensor("shard_x", [2048, TPC], BF16, kind="Internal").ap()
    shard_l = nc.dram_tensor("shard_l", [2048, TPC], BF16, kind="Internal").ap()
    l_all_x = nc.dram_tensor("l_all_x", [4 * 2048, TPC], BF16, kind="Internal").ap()
    l_all_l = nc.dram_tensor("l_all_l", [4 * 2048, TPC], BF16, kind="Internal").ap()
    q_dram = nc.dram_tensor("q_dram", [NSLOT * 256, SEQ], BF16, kind="Internal").ap()
    ctx_shardA = nc.dram_tensor("ctx_shardA", [256, SEQ], BF16, kind="Internal").ap()
    ctx_shardB = nc.dram_tensor("ctx_shardB", [256, SEQ], BF16, kind="Internal").ap()
    ctx_allA = nc.dram_tensor("ctx_allA", [1024, SEQ], BF16, kind="Internal").ap()
    ctx_allB = nc.dram_tensor("ctx_allB", [1024, SEQ], BF16, kind="Internal").ap()
    x2_dram = nc.dram_tensor("x2_dram", [TPC, D], F32, kind="Internal").ap()

    tap_aps = {}
    if "shard" in taps:
        tap_aps["t_shard"] = nc.dram_tensor("t_shard", [4 * 4096, TPC], F32, kind="ExternalOutput").ap()
    if "qdram" in taps:
        tap_aps["t_qdram"] = nc.dram_tensor("t_qdram", [NSLOT * 256, SEQ], F32, kind="ExternalOutput").ap()
    if "ctx" in taps:
        tap_aps["t_ctx"] = nc.dram_tensor("t_ctx", [2048, SEQ], F32, kind="ExternalOutput").ap()
    if "x2" in taps:
        tap_aps["t_x2"] = nc.dram_tensor("t_x2", [TPC, D], F32, kind="ExternalOutput").ap()
    if "xn2" in taps:
        tap_aps["t_xn2"] = nc.dram_tensor("t_xn2", [128, KC * TPC], F32, kind="ExternalOutput").ap()
    if "acc" in taps:
        tap_aps["t_acc"] = nc.dram_tensor("t_acc", [128, NT * D], F32, kind="ExternalOutput").ap()
    if "cxv" in taps:
        tap_aps["t_cxv"] = nc.dram_tensor("t_cxv", [128, KC * TPC], F32, kind="ExternalOutput").ap()

    st = ExitStack()
    sbuf = lambda n, s, d: st.enter_context(nc.sbuf_tensor(n, s, d))

    ident_bf = sbuf("ident_bf", [128, 128], BF16)
    rotm_bf = sbuf("rotm_bf", [128, 128], BF16)
    ones_cb = sbuf("ones_cb", [128, 1], BF16)
    ones_rf = sbuf("ones_rf", [1, 128], F32)
    eps_c = sbuf("eps_c", [128, 1], F32)
    meta_sb = st.enter_context(nc.sbuf_tensor("meta_sb", [1, 1], I32))
    k2p = sbuf("k2p", [1, NSLOT], F32)
    q2mp = sbuf("q2mp", [1, NSLOT * NTS], F32)

    for s in ("sd", "gd", "ad", "cc", "pe", "ve", "ac",
              "xd0", "xd1", "st0", "st1", "st2", "st3",
              "lx0", "lx1", "lq0", "lq1", "lk0", "lk1",
              "qt0", "qt1", "qs0", "qs1", "w2d", "wod",
              "wf0", "wf1", "shx", "cxs", "x2d", "wodb"):
        SEM[s] = st.enter_context(nc.semaphore(s))

    p.dma("gpsimd", ident_bf[:], identc)
    p.dma("gpsimd", rotm_bf[:], rotmc)
    p.dma("gpsimd", ones_cb[:], ones_cd)
    p.dma("sync", ones_rf[:], ones_rd)
    p.dma("sync", eps_c[:], eps_cd)
    p.dma("sync", meta_sb[:], meta)

    w2_st = ExitStack()
    sbr = lambda n, s, d: w2_st.enter_context(nc.sbuf_tensor(n, s, d, side="right"))
    wkr_bf = sbr("wkr_bf", [128, KC * 512], BF16)
    wuk_bf = sbr("wuk_bf", [128, NKV * 512], BF16)
    wuv_bf = sbr("wuv_bf", [128, NKV * 512], BF16)
    wuq_bf = sbr("wuq_bf", [128, NQ * 512], BF16)
    bkr_sb = sbr("bkr_sb", [128, NSLOT], F32)
    buk_sb = sbr("buk_sb", [128, NSLOT], F32)
    buv_sb = sbr("buv_sb", [128, NSLOT], F32)
    buq_sb = sbr("buq_sb", [128, NSLOT], F32)
    masks = sbr("masks", [128, 2048], F32)
    p.dma("gpsimd", wkr_bf[:].rearrange("p (c n) -> p c n", c=KC), wkr.rearrange("(c p) n -> p c n", p=128), sem="w2d")
    p.dma("gpsimd", wuk_bf[:].rearrange("p (c n) -> p c n", c=NKV), wuk.rearrange("(c p) n -> p c n", p=128), sem="w2d")
    p.dma("gpsimd", wuv_bf[:].rearrange("p (c n) -> p c n", c=NKV), wuv.rearrange("(c p) n -> p c n", p=128), sem="w2d")
    p.dma("gpsimd", wuq_bf[:].rearrange("p (c n) -> p c n", c=NQ), wuq.rearrange("(c p) n -> p c n", p=128), sem="w2d")
    p.dma("gpsimd", masks[:], mask_t, sem="w2d")
    p.dma("sync", bkr_sb[:], bkr)
    p.dma("sync", buk_sb[:], buk)
    p.dma("sync", buv_sb[:], buv)
    p.dma("sync", buq_sb[:], buq)

    with ExitStack() as st1:
        sb1 = lambda n, s, d: st1.enter_context(nc.sbuf_tensor(n, s, d))
        ps1s = lambda n, s, d: st1.enter_context(nc.psum_tensor(n, s, d))
        xt = [sb1(f"xt{i}", [128, D], F32) for i in range(2)]
        bn6 = [sb1(f"bn6_{i}", [128, 24], F32) for i in range(2)]
        mv = [sb1(f"mv_{i}", [128, 2], F32) for i in range(2)]
        stdc = [sb1(f"stdc_{i}", [128, 1], F32) for i in range(2)]
        rstd = [sb1(f"rstd_{i}", [128, 1], F32) for i in range(2)]
        nb = [sb1(f"nb_{i}", [128, 1], F32) for i in range(2)]
        xn_tm = [sb1(f"xntm{i}", [128, D], BF16) for i in range(2)]
        xn1 = sb1("xn1", [128, KC * TPC], BF16)
        wdkv_bf = sb1("wdkv_bf", [128, KC * DKV], BF16)
        wdq_bf = sb1("wdq_bf", [128, KC * DQ], BF16)
        bdkv_sb = sb1("bdkv_sb", [128, NKV], F32)
        bdq_sb = sb1("bdq_sb", [128, NQ], F32)
        stage = [sb1(f"stg{i}", [128, 512], BF16) for i in range(4)]
        ps_tr = ps1s("ps_tr", [128, 512], BF16)
        ps_p = [ps1s(f"ps_p{i}", [128, 512], F32) for i in range(2)]

        p.dma("gpsimd", wdkv_bf[:].rearrange("p (c n) -> p c n", c=KC), wdkv.rearrange("(c p) n -> p c n", p=128))
        p.dma("gpsimd", wdq_bf[:].rearrange("p (c n) -> p c n", c=KC), wdq.rearrange("(c p) n -> p c n", p=128))
        p.dma("sync", bdkv_sb[:], bdkv)
        p.dma("sync", bdq_sb[:], bdq)

        tr_evacs_box = {"v": 0}
        xt_nrm = {0: 0, 1: 0}
        tr_done = {0: 0, 1: 0}
        evac_t = {0: 0, 1: 0}
        stage_dma = {0: 0, 1: 0, 2: 0, 3: 0}

        def do_tile(i):
            par = i % 2
            p.wait("sync", "ac", xt_nrm[par])
            x_ld = p.dma("sync", xt[par][:], x_in[128 * i:128 * (i + 1), :], sem="xd%d" % par)
            p.wait("vector", "xd%d" % par, x_ld)
            p.wait("scalar", "ve", tr_done[par])
            nrm = ln_chain(p, xt[par], bn6[par], mv[par], stdc[par], rstd[par], nb[par], eps_c,
                           lambda s, b=xt[par], o=xn_tm[par], par=par: s.activation(o[:], b[:], AF.Identity, bias=nb[par][:], scale=rstd[par][:]))
            xt_nrm[par] = nrm
            p.wait("tensor", "ac", nrm)
            for j4 in range(4):
                p.wait("tensor", "ve", tr_evacs_box["v"])
                for j in range(4):
                    c = j4 * 4 + j
                    p.op("tensor", "pe", lambda t, c=c, j=j, b=xn_tm[par]: t.transpose(
                        ps_tr[:, 128 * j:128 * (j + 1)], b[:, 128 * c:128 * (c + 1)], ident_bf[:]))
                p.wait("vector", "pe")
                def evac(v, i=i, j4=j4):
                    dv = xn1[:, :].rearrange("p (c n) -> p c n", n=TPC)
                    return v.tensor_copy(dv[:, j4 * 4:(j4 + 1) * 4, 128 * i:128 * (i + 1)],
                                         ps_tr[:].rearrange("p (c n) -> p c n", n=128))
                tr_evacs_box["v"] = p.op("vector", "ve", evac)
            tr_done[par] = tr_evacs_box["v"]
            if serial:
                p.barrier()

        def do_proj(th):
            p.wait("tensor", "ve", tr_evacs_box["v"])
            for oc in range(KC):
                is_kv = oc < NKV
                par = (2 * oc + th) % 2
                sp4 = (2 * oc + th) % 4
                pp = ps_p[par]
                p.wait("tensor", "ac", evac_t[par])
                for k in range(KC):
                    if is_kv:
                        lhs = wdkv_bf[:, DKV * k + 128 * oc: DKV * k + 128 * (oc + 1)]
                    else:
                        lhs = wdq_bf[:, DQ * k + 128 * (oc - NKV): DQ * k + 128 * (oc - NKV + 1)]
                    p.op("tensor", "pe", lambda t, l=lhs, k=k, pp=pp, th=th: t.matmul(
                        pp[:], l, xn1[:, TPC * k + 512 * th: TPC * k + 512 * (th + 1)],
                        start=(k == 0), stop=(k == KC - 1)))
                p.wait("scalar", "pe")
                p.wait("scalar", "st%d" % sp4, stage_dma[sp4])
                bcol = bdkv_sb[:, oc:oc + 1] if is_kv else bdq_sb[:, oc - NKV:oc - NKV + 1]
                sg = stage[sp4]
                evac_t[par] = p.op("scalar", "ac", lambda s, pp=pp, b=bcol, sg=sg: s.activation(sg[:], pp[:], AF.Identity, bias=b, scale=1.0))
                p.wait("sync", "ac", evac_t[par])
                row = 128 * oc if is_kv else 512 + 128 * (oc - NKV)
                stage_dma[sp4] = p.dma("sync", shard_l[row:row + 128, 512 * th:512 * (th + 1)], sg[:], sem="st%d" % sp4)
                if serial:
                    p.barrier()

        p.wait("tensor", "gd")
        for i in range(4):
            do_tile(i)
        do_proj(0)
        for i in range(4, NT):
            do_tile(i)
        p.wait("sync", "ve")
        for c in range(KC):
            p.dma("sync", shard_x[128 * c:128 * (c + 1), :], xn1[:, TPC * c:TPC * (c + 1)], sem="shx")
        p.wait("gpsimd", "shx")
        p.op("gpsimd", "cc", lambda g: g.collective_compute(
            "AllGather", ALU.bypass, replica_groups=GROUPS, ins=[shard_x.opt()], outs=[l_all_x.opt()]))
        do_proj(1)

        for j in range(4):
            p.wait("gpsimd", "st%d" % j)
        p.op("gpsimd", "cc", lambda g: g.collective_compute(
            "AllGather", ALU.bypass, replica_groups=GROUPS, ins=[shard_l.opt()], outs=[l_all_l.opt()]))

    if "t_shard" in tap_aps:
        p.wait("gpsimd", "cc")
        for r in range(4):
            p.dma("gpsimd", tap_aps["t_shard"][4096 * r:4096 * r + 512, :], l_all_l[2048 * r:2048 * r + 512, :])
            p.dma("gpsimd", tap_aps["t_shard"][4096 * r + 512:4096 * r + 2048, :], l_all_l[2048 * r + 512:2048 * (r + 1), :])
            p.dma("gpsimd", tap_aps["t_shard"][4096 * r + 2048:4096 * (r + 1), :], l_all_x[2048 * r:2048 * (r + 1), :])
    if phases == 1:
        p.barrier()
        finalize(nc, p)
        st.close()
        return nc

    p.barrier()
    with ExitStack() as stS:
        sbS = lambda n, s, d: stS.enter_context(nc.sbuf_tensor(n, s, d))
        kC = [sbS(f"kC{s}", [128, SEQ], BF16) for s in range(NSLOT)]
        kR = [sbS(f"kR{s}", [128, SEQ], BF16) for s in range(NSLOT)]
        vS = [sbS(f"vS{s}", [128, SEQ], BF16) for s in range(NSLOT)]

        with ExitStack() as stB:
            sb2 = lambda n, s, d: stB.enter_context(nc.sbuf_tensor(n, s, d))
            ps2s = lambda n, s, d: stB.enter_context(nc.psum_tensor(n, s, d))
            lat_x = [sb2(f"latx{c}", [128, 512], BF16) for c in range(KC)]
            lat_q = [sb2(f"latq{c}", [128, 512], BF16) for c in range(NQ)]
            lat_kv = [sb2(f"latkv{c}", [128, 512], BF16) for c in range(NKV)]
            cos_s = sb2("cos_s", [128, 512], F32)
            sin_s = sb2("sin_s", [128, 512], F32)
            krbf_b = [sb2(f"krbf{i}", [128, 512], BF16) for i in range(2)]
            vfm = sb2("vfm", [128, 512], BF16)
            qbf_b = [sb2(f"qbf{i}", [128, 512], BF16) for i in range(2)]
            t1_b = [sb2(f"t1_{i}", [128, 512], F32) for i in range(2)]
            t2_b = [sb2(f"t2_{i}", [128, 512], F32) for i in range(2)]
            ksq = sb2("ksq", [128, 512], BF16)
            qsq2 = sb2("qsq2", [128, 512], BF16)
            k2 = sb2("k2", [1, NSLOT], F32)
            k2t = sb2("k2t", [1, 1], F32)
            q2m = sb2("q2m", [1, NSLOT * NTS], F32)
            ps_a = [ps2s(f"ps_a{s}", [128, 512], F32) for s in range(NSLOT)]
            ps_rot_b = [ps2s(f"ps_rot{i}", [128, 512], F32) for i in range(2)]
            ps_vt = ps2s("ps_vt", [128, 512], BF16)
            ps_n = ps2s("ps_n", [1, 512], F32)

            p.wait("tensor", "w2d")

            p.op("vector", "ve", lambda v: v.memset(k2[:], 0.0))
            kr_done = q_done = kv_done = 0
            q_cv = {0: 0, 1: 0}
            q_dma = {0: 0, 1: 0}
            for ts in range(NTS):
                ts_ve0 = p.val("ve")
                ts_sd0 = p.val("sd")
                r = ts // 2
                cs = 512 * (ts % 2)
                rb = 4096 * r
                rbx = 2048 * r
                if ts == 0:
                    p.wait("sync", "cc", 1)
                lxs = "lx%d" % (ts % 2)
                lqs = "lq%d" % (ts % 2)
                lks = "lk%d" % (ts % 2)
                p.wait("sync", "pe", kr_done)
                for c in range(KC):
                    p.dma("sync", lat_x[c][:], l_all_x[rbx + 128 * c: rbx + 128 * (c + 1), cs:cs + 512], sem=lxs)
                p.wait("sync", "ve")
                ld = p.dma("sync", cos_s[:], cos_td[:, 512 * ts:512 * (ts + 1)], sem=lxs)
                ldx = p.dma("sync", sin_s[:], sin_td[:, 512 * ts:512 * (ts + 1)], sem=lxs)
                if ts == 0:
                    p.wait("sync", "cc", 2)
                p.wait("sync", "pe", q_done)
                for c in range(NQ):
                    ldq = p.dma("sync", lat_q[c][:], l_all_l[rbx + 512 + 128 * c: rbx + 512 + 128 * (c + 1), cs:cs + 512], sem=lqs)
                p.wait("sync", "pe", kv_done)
                for c in range(NKV):
                    ldkv = p.dma("sync", lat_kv[c][:], l_all_l[rbx + 128 * c: rbx + 128 * (c + 1), cs:cs + 512], sem=lks)
                p.wait("tensor", lxs, ldx)

                for s in range(NSLOT):
                    for k in range(KC):
                        p.op("tensor", "pe", lambda t, s=s, k=k: t.matmul(
                            ps_a[s][:], wkr_bf[:, 512 * k + 128 * s: 512 * k + 128 * (s + 1)], lat_x[k][:],
                            start=(k == 0), stop=(k == KC - 1)))
                kr_done = p.val("pe")
                p.wait("vector", lxs, ldx)
                kr_fin = {0: 0, 1: 0}
                kr_ev = 0
                for s in range(NSLOT):
                    par = s % 2
                    krbf = krbf_b[par]; t1 = t1_b[par]; t2 = t2_b[par]; ps_rot = ps_rot_b[par]
                    p.wait("scalar", "pe", kr_done - (NSLOT - 1 - s) * KC)
                    p.wait("scalar", "ve", ts_ve0)
                    p.wait("scalar", "qs%d" % par, q_dma[par])
                    p.wait("scalar", "ve", kr_fin[par])
                    ev = p.op("scalar", "ac", lambda sc, s=s, krbf=krbf: sc.activation(krbf[:], ps_a[s][:], AF.Identity, bias=bkr_sb[:, s:s + 1], scale=1.0))
                    kr_ev = ev
                    p.wait("tensor", "ac", ev)
                    p.wait("tensor", "ve", kr_fin[par])
                    rt = p.op("tensor", "pe", lambda t, krbf=krbf, ps_rot=ps_rot: t.matmul(ps_rot[:], rotm_bf[:], krbf[:], start=True, stop=True))
                    p.wait("vector", "ac", ev)
                    m1 = p.op("vector", "ve", lambda v, krbf=krbf, t1=t1: v.tensor_mul(t1[:], krbf[:], cos_s[:]))
                    p.wait("vector", "pe", rt)
                    m2 = p.op("vector", "ve", lambda v, t2=t2, ps_rot=ps_rot: v.tensor_mul(t2[:], ps_rot[:], sin_s[:]))
                    p.wait("vector", "ve", m2)
                    kr_fin[par] = p.op("vector", "ve", lambda v, s=s, ts=ts, t1=t1, t2=t2: v.tensor_add(kR[s][:, 512 * ts:512 * (ts + 1)], t1[:], t2[:]))
                p.wait("tensor", "ac", kr_ev)
                krfin_final = dict(kr_fin)

                p.wait("tensor", lqs, ldq)
                for s in range(NSLOT):
                    for k in range(NQ):
                        p.op("tensor", "pe", lambda t, s=s, k=k: t.matmul(
                            ps_a[s][:], wuq_bf[:, 512 * k + 128 * s: 512 * k + 128 * (s + 1)], lat_q[k][:],
                            start=(k == 0), stop=(k == NQ - 1)))
                q_done = p.val("pe")
                nrd_last = 0
                for s in range(NSLOT):
                    par = s % 2
                    qbf = qbf_b[par]; krbf = krbf_b[par]; t1 = t1_b[par]; t2 = t2_b[par]; ps_rot = ps_rot_b[par]
                    p.wait("scalar", "pe", q_done - (NSLOT - 1 - s) * NQ)
                    p.wait("scalar", "ve", ts_ve0)
                    p.wait("scalar", "qs%d" % par, q_dma[par])
                    p.wait("scalar", "ve", q_cv[par])
                    ev = p.op("scalar", "ac", lambda sc, s=s, qbf=qbf: sc.activation(qbf[:], ps_a[s][:], AF.Identity, bias=buq_sb[:, s:s + 1], scale=1.0))
                    q_last_ev = ev
                    p.wait("sync", "ac", ev)
                    p.dma("sync", q_dram[256 * s:256 * s + 128, 512 * ts:512 * (ts + 1)], qbf[:], sem="qs%d" % par)
                    p.wait("tensor", "ac", ev)
                    p.wait("tensor", "ve", q_cv[par])
                    p.wait("tensor", "ve", krfin_final[par])
                    rt = p.op("tensor", "pe", lambda t, qbf=qbf, ps_rot=ps_rot: t.matmul(ps_rot[:], rotm_bf[:], qbf[:], start=True, stop=True))
                    p.wait("vector", "ac", ev)
                    m1 = p.op("vector", "ve", lambda v, qbf=qbf, t1=t1: v.tensor_mul(t1[:], qbf[:], cos_s[:]))
                    p.wait("vector", "pe", rt)
                    m2 = p.op("vector", "ve", lambda v, t2=t2, ps_rot=ps_rot: v.tensor_mul(t2[:], ps_rot[:], sin_s[:]))
                    p.wait("vector", "ve", m2)
                    ad = p.op("vector", "ve", lambda v, t1=t1, t2=t2: v.tensor_add(t1[:], t1[:], t2[:]))
                    p.wait("vector", "ve", ad)
                    cv = p.op("vector", "ve", lambda v, krbf=krbf, t1=t1: v.tensor_copy(krbf[:], t1[:]))
                    q_cv[par] = cv
                    p.wait("sync", "ve", cv)
                    q_dma[par] = p.dma("sync", q_dram[256 * s + 128:256 * s + 256, 512 * ts:512 * (ts + 1)], krbf[:], sem="qs%d" % par)
                    p.wait("scalar", "ve", cv)
                    p.wait("scalar", "pe")
                    n1 = p.op("scalar", "ac", lambda sc, qbf=qbf: sc.activation(qsq2[:], qbf[:], AF.Square))
                    p.wait("tensor", "ac", n1)
                    p.wait("tensor", "ve", nrd_last)
                    p.op("tensor", "pe", lambda t: t.matmul(ps_n[:], ones_cb[:], qsq2[:], start=True, stop=False))
                    p.wait("scalar", "pe")
                    n2 = p.op("scalar", "ac", lambda sc, krbf=krbf: sc.activation(qsq2[:], krbf[:], AF.Square))
                    p.wait("tensor", "ac", n2)
                    nm = p.op("tensor", "pe", lambda t: t.matmul(ps_n[:], ones_cb[:], qsq2[:], start=False, stop=True))
                    p.wait("vector", "pe", nm)
                    nrd_last = p.op("vector", "ve", lambda v, s=s, ts=ts: v.tensor_reduce(q2m[:, 8 * s + ts:8 * s + ts + 1], ps_n[:], AX, ALU.max))
                p.wait("tensor", "ac", q_last_ev)

                p.wait("tensor", lks, ldkv)
                for s in range(NSLOT):
                    for k in range(NKV):
                        p.op("tensor", "pe", lambda t, s=s, k=k: t.matmul(
                            ps_a[s][:], wuk_bf[:, 512 * k + 128 * s: 512 * k + 128 * (s + 1)], lat_kv[k][:],
                            start=(k == 0), stop=(k == NKV - 1)))
                for s in range(NSLOT):
                    p.wait("scalar", "pe")
                    p.op("scalar", "ac", lambda sc, s=s, ts=ts: sc.activation(
                        kC[s][:, 512 * ts:512 * (ts + 1)], ps_a[s][:], AF.Identity, bias=buk_sb[:, s:s + 1], scale=1.0))
                p.wait("tensor", "ac")

                for s in range(NSLOT):
                    for k in range(NKV):
                        p.op("tensor", "pe", lambda t, s=s, k=k: t.matmul(
                            ps_a[s][:], wuv_bf[:, 512 * k + 128 * s: 512 * k + 128 * (s + 1)], lat_kv[k][:],
                            start=(k == 0), stop=(k == NKV - 1)))
                kv_done = p.val("pe")
                for s in range(NSLOT):
                    p.wait("scalar", "pe")
                    ev = p.op("scalar", "ac", lambda sc, s=s: sc.activation(vfm[:], ps_a[s][:], AF.Identity, bias=buv_sb[:, s:s + 1], scale=1.0))
                    p.wait("tensor", "ac", ev)
                    p.wait("tensor", "ve")
                    for j in range(4):
                        p.op("tensor", "pe", lambda t, j=j: t.transpose(ps_vt[:, 128 * j:128 * (j + 1)], vfm[:, 128 * j:128 * (j + 1)], ident_bf[:]))
                    p.wait("vector", "pe")
                    p.op("vector", "ve", lambda v, s=s, ts=ts: v.tensor_copy(vS[s][:, 512 * ts:512 * (ts + 1)], ps_vt[:]))
                    p.wait("scalar", "pe")
                p.wait("tensor", "ac")
                p.wait("scalar", "ve")
                for s in range(NSLOT):
                    p.wait("scalar", "pe")
                    sq1 = p.op("scalar", "ac", lambda sc, s=s, ts=ts: sc.activation(ksq[:], kC[s][:, 512 * ts:512 * (ts + 1)], AF.Square))
                    p.wait("tensor", "ac", sq1)
                    p.wait("tensor", "ve")
                    p.op("tensor", "pe", lambda t: t.matmul(ps_n[:], ones_cb[:], ksq[:], start=True, stop=False))
                    p.wait("scalar", "pe")
                    sq2 = p.op("scalar", "ac", lambda sc, s=s, ts=ts: sc.activation(ksq[:], kR[s][:, 512 * ts:512 * (ts + 1)], AF.Square))
                    p.wait("tensor", "ac", sq2)
                    mm = p.op("tensor", "pe", lambda t: t.matmul(ps_n[:], ones_cb[:], ksq[:], start=False, stop=True))
                    p.wait("vector", "pe", mm)
                    rd = p.op("vector", "ve", lambda v: v.tensor_reduce(k2t[:], ps_n[:], AX, ALU.max))
                    p.wait("vector", "ve", rd)
                    p.op("vector", "ve", lambda v, s=s: v.tensor_tensor(k2[:, s:s + 1], k2[:, s:s + 1], k2t[:], ALU.max))
                    p.wait("vector", "ve")
                p.wait("tensor", "ve")
                if serial:
                    p.barrier()

            p.op("vector", "ve", lambda v: v.tensor_copy(k2p[:], k2[:]))
            p.wait("vector", "ve")
            p.op("vector", "ve", lambda v: v.tensor_copy(q2mp[:], q2m[:]))
            p.barrier()

        with ExitStack() as stA2:
            sb3 = lambda n, s, d: stA2.enter_context(nc.sbuf_tensor(n, s, d))
            ps3s = lambda n, s, d: stA2.enter_context(nc.psum_tensor(n, s, d))
            qc_t = [sb3(f"qc_t{i}", [128, 512], BF16) for i in range(2)]
            qr_t = [sb3(f"qr_t{i}", [128, 512], BF16) for i in range(2)]
            prb = [sb3(f"prb{i}", [128, 512], BF16) for i in range(4)]
            ctxs = sb3("ctxs", [128, SEQ], BF16)
            craw = [sb3(f"craw{i}", [128, 512], F32) for i in range(2)]
            bias_all = sb3("bias_all", [128, NSLOT * NTS], F32)
            b2r = sb3("b2r", [1, NSLOT * NTS], F32)
            srow = [sb3(f"srow{i}", [1, 512], F32) for i in range(2)]
            rrow = [sb3(f"rrow{i}", [1, 512], F32) for i in range(2)]
            bc_sb = [sb3(f"bc_sb{i}", [128, 512], F32) for i in range(2)]
            ps_sc = [ps3s(f"ps_sc{i}", [128, 512], F32) for i in range(3)]
            ps_ctx = [ps3s(f"ps_ctx{i}", [128, 512], F32) for i in range(2)]
            ps_sum = [ps3s(f"ps_sum{i}", [1, 512], F32) for i in range(2)]
            ps_bc = ps3s("ps_bc", [128, 512], F32)

            p.wait("vector", "w2d")
            bb = p.op("vector", "ve", lambda v: v.tensor_tensor(
                b2r[:].rearrange("a (s t) -> a s t", s=NSLOT),
                q2mp[:].rearrange("a (s t) -> a s t", s=NSLOT),
                k2p[:].rearrange("a (s t) -> a s t", t=1).broadcast_to((1, NSLOT, NTS)),
                ALU.mult))
            p.wait("scalar", "ve", bb)
            bsq = p.op("scalar", "ac", lambda sc: sc.activation(b2r[:], b2r[:], AF.Sqrt))
            p.wait("tensor", "ac", bsq)
            bmm = p.op("tensor", "pe", lambda t: t.matmul(ps_bc[:, 0:NSLOT * NTS], ones_rf[:], b2r[:], start=True, stop=True))
            p.wait("scalar", "pe", bmm)
            bev = p.op("scalar", "ac", lambda sc: sc.activation(bias_all[:], ps_bc[:, 0:NSLOT * NTS], AF.Identity, scale=-0.0625))
            p.wait("scalar", "ac", bev)
            p.wait("tensor", "ac", bev)

            exp_cnt = {i: 0 for i in range(3)}
            cons_cnt = {i: 0 for i in range(4)}
            ctx_free = {0: 0, 1: 0}
            qtile_free = {0: 0, 1: 0}
            bc_free = 0
            LOOK = 2

            def load_q(s, qc):
                qp = qc % 2
                p.wait("sync", "pe", qtile_free[qp])
                p.dma("sync", qc_t[qp][:], q_dram[256 * s:256 * s + 128, 512 * qc:512 * (qc + 1)], sem="qt%d" % qp)
                return p.dma("sync", qr_t[qp][:], q_dram[256 * s + 128:256 * s + 256, 512 * qc:512 * (qc + 1)], sem="qt%d" % qp)

            qld = {(0, 0): load_q(0, 0)}
            for s in range(NSLOT):
                for qc in range(NTS):
                    qp = qc % 2
                    p.wait("tensor", "qt%d" % qp, qld[(s, qc)])
                    p.wait("tensor", "ve", ctx_free[qp])
                    bcol = bias_all[:, 8 * s + qc:8 * s + qc + 1]
                    nkt = 4 * (qc + 1)
                    pend = []
                    last_mm = 0

                    for ik in range(nkt):
                        kb, kt = divmod(ik, 4)
                        par = ik % 3
                        ring = ik % 4
                        diag = kb == qc
                        q0 = 128 * kt if diag else 0
                        if pend and len(pend) > LOOK - 1:
                            p.wait("tensor", "ac", pend[0][2])
                        elif exp_cnt[par]:
                            p.wait("tensor", "ac", exp_cnt[par])
                        if len(pend) > LOOK:
                            jk, jring, jex, jq0 = pend.pop(0)
                            p.op("tensor", "pe", lambda t, s=s, jk=jk, jring=jring, qp=qp, jq0=jq0, first=(jk == 0), last=(jk == nkt - 1): t.matmul(
                                ps_ctx[qp][:, jq0:512], vS[s][:, 128 * jk: 128 * (jk + 1)], prb[jring][:, jq0:512], start=first, stop=last))
                            cmm = p.op("tensor", "pe", lambda t, jring=jring, qp=qp, jq0=jq0, first=(jk == 0), last=(jk == nkt - 1): t.matmul(
                                ps_sum[qp][:, jq0:512], ones_cb[:], prb[jring][:, jq0:512], start=first, stop=last))
                            cons_cnt[jring] = cmm
                        p.op("tensor", "pe", lambda t, s=s, kb=kb, kt=kt, par=par, qp=qp, q0=q0: t.matmul(
                            ps_sc[par][:, q0:512], kC[s][:, 512 * kb + 128 * kt: 512 * kb + 128 * (kt + 1)], qc_t[qp][:, q0:512], start=True, stop=False))
                        mm = p.op("tensor", "pe", lambda t, s=s, kb=kb, kt=kt, par=par, qp=qp, q0=q0: t.matmul(
                            ps_sc[par][:, q0:512], kR[s][:, 512 * kb + 128 * kt: 512 * kb + 128 * (kt + 1)], qr_t[qp][:, q0:512], start=False, stop=True))
                        last_mm = mm
                        gate = ("pe", mm)
                        if diag:
                            p.wait("vector", "pe", mm)
                            mk = p.op("vector", "ve", lambda v, kt=kt, par=par, q0=q0: v.tensor_tensor(
                                ps_sc[par][:, q0:512], ps_sc[par][:, q0:512], masks[:, 512 * kt + q0:512 * (kt + 1)], ALU.add))
                            gate = ("ve", mk)
                        p.wait("scalar", gate[0], gate[1])
                        p.wait("scalar", "pe", cons_cnt[ring])
                        ex = p.op("scalar", "ac", lambda sc, par=par, ring=ring, bcol=bcol, q0=q0: sc.activation(
                            prb[ring][:, q0:512], ps_sc[par][:, q0:512], AF.Exp, bias=bcol, scale=0.0625))
                        exp_cnt[par] = ex
                        pend.append((ik, ring, ex, q0))
                    nqc = qc + 1
                    ns = s
                    if nqc == NTS:
                        nqc = 0
                        ns = s + 1
                    if ns < NSLOT:
                        qld[(ns, nqc)] = load_q(ns, nqc)
                    for jk, jring, jex, jq0 in pend:
                        p.wait("tensor", "ac", jex)
                        p.op("tensor", "pe", lambda t, s=s, jk=jk, jring=jring, qp=qp, jq0=jq0, first=(jk == 0), last=(jk == nkt - 1): t.matmul(
                            ps_ctx[qp][:, jq0:512], vS[s][:, 128 * jk: 128 * (jk + 1)], prb[jring][:, jq0:512], start=first, stop=last))
                        cmm = p.op("tensor", "pe", lambda t, jring=jring, qp=qp, jq0=jq0, first=(jk == 0), last=(jk == nkt - 1): t.matmul(
                            ps_sum[qp][:, jq0:512], ones_cb[:], prb[jring][:, jq0:512], start=first, stop=last))
                        cons_cnt[jring] = cmm
                    qtile_free[qp] = last_mm
                    p.wait("vector", "pe")
                    e1 = p.op("vector", "ve", lambda v, qp=qp: v.tensor_copy(srow[qp][:], ps_sum[qp][:]))
                    cr = p.op("vector", "ve", lambda v, qp=qp: v.tensor_copy(craw[qp][:], ps_ctx[qp][:]))
                    p.wait("vector", "ve", e1)
                    e2 = p.op("vector", "ve", lambda v, qp=qp: v.reciprocal(rrow[qp][:], srow[qp][:]))
                    p.wait("tensor", "ve", bc_free)
                    p.wait("tensor", "ve", e2)
                    bm = p.op("tensor", "pe", lambda t, qp=qp: t.matmul(ps_bc[:], ones_rf[:], rrow[qp][:], start=True, stop=True))
                    p.wait("vector", "pe", bm)
                    e3 = p.op("vector", "ve", lambda v, qp=qp: v.tensor_copy(bc_sb[qp][:], ps_bc[:]))
                    bc_free = e3
                    p.wait("vector", "ve", e3)
                    mf = p.op("vector", "ve", lambda v, qc=qc, qp=qp: v.tensor_mul(ctxs[:, 512 * qc:512 * (qc + 1)], craw[qp][:], bc_sb[qp][:]))
                    ctx_free[qp] = cr
                    if serial:
                        p.barrier()
                p.wait("sync", "ve")
                dst = ctx_shardA[128 * s:128 * (s + 1), :] if s < 2 else ctx_shardB[128 * (s - 2):128 * (s - 1), :]
                p.dma("sync", dst, ctxs[:], sem="cxs")
                p.wait("vector", "cxs")
                if s == 1:
                    p.wait("gpsimd", "cxs")
                    p.op("gpsimd", "cc", lambda g: g.collective_compute(
                        "AllGather", ALU.bypass, replica_groups=GROUPS, ins=[ctx_shardA.opt()], outs=[ctx_allA.opt()]))

            p.wait("gpsimd", "cxs")
            p.op("gpsimd", "cc", lambda g: g.collective_compute(
                "AllGather", ALU.bypass, replica_groups=GROUPS, ins=[ctx_shardB.opt()], outs=[ctx_allB.opt()]))

    if "t_qdram" in tap_aps:
        p.wait("gpsimd", "cc")
        p.dma("gpsimd", tap_aps["t_qdram"][:, :], q_dram)
    if "t_ctx" in tap_aps:
        p.wait("gpsimd", "cc")
        for r in range(4):
            for s2 in range(2):
                p.dma("gpsimd", tap_aps["t_ctx"][512 * r + 128 * s2:512 * r + 128 * (s2 + 1), :], ctx_allA[256 * r + 128 * s2:256 * r + 128 * (s2 + 1), :])
                p.dma("gpsimd", tap_aps["t_ctx"][512 * r + 128 * (s2 + 2):512 * r + 128 * (s2 + 3), :], ctx_allB[256 * r + 128 * s2:256 * r + 128 * (s2 + 1), :])
    if phases == 2:
        p.barrier()
        finalize(nc, p)
        st.close()
        return nc

    w2_st.close()
    p.wait("sync", "pe")
    p.wait("sync", "ve")
    p.wait("sync", "ac")
    p.wait("sync", "gd")
    with ExitStack() as stA:
        sbA = lambda n, s, d: stA.enter_context(nc.sbuf_tensor(n, s, d))
        bo_row = sbA("bo_row", [1, D], F32)
        ds_row = sbA("ds_row", [1, D], F32)
        ds_bc = sbA("ds_bc", [128, D], F32)
        xn2 = sbA("xn2", [128, KC * TPC], BF16)
        bn6b = [sbA(f"bn6b{i}", [128, 24], F32) for i in range(2)]
        mvb = [sbA(f"mvb{i}", [128, 2], F32) for i in range(2)]
        stdb = [sbA(f"stdb{i}", [128, 1], F32) for i in range(2)]
        rstdb = [sbA(f"rstdb{i}", [128, 1], F32) for i in range(2)]
        nbb = [sbA(f"nbb{i}", [128, 1], F32) for i in range(2)]

        p.dma("sync", bo_row[:], bo_r)
        p.dma("sync", ds_row[:], ds_r)

        with ExitStack() as stB3:
            sbB = lambda n, s, d: stB3.enter_context(nc.sbuf_tensor(n, s, d))
            psB = lambda n, s, d: stB3.enter_context(nc.psum_tensor(n, s, d))
            wo_bf = sbB("wo_bf", [128, KC * D], BF16)
            cx = sbB("cx", [128, KC * TPC], BF16)
            bo_bc = sbB("bo_bc", [128, D], F32)
            xt3 = [sbB(f"xt3{i}", [128, D], F32) for i in range(2)]
            y_tb = [sbB(f"y_tb{i}", [128, D], F32) for i in range(2)]
            yn_tb1 = sbB("yn_tb1", [128, D], F32)
            yn_tb = [yn_tb1, yn_tb1]
            xn2_tm1 = sbB("xn2_tm1", [128, D], BF16)
            xn2_tm = [xn2_tm1, xn2_tm1]
            ps_o = [psB(f"ps_o{i}", [128, TPC], F32) for i in range(2)]
            ps_tr3 = psB("ps_tr3", [128, 512], BF16)
            ps_bb = psB("ps_bb", [128, 512], F32)

            p.dma("gpsimd", wo_bf[:].rearrange("p (c n) -> p c n", c=KC), wo.rearrange("(c p) n -> p c n", p=128), sem="wod")
            p.wait("gpsimd", "sd")
            p.wait("gpsimd", "cc", 3)

            def ctx_reads_a(g):
                last = None
                with g.register("tok") as tok:
                    g.reg_load(tok, meta_sb[0:1, 0:1])
                    off = g.snap(tok, min_val=0, max_val=3 * TPC)
                    for r in range(4):
                        for s2 in range(2):
                            ch = 4 * r + s2
                            last = g.dma_start(out=cx[:, TPC * ch:TPC * (ch + 1)],
                                               in_=ctx_allA[256 * r + 128 * s2:256 * r + 128 * (s2 + 1), ds(off, TPC)])
                            last.then_inc(SEM["wod"], 16)
                return last
            p.emit("gpsimd", ctx_reads_a)
            p.bump("wod", 16 * 8)

            def ctx_reads_b(g):
                last = None
                with g.register("tok2") as tok:
                    g.reg_load(tok, meta_sb[0:1, 0:1])
                    off = g.snap(tok, min_val=0, max_val=3 * TPC)
                    for r in range(4):
                        for s2 in range(2):
                            ch = 4 * r + s2 + 2
                            last = g.dma_start(out=cx[:, TPC * ch:TPC * (ch + 1)],
                                               in_=ctx_allB[256 * r + 128 * s2:256 * r + 128 * (s2 + 1), ds(off, TPC)])
                            last.then_inc(SEM["wodb"], 16)
                return last
            p.wait("gpsimd", "cc")
            p.emit("gpsimd", ctx_reads_b)
            p.bump("wodb", 16 * 8)

            p.wait("tensor", "sd")
            for h in range(4):
                p.wait("tensor", "ve")
                bm = p.op("tensor", "pe", lambda t, h=h: t.matmul(ps_bb[:], ones_rf[:], bo_row[:, 512 * h:512 * (h + 1)], start=True, stop=True))
                p.wait("vector", "pe", bm)
                p.op("vector", "ve", lambda v, h=h: v.tensor_copy(bo_bc[:, 512 * h:512 * (h + 1)], ps_bb[:]))
                p.wait("tensor", "ve")
                bm = p.op("tensor", "pe", lambda t, h=h: t.matmul(ps_bb[:], ones_rf[:], ds_row[:, 512 * h:512 * (h + 1)], start=True, stop=True))
                p.wait("vector", "pe", bm)
                p.op("vector", "ve", lambda v, h=h: v.tensor_copy(ds_bc[:, 512 * h:512 * (h + 1)], ps_bb[:]))

            if "t_cxv" in tap_aps:
                p.wait("gpsimd", "wod")
                p.wait("gpsimd", "wodb")
                p.dma("gpsimd", tap_aps["t_cxv"][:, :], cx[:])
            p.wait("tensor", "wod")
            chunk_order = [4 * r + s2 for s2 in range(2) for r in range(4)] + [4 * r + s2 + 2 for s2 in range(2) for r in range(4)]
            tr_ev = 0
            oev = {0: 0, 1: 0}
            pend_tr = []
            tr_state = {"ev": 0}

            def emit_tr3():
                nonlocal tr_ev
                while pend_tr:
                    ti, tn3 = pend_tr.pop(0)
                    p.wait("tensor", "ac", tn3)
                    for j4 in range(4):
                        p.wait("tensor", "ve", tr_ev)
                        for j in range(4):
                            c = j4 * 4 + j
                            p.op("tensor", "pe", lambda t, c=c, j=j, ti=ti: t.transpose(
                                ps_tr3[:, 128 * j:128 * (j + 1)], xn2_tm[ti % 2][:, 128 * c:128 * (c + 1)], ident_bf[:]))
                        p.wait("vector", "pe")
                        def evac3(v, ti=ti, j4=j4):
                            dv = xn2[:, :].rearrange("p (c n) -> p c n", n=TPC)
                            return v.tensor_copy(dv[:, j4 * 4:(j4 + 1) * 4, 128 * ti:128 * (ti + 1)],
                                                 ps_tr3[:].rearrange("p (c n) -> p c n", n=128))
                        tr_ev = p.op("vector", "ve", evac3)
            for i in range(NT):
                p.wait("sync", "ve")
                xl = p.dma("sync", xt3[i % 2][:], x_in[128 * i:128 * (i + 1), :], sem="xd%d" % (i % 2))
                for oh in range(2):
                    p.wait("tensor", "ve", oev[oh])
                    if oh == 1:
                        emit_tr3()
                    for ci, c in enumerate(chunk_order):
                        if i == 0 and ci == 8:
                            p.wait("tensor", "wodb")
                        for h in range(2):
                            p.op("tensor", "pe", lambda t, c=c, h=h, i=i, oh=oh, ci=ci: t.matmul(
                                ps_o[oh][:, 512 * h:512 * (h + 1)],
                                cx[:, TPC * c + 128 * i: TPC * c + 128 * (i + 1)],
                                wo_bf[:, D * c + TPC * oh + 512 * h: D * c + TPC * oh + 512 * (h + 1)],
                                start=(ci == 0), stop=(ci == KC - 1)))
                    p.wait("vector", "pe")
                    p.wait("vector", "ac")
                    oev[oh] = p.op("vector", "ve", lambda v, i=i, oh=oh: v.tensor_add(
                        y_tb[i % 2][:, TPC * oh:TPC * (oh + 1)], ps_o[oh][:], bo_bc[:, TPC * oh:TPC * (oh + 1)]))
                p.wait("vector", "ve")
                n2 = ln_chain(p, y_tb[i % 2], bn6b[i % 2], mvb[i % 2], stdb[i % 2], rstdb[i % 2], nbb[i % 2], eps_c,
                              lambda s, i=i: s.activation(yn_tb[i % 2][:], y_tb[i % 2][:], AF.Identity, bias=nbb[i % 2][:], scale=rstdb[i % 2][:]))
                p.wait("vector", "ac", n2)
                p.wait("vector", "xd%d" % (i % 2), xl)
                x2a = p.op("vector", "ve", lambda v, b=xt3[i % 2], i=i: v.tensor_add(b[:], b[:], yn_tb[i % 2][:]))
                p.wait("sync", "ve", x2a)
                p.dma("sync", x2_dram[128 * i:128 * (i + 1), :], xt3[i % 2][:], sem="x2d")
                p.wait("scalar", "ve", tr_ev)
                n3 = ln_chain(p, xt3[i % 2], bn6b[i % 2], mvb[i % 2], stdb[i % 2], rstdb[i % 2], nbb[i % 2], eps_c,
                              lambda s, b=xt3[i % 2], i=i: s.activation(xn2_tm[i % 2][:], b[:], AF.Identity, bias=nbb[i % 2][:], scale=rstdb[i % 2][:]))
                pend_tr.append((i, n3))
                if serial:
                    emit_tr3()
                    p.barrier()

            emit_tr3()
        p.barrier()
        with ExitStack() as stC:
            sbC = lambda n, s, d: stC.enter_context(nc.sbuf_tensor(n, s, d))
            psC = lambda n, s, d: stC.enter_context(nc.psum_tensor(n, s, d))
            acc = sbC("acc", [128, NT * D], F32)
            wg_bf = [sbC(f"wg{i}", [128, KC * 128], BF16) for i in range(2)]
            wu_bf = [sbC(f"wu{i}", [128, KC * 128], BF16) for i in range(2)]
            wd_bf = [[sbC(f"wd{g}_{i}", [128, D], BF16) for i in range(GRP)] for g in range(2)]
            hdn = [sbC(f"hdn{i}", [128, TPC], BF16) for i in range(GRP)]
            silu_g = sbC("silu_g", [128, TPC], BF16)
            gs_sb = sbC("gs_sb", [128, IC], F32)
            us_sb = sbC("us_sb", [128, IC], F32)
            xt41 = sbC("xt41", [128, D], F32)
            xt4 = [xt41, xt41]
            y41 = sbC("y41", [128, D], F32)
            y4 = [y41, y41]
            yn41 = sbC("yn41", [128, D], F32)
            yn4 = [yn41, yn41]
            ps_g = psC("ps_g", [128, TPC], F32)
            ps_u = psC("ps_u", [128, TPC], F32)
            ps_d = [psC(f"ps_d{i}", [128, TPC], F32) for i in range(2)]

            p.dma("sync", gs_sb[:], gs)
            p.dma("sync", us_sb[:], us)
            p.op("vector", "ve", lambda v: v.memset(acc[:], 0.0))
            p.wait("tensor", "ve")
            p.wait("tensor", "sd")

            groups = [list(range(g, min(g + GRP, IC))) for g in range(0, IC, GRP)]
            dn_done = {0: 0, 1: 0}
            hv_done = {0: 0, 1: 0}
            for gi, grp in enumerate(groups):
                gp = gi % 2
                for ii, icx in enumerate(grp):
                    p.wait("gpsimd", "ve", hv_done[ii % 2])
                    p.wait("gpsimd", "pe", dn_done[gp])
                    wfs = "wf%d" % (ii % 2)
                    p.dma("gpsimd", wg_bf[ii % 2][:].rearrange("p (c n) -> p c n", c=KC),
                          gwt[:, 128 * icx:128 * (icx + 1)].rearrange("(c p) n -> p c n", p=128), sem=wfs)
                    p.dma("gpsimd", wu_bf[ii % 2][:].rearrange("p (c n) -> p c n", c=KC),
                          upt[:, 128 * icx:128 * (icx + 1)].rearrange("(c p) n -> p c n", p=128), sem=wfs)
                    wld = p.dma("gpsimd", wd_bf[gp][ii][:], dwt[128 * icx:128 * (icx + 1), :], sem=wfs)
                    p.wait("tensor", wfs, wld)
                    p.wait("tensor", "ve", hv_done[ii % 2])
                    for k in range(KC):
                        for th in range(2):
                            p.op("tensor", "pe", lambda t, k=k, th=th, ii=ii: t.matmul(
                                ps_g[:, 512 * th:512 * (th + 1)], wg_bf[ii % 2][:, 128 * k:128 * (k + 1)],
                                xn2[:, TPC * k + 512 * th: TPC * k + 512 * (th + 1)],
                                start=(k == 0), stop=(k == KC - 1)))
                    for k in range(KC):
                        for th in range(2):
                            p.op("tensor", "pe", lambda t, k=k, th=th, ii=ii: t.matmul(
                                ps_u[:, 512 * th:512 * (th + 1)], wu_bf[ii % 2][:, 128 * k:128 * (k + 1)],
                                xn2[:, TPC * k + 512 * th: TPC * k + 512 * (th + 1)],
                                start=(k == 0), stop=(k == KC - 1)))
                    p.wait("scalar", "pe")
                    p.wait("scalar", "ve", hv_done[ii % 2])
                    sl = p.op("scalar", "ac", lambda s, icx=icx: s.activation(silu_g[:], ps_g[:], AF.Silu, scale=gs_sb[:, icx:icx + 1]))
                    p.wait("vector", "ac", sl)
                    p.wait("vector", "pe")
                    hv = p.op("vector", "ve", lambda v, ii=ii, icx=icx: v.scalar_tensor_tensor(
                        hdn[ii][:], ps_u[:], us_sb[:, icx:icx + 1], silu_g[:], ALU.mult, ALU.mult))
                    hv_done[ii % 2] = hv
                    p.wait("tensor", "ve", hv)
                dev = {0: 0, 1: 0}
                for tt in range(NT):
                    for oh in range(2):
                        dp = oh
                        p.wait("tensor", "ve", dev[dp])
                        for ii in range(len(grp)):
                            for th in range(2):
                                p.op("tensor", "pe", lambda t, ii=ii, tt=tt, th=th, oh=oh, dp=dp, gp=gp, first=(ii == 0), last=(ii == len(grp) - 1): t.matmul(
                                    ps_d[dp][:, 512 * th:512 * (th + 1)], hdn[ii][:, 128 * tt:128 * (tt + 1)],
                                    wd_bf[gp][ii][:, TPC * oh + 512 * th: TPC * oh + 512 * (th + 1)],
                                    start=first, stop=last))
                        p.wait("vector", "pe")
                        dev[dp] = p.op("vector", "ve", lambda v, tt=tt, oh=oh, dp=dp: v.tensor_add(
                            acc[:, D * tt + TPC * oh:D * tt + TPC * (oh + 1)], acc[:, D * tt + TPC * oh:D * tt + TPC * (oh + 1)], ps_d[dp][:]))
                dn_done[gp] = p.val("pe")
                if serial:
                    p.barrier()

            if "t_xn2" in tap_aps:
                p.barrier()
                p.dma("gpsimd", tap_aps["t_xn2"][:, :], xn2[:])
            if "t_acc" in tap_aps:
                p.barrier()
                p.dma("sync", tap_aps["t_acc"][:, :], acc[:])
            if "t_x2" in tap_aps:
                p.barrier()
                p.dma("sync", tap_aps["t_x2"][:, :], x2_dram)
            p.wait("sync", "x2d")
            for i in range(NT):
                p.wait("sync", "ve")
                xl = p.dma("sync", xt4[i % 2][:], x2_dram[128 * i:128 * (i + 1), :], sem="xd%d" % (i % 2))
                p.wait("vector", "ac")
                ym = p.op("vector", "ve", lambda v, i=i: v.tensor_mul(y4[i % 2][:], acc[:, D * i:D * (i + 1)], ds_bc[:]))
                p.wait("vector", "ve", ym)
                n4 = ln_chain(p, y4[i % 2], bn6b[i % 2], mvb[i % 2], stdb[i % 2], rstdb[i % 2], nbb[i % 2], eps_c,
                              lambda s, i=i: s.activation(yn4[i % 2][:], y4[i % 2][:], AF.Identity, bias=nbb[i % 2][:], scale=rstdb[i % 2][:]))
                p.wait("vector", "ac", n4)
                p.wait("vector", "xd%d" % (i % 2), xl)
                fo = p.op("vector", "ve", lambda v, i=i: v.tensor_add(xt4[i % 2][:], xt4[i % 2][:], yn4[i % 2][:]))
                p.wait("sync", "ve", fo)
                p.dma("sync", out[128 * i:128 * (i + 1), :], xt4[i % 2][:])
                if serial:
                    p.barrier()

    finalize(nc, p)
    st.close()
    return nc


def finalize(nc, p):
    p.barrier()
    with nc.Block() as block:
        @block.sync
        def _(e):
            for f in p.q["sync"]:
                f(e)

        @block.gpsimd
        def _(e):
            for f in p.q["gpsimd"]:
                f(e)

        @block.scalar
        def _(e):
            for f in p.q["scalar"]:
                f(e)

        @block.vector
        def _(e):
            for f in p.q["vector"]:
                f(e)

        @block.tensor
        def _(e):
            for f in p.q["tensor"]:
                f(e)


def make_consts():
    ident = np.eye(128, dtype=np.float32)
    rotm = np.zeros((128, 128), np.float32)
    for j in range(64):
        rotm[2 * j + 1, 2 * j] = -1.0
        rotm[2 * j, 2 * j + 1] = 1.0
    dd = np.arange(128) // 2
    inv = 1.0 / ROPE_BASE ** (dd.astype(np.float64) * 2.0 / 128.0)
    tpos = np.arange(SEQ, dtype=np.float64)
    ang = inv[:, None] * tpos[None, :]
    cos_t = np.cos(ang).astype(np.float32)
    sin_t = np.sin(ang).astype(np.float32)
    kk = np.arange(128)[:, None]
    qq = np.arange(512)[None, :]
    mask = np.concatenate([np.where(kt * 128 + kk > qq, NEG, 0.0) for kt in range(4)], axis=1).astype(np.float32)
    return dict(identc=ident, rotmc=rotm, cos_t=cos_t, sin_t=sin_t, mask_t=mask,
                ones_c=np.ones((128, 1), np.float32), ones_r=np.ones((1, 128), np.float32),
                eps_cd=np.full((128, 1), 1e-5, np.float32))


def cols(v, n):
    return np.ascontiguousarray(np.asarray(v, np.float32).reshape(n, 128).T)


def make_inmaps(inp):
    x = np.asarray(inp["L0_attn_h"], np.float32)
    consts = make_consts()
    g = lambda k: np.ascontiguousarray(np.asarray(inp[k], np.float32))
    wdkv_f = g("L0_attn_W_DKV"); bdkv_f = cols(inp["L0_attn_b_DKV"], NKV)
    wdq_f = g("L0_attn_W_DQ"); bdq_f = cols(inp["L0_attn_b_DQ"], NQ)
    wkr_f = g("L0_attn_W_KR"); bkr_f = np.asarray(inp["L0_attn_b_KR"], np.float32)
    wuk_f = g("L0_attn_W_UK"); buk_f = np.asarray(inp["L0_attn_b_UK"], np.float32)
    wuv_f = g("L0_attn_W_UV"); buv_f = np.asarray(inp["L0_attn_b_UV"], np.float32)
    wuq_f = g("L0_attn_W_UQ"); buq_f = np.asarray(inp["L0_attn_b_UQ"], np.float32)
    wo_f = g("L0_attn_W_O"); bo_f = np.asarray(inp["L0_attn_b_O"], np.float32).reshape(1, D)
    gwt_f = np.ascontiguousarray(g("L0_ffn_gate_w").T)
    upt_f = np.ascontiguousarray(g("L0_ffn_up_w").T)
    dwt_f = np.ascontiguousarray(g("L0_ffn_down_w").T)
    gs_f = cols(inp["L0_ffn_gate_s"], IC)
    us_f = cols(inp["L0_ffn_up_s"], IC)
    ds_f = np.asarray(inp["L0_ffn_down_s"], np.float32).reshape(1, D)

    maps = []
    for c in range(8):
        b, m = divmod(c, 4)
        sl = slice(512 * m, 512 * (m + 1))
        maps.append(dict(
            x=np.ascontiguousarray(x[b, TPC * m:TPC * (m + 1), :]),
            meta=np.array([[TPC * m]], np.int32),
            wdkv=wdkv_f, bdkv=bdkv_f, wdq=wdq_f, bdq=bdq_f,
            wkr=np.ascontiguousarray(wkr_f[:, sl]), bkr=cols(bkr_f[sl], NSLOT),
            wuk=np.ascontiguousarray(wuk_f[:, sl]), buk=cols(buk_f[sl], NSLOT),
            wuv=np.ascontiguousarray(wuv_f[:, sl]), buv=cols(buv_f[sl], NSLOT),
            wuq=np.ascontiguousarray(wuq_f[:, sl]), buq=cols(buq_f[sl], NSLOT),
            wo=wo_f, bo_r=bo_f, gwt=gwt_f, upt=upt_f, dwt=dwt_f,
            gs=gs_f, us=us_f, ds_r=ds_f,
            **consts,
        ))
    return maps


def kernel(**inp):
    nc = build_nc(phases=3)
    maps = make_inmaps(inp)
    res = run_bass_kernel_spmd(nc, maps, core_ids=list(range(8)))
    outv = np.empty((2, SEQ, D), np.float32)
    for c in range(8):
        b, m = divmod(c, 4)
        outv[b, TPC * m:TPC * (m + 1), :] = res.results[c]["out"]
    return outv
